# version 03
# baseline (speedup 1.0000x reference)
import numpy as np
import concourse.bass as bass
import concourse.mybir as mybir
from concourse.bass import IndirectOffsetOnAxis
from concourse.bass_utils import run_bass_kernel_spmd

F32 = mybir.dt.float32
BF16 = mybir.dt.bfloat16
U32 = mybir.dt.uint32
AF = mybir.ActivationFunctionType
ALU = mybir.AluOpType
AX = mybir.AxisListType

MAGIC = 1.5 * 2 ** 23
NCORES = 8
B_PER_CORE = 2
T = 2048
D = 1024
NT = T // 128
ND = D // 128
HEADS = 16
HD = 64
E_EXP = 16
CAP = 256
F_EXP = 2048
F_SH = 2048
H_FFN = 4096
LN_EPS = 1e-5
TWO_PI_HI = np.float32(6.2831855)
TWO_PI_LO = np.float32(2 * np.pi - np.float64(np.float32(6.2831855)))
BIG_IDX = 1.0e7

ENGINES = ("sync", "vector", "scalar", "tensor", "gpsimd")


class Prog:

    def __init__(self, nc):
        self.nc = nc
        self.ops = []
        self.last_write = {}
        self.readers = {}
        self.out_dmas = []

    def op(self, engine, fn, reads=(), writes=(), inc=1, tag="", relax=()):
        idx = len(self.ops)
        deps = {}
        for d in getattr(self, "_fence_deps", ()):
            deps.setdefault(d, set())
        for k in reads:
            w = self.last_write.get(k)
            if w is not None:
                deps.setdefault(w, set()).add(k)
        for k in writes:
            w = self.last_write.get(k)
            if w is not None:
                deps.setdefault(w, set()).add(k)
            for rd in self.readers.get(k, ()):
                deps.setdefault(rd, set()).add(k)
        self.ops.append(dict(engine=engine, fn=fn, inc=inc, idx=idx,
                             deps=deps, need_inc=False, tag=tag,
                             relax=frozenset(relax)))
        for k in reads:
            self.readers.setdefault(k, []).append(idx)
        for k in writes:
            self.last_write[k] = idx
            self.readers[k] = []
        return idx

    def dma(self, queue, out, in_, reads=(), writes=(), tag="", is_output=False):
        idx = self.op(queue, lambda e: e.dma_start(out=out, in_=in_),
                      reads=reads, writes=writes, inc=16, tag=tag)
        if is_output:
            self.out_dmas.append(idx)
        return idx

    def dma_transpose(self, queue, out, in_, reads=(), writes=(), tag=""):
        return self.op(queue, lambda e: e.dma_start_transpose(out, in_),
                       reads=reads, writes=writes, inc=16, tag=tag)

    def fence(self):
        start = getattr(self, "_fence_start", 0)
        prior = set(range(start, len(self.ops)))
        markers = []
        for eng in ENGINES:
            i = self.op(eng, lambda e: e.nop(), tag="fence")
            self.ops[i]["deps"] = {d: set() for d in prior}
            markers.append(i)
        self._fence_deps = markers
        self._fence_start = len(self.ops)

    NSLOT = 6
    SERIALIZE = False

    DRIVER_MODE = False
    NSEM_DRIVER = 1
    DRIVER_Q = "sync"
    ALL_INC = False
    NO_DEDUP = False
    TENSOR_EDGES = False
    OWN_EDGES = True

    def finalize(self, block, sems, dma_sems):
        if Prog.DRIVER_MODE:
            nsem = Prog.NSEM_DRIVER
            gsems = [dma_sems[Prog.DRIVER_Q][i] for i in range(nsem)]
            cums = []
            tot = [0] * nsem
            for i, op in enumerate(self.ops):
                cums.append(tuple(tot))
                tot[i % nsem] += op["inc"]
                op["dsem"] = i % nsem

            def section(eng_name, eng):
                for op, before in zip(self.ops, cums):
                    if op["engine"] != eng_name:
                        continue
                    for s in range(nsem):
                        if before[s]:
                            eng.wait_ge(gsems[s], before[s])
                    inst = op["fn"](eng)
                    inst.then_inc(gsems[op["dsem"]], op["inc"])
                if eng_name == "sync":
                    for s in range(nsem):
                        if tot[s]:
                            eng.wait_ge(gsems[s], tot[s])
            for name in ENGINES:
                deco = getattr(block, name)

                def make(n):
                    def body(eng):
                        section(n, eng)
                    return body
                deco(make(name))
            return

        if Prog.SERIALIZE:
            for op in self.ops:
                op["deps"] = {i: set() for i in range(op["idx"])}
        if Prog.ALL_INC:
            for op in self.ops:
                op["need_inc"] = True
        """sems: per-engine compute sems. dma_sems: dict queue ->
        list of NSLOT slot semaphores (queues that issue DMAs)."""
        ops = self.ops
        slot_counts = {}
        for op in ops:
            if op["inc"] == 16:
                q = op["engine"]
                cnt = slot_counts.setdefault(q, [0] * Prog.NSLOT)
                seq = sum(cnt)
                s = seq % Prog.NSLOT
                cnt[s] += 1
                op["slot"] = s
                op["slot_ord"] = cnt[s]
                op["need_inc"] = True
        def edge_needed(op, d, via):
            dop = ops[d]
            if dop["engine"] != op["engine"]:
                return True
            if not Prog.OWN_EDGES:
                return False
            if op["engine"] == "tensor" and dop["engine"] == "tensor":
                if via and via <= op["relax"]:
                    return False
            return True

        for op in ops:
            for d, via in op["deps"].items():
                dop = ops[d]
                if dop["inc"] != 1:
                    continue
                if edge_needed(op, d, via):
                    dop["need_inc"] = True

        cum = {e: 0 for e in ENGINES}
        for op in ops:
            if op["inc"] == 1 and op["need_inc"]:
                cum[op["engine"]] += 1
            op["cum_after"] = cum[op["engine"]]

        clocks = {e: {} for e in ENGINES}

        def want(op, key, v):
            eng = op["engine"]
            if not Prog.NO_DEDUP and clocks[eng].get(key, 0) >= v:
                return
            if Prog.NO_DEDUP and op["waits"].get(key, 0) >= v:
                return
            op["waits"][key] = max(op["waits"].get(key, 0), v)
            clocks[eng][key] = v

        for op in ops:
            eng = op["engine"]
            op["waits"] = {}
            for d in sorted(op["deps"]):
                dop = ops[d]
                via = op["deps"][d]
                if dop["inc"] == 16:
                    key = ("dma", dop["engine"], dop["slot"])
                    want(op, key, 16 * dop["slot_ord"])
                elif edge_needed(op, d, via):
                    want(op, ("eng", dop["engine"]), dop["cum_after"])
            if op["inc"] == 16 and op["slot_ord"] > 1:
                want(op, ("dma", op["engine"], op["slot"]),
                     16 * (op["slot_ord"] - 1))

        final_waits = {}
        for idx in self.out_dmas:
            op = ops[idx]
            key = ("dma", op["engine"], op["slot"])
            final_waits[key] = max(final_waits.get(key, 0), 16 * op["slot_ord"])

        def sem_of(key):
            if key[0] == "dma":
                return dma_sems[key[1]][key[2]]
            return sems[key[1]]

        def section(eng_name, eng):
            for op in ops:
                if op["engine"] != eng_name:
                    continue
                for key in sorted(op["waits"], key=str):
                    eng.wait_ge(sem_of(key), op["waits"][key])
                inst = op["fn"](eng)
                if op["need_inc"]:
                    if op["inc"] == 16:
                        inst.then_inc(dma_sems[eng_name][op["slot"]], 16)
                    else:
                        inst.then_inc(sems[eng_name], 1)
            if eng_name == "sync":
                for key in sorted(final_waits, key=str):
                    eng.wait_ge(sem_of(key), final_waits[key])

        for name in ENGINES:
            deco = getattr(block, name)

            def make(n):
                def body(eng):
                    section(n, eng)
                return body
            deco(make(name))


class Mem:

    COLS = 106016

    def __init__(self, nc, stack):
        en = stack.enter_context
        self.arena = en(nc.sbuf_tensor("arena", [128, Mem.COLS], BF16))
        self.a16 = self.arena[:]
        self.a32 = self.arena[:].bitcast(F32)
        self.au32 = self.arena[:].bitcast(U32)
        self.p8 = en(nc.psum_tensor("pp8", [128, 2048], F32))
        self.p4 = [en(nc.psum_tensor(f"pp4_{i}", [128, 512], F32))
                   for i in range(3)]
        self.pB = en(nc.psum_tensor("ppB", [128, 1024], BF16))
        self.reset()

    def reset(self):
        self.off = 0
        self.ps_off = 0
        self.ps_i = 0

    def sb(self, name, shape, dtype):
        p, n = shape
        width = {BF16: 1, F32: 2, U32: 2}[dtype] * n
        width = (width + 31) & ~31
        assert self.off + width <= Mem.COLS, \
            f"SBUF arena overflow at {name}: {self.off}+{width}"
        base = self.off
        self.off += width
        if dtype == BF16:
            return self.a16[0:p, base:base + n]
        if dtype == F32:
            return self.a32[0:p, base // 2: base // 2 + n]
        return self.au32[0:p, base // 2: base // 2 + n]

    def ps(self, shape, dtype):
        p, n = shape
        if dtype == BF16:
            assert n <= 1024
            return self.pB[0:p, 0:n]
        nb = (n + 511) // 512
        if self.ps_off + nb * 512 <= 2048:
            base = self.ps_off
            self.ps_off += nb * 512
            return self.p8[0:p, base:base + n]
        assert nb == 1 and self.ps_i < 3, "PSUM pool overflow"
        t = self.p4[self.ps_i]
        self.ps_i += 1
        return t[0:p, 0:n]


DMA_QUEUES = ("sync", "gpsimd", "scalar", "vector")


def make_sems(nc, stack):
    sems = {e: stack.enter_context(nc.semaphore(f"sem_{e}")) for e in ENGINES}
    dma_sems = {q: [stack.enter_context(nc.semaphore(f"dsem_{q}_{i}"))
                    for i in range(Prog.NSLOT)] for q in DMA_QUEUES}
    nums = [h.num for h in sems.values()]
    nums += [h.num for hs in dma_sems.values() for h in hs]
    nc.gpsimd.sem_clear(range(min(nums), max(nums) + 1))
    nc.all_engine_barrier()
    return sems, dma_sems



def alloc_psum_banks(nc, stack, n_f32=8, n_bf16=0, prefix="pb"):
    banks = []
    for i in range(n_f32):
        banks.append(stack.enter_context(
            nc.psum_tensor(f"{prefix}{i}", [128, 512], F32)))
    for i in range(n_bf16):
        banks.append(stack.enter_context(
            nc.psum_tensor(f"{prefix}b{i}", [128, 1024], BF16)))
    return banks


def emit_ln_tile(P, sb, tag, x_tile_key, s_x, s_junk, s_small, col0,
                 out_fn):
    ksmall = tag + "_stats"
    P.op("vector", lambda e: e.tensor_reduce(
        s_small[:, col0 + 2:col0 + 3], s_x[:], AX.X, ALU.add),
        reads=[x_tile_key], writes=[ksmall])
    P.op("scalar", lambda e: e.activation(
        s_junk[:], s_x[:], AF.Square, accum_out=s_small[:, col0 + 3:col0 + 4]),
        reads=[x_tile_key], writes=[ksmall, tag + "_junk"])
    P.op("vector", lambda e: e.tensor_scalar(
        s_small[:, col0:col0 + 1], s_small[:, col0 + 2:col0 + 3],
        1.0 / D, None, op0=ALU.mult), reads=[ksmall], writes=[ksmall])
    P.op("vector", lambda e: e.tensor_scalar(
        s_small[:, col0 + 4:col0 + 5], s_small[:, col0 + 3:col0 + 4],
        1.0 / D, None, op0=ALU.mult), reads=[ksmall], writes=[ksmall])
    P.op("vector", lambda e: e.tensor_tensor(
        s_small[:, col0 + 5:col0 + 6], s_small[:, col0:col0 + 1],
        s_small[:, col0:col0 + 1], op=ALU.mult), reads=[ksmall], writes=[ksmall])
    P.op("vector", lambda e: e.tensor_tensor(
        s_small[:, col0 + 4:col0 + 5], s_small[:, col0 + 4:col0 + 5],
        s_small[:, col0 + 5:col0 + 6], op=ALU.subtract),
        reads=[ksmall], writes=[ksmall])
    P.op("vector", lambda e: e.tensor_scalar(
        s_small[:, col0 + 4:col0 + 5], s_small[:, col0 + 4:col0 + 5],
        LN_EPS, None, op0=ALU.add), reads=[ksmall], writes=[ksmall])
    P.op("scalar", lambda e: e.activation(
        s_small[:, col0 + 5:col0 + 6], s_small[:, col0 + 4:col0 + 5], AF.Ln),
        reads=[ksmall], writes=[ksmall])
    P.op("scalar", lambda e: e.activation(
        s_small[:, col0 + 1:col0 + 2], s_small[:, col0 + 5:col0 + 6],
        AF.Exp, scale=-0.5), reads=[ksmall], writes=[ksmall])
    out_fn(s_small[:, col0:col0 + 1], s_small[:, col0 + 1:col0 + 2], ksmall)


def build_ln0(nc, stack, P=None, mem=None, io=None):
    fused = P is not None
    if not fused:
        P = Prog(nc)
    x = io["x"] if fused else nc.dram_tensor("x", [B_PER_CORE, T, D], F32, kind="ExternalInput").ap()
    xn = io["xn"] if fused else nc.dram_tensor("xn", [B_PER_CORE, T, D], BF16, kind="ExternalOutput").ap()

    if not fused:
        sems, dma_sems = make_sems(nc, stack)
        mem = Mem(nc, stack)
    block = stack.enter_context(nc.Block()) if not fused else None

    s_x = [mem.sb(f"s_x{i}", [128, D], F32) for i in range(3)]
    s_xn = [mem.sb(f"s_xn{i}", [128, D], BF16) for i in range(3)]
    s_junk = mem.sb("s_junk", [128, D], BF16)
    s_small = mem.sb("s_small", [128, 8], F32)

    for b in range(B_PER_CORE):
        for tc in range(NT):
            i = (b * NT + tc) % 3
            kx = ("sx", i)
            kxn = ("sxn", i)
            P.dma("sync", s_x[i][:], x[b, tc * 128:(tc + 1) * 128, :],
                  writes=[kx], tag=f"ldx{b}_{tc}")

            def consume(m_ap, r_ap, ksmall, i=i, kx=kx, kxn=kxn, b=b, tc=tc):
                P.op("vector", lambda e: e.tensor_scalar(
                    s_xn[i][:], s_x[i][:], m_ap, r_ap,
                    op0=ALU.subtract, op1=ALU.mult),
                    reads=[kx, ksmall], writes=[kxn])
                P.dma("sync", xn[b, tc * 128:(tc + 1) * 128, :], s_xn[i][:],
                      reads=[kxn], tag=f"stxn{b}_{tc}", is_output=True)

            emit_ln_tile(P, None, f"ln{b}_{tc}", kx, s_x[i], s_junk, s_small,
                         0, consume)

    if not fused:
        P.finalize(block, sems, dma_sems)
    return P


def emit_identity(P, s_iden, s_scr128):
    P.op("gpsimd", lambda e: e.iota(s_scr128[:, 0:1], [[0, 1]], base=0,
                                    channel_multiplier=1,
                                    allow_small_or_imprecise_dtypes=True),
         writes=["iden_scr"])
    P.op("gpsimd", lambda e: e.iota(s_scr128[:, 1:129], [[1, 128]], base=0,
                                    channel_multiplier=0,
                                    allow_small_or_imprecise_dtypes=True),
         writes=["iden_scr"])
    P.op("vector", lambda e: e.tensor_scalar(
        s_iden[:], s_scr128[:, 1:129], s_scr128[:, 0:1], None, op0=ALU.is_equal),
        reads=["iden_scr"], writes=["iden"])


def emit_rope_tables(P, nc, mem):
    s_if = mem.sb("s_invfreq", [128, 32], F32)
    s_tc = mem.sb("s_tcol", [128, 16], F32)
    ctab = mem.sb("s_ctab", [128, 512], F32)
    stab = mem.sb("s_stab", [128, 512], F32)
    scr = [mem.sb(f"s_ropescr{i}", [128, 32], F32) for i in range(3)]
    P.op("gpsimd", lambda e: e.iota(s_if[:], [[1, 32]], base=0,
                                    channel_multiplier=0,
                                    allow_small_or_imprecise_dtypes=True),
         writes=["ropeI"])
    P.op("scalar", lambda e: e.activation(s_if[:], s_if[:], AF.Exp,
                                          scale=-float(np.log(10000.0) / 32.0)),
         reads=["ropeI"], writes=["ropeI"])
    emit_rope_tables.debug = (s_if, s_tc)
    P.op("gpsimd", lambda e: e.iota(
        s_tc[:], [[128, NT]], base=0, channel_multiplier=1,
        allow_small_or_imprecise_dtypes=True), writes=["ropeT"])
    for tc in range(NT):
        P.op("vector", lambda e, tc=tc: e.tensor_scalar(
            scr[0][:], s_if[:], s_tc[:, tc:tc + 1], None, op0=ALU.mult),
            reads=["ropeI", "ropeT"], writes=["rscr0"])
        for which, bias, tab in ((0, 0.0, stab), (1, np.pi / 2, ctab)):
            P.op("vector", lambda e, bias=bias: e.tensor_scalar(
                scr[1][:], scr[0][:], float(bias), float(1.0 / (2 * np.pi)),
                op0=ALU.add, op1=ALU.mult), reads=["rscr0"], writes=["rscr1"])
            P.op("vector", lambda e: e.tensor_scalar(
                scr[1][:], scr[1][:], MAGIC, MAGIC, op0=ALU.add, op1=ALU.subtract),
                reads=["rscr1"], writes=["rscr1"])
            P.op("vector", lambda e, bias=bias: e.tensor_scalar(
                scr[2][:], scr[0][:], float(bias), None, op0=ALU.add),
                reads=["rscr0"], writes=["rscr2"])
            P.op("vector", lambda e: e.scalar_tensor_tensor(
                scr[2][:], scr[1][:], -float(TWO_PI_HI), scr[2][:],
                op0=ALU.mult, op1=ALU.add), reads=["rscr1", "rscr2"], writes=["rscr2"])
            P.op("vector", lambda e: e.scalar_tensor_tensor(
                scr[2][:], scr[1][:], -float(TWO_PI_LO), scr[2][:],
                op0=ALU.mult, op1=ALU.add), reads=["rscr1", "rscr2"], writes=["rscr2"])
            P.op("vector", lambda e: e.tensor_scalar(
                scr[2][:], scr[2][:], 3.1415925, -3.1415925,
                op0=ALU.min, op1=ALU.max), reads=["rscr2"], writes=["rscr2"])
            P.op("scalar", lambda e, tab=tab, tc=tc: e.activation(
                tab[:, tc * 32:(tc + 1) * 32], scr[2][:], AF.Sin),
                reads=["rscr2"], writes=["ropetab"])
    return ctab, stab


def emit_rope_full_tables(P, ctab, stab, c_full, s_full, tc):
    cs = ctab[:, tc * 32:(tc + 1) * 32].rearrange("p (o f) -> p o f", o=1)
    ss = stab[:, tc * 32:(tc + 1) * 32].rearrange("p (o f) -> p o f", o=1)
    P.op("vector", lambda e: e.tensor_copy(
        c_full[:].rearrange("p (g f) -> p g f", f=32), cs.to_broadcast([128, 32, 32])),
        reads=["ropetab"], writes=["cfull"])
    sv = s_full[:].rearrange("p (h x f) -> p (h x) f", x=2, f=32)
    P.op("vector", lambda e: e.tensor_scalar(
        sv[:, 0::2, :], ss.to_broadcast([128, 16, 32]), -1.0, None, op0=ALU.mult),
        reads=["ropetab"], writes=["sfull"])
    P.op("vector", lambda e: e.tensor_scalar(
        sv[:, 1::2, :], ss.to_broadcast([128, 16, 32]), 1.0, None, op0=ALU.mult),
        reads=["ropetab"], writes=["sfull"])


def emit_rope_feat(P, c_full, s_full, s_w, kw, s_sw, ksw, out_tile, out_key):
    wv = s_w[:].rearrange("p (h x f) -> p h x f", x=2, f=32)
    swv = s_sw[:].rearrange("p (h x f) -> p h x f", x=2, f=32)
    P.op("vector", lambda e: e.tensor_copy(swv[:, :, 0, :], wv[:, :, 1, :]),
         reads=[kw], writes=[ksw])
    P.op("vector", lambda e: e.tensor_copy(swv[:, :, 1, :], wv[:, :, 0, :]),
         reads=[kw], writes=[ksw])
    P.op("gpsimd", lambda e: e.tensor_tensor(s_sw[:], s_sw[:], s_full[:], op=ALU.mult),
         reads=[ksw, "sfull"], writes=[ksw])
    P.op("vector", lambda e: e.tensor_tensor(s_w[:], s_w[:], c_full[:], op=ALU.mult),
         reads=[kw, "cfull"], writes=[kw])
    P.op("vector", lambda e: e.tensor_tensor(s_w[:], s_w[:], s_sw[:], op=ALU.add),
         reads=[kw, ksw], writes=[kw])
    P.op("vector", lambda e: e.tensor_scalar(s_sw[:], s_w[:], 0.0, None, op0=ALU.min),
         reads=[kw], writes=[ksw])
    P.op("scalar", lambda e: e.activation(s_sw[:], s_sw[:], AF.Exp),
         reads=[ksw], writes=[ksw])
    P.op("vector", lambda e: e.scalar_tensor_tensor(
        out_tile[:], s_w[:], 0.0, s_sw[:], op0=ALU.max, op1=ALU.add),
        reads=[kw, ksw], writes=[out_key])


ATTN0_CUT = "full"


def build_attn0(nc, stack, P=None, mem=None, io=None):
    fused = P is not None
    if not fused:
        P = Prog(nc)
    xn = io["xn"] if fused else nc.dram_tensor("xn", [B_PER_CORE, T, D], BF16, kind="ExternalInput").ap()
    wqkvT = io["wqkvT"] if fused else nc.dram_tensor("wqkvT", [D, 3 * D], F32, kind="ExternalInput").ap()
    woT = io["woT"] if fused else nc.dram_tensor("woT", [D, D], F32, kind="ExternalInput").ap()
    s0 = io["s0"] if fused else nc.dram_tensor("s0", [B_PER_CORE, T, D], BF16, kind="ExternalOutput").ap()

    if not fused:
        sems, dma_sems = make_sems(nc, stack)
        mem = Mem(nc, stack)
    en = stack.enter_context
    s_wqkv = mem.sb("s_wqkv", [128, ND * 3 * D], BF16)
    s_wo = mem.sb("s_wo", [128, ND * D], BF16)
    s_iden = mem.sb("s_iden", [128, 128], BF16)
    s_scr128 = mem.sb("s_scr128", [128, 129], F32)
    c_full = mem.sb("s_cfull", [128, D], F32)
    s_full = mem.sb("s_sfull", [128, D], F32)
    s_xn = [mem.sb(f"s_xn{i}", [128, D], BF16) for i in range(2)]
    s_xnT = [mem.sb(f"s_xnT{i}", [128, D], BF16) for i in range(2)]
    s_q = mem.sb("s_q", [128, D], F32)
    s_qsw = mem.sb("s_qsw", [128, D], F32)
    s_k = mem.sb("s_k", [128, D], F32)
    s_ksw = mem.sb("s_ksw", [128, D], F32)
    s_v = mem.sb("s_v", [128, D], BF16)
    s_kf = mem.sb("s_kf", [128, D], BF16)
    s_qf = [[mem.sb(f"s_qf{b}_{i}", [128, D], BF16) for i in range(NT)]
            for b in range(B_PER_CORE)]
    s_kvks = [mem.sb(f"s_kvks{b}", [128, 8 * 65], BF16)
              for b in range(B_PER_CORE)]
    s_kvacc = [mem.sb(f"s_kvacc{b}", [128, 8 * 65], F32)
               for b in range(B_PER_CORE)]
    s_ones = mem.sb("s_ones1", [128, 1], BF16)
    s_qfT = mem.sb("s_qfT", [128, D], BF16)
    s_rec = mem.sb("s_rec", [128, 16], F32)
    s_at = mem.sb("s_at", [128, D], BF16)
    s_atT = mem.sb("s_atT", [128, D], BF16)
    s_s0 = [mem.sb(f"s_s0{i}", [128, D], BF16) for i in range(2)]

    pb = [mem.ps([128, 512], F32) for i in range(6)]
    pt = mem.ps([128, 1024], BF16)
    block = en(nc.Block()) if not fused else None

    for c in range(ND):
        P.dma("gpsimd", s_wqkv[:, c * 3 * D:(c + 1) * 3 * D],
              wqkvT[c * 128:(c + 1) * 128, :], writes=[("wqkv", c)])
        P.dma("gpsimd", s_wo[:, c * D:(c + 1) * D],
              woT[c * 128:(c + 1) * 128, :], writes=[("wo", c)])
    emit_identity(P, s_iden, s_scr128)
    ctab, stab = emit_rope_tables(P, nc, mem)
    P.op("vector", lambda e: e.memset(s_ones[:], 1.0), writes=["ones1"])

    KVB = {0: (pb[2], pb[3]), 1: (pb[4], pb[5])}

    if ATTN0_CUT in ("wload", "tables"):
        P.dma("sync", s0[0, 0:128, :], s_wqkv[:, 0:1024], reads=[("wqkv", 0)],
              is_output=True)
        if ATTN0_CUT == "tables":
            P.dma("gpsimd", s0[1, 0:128, 0:512], ctab[:], reads=["ropetab"], is_output=True)
            P.dma("gpsimd", s0[1, 0:128, 512:1024], stab[:], reads=["ropetab"], is_output=True)
            dbg_if, dbg_tc = emit_rope_tables.debug
            P.dma("gpsimd", s0[1, 128:256, 0:32], dbg_if[:], reads=["ropeI"], is_output=True)
            P.dma("gpsimd", s0[1, 128:256, 32:48], dbg_tc[:], reads=["ropeT"], is_output=True)
        P.finalize(block, sems, dma_sems)
        return P

    for tc in range(NT if ATTN0_CUT != "passa1" else 1):
        emit_rope_full_tables(P, ctab, stab, c_full, s_full, tc)
        for b in range(B_PER_CORE):
            i = b % 2
            kxn = ("sxn", i)
            kxnT = ("sxnT", i)
            P.dma("sync", s_xn[i][:], xn[b, tc * 128:(tc + 1) * 128, :],
                  writes=[kxn])
            for c in range(ND):
                P.op("tensor", lambda e, c=c, i=i: e.transpose(
                    pt[:, c * 128:(c + 1) * 128],
                    s_xn[i][:, c * 128:(c + 1) * 128], s_iden[:]),
                    reads=[kxn, "iden"], writes=[("pt", c)])
            P.op("vector", lambda e, i=i: e.tensor_copy(s_xnT[i][:], pt[:]),
                 reads=[("pt", c) for c in range(ND)], writes=[kxnT])
            for ng in range(6):
                bank = pb[ng % 2]
                kbank = ("pb", ng % 2)
                for dc in range(ND):
                    P.op("tensor", lambda e, ng=ng, dc=dc, i=i, bank=bank: e.matmul(
                        bank[:], s_xnT[i][:, dc * 128:(dc + 1) * 128],
                        s_wqkv[:, dc * 3 * D + ng * 512: dc * 3 * D + (ng + 1) * 512],
                        start=(dc == 0), stop=(dc == ND - 1)),
                        reads=[kxnT, ("wqkv", dc)], writes=[kbank],
                        relax=[kbank] if dc > 0 else [])
                if ng < 2:
                    half = ng
                    P.op("scalar", lambda e, bank=bank, half=half: e.activation(
                        s_q[:, half * 512:(half + 1) * 512], bank[:], AF.Copy,
                        scale=0.125), reads=[kbank], writes=["s_q"])
                elif ng < 4:
                    half = ng - 2
                    P.op("scalar", lambda e, bank=bank, half=half: e.activation(
                        s_k[:, half * 512:(half + 1) * 512], bank[:], AF.Copy),
                        reads=[kbank], writes=["s_k"])
                else:
                    half = ng - 4
                    P.op("vector", lambda e, bank=bank, half=half: e.tensor_copy(
                        s_v[:, half * 512:(half + 1) * 512], bank[:]),
                        reads=[kbank], writes=["s_v"])
            emit_rope_feat(P, c_full, s_full, s_q, "s_q", s_qsw, "s_qsw",
                           s_qf[b][tc], ("qf", b, tc))
            emit_rope_feat(P, c_full, s_full, s_k, "s_k", s_ksw, "s_ksw",
                           s_kf, "s_kf")
            kvb = KVB[b]
            for h in range(HEADS):
                bank = kvb[h // 8]
                kkv = ("pb", 2 + 2 * b + h // 8)
                po = (h % 2) * 64
                fo = ((h // 2) % 4) * 65
                P.op("tensor", lambda e, bank=bank, po=po, fo=fo, h=h: e.matmul(
                    bank[po:po + 64, fo:fo + 64],
                    s_kf[:, h * 64:(h + 1) * 64], s_v[:, h * 64:(h + 1) * 64],
                    start=True, stop=False),
                    reads=["s_kf", "s_v"], writes=[kkv])
                P.op("tensor", lambda e, bank=bank, po=po, fo=fo, h=h: e.matmul(
                    bank[po:po + 64, fo + 64:fo + 65],
                    s_kf[:, h * 64:(h + 1) * 64], s_ones[:],
                    start=False, stop=True),
                    reads=["s_kf", "ones1"], writes=[kkv], relax=[kkv])
            for half in range(2):
                kkv = ("pb", 2 + 2 * b + half)
                if tc == 0:
                    P.op("vector", lambda e, b=b, half=half, kvb=kvb: e.tensor_copy(
                        s_kvacc[b][:, half * 260:(half + 1) * 260],
                        kvb[half][:, 0:260]),
                        reads=[kkv], writes=[("kvacc", b)])
                else:
                    P.op("vector", lambda e, b=b, half=half, kvb=kvb: e.tensor_tensor(
                        s_kvacc[b][:, half * 260:(half + 1) * 260],
                        s_kvacc[b][:, half * 260:(half + 1) * 260],
                        kvb[half][:, 0:260], op=ALU.add),
                        reads=[kkv, ("kvacc", b)], writes=[("kvacc", b)])

    for b in range(B_PER_CORE):
        P.op("vector", lambda e, b=b: e.tensor_copy(
            s_kvks[b][:], s_kvacc[b][:]),
            reads=[("kvacc", b)], writes=[("kvks", b)])

    if ATTN0_CUT in ("passa", "passa1"):
        P.dma("sync", s0[0, 0:128, 0:520], s_kvks[0][:, 0:520], reads=[("kvks", 0)],
              is_output=True)
        P.dma("sync", s0[1, 0:128, :], s_qf[0][0][:], reads=[("qf", 0, 0)],
              is_output=True)
        P.finalize(block, sems, dma_sems)
        return P

    for b in range(B_PER_CORE):
        for tc in range(NT):
            kqfT = "qfT"
            for c in range(ND):
                P.op("tensor", lambda e, c=c, b=b, tc=tc: e.transpose(
                    pt[:, c * 128:(c + 1) * 128],
                    s_qf[b][tc][:, c * 128:(c + 1) * 128], s_iden[:]),
                    reads=[("qf", b, tc), "iden"], writes=[("pt", c)])
            P.op("vector", lambda e: e.tensor_copy(s_qfT[:], pt[:]),
                 reads=[("pt", c) for c in range(ND)], writes=[kqfT])
            if ATTN0_CUT == "passb00":
                P.dma("sync", s0[b, tc * 128:(tc + 1) * 128, :], s_qfT[:],
                      reads=["qfT"], is_output=True)
                continue
            for h in range(HEADS):
                bank = pb[h // 4]
                kbank = ("pb", h // 4)
                fo = (h % 4) * 65
                po = (h % 2) * 64
                co = (h // 2) * 128
                kvfo = ((h // 2) % 4) * 65
                P.op("tensor", lambda e, bank=bank, fo=fo, po=po, co=co,
                     kvfo=kvfo, b=b, h=h: e.matmul(
                         bank[:, fo:fo + 65],
                         s_qfT[po:po + 64, co:co + 128],
                         s_kvks[b][po:po + 64, (h // 8) * 260 + kvfo:(h // 8) * 260 + kvfo + 65],
                         start=(h % 4 == 0), stop=(h % 4 == 3)),
                     reads=[kqfT, ("kvks", b)], writes=[kbank])
            if ATTN0_CUT == "passb0":
                for g in range(4):
                    P.op("vector", lambda e, g=g: e.tensor_copy(
                        s_at[:, g * 256:g * 256 + 256], pb[g][:, 0:256]),
                        reads=[("pb", g)], writes=["sat"])
                P.dma("sync", s0[b, tc * 128:(tc + 1) * 128, :], s_at[:],
                      reads=["sat"], is_output=True)
                continue
            for g in range(4):
                bank = pb[g]
                kbank = ("pb", g)
                bv = bank[:, 0:260].rearrange("p (h x) -> p h x", x=65)
                P.op("vector", lambda e, bv=bv, g=g: e.tensor_scalar(
                    s_rec[:, g * 4:(g + 1) * 4], bv[:, 0:4, 64:65], 1e-6, None,
                    op0=ALU.max), reads=[kbank], writes=["rec"])
            P.op("vector", lambda e: e.reciprocal(s_rec[:], s_rec[:]),
                 reads=["rec"], writes=["rec"])
            for g in range(4):
                bank = pb[g]
                kbank = ("pb", g)
                bv = bank[:, 0:260].rearrange("p (h x) -> p h x", x=65)
                rv = s_rec[:, g * 4:(g + 1) * 4].rearrange(
                    "p (h o) -> p h o", o=1).to_broadcast([128, 4, 64])
                ov = s_at[:, g * 256:(g + 1) * 256].rearrange(
                    "p (h x) -> p h x", x=64)
                P.op("vector", lambda e, bv=bv, rv=rv, ov=ov: e.tensor_tensor(
                    ov, bv[:, 0:4, 0:64], rv, op=ALU.mult),
                    reads=[kbank, "rec"], writes=["sat"])
            if ATTN0_CUT == "passb1":
                P.dma("sync", s0[b, tc * 128:(tc + 1) * 128, :], s_at[:],
                      reads=["sat"], is_output=True)
                continue
            for c in range(ND):
                P.op("tensor", lambda e, c=c: e.transpose(
                    pt[:, c * 128:(c + 1) * 128],
                    s_at[:, c * 128:(c + 1) * 128], s_iden[:]),
                    reads=["sat", "iden"], writes=[("pt", c)])
            P.op("vector", lambda e: e.tensor_copy(s_atT[:], pt[:]),
                 reads=[("pt", c) for c in range(ND)], writes=["atT"])
            for ng in range(2):
                bank = pb[4 + ng]
                kbank = ("pb", 4 + ng)
                for dc in range(ND):
                    P.op("tensor", lambda e, ng=ng, dc=dc, bank=bank: e.matmul(
                        bank[:], s_atT[:, dc * 128:(dc + 1) * 128],
                        s_wo[:, dc * D + ng * 512: dc * D + (ng + 1) * 512],
                        start=(dc == 0), stop=(dc == ND - 1)),
                        reads=["atT", ("wo", dc)], writes=[kbank],
                        relax=[kbank] if dc > 0 else [])
                P.op("vector", lambda e, ng=ng, bank=bank, b=b, tc=tc: e.tensor_copy(
                    s_s0[tc % 2][:, ng * 512:(ng + 1) * 512], bank[:]),
                    reads=[kbank], writes=[("ss0", tc % 2)])
            P.dma("sync", s0[b, tc * 128:(tc + 1) * 128, :], s_s0[tc % 2][:],
                  reads=[("ss0", tc % 2)], is_output=True, tag=f"st_s0_{b}_{tc}")

    if not fused:
        P.finalize(block, sems, dma_sems)
    return P


def build_moe0(nc, stack, P=None, mem=None, io=None):
    fused = P is not None
    if not fused:
        P = Prog(nc)
    xn = io["xn"] if fused else nc.dram_tensor("xn", [B_PER_CORE, T, D], BF16, kind="ExternalInput").ap()
    gateT = io["gateT"] if fused else nc.dram_tensor("gateT", [D, E_EXP], F32, kind="ExternalInput").ap()
    WgT = io["WgT"] if fused else nc.dram_tensor("WgT", [E_EXP, D, F_EXP], F32, kind="ExternalInput").ap()
    WuT = io["WuT"] if fused else nc.dram_tensor("WuT", [E_EXP, D, F_EXP], F32, kind="ExternalInput").ap()
    WdT = io["WdT"] if fused else nc.dram_tensor("WdT", [E_EXP, F_EXP, D], F32, kind="ExternalInput").ap()
    WsgT = io["WsgT"] if fused else nc.dram_tensor("WsgT", [D, F_SH], F32, kind="ExternalInput").ap()
    WsuT = io["WsuT"] if fused else nc.dram_tensor("WsuT", [D, F_SH], F32, kind="ExternalInput").ap()
    WsdT = io["WsdT"] if fused else nc.dram_tensor("WsdT", [F_SH, D], F32, kind="ExternalInput").ap()
    moe = io["moe"] if fused else nc.dram_tensor("moe", [B_PER_CORE, T, D], F32, kind="ExternalOutput").ap()

    if not fused:
        sems, dma_sems = make_sems(nc, stack)
        mem = Mem(nc, stack)
    en = stack.enter_context
    s_w1 = mem.sb("m_w1", [128, ND * F_EXP], BF16)
    s_w2 = mem.sb("m_w2", [128, ND * F_EXP], BF16)
    s_w3 = mem.sb("m_w3", [128, 16 * D], BF16)
    s_hs = mem.sb("m_hs", [128, 16 * 512], BF16)
    s_xn = [mem.sb(f"m_xn{i}", [128, D], BF16) for i in range(2)]
    s_xnTt = mem.sb("m_xnTt", [128, D], BF16)
    s_xnTg = mem.sb("m_xnTg", [128, ND * 512], BF16)
    s_gate = mem.sb("m_gate", [128, ND * E_EXP], BF16)
    s_iden = mem.sb("m_iden", [128, 128], BF16)
    s_idenf = mem.sb("m_idenf", [128, 128], F32)
    s_scr128 = mem.sb("m_scr128", [128, 129], F32)
    s_iotaJ = mem.sb("m_iotaJ", [128, 256], BF16)
    s_tri = mem.sb("m_tri", [128, 128], BF16)
    s_one128 = mem.sb("m_one128", [128, 128], BF16)
    s_scores = mem.sb("m_scores", [128, NT * E_EXP], F32)
    s_scoT = mem.sb("m_scoT", [16, T], F32)
    s_junk16 = mem.sb("m_junk16", [16, T], BF16)
    s_bis = mem.sb("m_bis", [16, 8], F32)
    s_maskT = mem.sb("m_maskT", [16, T], BF16)
    s_maskt = mem.sb("m_maskt", [128, NT * E_EXP], BF16)
    s_slot = mem.sb("m_slot", [128, NT * E_EXP], F32)
    s_rhs4 = mem.sb("m_rhs4", [128, NT * 64], BF16)
    s_P = [mem.sb(f"m_P{i}", [128, 256], BF16) for i in range(2)]
    s_ixw1 = mem.sb("m_ixw1", [4, 256], F32)
    s_col = [mem.sb(f"m_col{b}", [128, 128], F32)
             for b in range(B_PER_CORE)]
    s_idxu = [mem.sb(f"m_idxu{b}", [128, 32], U32)
              for b in range(B_PER_CORE)]
    s_wcol = [mem.sb(f"m_wcol{b}", [128, 32], F32)
              for b in range(B_PER_CORE)]
    s_shout = [mem.sb(f"m_shout{i}", [128, D], F32) for i in range(2)]
    s_xsel = mem.sb("m_xsel", [128, D], BF16)
    s_xselT = mem.sb("m_xselT", [128, ND * 256], BF16)
    s_hg = mem.sb("m_hg", [128, 512], BF16)
    s_eo = [mem.sb(f"m_eo{i}", [128, D], BF16) for i in range(2)]

    pb = [mem.ps([128, 512], F32) for i in range(6)]
    pt = mem.ps([128, 1024], BF16)
    block = en(nc.Block()) if not fused else None

    emit_identity(P, s_iden, s_scr128)
    P.op("vector", lambda e: e.tensor_copy(s_idenf[:], s_iden[:]),
         reads=["iden"], writes=["idenf"])
    P.op("gpsimd", lambda e: e.iota(s_iotaJ[:], [[1, 256]], base=1,
                                    channel_multiplier=0,
                                    allow_small_or_imprecise_dtypes=True),
         writes=["iotaJ"])
    P.op("gpsimd", lambda e: e.iota(s_scr128[:, 0:1], [[0, 1]], base=0,
                                    channel_multiplier=1,
                                    allow_small_or_imprecise_dtypes=True),
         writes=["iden_scr"])
    P.op("gpsimd", lambda e: e.iota(s_scr128[:, 1:129], [[1, 128]], base=0,
                                    channel_multiplier=0,
                                    allow_small_or_imprecise_dtypes=True),
         writes=["iden_scr"])
    P.op("vector", lambda e: e.tensor_scalar(
        s_tri[:], s_scr128[:, 1:129], s_scr128[:, 0:1], None, op0=ALU.is_ge),
        reads=["iden_scr"], writes=["tri"])
    P.op("vector", lambda e: e.memset(s_one128[:], 1.0), writes=["ones128"])
    for c in range(ND):
        P.dma("gpsimd", s_gate[:, c * E_EXP:(c + 1) * E_EXP],
              gateT[c * 128:(c + 1) * 128, :], writes=["gate"])

    def transpose_tile(src_tile, src_key, dst_view, dst_key):
        for c in range(ND):
            P.op("tensor", lambda e, c=c: e.transpose(
                pt[:, c * 128:(c + 1) * 128],
                src_tile[:, c * 128:(c + 1) * 128], s_iden[:]),
                reads=[src_key, "iden"], writes=[("pt", c)])
        if dst_view.ndim == 2:
            dst_view = dst_view.rearrange("p (c t) -> p c t", t=128)
        P.op("vector", lambda e: e.tensor_copy(
            dst_view, pt[:].rearrange("p (c t) -> p c t", t=128)),
            reads=[("pt", c) for c in range(ND)], writes=[dst_key])

    for b in range(B_PER_CORE):
        for tc in range(NT):
            i = tc % 2
            kxn = ("mxn", i)
            P.dma("sync", s_xn[i][:], xn[b, tc * 128:(tc + 1) * 128, :],
                  writes=[kxn])
            transpose_tile(s_xn[i], kxn, s_xnTt[:], "xnTt")
            for dc in range(ND):
                P.op("tensor", lambda e, dc=dc: e.matmul(
                    pb[0][:, 0:E_EXP], s_xnTt[:, dc * 128:(dc + 1) * 128],
                    s_gate[:, dc * E_EXP:(dc + 1) * E_EXP],
                    start=(dc == 0), stop=(dc == ND - 1)),
                    reads=["xnTt", "gate"], writes=[("pb", 0)],
                    relax=[("pb", 0)] if dc > 0 else [])
            P.op("vector", lambda e: e.tensor_reduce(
                s_scr128[:, 0:1], pb[0][:, 0:E_EXP], AX.X, ALU.max, negate=True),
                reads=[("pb", 0)], writes=["sm_scr"])
            P.op("scalar", lambda e, tc=tc: e.activation(
                s_scores[:, tc * E_EXP:(tc + 1) * E_EXP], pb[0][:, 0:E_EXP],
                AF.Exp, bias=s_scr128[:, 0:1], accum_out=s_scr128[:, 1:2]),
                reads=[("pb", 0), "sm_scr"], writes=["scores", "sm_scr"])
            P.op("vector", lambda e: e.reciprocal(s_scr128[:, 2:3], s_scr128[:, 1:2]),
                 reads=["sm_scr"], writes=["sm_scr"])
            P.op("vector", lambda e, tc=tc: e.tensor_scalar(
                s_scores[:, tc * E_EXP:(tc + 1) * E_EXP],
                s_scores[:, tc * E_EXP:(tc + 1) * E_EXP],
                s_scr128[:, 2:3], None, op0=ALU.mult),
                reads=["scores", "sm_scr"], writes=["scores"])
            P.op("tensor", lambda e, tc=tc: e.transpose(
                pb[1][0:E_EXP, 0:128],
                s_scores[:, tc * E_EXP:(tc + 1) * E_EXP], s_idenf[:]),
                reads=["scores", "idenf"], writes=[("pb", 1)])
            P.op("vector", lambda e, tc=tc: e.tensor_copy(
                s_scoT[:, tc * 128:(tc + 1) * 128], pb[1][0:E_EXP, 0:128]),
                reads=[("pb", 1)], writes=["scoT"])

        kb = "bis"
        P.op("vector", lambda e: e.memset(s_bis[:, 0:1], 0.0), writes=[kb])
        P.op("vector", lambda e: e.memset(s_bis[:, 1:2], 1.0), writes=[kb])
        P.op("vector", lambda e: e.memset(s_bis[:, 2:3], 0.5), writes=[kb])
        for it in range(30):
            P.op("vector", lambda e: e.tensor_scalar(
                s_junk16[:], s_scoT[:], s_bis[:, 2:3], 0.0,
                op0=ALU.is_gt, op1=ALU.add, accum_out=s_bis[:, 3:4]),
                reads=["scoT", kb], writes=["junk16", kb])
            P.op("vector", lambda e: e.tensor_scalar(
                s_bis[:, 4:5], s_bis[:, 3:4], float(CAP), None, op0=ALU.is_gt),
                reads=[kb], writes=[kb])
            P.op("vector", lambda e: e.tensor_tensor(
                s_bis[:, 5:6], s_bis[:, 2:3], s_bis[:, 0:1], op=ALU.subtract),
                reads=[kb], writes=[kb])
            P.op("vector", lambda e: e.tensor_tensor(
                s_bis[:, 5:6], s_bis[:, 5:6], s_bis[:, 4:5], op=ALU.mult),
                reads=[kb], writes=[kb])
            P.op("vector", lambda e: e.tensor_tensor(
                s_bis[:, 0:1], s_bis[:, 0:1], s_bis[:, 5:6], op=ALU.add),
                reads=[kb], writes=[kb])
            P.op("vector", lambda e: e.tensor_tensor(
                s_bis[:, 6:7], s_bis[:, 2:3], s_bis[:, 1:2], op=ALU.subtract),
                reads=[kb], writes=[kb])
            P.op("vector", lambda e: e.tensor_scalar(
                s_bis[:, 7:8], s_bis[:, 4:5], -1.0, 1.0, op0=ALU.mult, op1=ALU.add),
                reads=[kb], writes=[kb])
            P.op("vector", lambda e: e.tensor_tensor(
                s_bis[:, 6:7], s_bis[:, 6:7], s_bis[:, 7:8], op=ALU.mult),
                reads=[kb], writes=[kb])
            P.op("vector", lambda e: e.tensor_tensor(
                s_bis[:, 1:2], s_bis[:, 1:2], s_bis[:, 6:7], op=ALU.add),
                reads=[kb], writes=[kb])
            P.op("vector", lambda e: e.tensor_scalar(
                s_bis[:, 2:3], s_bis[:, 0:1], 0.5, None, op0=ALU.mult),
                reads=[kb], writes=[kb])
            P.op("vector", lambda e: e.scalar_tensor_tensor(
                s_bis[:, 2:3], s_bis[:, 1:2], 0.5, s_bis[:, 2:3],
                op0=ALU.mult, op1=ALU.add), reads=[kb], writes=[kb])
        P.op("vector", lambda e: e.tensor_scalar(
            s_maskT[:], s_scoT[:], s_bis[:, 2:3], None, op0=ALU.is_gt),
            reads=["scoT", kb], writes=["maskT"])
        for tc in range(NT):
            P.op("tensor", lambda e, tc=tc: e.transpose(
                pt[:, 0:E_EXP], s_maskT[:, tc * 128:(tc + 1) * 128],
                s_iden[0:16, 0:16]),
                reads=["maskT", "iden"], writes=[("pt", 0)])
            P.op("vector", lambda e, tc=tc: e.tensor_copy(
                s_maskt[:, tc * E_EXP:(tc + 1) * E_EXP], pt[:, 0:E_EXP]),
                reads=[("pt", 0)], writes=["maskt"])
        for tc in range(NT):
            kps = ("pslot", tc)
            P.op("tensor", lambda e, tc=tc: e.matmul(
                pb[2][:, tc * E_EXP:(tc + 1) * E_EXP], s_tri[:],
                s_maskt[:, tc * E_EXP:(tc + 1) * E_EXP],
                start=True, stop=(tc == 0)),
                reads=["tri", "maskt"], writes=[kps])
            for tp in range(tc):
                P.op("tensor", lambda e, tc=tc, tp=tp: e.matmul(
                    pb[2][:, tc * E_EXP:(tc + 1) * E_EXP], s_one128[:],
                    s_maskt[:, tp * E_EXP:(tp + 1) * E_EXP],
                    start=False, stop=(tp == tc - 1)),
                    reads=["ones128", "maskt"], writes=[kps], relax=[kps])
        P.op("vector", lambda e: e.tensor_tensor(
            s_slot[:], pb[2][:, 0:256], s_maskt[:], op=ALU.mult),
            reads=["maskt"] + [("pslot", tc) for tc in range(NT)],
            writes=["slot"])

        kr = "rhs4"
        r4 = s_rhs4[:].rearrange("p (x f) -> p x f", f=4)
        P.op("gpsimd", lambda e: e.iota(r4[:, :, 0:1], [[0, 256]], base=0,
                                        channel_multiplier=1,
                                        allow_small_or_imprecise_dtypes=True),
             writes=[kr])
        P.op("vector", lambda e: e.memset(r4[:, :, 1:2], 1.0), writes=[kr])
        P.op("gpsimd", lambda e: e.iota(r4[:, :, 3:4], [[1, NT], [0, E_EXP]],
                                        base=0, channel_multiplier=0,
                                        allow_small_or_imprecise_dtypes=True),
             writes=[kr])
        for tc in range(NT):
            P.op("vector", lambda e, tc=tc: e.tensor_copy(
                s_rhs4[:, tc * 64:(tc + 1) * 64].rearrange(
                    "p (x f) -> p x f", f=4)[:, :, 2:3],
                s_scores[:, tc * E_EXP:(tc + 1) * E_EXP].rearrange(
                    "p (x f) -> p x f", f=1)),
                reads=["scores"], writes=[kr])

        for ee in range(E_EXP):
            kx = "pixw"
            for tc in range(NT):
                ip = tc % 2
                kP = ("Ptile", ip)
                P.op("vector", lambda e, ee=ee, tc=tc, ip=ip: e.tensor_scalar(
                    s_P[ip][:], s_iotaJ[:],
                    s_slot[:, tc * E_EXP + ee: tc * E_EXP + ee + 1],
                    None, op0=ALU.is_equal),
                    reads=["iotaJ", "slot"], writes=[kP])
                P.op("tensor", lambda e, ee=ee, tc=tc, ip=ip: e.matmul(
                    pb[3][0:4, 0:256],
                    s_rhs4[:, tc * 64 + 4 * ee: tc * 64 + 4 * ee + 4],
                    s_P[ip][:], start=(tc == 0), stop=(tc == NT - 1)),
                    reads=[kr, kP], writes=[kx], relax=[kx] if tc > 0 else [])
            P.op("vector", lambda e: e.tensor_copy(s_ixw1[:], pb[3][0:4, 0:256]),
                 reads=[kx], writes=["ixw1"])
            for jc in range(2):
                ci = ee * 2 + jc
                P.op("tensor", lambda e, jc=jc: e.transpose(
                    pb[1][0:128, 0:4],
                    s_ixw1[0:4, jc * 128:(jc + 1) * 128], s_idenf[0:4, 0:4]),
                    reads=["ixw1", "idenf"], writes=[("pb", 1)])
                kc = ("col", b)
                P.op("vector", lambda e, b=b, ci=ci: e.tensor_copy(
                    s_col[b][:, (ci % 32) * 4:(ci % 32) * 4 + 4],
                    pb[1][0:128, 0:4]),
                    reads=[("pb", 1)], writes=[kc])
        kc = ("col", b)
        for ci in range(32):
            c4 = ci * 4
            P.op("vector", lambda e, b=b, c4=c4: e.scalar_tensor_tensor(
                s_col[b][:, c4:c4 + 1], s_col[b][:, c4 + 3:c4 + 4], 128.0,
                s_col[b][:, c4:c4 + 1], op0=ALU.mult, op1=ALU.add),
                reads=[kc], writes=[kc])
            P.op("vector", lambda e, b=b, c4=c4: e.tensor_scalar(
                s_col[b][:, c4:c4 + 1], s_col[b][:, c4:c4 + 1],
                float(b * T), None, op0=ALU.add),
                reads=[kc], writes=[kc])
            P.op("vector", lambda e, b=b, c4=c4: e.scalar_tensor_tensor(
                s_col[b][:, c4:c4 + 1], s_col[b][:, c4 + 1:c4 + 2], -BIG_IDX,
                s_col[b][:, c4:c4 + 1], op0=ALU.mult, op1=ALU.add),
                reads=[kc], writes=[kc])
            P.op("vector", lambda e, b=b, c4=c4: e.tensor_scalar(
                s_col[b][:, c4:c4 + 1], s_col[b][:, c4:c4 + 1], BIG_IDX, None,
                op0=ALU.add), reads=[kc], writes=[kc])
            P.op("vector", lambda e, b=b, ci=ci, c4=c4: e.tensor_copy(
                s_idxu[b][:, ci:ci + 1], s_col[b][:, c4:c4 + 1]),
                reads=[kc], writes=[("idxu", b)])
            P.op("vector", lambda e, b=b, ci=ci, c4=c4: e.tensor_copy(
                s_wcol[b][:, ci:ci + 1], s_col[b][:, c4 + 2:c4 + 3]),
                reads=[kc], writes=[("wcol", b)])

    P.dma("gpsimd", s_w3[:].rearrange("p (c d) -> p c d", d=D),
          WsdT.rearrange("(c p) d -> p c d", p=128), writes=["w3"])
    P.dma("gpsimd", s_w1[:].rearrange("p (c f) -> p c f", f=F_SH),
          WsgT.rearrange("(c p) f -> p c f", p=128), writes=["w1"])
    P.dma("gpsimd", s_w2[:].rearrange("p (c f) -> p c f", f=F_SH),
          WsuT.rearrange("(c p) f -> p c f", p=128), writes=["w2"])
    for b in range(B_PER_CORE):
        for tg in range(4):
            tbase = tg * 512
            for tq in range(4):
                i = tq % 2
                kxn = ("mxn", i)
                tci = tg * 4 + tq
                P.dma("sync", s_xn[i][:], xn[b, tci * 128:(tci + 1) * 128, :],
                      writes=[kxn])
                transpose_tile(
                    s_xn[i], kxn,
                    s_xnTg[:].rearrange("p (c t) -> p c t", t=512)[
                        :, :, tq * 128:(tq + 1) * 128],
                    "xnTg")
            for fc in range(16):
                for dc in range(ND):
                    P.op("tensor", lambda e, dc=dc, fc=fc: e.matmul(
                        pb[0][:], s_w1[:, dc * F_SH + fc * 128: dc * F_SH + (fc + 1) * 128],
                        s_xnTg[:, dc * 512:(dc + 1) * 512],
                        start=(dc == 0), stop=(dc == ND - 1)),
                        reads=["w1", "xnTg"], writes=[("pb", 0)],
                        relax=[("pb", 0)] if dc > 0 else [])
                    P.op("tensor", lambda e, dc=dc, fc=fc: e.matmul(
                        pb[1][:], s_w2[:, dc * F_SH + fc * 128: dc * F_SH + (fc + 1) * 128],
                        s_xnTg[:, dc * 512:(dc + 1) * 512],
                        start=(dc == 0), stop=(dc == ND - 1)),
                        reads=["w2", "xnTg"], writes=[("pb", 1)],
                        relax=[("pb", 1)] if dc > 0 else [])
                P.op("scalar", lambda e: e.activation(
                    s_hg[:], pb[0][:], AF.Gelu),
                    reads=[("pb", 0)], writes=["hg"])
                P.op("vector", lambda e, fc=fc: e.tensor_tensor(
                    s_hs[:, fc * 512:(fc + 1) * 512], s_hg[:], pb[1][:],
                    op=ALU.mult),
                    reads=["hg", ("pb", 1)], writes=["hs"])
            for tq in range(4):
                for half in range(2):
                    kpd = ("pb", 2 + half)
                    for fc in range(16):
                        P.op("tensor", lambda e, fc=fc, tq=tq, half=half: e.matmul(
                            pb[2 + half][:],
                            s_hs[:, fc * 512 + tq * 128: fc * 512 + (tq + 1) * 128],
                            s_w3[:, fc * D + half * 512: fc * D + half * 512 + 512],
                            start=(fc == 0), stop=(fc == 15)),
                            reads=["hs", "w3"], writes=[kpd],
                            relax=[kpd] if fc > 0 else [])
                i = tq % 2
                P.op("vector", lambda e, i=i: e.tensor_copy(
                    s_shout[i][:, 0:512], pb[2][:]),
                    reads=[("pb", 2)], writes=[("shout", i)])
                P.op("vector", lambda e, i=i: e.tensor_copy(
                    s_shout[i][:, 512:1024], pb[3][:]),
                    reads=[("pb", 3)], writes=[("shout", i)])
                P.dma("sync", moe[b, tbase + tq * 128: tbase + (tq + 1) * 128, :],
                      s_shout[i][:], reads=[("shout", i)],
                      writes=[("moe", b)], is_output=True)

    bc_reg = nc.gpsimd.to_reg(B_PER_CORE * T - 1)
    wg3 = WgT.rearrange("e (c p) f -> e p c f", p=128)
    wu3 = WuT.rearrange("e (c p) f -> e p c f", p=128)
    wd3 = WdT.rearrange("e (c p) d -> e p c d", p=128)
    for ee in range(E_EXP):
        P.dma("gpsimd", s_w1[:].rearrange("p (c f) -> p c f", f=F_EXP),
              wg3[ee], writes=["w1"], tag=f"wg{ee}")
        P.dma("gpsimd", s_w2[:].rearrange("p (c f) -> p c f", f=F_EXP),
              wu3[ee], writes=["w2"], tag=f"wu{ee}")
        P.dma("gpsimd", s_w3[:].rearrange("p (c d) -> p c d", d=D),
              wd3[ee], writes=["w3"], tag=f"wd{ee}")
        for b in range(B_PER_CORE):
            kxs = "xselT"
            for jc in range(2):
                ci = ee * 2 + jc
                P.op("gpsimd", lambda e, b=b, ci=ci: e.indirect_dma_start(
                    out=s_xsel[:], out_offset=None,
                    in_=xn.rearrange("b t d -> (b t) d"),
                    in_offset=IndirectOffsetOnAxis(
                        s_idxu[b][:, ci:ci + 1], axis=0),
                    bounds_check=bc_reg, oob_is_err=False),
                    reads=[("idxu", b)], writes=["xsel"], inc=16)
                transpose_tile(
                    s_xsel, "xsel",
                    s_xselT[:].rearrange("p (c j) -> p c j", j=256)[
                        :, :, jc * 128:(jc + 1) * 128],
                    kxs)
            for fc in range(16):
                for dc in range(ND):
                    P.op("tensor", lambda e, dc=dc, fc=fc: e.matmul(
                        pb[0][:, 0:256],
                        s_w1[:, dc * F_EXP + fc * 128: dc * F_EXP + (fc + 1) * 128],
                        s_xselT[:, dc * 256:(dc + 1) * 256],
                        start=(dc == 0), stop=(dc == ND - 1)),
                        reads=["w1", kxs], writes=[("pb", 0)],
                        relax=[("pb", 0)] if dc > 0 else [])
                    P.op("tensor", lambda e, dc=dc, fc=fc: e.matmul(
                        pb[1][:, 0:256],
                        s_w2[:, dc * F_EXP + fc * 128: dc * F_EXP + (fc + 1) * 128],
                        s_xselT[:, dc * 256:(dc + 1) * 256],
                        start=(dc == 0), stop=(dc == ND - 1)),
                        reads=["w2", kxs], writes=[("pb", 1)],
                        relax=[("pb", 1)] if dc > 0 else [])
                P.op("scalar", lambda e: e.activation(
                    s_hg[:, 0:256], pb[0][:, 0:256], AF.Gelu),
                    reads=[("pb", 0)], writes=["hg"])
                P.op("vector", lambda e, fc=fc: e.tensor_tensor(
                    s_hs[:, fc * 256:(fc + 1) * 256], s_hg[:, 0:256],
                    pb[1][:, 0:256], op=ALU.mult),
                    reads=["hg", ("pb", 1)], writes=["hs"])
            for jc in range(2):
                ci = ee * 2 + jc
                for half in range(2):
                    kpd = ("pb", 4 + half)
                    for fc in range(16):
                        P.op("tensor", lambda e, fc=fc, jc=jc, half=half: e.matmul(
                            pb[4 + half][:],
                            s_hs[:, fc * 256 + jc * 128: fc * 256 + (jc + 1) * 128],
                            s_w3[:, fc * D + half * 512: fc * D + half * 512 + 512],
                            start=(fc == 0), stop=(fc == 15)),
                            reads=["hs", "w3"], writes=[kpd],
                            relax=[kpd] if fc > 0 else [])
                i = jc % 2
                keo = ("eo", i)
                P.op("vector", lambda e, b=b, ci=ci, i=i: e.tensor_scalar(
                    s_eo[i][:, 0:512], pb[4][:], s_wcol[b][:, ci:ci + 1], None,
                    op0=ALU.mult),
                    reads=[("pb", 4), ("wcol", b)], writes=[keo])
                P.op("vector", lambda e, b=b, ci=ci, i=i: e.tensor_scalar(
                    s_eo[i][:, 512:1024], pb[5][:], s_wcol[b][:, ci:ci + 1], None,
                    op0=ALU.mult),
                    reads=[("pb", 5), ("wcol", b)], writes=[keo])
                sc = P.op("gpsimd", lambda e, b=b, ci=ci, i=i: e.indirect_dma_start(
                    out=moe.rearrange("b t d -> (b t) d"),
                    out_offset=IndirectOffsetOnAxis(
                        s_idxu[b][:, ci:ci + 1], axis=0),
                    in_=s_eo[i][:], in_offset=None,
                    bounds_check=bc_reg, oob_is_err=False,
                    compute_op=ALU.add),
                    reads=[keo, ("idxu", b)], writes=[("moe", b)], inc=16)
                P.out_dmas.append(sc)
    if not fused:
        P.finalize(block, sems, dma_sems)
    return P


def build_combine_ln1(nc, stack, P=None, mem=None, io=None):
    fused = P is not None
    if not fused:
        P = Prog(nc)
    x = io["x"] if fused else nc.dram_tensor("x", [B_PER_CORE, T, D], F32, kind="ExternalInput").ap()
    s0 = io["s0"] if fused else nc.dram_tensor("s0", [B_PER_CORE, T, D], BF16, kind="ExternalInput").ap()
    moe = io["moe"] if fused else nc.dram_tensor("moe", [B_PER_CORE, T, D], F32, kind="ExternalInput").ap()
    x1 = io["x1"] if fused else nc.dram_tensor("x1", [B_PER_CORE, T, D], F32, kind="ExternalOutput").ap()
    xn1 = io["xn1"] if fused else nc.dram_tensor("xn1", [B_PER_CORE, T, D], BF16, kind="ExternalOutput").ap()
    xq1 = io["xq1"] if fused else nc.dram_tensor("xq1", [B_PER_CORE, T, D], BF16, kind="ExternalOutput").ap()
    xs127 = io["xs127"] if fused else nc.dram_tensor("xs127", [B_PER_CORE, T], F32, kind="ExternalOutput").ap()

    if not fused:
        sems, dma_sems = make_sems(nc, stack)
        mem = Mem(nc, stack)
    en = stack.enter_context
    s_x = [mem.sb(f"c_x{i}", [128, D], F32) for i in range(2)]
    s_s0 = [mem.sb(f"c_s0{i}", [128, D], BF16) for i in range(2)]
    s_moe = [mem.sb(f"c_moe{i}", [128, D], F32) for i in range(2)]
    s_g0 = mem.sb("c_g0", [128, D], BF16)
    s_x1 = [mem.sb(f"c_x1{i}", [128, D], F32) for i in range(2)]
    s_tmp = mem.sb("c_tmp", [128, D], F32)
    s_tmp2 = mem.sb("c_tmp2", [128, D], F32)
    s_xn1 = [mem.sb(f"c_xn1{i}", [128, D], BF16) for i in range(2)]
    s_xq = [mem.sb(f"c_xq{i}", [128, D], BF16) for i in range(2)]
    s_junk = mem.sb("c_junk", [128, D], BF16)
    s_small = mem.sb("c_small", [128, 16], F32)
    block = en(nc.Block()) if not fused else None

    for b in range(B_PER_CORE):
        for tc in range(NT):
            i = tc % 2
            kx, ks, km = ("cx", i), ("cs", i), ("cm", i)
            sl = slice(tc * 128, (tc + 1) * 128)
            P.dma("sync", s_x[i][:], x[b, sl, :], writes=[kx])
            P.dma("sync", s_s0[i][:], s0[b, sl, :], writes=[ks])
            P.dma("sync", s_moe[i][:], moe[b, sl, :], writes=[km])
            P.op("scalar", lambda e, i=i: e.activation(
                s_g0[:], s_moe[i][:], AF.Silu), reads=[km], writes=["cg0"])
            P.op("vector", lambda e, i=i: e.tensor_tensor(
                s_g0[:], s_g0[:], s_s0[i][:], op=ALU.mult),
                reads=["cg0", ks], writes=["cg0"])
            kx1 = ("cx1", i)
            P.op("vector", lambda e, i=i: e.tensor_tensor(
                s_x1[i][:], s_x[i][:], s_g0[:], op=ALU.add),
                reads=[kx, "cg0"], writes=[kx1])
            P.dma("sync", x1[b, sl, :], s_x1[i][:], reads=[kx1], is_output=True)

            def consume(m_ap, r_ap, ksmall, i=i, kx1=kx1, b=b, tc=tc, sl=sl):
                kxn = ("cxn1", i)
                P.op("vector", lambda e, i=i: e.tensor_scalar(
                    s_xn1[i][:], s_x1[i][:], m_ap, r_ap,
                    op0=ALU.subtract, op1=ALU.mult),
                    reads=[kx1, ksmall], writes=[kxn])
                P.dma("sync", xn1[b, sl, :], s_xn1[i][:], reads=[kxn],
                      is_output=True)
                P.op("vector", lambda e, i=i: e.tensor_scalar(
                    s_tmp[:], s_x1[i][:], m_ap, None, op0=ALU.subtract),
                    reads=[kx1, ksmall], writes=["ctmp"])
                P.op("vector", lambda e: e.tensor_reduce(
                    s_small[:, 8:9], s_tmp[:], AX.X, ALU.max,
                    apply_absolute_value=True),
                    reads=["ctmp"], writes=["cq"])
                P.op("vector", lambda e: e.tensor_tensor(
                    s_small[:, 8:9], s_small[:, 8:9], r_ap, op=ALU.mult),
                    reads=["cq", ksmall], writes=["cq"])
                P.op("vector", lambda e: e.tensor_scalar(
                    s_small[:, 8:9], s_small[:, 8:9], 1e-5, None, op0=ALU.max),
                    reads=["cq"], writes=["cq"])
                P.op("vector", lambda e: e.reciprocal(
                    s_small[:, 9:10], s_small[:, 8:9]), reads=["cq"], writes=["cq"])
                P.op("vector", lambda e: e.tensor_scalar(
                    s_small[:, 10:11], s_small[:, 9:10], 127.0, None,
                    op0=ALU.mult), reads=["cq"], writes=["cq"])
                P.op("vector", lambda e: e.tensor_tensor(
                    s_small[:, 10:11], s_small[:, 10:11], r_ap, op=ALU.mult),
                    reads=["cq", ksmall], writes=["cq"])
                P.op("vector", lambda e: e.tensor_scalar(
                    s_small[:, 11:12], s_small[:, 8:9], 1.0 / 127.0, None,
                    op0=ALU.mult), reads=["cq"], writes=["cq"])
                P.dma("sync", xs127[b, sl], s_small[:, 11:12], reads=["cq"],
                      is_output=True)
                kxq = ("cxq", i)
                P.op("vector", lambda e: e.tensor_scalar(
                    s_tmp2[:], s_tmp[:], s_small[:, 10:11], MAGIC,
                    op0=ALU.mult, op1=ALU.add),
                    reads=["ctmp", "cq"], writes=["ctmp2"])
                P.op("vector", lambda e: e.tensor_scalar(
                    s_tmp2[:], s_tmp2[:], MAGIC, None, op0=ALU.subtract),
                    reads=["ctmp2"], writes=["ctmp2"])
                P.op("vector", lambda e, i=i: e.tensor_scalar(
                    s_xq[i][:], s_tmp2[:], 127.0, -128.0,
                    op0=ALU.min, op1=ALU.max),
                    reads=["ctmp2"], writes=[kxq])
                P.dma("sync", xq1[b, sl, :], s_xq[i][:], reads=[kxq],
                      is_output=True)

            emit_ln_tile(P, None, f"c{b}_{tc}", kx1, s_x1[i], s_junk, s_small,
                         0, consume)
    if not fused:
        P.finalize(block, sems, dma_sems)
    return P


def build_attn1(nc, stack, P=None, mem=None, io=None):
    fused = P is not None
    if not fused:
        P = Prog(nc)
    xn1 = io["xn1"] if fused else nc.dram_tensor("xn1", [B_PER_CORE, T, D], BF16, kind="ExternalInput").ap()
    wqkv1T = io["wqkv1T"] if fused else nc.dram_tensor("wqkv1T", [D, D], F32, kind="ExternalInput").ap()
    wo1T = io["wo1T"] if fused else nc.dram_tensor("wo1T", [D, D], F32, kind="ExternalInput").ap()
    s1 = io["s1"] if fused else nc.dram_tensor("s1", [B_PER_CORE, T, D], BF16, kind="ExternalOutput").ap()
    qtd = nc.dram_tensor("a1_qtd", [256, 128], F32, kind="Internal").ap()

    if not fused:
        sems, dma_sems = make_sems(nc, stack)
        mem = Mem(nc, stack)
    en = stack.enter_context
    s_wq = mem.sb("a1_wq", [128, ND * D], BF16)
    s_wo = mem.sb("a1_wo", [128, ND * D], BF16)
    s_iden = mem.sb("a1_iden", [128, 128], BF16)
    s_idenf = mem.sb("a1_idenf", [128, 128], F32)
    s_scr128 = mem.sb("a1_scr128", [128, 129], F32)
    s_ones1 = mem.sb("a1_ones1", [1, 128], BF16)
    s_ones1f = mem.sb("a1_ones1f", [1, 128], F32)
    s_mcol = mem.sb("a1_mcol", [128, HEADS], F32)
    s_xn = [mem.sb(f"a1_xn{i}", [128, D], BF16) for i in range(2)]
    s_xnTt = mem.sb("a1_xnTt", [128, D], BF16)
    s_yT = mem.sb("a1_yT", [128, ND * T], BF16)
    s_ytok = mem.sb("a1_ytok", [128, NT * D], BF16)
    s_sq = mem.sb("a1_sq", [128, D], F32)
    s_q2 = mem.sb("a1_q2", [128, NT * HEADS], F32)
    s_cneg = mem.sb("a1_cneg", [128, NT * HEADS], F32)
    s_E = mem.sb("a1_E", [128, NT * HEADS], F32)
    s_mrow = mem.sb("a1_mrow", [1, 2 * HEADS], F32)
    s_q2T = mem.sb("a1_q2T", [128, 256], F32)
    s_crow = mem.sb("a1_crow", [1, T], BF16)
    s_qrow = mem.sb("a1_qrow", [1, T], F32)
    s_vaug = [mem.sb(f"a1_vaug{i}", [128, 65], BF16) for i in range(2)]
    s_se = [mem.sb(f"a1_se{i}", [128, T], BF16) for i in range(2)]
    s_drow = mem.sb("a1_drow", [1, T], F32)
    s_au = mem.sb("a1_au", [128, 512], BF16)
    s_aT = mem.sb("a1_aT", [128, ND * T], BF16)
    s_s1 = [mem.sb(f"a1_s1{i}", [128, D], BF16) for i in range(2)]
    ps = mem.ps([128, 1024], F32)
    pv = [mem.ps([128, 512], F32) for g in range(4)]
    pbc = mem.ps([128, 512], F32)
    pt = mem.ps([128, 1024], BF16)
    block = en(nc.Block()) if not fused else None

    emit_identity(P, s_iden, s_scr128)
    P.op("vector", lambda e: e.tensor_copy(s_idenf[:], s_iden[:]),
         reads=["iden"], writes=["idenf"])
    P.op("vector", lambda e: e.memset(s_ones1[:], 1.0), writes=["ones1"])
    P.op("vector", lambda e: e.memset(s_ones1f[:], 1.0), writes=["ones1f"])
    for i in range(2):
        P.op("vector", lambda e, i=i: e.memset(s_vaug[i][:, 64:65], 1.0),
             writes=[("vones", i)])
    for c in range(ND):
        P.dma("gpsimd", s_wq[:, c * D:(c + 1) * D], wqkv1T[c * 128:(c + 1) * 128, :],
              writes=[("wq", c)])
        P.dma("gpsimd", s_wo[:, c * D:(c + 1) * D], wo1T[c * 128:(c + 1) * 128, :],
              writes=[("wo", c)])

    for b in range(B_PER_CORE):
        for tc in range(NT):
            i = tc % 2
            kxn = ("axn", i)
            P.dma("sync", s_xn[i][:], xn1[b, tc * 128:(tc + 1) * 128, :],
                  writes=[kxn])
            for c in range(ND):
                P.op("tensor", lambda e, c=c, i=i: e.transpose(
                    pt[:, c * 128:(c + 1) * 128],
                    s_xn[i][:, c * 128:(c + 1) * 128], s_iden[:]),
                    reads=[kxn, "iden"], writes=[("pt", c)])
            P.op("vector", lambda e: e.tensor_copy(
                s_xnTt[:].rearrange("p (c t) -> p c t", t=128),
                pt[:].rearrange("p (c t) -> p c t", t=128)),
                reads=[("pt", c) for c in range(ND)], writes=["xnTt"])
            for fc in range(ND):
                for dc in range(ND):
                    P.op("tensor", lambda e, fc=fc, dc=dc: e.matmul(
                        pbc[:, 0:128],
                        s_wq[:, dc * D + fc * 128: dc * D + (fc + 1) * 128],
                        s_xnTt[:, dc * 128:(dc + 1) * 128],
                        start=(dc == 0), stop=(dc == ND - 1)),
                        reads=[("wq", dc), "xnTt"], writes=[("pbc",)],
                        relax=[("pbc",)] if dc > 0 else [])
                P.op("vector", lambda e, fc=fc, tc=tc: e.tensor_copy(
                    s_yT[:, fc * T + tc * 128: fc * T + (tc + 1) * 128],
                    pbc[:, 0:128]), reads=[("pbc",)], writes=["yT"])
        for tc in range(NT):
            for c in range(ND):
                P.op("tensor", lambda e, c=c, tc=tc: e.transpose(
                    pt[:, c * 128:(c + 1) * 128],
                    s_yT[:, c * T + tc * 128: c * T + (tc + 1) * 128],
                    s_iden[:]),
                    reads=["yT", "iden"], writes=[("pt", c)])
            P.op("vector", lambda e, tc=tc: e.tensor_copy(
                s_ytok[:, tc * D:(tc + 1) * D].rearrange("p (c t) -> p c t", t=128),
                pt[:].rearrange("p (c t) -> p c t", t=128)),
                reads=[("pt", c) for c in range(ND)], writes=["ytok"])

        for tc in range(NT):
            P.op("scalar", lambda e, tc=tc: e.activation(
                s_sq[:], s_ytok[:, tc * D:(tc + 1) * D], AF.Square),
                reads=["ytok"], writes=["sq"])
            P.op("vector", lambda e, tc=tc: e.tensor_reduce(
                s_q2[:, tc * HEADS:(tc + 1) * HEADS],
                s_sq[:].rearrange("p (h f) -> p h f", f=HD), AX.X, ALU.add),
                reads=["sq"], writes=["q2"])
        P.op("scalar", lambda e: e.activation(s_q2[:], s_q2[:], AF.Sqrt),
             reads=["q2"], writes=["q2"])
        P.op("gpsimd", lambda e: e.tensor_reduce(
            s_q2T[0:1, 0:256], s_q2[:], AX.C, ALU.max),
            reads=["q2"], writes=["q2T"])
        P.op("vector", lambda e: e.tensor_reduce(
            s_mrow[0:1, 0:HEADS],
            s_q2T[0:1, 0:256].rearrange("o (tc h) -> o h tc", h=HEADS),
            AX.X, ALU.max), reads=["q2T"], writes=["mrow"])
        P.op("tensor", lambda e: e.matmul(
            pbc[:, 0:HEADS], s_ones1f[:], s_mrow[0:1, 0:HEADS],
            start=True, stop=True), reads=["ones1f", "mrow"], writes=[("pbc",)])
        P.op("vector", lambda e: e.tensor_copy(s_mcol[:], pbc[:, 0:HEADS]),
             reads=[("pbc",)], writes=["mcol"])
        for h in range(HEADS):
            P.op("vector", lambda e, h=h: e.tensor_scalar(
                s_cneg[:].rearrange("p (tc hh) -> p tc hh", hh=HEADS)[:, :, h:h + 1],
                s_q2[:].rearrange("p (tc hh) -> p tc hh", hh=HEADS)[:, :, h:h + 1],
                s_mcol[:, h:h + 1], None, op0=ALU.mult),
                reads=["q2", "mcol"], writes=["cneg"])
        for half in range(2):
            P.op("tensor", lambda e, half=half: e.transpose(
                pbc[:, 0:128], s_cneg[:, half * 128:(half + 1) * 128],
                s_idenf[:]), reads=["cneg", "idenf"], writes=[("pbc",)])
            P.op("vector", lambda e, half=half: e.tensor_copy(
                s_q2T[:, half * 128:(half + 1) * 128], pbc[:, 0:128]),
                reads=[("pbc",)], writes=["q2T"])
            P.dma("sync", qtd[half * 128:(half + 1) * 128, :],
                  s_q2T[:, half * 128:(half + 1) * 128],
                  reads=["q2T"], writes=["qtd"])

        for h in range(HEADS):
            po = (h % 2) * 64
            co = h // 2
            P.dma("sync", s_qrow[:].rearrange("o (tc p) -> o tc p", p=128),
                  qtd.rearrange("(tc h) p -> h tc p", h=HEADS)[h].unsqueeze(0),
                  reads=["qtd"], writes=["qrow"])
            P.op("vector", lambda e: e.tensor_scalar(
                s_crow[:], s_qrow[:], -1.0, None, op0=ALU.mult),
                reads=["qrow"], writes=["crow"])
            for ktc in range(NT):
                ip = ktc % 2
                kse = ("se", ip)
                kvau = ("vaug", ip)
                P.op("vector", lambda e, ip=ip, h=h, ktc=ktc: e.tensor_copy(
                    s_vaug[ip][:, 0:64],
                    s_ytok[:, ktc * D + h * 64: ktc * D + (h + 1) * 64]),
                    reads=["ytok", ("vones", ip)], writes=[kvau])
                for pshalf in range(2):
                    for qh in range(2):
                        qg = pshalf * 2 + qh
                        kps = ("ps", qh)
                        P.op("tensor", lambda e, po=po, co=co, ktc=ktc, qg=qg,
                             qh=qh: e.matmul(
                                 ps[:, qh * 512:(qh + 1) * 512],
                                 s_yT[po:po + 64, co * T + ktc * 128: co * T + (ktc + 1) * 128],
                                 s_yT[po:po + 64, co * T + qg * 512: co * T + (qg + 1) * 512],
                                 start=True, stop=False),
                             reads=["yT"], writes=[kps])
                        P.op("tensor", lambda e, qg=qg, qh=qh: e.matmul(
                            ps[:, qh * 512:(qh + 1) * 512],
                            s_ones1[0:1, 0:128],
                            s_crow[0:1, qg * 512:(qg + 1) * 512],
                            start=False, stop=True),
                            reads=["ones1", "crow"], writes=[kps],
                            relax=[kps])
                    P.op("scalar", lambda e, ip=ip, pshalf=pshalf: e.activation(
                        s_se[ip][:, pshalf * 1024:(pshalf + 1) * 1024],
                        ps[:, 0:1024], AF.Exp, scale=0.125),
                        reads=[("ps", 0), ("ps", 1)], writes=[kse])
                for qg in range(4):
                    kpa = ("pav", qg)
                    P.op("tensor", lambda e, ktc=ktc, qg=qg, ip=ip: e.matmul(
                        pv[qg][0:65, 0:512], s_vaug[ip][:],
                        s_se[ip][:, qg * 512:(qg + 1) * 512],
                        start=(ktc == 0), stop=(ktc == NT - 1)),
                        reads=[kvau, kse], writes=[kpa])
            for qg in range(4):
                P.op("vector", lambda e, qg=qg: e.tensor_copy(
                    s_drow[0:1, qg * 512:(qg + 1) * 512], pv[qg][64:65, 0:512]),
                    reads=[("pav", qg)], writes=["drow"])
            P.op("scalar", lambda e: e.activation(s_drow[:], s_drow[:], AF.Ln),
                 reads=["drow"], writes=["drow"])
            P.op("scalar", lambda e: e.activation(
                s_drow[:], s_drow[:], AF.Exp, scale=-1.0),
                reads=["drow"], writes=["drow"])
            for qg in range(4):
                P.op("tensor", lambda e, qg=qg: e.matmul(
                    pbc[0:64, 0:512], s_ones1f[0:1, 0:64],
                    s_drow[0:1, qg * 512:(qg + 1) * 512], start=True, stop=True),
                    reads=["ones1f", "drow"], writes=[("pbc",)])
                P.op("vector", lambda e, qg=qg: e.tensor_copy(
                    s_au[0:64, 0:512], pv[qg][0:64, 0:512]),
                    reads=[("pav", qg)], writes=["au"])
                P.op("vector", lambda e, po=po, co=co, qg=qg: e.tensor_tensor(
                    s_aT[po:po + 64, co * T + qg * 512: co * T + (qg + 1) * 512],
                    s_au[0:64, 0:512], pbc[0:64, 0:512], op=ALU.mult),
                    reads=["au", ("pbc",)], writes=["aT"])

        for tc in range(NT):
            i = tc % 2
            ks1 = ("s1t", i)
            for half in range(2):
                bank = pv[half]
                kpa = ("pav", half)
                for fc in range(ND):
                    P.op("tensor", lambda e, fc=fc, tc=tc, half=half, bank=bank: e.matmul(
                        bank[:, 0:512],
                        s_aT[:, fc * T + tc * 128: fc * T + (tc + 1) * 128],
                        s_wo[:, fc * D + half * 512: fc * D + (half + 1) * 512],
                        start=(fc == 0), stop=(fc == ND - 1)),
                        reads=["aT", ("wo", fc)], writes=[kpa],
                        relax=[kpa] if fc > 0 else [])
                P.op("vector", lambda e, i=i, half=half, bank=bank: e.tensor_copy(
                    s_s1[i][:, half * 512:(half + 1) * 512], bank[:, 0:512]),
                    reads=[kpa], writes=[ks1])
            P.dma("sync", s1[b, tc * 128:(tc + 1) * 128, :], s_s1[i][:],
                  reads=[ks1], is_output=True)
    if not fused:
        P.finalize(block, sems, dma_sems)
    return P


def build_ffn1(nc, stack, P=None, mem=None, io=None):
    fused = P is not None
    if not fused:
        P = Prog(nc)
    xq1 = io["xq1"] if fused else nc.dram_tensor("xq1", [B_PER_CORE, T, D], BF16, kind="ExternalInput").ap()
    xs127 = io["xs127"] if fused else nc.dram_tensor("xs127", [B_PER_CORE, T], F32, kind="ExternalInput").ap()
    x1 = io["x1"] if fused else nc.dram_tensor("x1", [B_PER_CORE, T, D], F32, kind="ExternalInput").ap()
    s1 = io["s1"] if fused else nc.dram_tensor("s1", [B_PER_CORE, T, D], BF16, kind="ExternalInput").ap()
    W1T = io["W1T"] if fused else nc.dram_tensor("W1T", [D, H_FFN], F32, kind="ExternalInput").ap()
    b1c = io["b1c"] if fused else nc.dram_tensor("b1c", [128, 32], F32, kind="ExternalInput").ap()
    W2T = io["W2T"] if fused else nc.dram_tensor("W2T", [H_FFN, D], F32, kind="ExternalInput").ap()
    b2r = io["b2r"] if fused else nc.dram_tensor("b2r", [1, D], F32, kind="ExternalInput").ap()
    out = io["out"] if fused else nc.dram_tensor("out", [B_PER_CORE, T, D], F32, kind="ExternalOutput").ap()

    if not fused:
        sems, dma_sems = make_sems(nc, stack)
        mem = Mem(nc, stack)
    en = stack.enter_context
    s_w1t = mem.sb("f_w1t", [128, ND * H_FFN], BF16)
    s_w2t = mem.sb("f_w2t", [128, 32 * D], BF16)
    s_wld = mem.sb("f_wld", [128, 1024], F32)
    s_iden = mem.sb("f_iden", [128, 128], BF16)
    s_idenf = mem.sb("f_idenf", [128, 128], F32)
    s_scr128 = mem.sb("f_scr128", [128, 129], F32)
    s_ones1f = mem.sb("f_ones1f", [1, 128], F32)
    s_ones1b = mem.sb("f_ones1b", [1, 128], BF16)
    s_acc = mem.sb("f_acc", [128, 33], F32)
    s_ws = mem.sb("f_ws", [1, 4], F32)
    s_wscol = mem.sb("f_wscol", [128, 4], F32)
    s_hstore = mem.sb("f_hstore", [128, 32 * 512], BF16)
    s_xq = [mem.sb(f"f_xq{i}", [128, D], BF16) for i in range(2)]
    s_xqT = mem.sb("f_xqT", [128, ND * 512], BF16)
    _g1view = None
    s_xsrow = mem.sb("f_xsrow", [1, T], F32)
    s_hsrow = s_xsrow[0:1, 0:512]
    s_rq2 = s_xsrow[0:1, 512:1024]
    s_sclrow = mem.sb("f_sclrow", [1, T], F32)
    s_sclb = mem.sb("f_sclb", [128, 512], F32)
    s_hdq = mem.sb("f_hdq", [128, 512], F32)
    s_hmax = mem.sb("f_hmax", [128, 512], BF16)
    s_hmin = mem.sb("f_hmin", [128, 512], BF16)
    s_rq2b = s_sclb
    s_hquant = s_hdq
    s_scl2 = mem.sb("f_scl2", [128, 4], F32)
    s_b2b = mem.sb("f_b2b", [128, D], BF16)
    s_x1 = [mem.sb("f_x1", [128, D], F32)] * 2
    s_s1 = s_xq
    s_g1 = None
    s_o = s_x1
    ph = mem.ps([128, 512], F32)
    pg = [mem.ps([128, 512], F32) for i in range(2)]
    pbc = mem.ps([128, 512], F32)
    pt = mem.ps([128, 1024], BF16)
    block = en(nc.Block()) if not fused else None

    emit_identity(P, s_iden, s_scr128)
    P.op("vector", lambda e: e.tensor_copy(s_idenf[:], s_iden[:]),
         reads=["iden"], writes=["idenf"])
    P.op("vector", lambda e: e.memset(s_ones1f[:], 1.0), writes=["ones1f"])
    P.op("vector", lambda e: e.memset(s_ones1b[:], 1.0), writes=["ones1f"])
    P.dma("gpsimd", s_b2b[0:1, :], b2r, writes=["b2r"])
    s_onescolf = mem.sb("f_onescolf", [128, 1], F32)
    s_b1 = mem.sb("f_b1", [128, 32], F32)
    P.dma("sync", s_b1[:], b1c, writes=["b1"])
    P.op("vector", lambda e: e.memset(s_onescolf[:], 1.0), writes=["onescolf"])

    for wsidx, (WT, wtern, nchunk, width, kw) in enumerate([
            (W1T, s_w1t, ND, H_FFN, "w1t"),
            (W2T, s_w2t, 32, D, "w2t")]):
        nw = max(1, width // 1024)
        wt = width // nw
        for c in range(nchunk):
            for wh in range(nw):
                P.dma("sync", s_wld[:, 0:wt],
                      WT[c * 128:(c + 1) * 128, wh * wt:(wh + 1) * wt],
                      writes=["wld"], tag=f"abs{wsidx}_{c}_{wh}")
                P.op("scalar", lambda e, c=c, wh=wh, nw=nw, wt=wt: e.activation(
                    s_hstore[:, 0:wt], s_wld[:, 0:wt], AF.Abs,
                    accum_out=s_acc[:, c * nw + wh:c * nw + wh + 1]),
                    reads=["wld"], writes=["hstore", "acc"])
        P.op("vector", lambda e, nchunk=nchunk, nw=nw: e.tensor_reduce(
            s_acc[:, 32:33], s_acc[:, 0:nchunk * nw], AX.X, ALU.add),
            reads=["acc"], writes=["acc"])
        P.op("tensor", lambda e: e.matmul(
            pbc[0:1, 0:1], s_acc[:, 32:33], s_onescolf[:],
            start=True, stop=True),
            reads=["acc", "onescolf"], writes=[("pbc",)])
        P.op("vector", lambda e, wsidx=wsidx, width=width, nchunk=nchunk: e.tensor_scalar(
            s_ws[0:1, wsidx:wsidx + 1], pbc[0:1, 0:1],
            1.0 / (nchunk * 128 * width), 1e-5, op0=ALU.mult, op1=ALU.max),
            reads=[("pbc",)], writes=["ws"])
        P.op("vector", lambda e, wsidx=wsidx: e.reciprocal(
            s_ws[0:1, wsidx + 2:wsidx + 3], s_ws[0:1, wsidx:wsidx + 1]),
            reads=["ws"], writes=["ws"])
        P.op("tensor", lambda e, wsidx=wsidx: e.matmul(
            pbc[:, 0:1], s_ones1f[:], s_ws[0:1, wsidx + 2:wsidx + 3],
            start=True, stop=True), reads=["ones1f", "ws"], writes=[("pbc",)])
        P.op("vector", lambda e, wsidx=wsidx: e.tensor_copy(
            s_wscol[:, wsidx:wsidx + 1], pbc[:, 0:1]),
            reads=[("pbc",)], writes=["wscol"])
        if wsidx == 1:
            P.op("tensor", lambda e: e.matmul(
                pbc[:, 0:1], s_ones1f[:], s_ws[0:1, 1:2],
                start=True, stop=True), reads=["ones1f", "ws"], writes=[("pbc",)])
            P.op("vector", lambda e: e.tensor_copy(
                s_wscol[:, 2:3], pbc[:, 0:1]), reads=[("pbc",)], writes=["ws"])
        for c in range(nchunk):
            for wh in range(nw):
                P.dma("sync", s_wld[:, 0:wt],
                      WT[c * 128:(c + 1) * 128, wh * wt:(wh + 1) * wt],
                      writes=["wld"], tag=f"tern{wsidx}_{c}_{wh}")
                P.op("vector", lambda e, wsidx=wsidx, wt=wt: e.tensor_scalar(
                    s_wld[:, 0:wt], s_wld[:, 0:wt],
                    s_wscol[:, wsidx:wsidx + 1], None, op0=ALU.mult),
                    reads=["wld", "wscol"], writes=["wld"])
                P.op("vector", lambda e, wt=wt: e.tensor_scalar(
                    s_wld[:, 0:wt], s_wld[:, 0:wt], MAGIC, MAGIC,
                    op0=ALU.add, op1=ALU.subtract),
                    reads=["wld"], writes=["wld"])
                P.op("vector", lambda e, c=c, wh=wh, wt=wt, width=width,
                     wtern=wtern, kw=kw: e.tensor_scalar(
                    wtern[:, c * width + wh * wt: c * width + (wh + 1) * wt],
                    s_wld[:, 0:wt], 1.0, -1.0, op0=ALU.min, op1=ALU.max),
                    reads=["wld"], writes=[kw])

    for half in range(2):
        P.op("tensor", lambda e, half=half: e.matmul(
            pbc[:], s_ones1b[:], s_b2b[0:1, half * 512:(half + 1) * 512],
            start=True, stop=True), reads=["ones1f", "b2r"], writes=[("pbc",)])
        P.op("vector", lambda e, half=half: e.tensor_copy(
            s_hdq[:, 0:512], pbc[:]),
            reads=[("pbc",)], writes=[("b2s", half)])
        P.op("vector", lambda e, half=half: e.tensor_copy(
            s_b2b[:, half * 512:(half + 1) * 512], s_hdq[:, 0:512]),
            reads=[("b2s", half)], writes=["b2b"])

    for b in range(B_PER_CORE):
        P.dma("sync", s_xsrow[:], xs127[b].unsqueeze(0), writes=["xsrow"])
        P.op("vector", lambda e: e.tensor_scalar(
            s_sclrow[:], s_xsrow[:], s_ws[0:1, 0:1], None, op0=ALU.mult),
            reads=["xsrow", "ws"], writes=["sclrow"])
        for tg in range(4):
            for tq in range(4):
                i = tq % 2
                tci = tg * 4 + tq
                kxq = ("fxq", i)
                P.dma("sync", s_xq[i][:], xq1[b, tci * 128:(tci + 1) * 128, :],
                      writes=[kxq])
                for c in range(ND):
                    P.op("tensor", lambda e, c=c, i=i: e.transpose(
                        pt[:, c * 128:(c + 1) * 128],
                        s_xq[i][:, c * 128:(c + 1) * 128], s_iden[:]),
                        reads=[kxq, "iden"], writes=[("pt", c)])
                P.op("vector", lambda e, tq=tq: e.tensor_copy(
                    s_xqT[:].rearrange("p (c t) -> p c t", t=512)[
                        :, :, tq * 128:(tq + 1) * 128],
                    pt[:].rearrange("p (c t) -> p c t", t=128)),
                    reads=[("pt", c) for c in range(ND)], writes=["xqT"])
            P.op("tensor", lambda e, tg=tg: e.matmul(
                pbc[:], s_ones1f[:], s_sclrow[0:1, tg * 512:(tg + 1) * 512],
                start=True, stop=True), reads=["ones1f", "sclrow"], writes=[("pbc",)])
            P.op("vector", lambda e: e.tensor_copy(s_sclb[:], pbc[:]),
                 reads=[("pbc",)], writes=["sclb"])
            P.op("vector", lambda e: e.memset(s_hmax[:], 0.0), writes=["hmax"])
            P.op("vector", lambda e: e.memset(s_hmin[:], 0.0), writes=["hmin"])
            for Hc in range(32):
                for dc in range(ND):
                    P.op("tensor", lambda e, Hc=Hc, dc=dc: e.matmul(
                        ph[:], s_w1t[:, dc * H_FFN + Hc * 128: dc * H_FFN + (Hc + 1) * 128],
                        s_xqT[:, dc * 512:(dc + 1) * 512],
                        start=(dc == 0), stop=(dc == ND - 1)),
                        reads=["w1t", "xqT"], writes=[("ph",)],
                        relax=[("ph",)] if dc > 0 else [])
                P.op("vector", lambda e: e.tensor_tensor(
                    s_hdq[:], ph[:], s_sclb[:], op=ALU.mult),
                    reads=[("ph",), "sclb"], writes=["hdq"])
                P.op("vector", lambda e, Hc=Hc: e.tensor_scalar(
                    s_hdq[:], s_hdq[:], s_b1[:, Hc:Hc + 1], None, op0=ALU.add),
                    reads=["hdq", "b1"], writes=["hdq"])
                P.op("scalar", lambda e, Hc=Hc: e.activation(
                    s_hstore[:, Hc * 512:(Hc + 1) * 512], s_hdq[:], AF.Gelu),
                    reads=["hdq"], writes=["hstore"])
                P.op("vector", lambda e, Hc=Hc: e.tensor_tensor(
                    s_hmax[:], s_hmax[:], s_hstore[:, Hc * 512:(Hc + 1) * 512],
                    op=ALU.max),
                    reads=["hmax", "hstore"], writes=["hmax"])
                P.op("vector", lambda e, Hc=Hc: e.tensor_tensor(
                    s_hmin[:], s_hmin[:], s_hstore[:, Hc * 512:(Hc + 1) * 512],
                    op=ALU.min),
                    reads=["hmin", "hstore"], writes=["hmin"])
            P.op("vector", lambda e: e.tensor_scalar(
                s_hmin[:], s_hmin[:], -1.0, None, op0=ALU.mult),
                reads=["hmin"], writes=["hmin"])
            P.op("vector", lambda e: e.tensor_tensor(
                s_hmax[:], s_hmax[:], s_hmin[:], op=ALU.max),
                reads=["hmax", "hmin"], writes=["hmax"])
            P.op("gpsimd", lambda e: e.tensor_reduce(
                s_hsrow, s_hmax[:], AX.C, ALU.max),
                reads=["hmax"], writes=["hsrow", "xsrow"])
            P.op("vector", lambda e: e.tensor_scalar(
                s_hsrow, s_hsrow, 1e-5, None, op0=ALU.max),
                reads=["hsrow"], writes=["hsrow"])
            P.op("vector", lambda e: e.reciprocal(s_rq2, s_hsrow),
                 reads=["hsrow"], writes=["rq2"])
            P.op("vector", lambda e: e.tensor_scalar(
                s_rq2, s_rq2, 127.0, None, op0=ALU.mult),
                reads=["rq2"], writes=["rq2"])
            P.op("tensor", lambda e: e.matmul(
                pbc[:], s_ones1f[:], s_rq2, start=True, stop=True),
                reads=["ones1f", "rq2"], writes=[("pbc",)])
            P.op("vector", lambda e: e.tensor_copy(s_rq2b[:], pbc[:]),
                 reads=[("pbc",)], writes=["sclb"])
            for tq in range(4):
                P.op("tensor", lambda e, tq=tq: e.transpose(
                    pbc[0:128, 0:1], s_hsrow[0:1, tq * 128:(tq + 1) * 128],
                    s_idenf[0:1, 0:1]),
                    reads=["hsrow", "idenf"], writes=[("pbc",)])
                P.op("vector", lambda e, tq=tq: e.tensor_copy(
                    s_scl2[:, tq:tq + 1], pbc[0:128, 0:1]),
                    reads=[("pbc",)], writes=["scl2"])
            P.op("vector", lambda e: e.tensor_scalar(
                s_scl2[:], s_scl2[:], s_wscol[:, 2:3], None, op0=ALU.mult),
                reads=["scl2", "ws"], writes=["scl2"])
            P.op("vector", lambda e: e.tensor_scalar(
                s_scl2[:], s_scl2[:], 1.0 / 127.0, None, op0=ALU.mult),
                reads=["scl2"], writes=["scl2"])
            for Hc in range(32):
                P.op("vector", lambda e, Hc=Hc: e.tensor_tensor(
                    s_hquant[:], s_hstore[:, Hc * 512:(Hc + 1) * 512],
                    s_rq2b[:], op=ALU.mult),
                    reads=["hstore", "sclb"], writes=["hdq"])
                P.op("vector", lambda e: e.tensor_scalar(
                    s_hquant[:], s_hquant[:], MAGIC, MAGIC,
                    op0=ALU.add, op1=ALU.subtract),
                    reads=["hdq"], writes=["hdq"])
                P.op("vector", lambda e, Hc=Hc: e.tensor_scalar(
                    s_hstore[:, Hc * 512:(Hc + 1) * 512], s_hquant[:],
                    127.0, -128.0, op0=ALU.min, op1=ALU.max),
                    reads=["hdq"], writes=["hstore"])
            for tq in range(4):
                i = tq % 2
                tci = tg * 4 + tq
                kx1, ks1 = ("fx1",), ("fs1", i)
                P.dma("sync", s_x1[i][:], x1[b, tci * 128:(tci + 1) * 128, :],
                      writes=[kx1])
                P.dma("sync", s_s1[i][:], s1[b, tci * 128:(tci + 1) * 128, :],
                      writes=[ks1, ("fxq", i)])
                for half in range(2):
                    kpg = ("pg", half)
                    for Hc in range(32):
                        P.op("tensor", lambda e, Hc=Hc, tq=tq, half=half: e.matmul(
                            pg[half][:],
                            s_hstore[:, Hc * 512 + tq * 128: Hc * 512 + (tq + 1) * 128],
                            s_w2t[:, Hc * D + half * 512: Hc * D + (half + 1) * 512],
                            start=(Hc == 0), stop=(Hc == 31)),
                            reads=["hstore", "w2t"], writes=[kpg],
                            relax=[kpg] if Hc > 0 else [])
                    P.op("vector", lambda e, half=half, tq=tq: e.tensor_scalar(
                        s_xqT[:, 1024 + half * 512: 1024 + (half + 1) * 512], pg[half][:],
                        s_scl2[:, tq:tq + 1], None, op0=ALU.mult),
                        reads=[("pg", half), "scl2"], writes=["g1", "xqT"])
                P.op("vector", lambda e: e.tensor_tensor(
                    s_xqT[:, 1024:2048], s_xqT[:, 1024:2048], s_b2b[:], op=ALU.add),
                    reads=["g1", "b2b"], writes=["g1"])
                P.op("scalar", lambda e: e.activation(
                    s_xqT[:, 1024:2048], s_xqT[:, 1024:2048], AF.Silu), reads=["g1"], writes=["g1"])
                ko = ("fo",)
                P.op("vector", lambda e, i=i: e.tensor_tensor(
                    s_xqT[:, 1024:2048], s_xqT[:, 1024:2048], s_s1[i][:], op=ALU.mult),
                    reads=["g1", ks1], writes=["g1"])
                P.op("vector", lambda e, i=i: e.tensor_tensor(
                    s_o[i][:], s_x1[i][:], s_xqT[:, 1024:2048], op=ALU.add),
                    reads=[kx1, "g1"], writes=[ko, kx1])
                P.dma("sync", out[b, tci * 128:(tci + 1) * 128, :], s_o[i][:],
                      reads=[ko], is_output=True)
    if not fused:
        P.finalize(block, sems, dma_sems)
    return P



def prep_weights(inputs):
    tr = lambda a: np.ascontiguousarray(np.asarray(a).T)
    w = {}
    w["wqkvT"] = tr(inputs["L0_attn_w_qkv"])
    w["woT"] = tr(inputs["L0_attn_w_out"])
    w["gateT"] = tr(inputs["L0_ffn_gate_w"])
    w["WgT"] = np.ascontiguousarray(np.asarray(inputs["L0_ffn_Wg"]).transpose(0, 2, 1))
    w["WuT"] = np.ascontiguousarray(np.asarray(inputs["L0_ffn_Wu"]).transpose(0, 2, 1))
    w["WdT"] = np.ascontiguousarray(np.asarray(inputs["L0_ffn_Wd"]).transpose(0, 2, 1))
    w["WsgT"] = tr(inputs["L0_ffn_Ws_g"])
    w["WsuT"] = tr(inputs["L0_ffn_Ws_u"])
    w["WsdT"] = tr(inputs["L0_ffn_Ws_d"])
    w["wqkv1T"] = tr(inputs["L1_attn_W_qkv"])
    w["wo1T"] = tr(inputs["L1_attn_W_o"])
    w["W1T"] = tr(inputs["L1_ffn_W1"])
    w["b1c"] = np.ascontiguousarray(np.asarray(inputs["L1_ffn_b1"]).reshape(H_FFN // 128, 128).T)
    w["W2T"] = tr(inputs["L1_ffn_W2"])
    w["b2c"] = np.ascontiguousarray(np.asarray(inputs["L1_ffn_b2"]).reshape(D // 128, 128).T)
    w["b2r"] = np.ascontiguousarray(np.asarray(inputs["L1_ffn_b2"]).reshape(1, D))
    return w


def build_fused(nc, stack):
    P = Prog(nc)
    sems, dma_sems = make_sems(nc, stack)
    mem = Mem(nc, stack)
    io = {}

    def ext(name, shape, dt):
        io[name] = nc.dram_tensor(name, shape, dt, kind="ExternalInput").ap()

    def mid(name, shape, dt):
        io[name] = nc.dram_tensor(name, shape, dt, kind="Internal").ap()

    ext("x", [B_PER_CORE, T, D], F32)
    ext("wqkvT", [D, 3 * D], F32)
    ext("woT", [D, D], F32)
    ext("gateT", [D, E_EXP], F32)
    ext("WgT", [E_EXP, D, F_EXP], F32)
    ext("WuT", [E_EXP, D, F_EXP], F32)
    ext("WdT", [E_EXP, F_EXP, D], F32)
    ext("WsgT", [D, F_SH], F32)
    ext("WsuT", [D, F_SH], F32)
    ext("WsdT", [F_SH, D], F32)
    ext("wqkv1T", [D, D], F32)
    ext("wo1T", [D, D], F32)
    ext("W1T", [D, H_FFN], F32)
    ext("b1c", [128, 32], F32)
    ext("W2T", [H_FFN, D], F32)
    ext("b2r", [1, D], F32)
    mid("xn", [B_PER_CORE, T, D], BF16)
    mid("s0", [B_PER_CORE, T, D], BF16)
    mid("moe", [B_PER_CORE, T, D], F32)
    mid("x1", [B_PER_CORE, T, D], F32)
    mid("xn1", [B_PER_CORE, T, D], BF16)
    mid("xq1", [B_PER_CORE, T, D], BF16)
    mid("xs127", [B_PER_CORE, T], F32)
    mid("s1", [B_PER_CORE, T, D], BF16)
    io["out"] = nc.dram_tensor("out", [B_PER_CORE, T, D], F32,
                               kind="ExternalOutput").ap()

    for bld in (build_ln0, build_attn0, build_moe0, build_combine_ln1,
                build_attn1, build_ffn1):
        mem.reset()
        bld(nc, stack, P=P, mem=mem, io=io)
        P.fence()

    block = stack.enter_context(nc.Block())
    P.finalize(block, sems, dma_sems)
    return P


FUSED = True


def _launch(build_fn, in_maps):
    from contextlib import ExitStack
    with ExitStack() as stack:
        nc = bass.Bass("TRN2", target_bir_lowering=False,
                       detect_race_conditions=False)
        build_fn(nc, stack)
        res = run_bass_kernel_spmd(nc, in_maps, core_ids=list(range(NCORES)))
    return [{k: np.asarray(v) for k, v in r.items()} for r in res.results]


def kernel(**inputs):
    w = prep_weights(inputs)
    x = np.ascontiguousarray(np.asarray(inputs["L0_attn_x"], dtype=np.float32))
    shards = [x[c * B_PER_CORE:(c + 1) * B_PER_CORE] for c in range(NCORES)]

    if FUSED:
        rf = _launch(build_fused, [
            {"x": shards[c], "wqkvT": w["wqkvT"], "woT": w["woT"],
             "gateT": w["gateT"], "WgT": w["WgT"], "WuT": w["WuT"],
             "WdT": w["WdT"], "WsgT": w["WsgT"], "WsuT": w["WsuT"],
             "WsdT": w["WsdT"], "wqkv1T": w["wqkv1T"], "wo1T": w["wo1T"],
             "W1T": w["W1T"], "b1c": w["b1c"], "W2T": w["W2T"],
             "b2r": w["b2r"]} for c in range(NCORES)])
        out = np.concatenate([r["out"] for r in rf], axis=0)
        return out.astype(np.float32)

    r1 = _launch(build_ln0, [{"x": s} for s in shards])
    xn = [r["xn"] for r in r1]

    r2 = _launch(build_attn0, [
        {"xn": xn[c], "wqkvT": w["wqkvT"], "woT": w["woT"]}
        for c in range(NCORES)])
    s0 = [r["s0"] for r in r2]

    r3 = _launch(build_moe0, [
        {"xn": xn[c], "gateT": w["gateT"], "WgT": w["WgT"], "WuT": w["WuT"],
         "WdT": w["WdT"], "WsgT": w["WsgT"], "WsuT": w["WsuT"],
         "WsdT": w["WsdT"]} for c in range(NCORES)])
    moe = [r["moe"] for r in r3]

    r4 = _launch(build_combine_ln1, [
        {"x": shards[c], "s0": s0[c], "moe": moe[c]} for c in range(NCORES)])

    r5 = _launch(build_attn1, [
        {"xn1": r4[c]["xn1"], "wqkv1T": w["wqkv1T"], "wo1T": w["wo1T"]}
        for c in range(NCORES)])

    r6 = _launch(build_ffn1, [
        {"xq1": r4[c]["xq1"], "xs127": r4[c]["xs127"], "x1": r4[c]["x1"],
         "s1": r5[c]["s1"], "W1T": w["W1T"], "b1c": w["b1c"],
         "W2T": w["W2T"], "b2r": w["b2r"]} for c in range(NCORES)])

    out = np.concatenate([r["out"] for r in r6], axis=0)
    return out.astype(np.float32)


# version 04
# speedup vs baseline: 1.1128x; 1.1128x over previous
import numpy as np
import concourse.bass as bass
import concourse.mybir as mybir
from concourse.bass import IndirectOffsetOnAxis
from concourse.bass_utils import run_bass_kernel_spmd

F32 = mybir.dt.float32
BF16 = mybir.dt.bfloat16
U32 = mybir.dt.uint32
AF = mybir.ActivationFunctionType
ALU = mybir.AluOpType
AX = mybir.AxisListType

MAGIC = 1.5 * 2 ** 23
NCORES = 8
B_PER_CORE = 2
T = 2048
D = 1024
NT = T // 128
ND = D // 128
HEADS = 16
HD = 64
E_EXP = 16
CAP = 256
F_EXP = 2048
F_SH = 2048
H_FFN = 4096
LN_EPS = 1e-5
TWO_PI_HI = np.float32(6.2831855)
TWO_PI_LO = np.float32(2 * np.pi - np.float64(np.float32(6.2831855)))
BIG_IDX = 1.0e7

ENGINES = ("sync", "vector", "scalar", "tensor", "gpsimd")


class Prog:

    def __init__(self, nc):
        self.nc = nc
        self.ops = []
        self.last_write = {}
        self.readers = {}
        self.out_dmas = []

    def op(self, engine, fn, reads=(), writes=(), inc=1, tag="", relax=()):
        idx = len(self.ops)
        deps = {}
        for d in getattr(self, "_fence_deps", ()):
            deps.setdefault(d, set())
        for k in reads:
            w = self.last_write.get(k)
            if w is not None:
                deps.setdefault(w, set()).add(k)
        for k in writes:
            w = self.last_write.get(k)
            if w is not None:
                deps.setdefault(w, set()).add(k)
            for rd in self.readers.get(k, ()):
                deps.setdefault(rd, set()).add(k)
        self.ops.append(dict(engine=engine, fn=fn, inc=inc, idx=idx,
                             deps=deps, need_inc=False, tag=tag,
                             relax=frozenset(relax)))
        for k in reads:
            self.readers.setdefault(k, []).append(idx)
        for k in writes:
            self.last_write[k] = idx
            self.readers[k] = []
        return idx

    def dma(self, queue, out, in_, reads=(), writes=(), tag="", is_output=False):
        idx = self.op(queue, lambda e: e.dma_start(out=out, in_=in_),
                      reads=reads, writes=writes, inc=16, tag=tag)
        if is_output:
            self.out_dmas.append(idx)
        return idx

    def dma_transpose(self, queue, out, in_, reads=(), writes=(), tag=""):
        return self.op(queue, lambda e: e.dma_start_transpose(out, in_),
                       reads=reads, writes=writes, inc=16, tag=tag)

    def fence(self):
        start = getattr(self, "_fence_start", 0)
        prior = set(range(start, len(self.ops)))
        markers = []
        for eng in ENGINES:
            i = self.op(eng, lambda e: e.nop(), tag="fence")
            self.ops[i]["deps"] = {d: set() for d in prior}
            markers.append(i)
        self._fence_deps = markers
        self._fence_start = len(self.ops)

    NSLOT = 6
    SERIALIZE = False

    DRIVER_MODE = False
    NSEM_DRIVER = 1
    DRIVER_Q = "sync"
    ALL_INC = False
    NO_DEDUP = False
    TENSOR_EDGES = False
    OWN_EDGES = True

    def finalize(self, block, sems, dma_sems):
        if Prog.DRIVER_MODE:
            nsem = Prog.NSEM_DRIVER
            gsems = [dma_sems[Prog.DRIVER_Q][i] for i in range(nsem)]
            cums = []
            tot = [0] * nsem
            for i, op in enumerate(self.ops):
                cums.append(tuple(tot))
                tot[i % nsem] += op["inc"]
                op["dsem"] = i % nsem

            def section(eng_name, eng):
                for op, before in zip(self.ops, cums):
                    if op["engine"] != eng_name:
                        continue
                    for s in range(nsem):
                        if before[s]:
                            eng.wait_ge(gsems[s], before[s])
                    inst = op["fn"](eng)
                    inst.then_inc(gsems[op["dsem"]], op["inc"])
                if eng_name == "sync":
                    for s in range(nsem):
                        if tot[s]:
                            eng.wait_ge(gsems[s], tot[s])
            for name in ENGINES:
                deco = getattr(block, name)

                def make(n):
                    def body(eng):
                        section(n, eng)
                    return body
                deco(make(name))
            return

        if Prog.SERIALIZE:
            for op in self.ops:
                op["deps"] = {i: set() for i in range(op["idx"])}
        if Prog.ALL_INC:
            for op in self.ops:
                op["need_inc"] = True
        """sems: per-engine compute sems. dma_sems: dict queue ->
        list of NSLOT slot semaphores (queues that issue DMAs)."""
        ops = self.ops
        slot_counts = {}
        for op in ops:
            if op["inc"] == 16:
                q = op["engine"]
                cnt = slot_counts.setdefault(q, [0] * Prog.NSLOT)
                seq = sum(cnt)
                s = seq % Prog.NSLOT
                cnt[s] += 1
                op["slot"] = s
                op["slot_ord"] = cnt[s]
                op["need_inc"] = True
        def edge_needed(op, d, via):
            dop = ops[d]
            if dop["engine"] != op["engine"]:
                return True
            if not Prog.OWN_EDGES:
                return False
            if op["engine"] == "tensor" and dop["engine"] == "tensor":
                if via and via <= op["relax"]:
                    return False
            return True

        for op in ops:
            for d, via in op["deps"].items():
                dop = ops[d]
                if dop["inc"] != 1:
                    continue
                if edge_needed(op, d, via):
                    dop["need_inc"] = True

        cum = {e: 0 for e in ENGINES}
        for op in ops:
            if op["inc"] == 1 and op["need_inc"]:
                cum[op["engine"]] += 1
            op["cum_after"] = cum[op["engine"]]

        clocks = {e: {} for e in ENGINES}

        def want(op, key, v):
            eng = op["engine"]
            if not Prog.NO_DEDUP and clocks[eng].get(key, 0) >= v:
                return
            if Prog.NO_DEDUP and op["waits"].get(key, 0) >= v:
                return
            op["waits"][key] = max(op["waits"].get(key, 0), v)
            clocks[eng][key] = v

        for op in ops:
            eng = op["engine"]
            op["waits"] = {}
            for d in sorted(op["deps"]):
                dop = ops[d]
                via = op["deps"][d]
                if dop["inc"] == 16:
                    key = ("dma", dop["engine"], dop["slot"])
                    want(op, key, 16 * dop["slot_ord"])
                elif edge_needed(op, d, via):
                    want(op, ("eng", dop["engine"]), dop["cum_after"])
            if op["inc"] == 16 and op["slot_ord"] > 1:
                want(op, ("dma", op["engine"], op["slot"]),
                     16 * (op["slot_ord"] - 1))

        final_waits = {}
        for idx in self.out_dmas:
            op = ops[idx]
            key = ("dma", op["engine"], op["slot"])
            final_waits[key] = max(final_waits.get(key, 0), 16 * op["slot_ord"])

        def sem_of(key):
            if key[0] == "dma":
                return dma_sems[key[1]][key[2]]
            return sems[key[1]]

        def section(eng_name, eng):
            for op in ops:
                if op["engine"] != eng_name:
                    continue
                for key in sorted(op["waits"], key=str):
                    eng.wait_ge(sem_of(key), op["waits"][key])
                inst = op["fn"](eng)
                if op["need_inc"]:
                    if op["inc"] == 16:
                        inst.then_inc(dma_sems[eng_name][op["slot"]], 16)
                    else:
                        inst.then_inc(sems[eng_name], 1)
            if eng_name == "sync":
                for key in sorted(final_waits, key=str):
                    eng.wait_ge(sem_of(key), final_waits[key])

        for name in ENGINES:
            deco = getattr(block, name)

            def make(n):
                def body(eng):
                    section(n, eng)
                return body
            deco(make(name))


class Mem:

    COLS = 106016

    def __init__(self, nc, stack):
        en = stack.enter_context
        self.arena = en(nc.sbuf_tensor("arena", [128, Mem.COLS], BF16))
        self.a16 = self.arena[:]
        self.a32 = self.arena[:].bitcast(F32)
        self.au32 = self.arena[:].bitcast(U32)
        self.p8 = en(nc.psum_tensor("pp8", [128, 2048], F32))
        self.p4 = [en(nc.psum_tensor(f"pp4_{i}", [128, 512], F32))
                   for i in range(3)]
        self.pB = en(nc.psum_tensor("ppB", [128, 1024], BF16))
        self.reset()

    def reset(self):
        self.off = 0
        self.ps_off = 0
        self.ps_i = 0

    def sb(self, name, shape, dtype):
        p, n = shape
        width = {BF16: 1, F32: 2, U32: 2}[dtype] * n
        width = (width + 31) & ~31
        assert self.off + width <= Mem.COLS, \
            f"SBUF arena overflow at {name}: {self.off}+{width}"
        base = self.off
        self.off += width
        if dtype == BF16:
            return self.a16[0:p, base:base + n]
        if dtype == F32:
            return self.a32[0:p, base // 2: base // 2 + n]
        return self.au32[0:p, base // 2: base // 2 + n]

    def ps(self, shape, dtype):
        p, n = shape
        if dtype == BF16:
            assert n <= 1024
            return self.pB[0:p, 0:n]
        nb = (n + 511) // 512
        if self.ps_off + nb * 512 <= 2048:
            base = self.ps_off
            self.ps_off += nb * 512
            return self.p8[0:p, base:base + n]
        assert nb == 1 and self.ps_i < 3, "PSUM pool overflow"
        t = self.p4[self.ps_i]
        self.ps_i += 1
        return t[0:p, 0:n]


DMA_QUEUES = ("sync", "gpsimd", "scalar", "vector")


def make_sems(nc, stack):
    sems = {e: stack.enter_context(nc.semaphore(f"sem_{e}")) for e in ENGINES}
    dma_sems = {q: [stack.enter_context(nc.semaphore(f"dsem_{q}_{i}"))
                    for i in range(Prog.NSLOT)] for q in DMA_QUEUES}
    nums = [h.num for h in sems.values()]
    nums += [h.num for hs in dma_sems.values() for h in hs]
    nc.gpsimd.sem_clear(range(min(nums), max(nums) + 1))
    nc.all_engine_barrier()
    return sems, dma_sems



def alloc_psum_banks(nc, stack, n_f32=8, n_bf16=0, prefix="pb"):
    banks = []
    for i in range(n_f32):
        banks.append(stack.enter_context(
            nc.psum_tensor(f"{prefix}{i}", [128, 512], F32)))
    for i in range(n_bf16):
        banks.append(stack.enter_context(
            nc.psum_tensor(f"{prefix}b{i}", [128, 1024], BF16)))
    return banks


def emit_ln_tile(P, sb, tag, x_tile_key, s_x, s_junk, s_small, col0,
                 out_fn):
    ksmall = tag + "_stats"
    P.op("vector", lambda e: e.tensor_reduce(
        s_small[:, col0 + 2:col0 + 3], s_x[:], AX.X, ALU.add),
        reads=[x_tile_key], writes=[ksmall])
    P.op("scalar", lambda e: e.activation(
        s_junk[:], s_x[:], AF.Square, accum_out=s_small[:, col0 + 3:col0 + 4]),
        reads=[x_tile_key], writes=[ksmall, tag + "_junk"])
    P.op("vector", lambda e: e.tensor_scalar(
        s_small[:, col0:col0 + 1], s_small[:, col0 + 2:col0 + 3],
        1.0 / D, None, op0=ALU.mult), reads=[ksmall], writes=[ksmall])
    P.op("vector", lambda e: e.tensor_scalar(
        s_small[:, col0 + 4:col0 + 5], s_small[:, col0 + 3:col0 + 4],
        1.0 / D, None, op0=ALU.mult), reads=[ksmall], writes=[ksmall])
    P.op("vector", lambda e: e.tensor_tensor(
        s_small[:, col0 + 5:col0 + 6], s_small[:, col0:col0 + 1],
        s_small[:, col0:col0 + 1], op=ALU.mult), reads=[ksmall], writes=[ksmall])
    P.op("vector", lambda e: e.tensor_tensor(
        s_small[:, col0 + 4:col0 + 5], s_small[:, col0 + 4:col0 + 5],
        s_small[:, col0 + 5:col0 + 6], op=ALU.subtract),
        reads=[ksmall], writes=[ksmall])
    P.op("vector", lambda e: e.tensor_scalar(
        s_small[:, col0 + 4:col0 + 5], s_small[:, col0 + 4:col0 + 5],
        LN_EPS, None, op0=ALU.add), reads=[ksmall], writes=[ksmall])
    P.op("scalar", lambda e: e.activation(
        s_small[:, col0 + 5:col0 + 6], s_small[:, col0 + 4:col0 + 5], AF.Ln),
        reads=[ksmall], writes=[ksmall])
    P.op("scalar", lambda e: e.activation(
        s_small[:, col0 + 1:col0 + 2], s_small[:, col0 + 5:col0 + 6],
        AF.Exp, scale=-0.5), reads=[ksmall], writes=[ksmall])
    out_fn(s_small[:, col0:col0 + 1], s_small[:, col0 + 1:col0 + 2], ksmall)


def build_ln0(nc, stack, P=None, mem=None, io=None):
    fused = P is not None
    if not fused:
        P = Prog(nc)
    x = io["x"] if fused else nc.dram_tensor("x", [B_PER_CORE, T, D], F32, kind="ExternalInput").ap()
    xn = io["xn"] if fused else nc.dram_tensor("xn", [B_PER_CORE, T, D], BF16, kind="ExternalOutput").ap()

    if not fused:
        sems, dma_sems = make_sems(nc, stack)
        mem = Mem(nc, stack)
    block = stack.enter_context(nc.Block()) if not fused else None

    s_x = [mem.sb(f"s_x{i}", [128, D], F32) for i in range(3)]
    s_xn = [mem.sb(f"s_xn{i}", [128, D], BF16) for i in range(3)]
    s_junk = mem.sb("s_junk", [128, D], BF16)
    s_small = mem.sb("s_small", [128, 8], F32)

    for b in range(B_PER_CORE):
        for tc in range(NT):
            i = (b * NT + tc) % 3
            kx = ("sx", i)
            kxn = ("sxn", i)
            P.dma("sync", s_x[i][:], x[b, tc * 128:(tc + 1) * 128, :],
                  writes=[kx], tag=f"ldx{b}_{tc}")

            def consume(m_ap, r_ap, ksmall, i=i, kx=kx, kxn=kxn, b=b, tc=tc):
                P.op("vector", lambda e: e.tensor_scalar(
                    s_xn[i][:], s_x[i][:], m_ap, r_ap,
                    op0=ALU.subtract, op1=ALU.mult),
                    reads=[kx, ksmall], writes=[kxn])
                P.dma("sync", xn[b, tc * 128:(tc + 1) * 128, :], s_xn[i][:],
                      reads=[kxn], tag=f"stxn{b}_{tc}", is_output=True)

            emit_ln_tile(P, None, f"ln{b}_{tc}", kx, s_x[i], s_junk, s_small,
                         0, consume)

    if not fused:
        P.finalize(block, sems, dma_sems)
    return P


def emit_identity(P, s_iden, s_scr128):
    P.op("gpsimd", lambda e: e.iota(s_scr128[:, 0:1], [[0, 1]], base=0,
                                    channel_multiplier=1,
                                    allow_small_or_imprecise_dtypes=True),
         writes=["iden_scr"])
    P.op("gpsimd", lambda e: e.iota(s_scr128[:, 1:129], [[1, 128]], base=0,
                                    channel_multiplier=0,
                                    allow_small_or_imprecise_dtypes=True),
         writes=["iden_scr"])
    P.op("vector", lambda e: e.tensor_scalar(
        s_iden[:], s_scr128[:, 1:129], s_scr128[:, 0:1], None, op0=ALU.is_equal),
        reads=["iden_scr"], writes=["iden"])


def emit_rope_tables(P, nc, mem):
    s_if = mem.sb("s_invfreq", [128, 32], F32)
    s_tc = mem.sb("s_tcol", [128, 16], F32)
    ctab = mem.sb("s_ctab", [128, 512], F32)
    stab = mem.sb("s_stab", [128, 512], F32)
    scr = [mem.sb(f"s_ropescr{i}", [128, 32], F32) for i in range(3)]
    P.op("gpsimd", lambda e: e.iota(s_if[:], [[1, 32]], base=0,
                                    channel_multiplier=0,
                                    allow_small_or_imprecise_dtypes=True),
         writes=["ropeI"])
    P.op("scalar", lambda e: e.activation(s_if[:], s_if[:], AF.Exp,
                                          scale=-float(np.log(10000.0) / 32.0)),
         reads=["ropeI"], writes=["ropeI"])
    emit_rope_tables.debug = (s_if, s_tc)
    P.op("gpsimd", lambda e: e.iota(
        s_tc[:], [[128, NT]], base=0, channel_multiplier=1,
        allow_small_or_imprecise_dtypes=True), writes=["ropeT"])
    for tc in range(NT):
        P.op("vector", lambda e, tc=tc: e.tensor_scalar(
            scr[0][:], s_if[:], s_tc[:, tc:tc + 1], None, op0=ALU.mult),
            reads=["ropeI", "ropeT"], writes=["rscr0"])
        for which, bias, tab in ((0, 0.0, stab), (1, np.pi / 2, ctab)):
            P.op("vector", lambda e, bias=bias: e.tensor_scalar(
                scr[1][:], scr[0][:], float(bias), float(1.0 / (2 * np.pi)),
                op0=ALU.add, op1=ALU.mult), reads=["rscr0"], writes=["rscr1"])
            P.op("vector", lambda e: e.tensor_scalar(
                scr[1][:], scr[1][:], MAGIC, MAGIC, op0=ALU.add, op1=ALU.subtract),
                reads=["rscr1"], writes=["rscr1"])
            P.op("vector", lambda e, bias=bias: e.tensor_scalar(
                scr[2][:], scr[0][:], float(bias), None, op0=ALU.add),
                reads=["rscr0"], writes=["rscr2"])
            P.op("vector", lambda e: e.scalar_tensor_tensor(
                scr[2][:], scr[1][:], -float(TWO_PI_HI), scr[2][:],
                op0=ALU.mult, op1=ALU.add), reads=["rscr1", "rscr2"], writes=["rscr2"])
            P.op("vector", lambda e: e.scalar_tensor_tensor(
                scr[2][:], scr[1][:], -float(TWO_PI_LO), scr[2][:],
                op0=ALU.mult, op1=ALU.add), reads=["rscr1", "rscr2"], writes=["rscr2"])
            P.op("vector", lambda e: e.tensor_scalar(
                scr[2][:], scr[2][:], 3.1415925, -3.1415925,
                op0=ALU.min, op1=ALU.max), reads=["rscr2"], writes=["rscr2"])
            P.op("scalar", lambda e, tab=tab, tc=tc: e.activation(
                tab[:, tc * 32:(tc + 1) * 32], scr[2][:], AF.Sin),
                reads=["rscr2"], writes=["ropetab"])
    return ctab, stab


def emit_rope_full_tables(P, ctab, stab, c_full, s_full, tc):
    cs = ctab[:, tc * 32:(tc + 1) * 32].rearrange("p (o f) -> p o f", o=1)
    ss = stab[:, tc * 32:(tc + 1) * 32].rearrange("p (o f) -> p o f", o=1)
    P.op("vector", lambda e: e.tensor_copy(
        c_full[:].rearrange("p (g f) -> p g f", f=32), cs.to_broadcast([128, 32, 32])),
        reads=["ropetab"], writes=["cfull"])
    sv = s_full[:].rearrange("p (h x f) -> p (h x) f", x=2, f=32)
    P.op("vector", lambda e: e.tensor_scalar(
        sv[:, 0::2, :], ss.to_broadcast([128, 16, 32]), -1.0, None, op0=ALU.mult),
        reads=["ropetab"], writes=["sfull"])
    P.op("vector", lambda e: e.tensor_scalar(
        sv[:, 1::2, :], ss.to_broadcast([128, 16, 32]), 1.0, None, op0=ALU.mult),
        reads=["ropetab"], writes=["sfull"])


def emit_rope_feat(P, c_full, s_full, s_w, kw, s_sw, ksw, out_tile, out_key):
    wv = s_w[:].rearrange("p (h x f) -> p h x f", x=2, f=32)
    swv = s_sw[:].rearrange("p (h x f) -> p h x f", x=2, f=32)
    P.op("vector", lambda e: e.tensor_copy(swv[:, :, 0, :], wv[:, :, 1, :]),
         reads=[kw], writes=[ksw])
    P.op("vector", lambda e: e.tensor_copy(swv[:, :, 1, :], wv[:, :, 0, :]),
         reads=[kw], writes=[ksw])
    P.op("gpsimd", lambda e: e.tensor_tensor(s_sw[:], s_sw[:], s_full[:], op=ALU.mult),
         reads=[ksw, "sfull"], writes=[ksw])
    P.op("vector", lambda e: e.tensor_tensor(s_w[:], s_w[:], c_full[:], op=ALU.mult),
         reads=[kw, "cfull"], writes=[kw])
    P.op("vector", lambda e: e.tensor_tensor(s_w[:], s_w[:], s_sw[:], op=ALU.add),
         reads=[kw, ksw], writes=[kw])
    P.op("vector", lambda e: e.tensor_scalar(s_sw[:], s_w[:], 0.0, None, op0=ALU.min),
         reads=[kw], writes=[ksw])
    P.op("scalar", lambda e: e.activation(s_sw[:], s_sw[:], AF.Exp),
         reads=[ksw], writes=[ksw])
    P.op("vector", lambda e: e.scalar_tensor_tensor(
        out_tile[:], s_w[:], 0.0, s_sw[:], op0=ALU.max, op1=ALU.add),
        reads=[kw, ksw], writes=[out_key])


ATTN0_CUT = "full"


def build_attn0(nc, stack, P=None, mem=None, io=None):
    fused = P is not None
    if not fused:
        P = Prog(nc)
    xn = io["xn"] if fused else nc.dram_tensor("xn", [B_PER_CORE, T, D], BF16, kind="ExternalInput").ap()
    wqkvT = io["wqkvT"] if fused else nc.dram_tensor("wqkvT", [D, 3 * D], F32, kind="ExternalInput").ap()
    woT = io["woT"] if fused else nc.dram_tensor("woT", [D, D], F32, kind="ExternalInput").ap()
    s0 = io["s0"] if fused else nc.dram_tensor("s0", [B_PER_CORE, T, D], BF16, kind="ExternalOutput").ap()

    if not fused:
        sems, dma_sems = make_sems(nc, stack)
        mem = Mem(nc, stack)
    en = stack.enter_context
    s_wqkv = mem.sb("s_wqkv", [128, ND * 3 * D], BF16)
    s_wo = mem.sb("s_wo", [128, ND * D], BF16)
    s_iden = mem.sb("s_iden", [128, 128], BF16)
    s_scr128 = mem.sb("s_scr128", [128, 129], F32)
    c_full = mem.sb("s_cfull", [128, D], F32)
    s_full = mem.sb("s_sfull", [128, D], F32)
    s_xn = [mem.sb(f"s_xn{i}", [128, D], BF16) for i in range(2)]
    s_xnT = [mem.sb(f"s_xnT{i}", [128, D], BF16) for i in range(2)]
    s_q = mem.sb("s_q", [128, D], F32)
    s_qsw = mem.sb("s_qsw", [128, D], F32)
    s_k = mem.sb("s_k", [128, D], F32)
    s_ksw = mem.sb("s_ksw", [128, D], F32)
    s_v = mem.sb("s_v", [128, D], BF16)
    s_kf = mem.sb("s_kf", [128, D], BF16)
    s_qf = [[mem.sb(f"s_qf{b}_{i}", [128, D], BF16) for i in range(NT)]
            for b in range(B_PER_CORE)]
    s_kvks = [mem.sb(f"s_kvks{b}", [128, 8 * 65], BF16)
              for b in range(B_PER_CORE)]
    s_kvacc = [mem.sb(f"s_kvacc{b}", [128, 8 * 65], F32)
               for b in range(B_PER_CORE)]
    s_ones = mem.sb("s_ones1", [128, 1], BF16)
    s_qfT = mem.sb("s_qfT", [128, D], BF16)
    s_rec = mem.sb("s_rec", [128, 16], F32)
    s_at = mem.sb("s_at", [128, D], BF16)
    s_atT = mem.sb("s_atT", [128, D], BF16)
    s_s0 = [mem.sb(f"s_s0{i}", [128, D], BF16) for i in range(2)]

    pb = [mem.ps([128, 512], F32) for i in range(6)]
    pt = mem.ps([128, 1024], BF16)
    block = en(nc.Block()) if not fused else None

    for c in range(ND):
        P.dma("gpsimd", s_wqkv[:, c * 3 * D:(c + 1) * 3 * D],
              wqkvT[c * 128:(c + 1) * 128, :], writes=[("wqkv", c)])
        P.dma("gpsimd", s_wo[:, c * D:(c + 1) * D],
              woT[c * 128:(c + 1) * 128, :], writes=[("wo", c)])
    emit_identity(P, s_iden, s_scr128)
    ctab, stab = emit_rope_tables(P, nc, mem)
    P.op("vector", lambda e: e.memset(s_ones[:], 1.0), writes=["ones1"])

    KVB = {0: (pb[2], pb[3]), 1: (pb[4], pb[5])}

    if ATTN0_CUT in ("wload", "tables"):
        P.dma("sync", s0[0, 0:128, :], s_wqkv[:, 0:1024], reads=[("wqkv", 0)],
              is_output=True)
        if ATTN0_CUT == "tables":
            P.dma("gpsimd", s0[1, 0:128, 0:512], ctab[:], reads=["ropetab"], is_output=True)
            P.dma("gpsimd", s0[1, 0:128, 512:1024], stab[:], reads=["ropetab"], is_output=True)
            dbg_if, dbg_tc = emit_rope_tables.debug
            P.dma("gpsimd", s0[1, 128:256, 0:32], dbg_if[:], reads=["ropeI"], is_output=True)
            P.dma("gpsimd", s0[1, 128:256, 32:48], dbg_tc[:], reads=["ropeT"], is_output=True)
        P.finalize(block, sems, dma_sems)
        return P

    for tc in range(NT if ATTN0_CUT != "passa1" else 1):
        emit_rope_full_tables(P, ctab, stab, c_full, s_full, tc)
        for b in range(B_PER_CORE):
            i = b % 2
            kxn = ("sxn", i)
            kxnT = ("sxnT", i)
            P.dma("sync", s_xn[i][:], xn[b, tc * 128:(tc + 1) * 128, :],
                  writes=[kxn])
            for c in range(ND):
                P.op("tensor", lambda e, c=c, i=i: e.transpose(
                    pt[:, c * 128:(c + 1) * 128],
                    s_xn[i][:, c * 128:(c + 1) * 128], s_iden[:]),
                    reads=[kxn, "iden"], writes=[("pt", c)])
            P.op("vector", lambda e, i=i: e.tensor_copy(s_xnT[i][:], pt[:]),
                 reads=[("pt", c) for c in range(ND)], writes=[kxnT])
            for ng in range(6):
                bank = pb[ng % 2]
                kbank = ("pb", ng % 2)
                for dc in range(ND):
                    P.op("tensor", lambda e, ng=ng, dc=dc, i=i, bank=bank: e.matmul(
                        bank[:], s_xnT[i][:, dc * 128:(dc + 1) * 128],
                        s_wqkv[:, dc * 3 * D + ng * 512: dc * 3 * D + (ng + 1) * 512],
                        start=(dc == 0), stop=(dc == ND - 1)),
                        reads=[kxnT, ("wqkv", dc)], writes=[kbank],
                        relax=[kbank] if dc > 0 else [])
                if ng < 2:
                    half = ng
                    P.op("scalar", lambda e, bank=bank, half=half: e.activation(
                        s_q[:, half * 512:(half + 1) * 512], bank[:], AF.Copy,
                        scale=0.125), reads=[kbank], writes=["s_q"])
                elif ng < 4:
                    half = ng - 2
                    P.op("scalar", lambda e, bank=bank, half=half: e.activation(
                        s_k[:, half * 512:(half + 1) * 512], bank[:], AF.Copy),
                        reads=[kbank], writes=["s_k"])
                else:
                    half = ng - 4
                    P.op("vector", lambda e, bank=bank, half=half: e.tensor_copy(
                        s_v[:, half * 512:(half + 1) * 512], bank[:]),
                        reads=[kbank], writes=["s_v"])
            emit_rope_feat(P, c_full, s_full, s_q, "s_q", s_qsw, "s_qsw",
                           s_qf[b][tc], ("qf", b, tc))
            emit_rope_feat(P, c_full, s_full, s_k, "s_k", s_ksw, "s_ksw",
                           s_kf, "s_kf")
            kvb = KVB[b]
            for h in range(HEADS):
                bank = kvb[h // 8]
                kkv = ("pb", 2 + 2 * b + h // 8)
                po = (h % 2) * 64
                fo = ((h // 2) % 4) * 65
                P.op("tensor", lambda e, bank=bank, po=po, fo=fo, h=h: e.matmul(
                    bank[po:po + 64, fo:fo + 64],
                    s_kf[:, h * 64:(h + 1) * 64], s_v[:, h * 64:(h + 1) * 64],
                    start=True, stop=False),
                    reads=["s_kf", "s_v"], writes=[kkv])
                P.op("tensor", lambda e, bank=bank, po=po, fo=fo, h=h: e.matmul(
                    bank[po:po + 64, fo + 64:fo + 65],
                    s_kf[:, h * 64:(h + 1) * 64], s_ones[:],
                    start=False, stop=True),
                    reads=["s_kf", "ones1"], writes=[kkv], relax=[kkv])
            for half in range(2):
                kkv = ("pb", 2 + 2 * b + half)
                if tc == 0:
                    P.op("vector", lambda e, b=b, half=half, kvb=kvb: e.tensor_copy(
                        s_kvacc[b][:, half * 260:(half + 1) * 260],
                        kvb[half][:, 0:260]),
                        reads=[kkv], writes=[("kvacc", b)])
                else:
                    P.op("vector", lambda e, b=b, half=half, kvb=kvb: e.tensor_tensor(
                        s_kvacc[b][:, half * 260:(half + 1) * 260],
                        s_kvacc[b][:, half * 260:(half + 1) * 260],
                        kvb[half][:, 0:260], op=ALU.add),
                        reads=[kkv, ("kvacc", b)], writes=[("kvacc", b)])

    for b in range(B_PER_CORE):
        P.op("vector", lambda e, b=b: e.tensor_copy(
            s_kvks[b][:], s_kvacc[b][:]),
            reads=[("kvacc", b)], writes=[("kvks", b)])

    if ATTN0_CUT in ("passa", "passa1"):
        P.dma("sync", s0[0, 0:128, 0:520], s_kvks[0][:, 0:520], reads=[("kvks", 0)],
              is_output=True)
        P.dma("sync", s0[1, 0:128, :], s_qf[0][0][:], reads=[("qf", 0, 0)],
              is_output=True)
        P.finalize(block, sems, dma_sems)
        return P

    for b in range(B_PER_CORE):
        for tc in range(NT):
            kqfT = "qfT"
            for c in range(ND):
                P.op("tensor", lambda e, c=c, b=b, tc=tc: e.transpose(
                    pt[:, c * 128:(c + 1) * 128],
                    s_qf[b][tc][:, c * 128:(c + 1) * 128], s_iden[:]),
                    reads=[("qf", b, tc), "iden"], writes=[("pt", c)])
            P.op("vector", lambda e: e.tensor_copy(s_qfT[:], pt[:]),
                 reads=[("pt", c) for c in range(ND)], writes=[kqfT])
            if ATTN0_CUT == "passb00":
                P.dma("sync", s0[b, tc * 128:(tc + 1) * 128, :], s_qfT[:],
                      reads=["qfT"], is_output=True)
                continue
            for h in range(HEADS):
                bank = pb[h // 4]
                kbank = ("pb", h // 4)
                fo = (h % 4) * 65
                po = (h % 2) * 64
                co = (h // 2) * 128
                kvfo = ((h // 2) % 4) * 65
                P.op("tensor", lambda e, bank=bank, fo=fo, po=po, co=co,
                     kvfo=kvfo, b=b, h=h: e.matmul(
                         bank[:, fo:fo + 65],
                         s_qfT[po:po + 64, co:co + 128],
                         s_kvks[b][po:po + 64, (h // 8) * 260 + kvfo:(h // 8) * 260 + kvfo + 65],
                         start=(h % 4 == 0), stop=(h % 4 == 3)),
                     reads=[kqfT, ("kvks", b)], writes=[kbank])
            if ATTN0_CUT == "passb0":
                for g in range(4):
                    P.op("vector", lambda e, g=g: e.tensor_copy(
                        s_at[:, g * 256:g * 256 + 256], pb[g][:, 0:256]),
                        reads=[("pb", g)], writes=["sat"])
                P.dma("sync", s0[b, tc * 128:(tc + 1) * 128, :], s_at[:],
                      reads=["sat"], is_output=True)
                continue
            for g in range(4):
                bank = pb[g]
                kbank = ("pb", g)
                bv = bank[:, 0:260].rearrange("p (h x) -> p h x", x=65)
                P.op("vector", lambda e, bv=bv, g=g: e.tensor_scalar(
                    s_rec[:, g * 4:(g + 1) * 4], bv[:, 0:4, 64:65], 1e-6, None,
                    op0=ALU.max), reads=[kbank], writes=["rec"])
            P.op("vector", lambda e: e.reciprocal(s_rec[:], s_rec[:]),
                 reads=["rec"], writes=["rec"])
            for g in range(4):
                bank = pb[g]
                kbank = ("pb", g)
                bv = bank[:, 0:260].rearrange("p (h x) -> p h x", x=65)
                rv = s_rec[:, g * 4:(g + 1) * 4].rearrange(
                    "p (h o) -> p h o", o=1).to_broadcast([128, 4, 64])
                ov = s_at[:, g * 256:(g + 1) * 256].rearrange(
                    "p (h x) -> p h x", x=64)
                P.op("vector", lambda e, bv=bv, rv=rv, ov=ov: e.tensor_tensor(
                    ov, bv[:, 0:4, 0:64], rv, op=ALU.mult),
                    reads=[kbank, "rec"], writes=["sat"])
            if ATTN0_CUT == "passb1":
                P.dma("sync", s0[b, tc * 128:(tc + 1) * 128, :], s_at[:],
                      reads=["sat"], is_output=True)
                continue
            for c in range(ND):
                P.op("tensor", lambda e, c=c: e.transpose(
                    pt[:, c * 128:(c + 1) * 128],
                    s_at[:, c * 128:(c + 1) * 128], s_iden[:]),
                    reads=["sat", "iden"], writes=[("pt", c)])
            P.op("vector", lambda e: e.tensor_copy(s_atT[:], pt[:]),
                 reads=[("pt", c) for c in range(ND)], writes=["atT"])
            for ng in range(2):
                bank = pb[4 + ng]
                kbank = ("pb", 4 + ng)
                for dc in range(ND):
                    P.op("tensor", lambda e, ng=ng, dc=dc, bank=bank: e.matmul(
                        bank[:], s_atT[:, dc * 128:(dc + 1) * 128],
                        s_wo[:, dc * D + ng * 512: dc * D + (ng + 1) * 512],
                        start=(dc == 0), stop=(dc == ND - 1)),
                        reads=["atT", ("wo", dc)], writes=[kbank],
                        relax=[kbank] if dc > 0 else [])
                P.op("vector", lambda e, ng=ng, bank=bank, b=b, tc=tc: e.tensor_copy(
                    s_s0[tc % 2][:, ng * 512:(ng + 1) * 512], bank[:]),
                    reads=[kbank], writes=[("ss0", tc % 2)])
            P.dma("sync", s0[b, tc * 128:(tc + 1) * 128, :], s_s0[tc % 2][:],
                  reads=[("ss0", tc % 2)], is_output=True, tag=f"st_s0_{b}_{tc}")

    if not fused:
        P.finalize(block, sems, dma_sems)
    return P


def build_moe0(nc, stack, P=None, mem=None, io=None):
    fused = P is not None
    if not fused:
        P = Prog(nc)
    xn = io["xn"] if fused else nc.dram_tensor("xn", [B_PER_CORE, T, D], BF16, kind="ExternalInput").ap()
    gateT = io["gateT"] if fused else nc.dram_tensor("gateT", [D, E_EXP], F32, kind="ExternalInput").ap()
    WgT = io["WgT"] if fused else nc.dram_tensor("WgT", [E_EXP, D, F_EXP], F32, kind="ExternalInput").ap()
    WuT = io["WuT"] if fused else nc.dram_tensor("WuT", [E_EXP, D, F_EXP], F32, kind="ExternalInput").ap()
    WdT = io["WdT"] if fused else nc.dram_tensor("WdT", [E_EXP, F_EXP, D], F32, kind="ExternalInput").ap()
    WsgT = io["WsgT"] if fused else nc.dram_tensor("WsgT", [D, F_SH], F32, kind="ExternalInput").ap()
    WsuT = io["WsuT"] if fused else nc.dram_tensor("WsuT", [D, F_SH], F32, kind="ExternalInput").ap()
    WsdT = io["WsdT"] if fused else nc.dram_tensor("WsdT", [F_SH, D], F32, kind="ExternalInput").ap()
    moe = io["moe"] if fused else nc.dram_tensor("moe", [B_PER_CORE, T, D], F32, kind="ExternalOutput").ap()

    if not fused:
        sems, dma_sems = make_sems(nc, stack)
        mem = Mem(nc, stack)
    en = stack.enter_context
    s_w1 = mem.sb("m_w1", [128, ND * F_EXP], BF16)
    s_w2 = mem.sb("m_w2", [128, ND * F_EXP], BF16)
    s_w3 = mem.sb("m_w3", [128, 16 * D], BF16)
    s_hs = mem.sb("m_hs", [128, 16 * 512], BF16)
    s_xn = [mem.sb(f"m_xn{i}", [128, D], BF16) for i in range(2)]
    s_xnTt = mem.sb("m_xnTt", [128, D], BF16)
    s_xnTg = mem.sb("m_xnTg", [128, ND * 512], BF16)
    s_gate = mem.sb("m_gate", [128, ND * E_EXP], BF16)
    s_iden = mem.sb("m_iden", [128, 128], BF16)
    s_idenf = mem.sb("m_idenf", [128, 128], F32)
    s_scr128 = mem.sb("m_scr128", [128, 129], F32)
    s_iotaJ = mem.sb("m_iotaJ", [128, 256], BF16)
    s_tri = mem.sb("m_tri", [128, 128], BF16)
    s_one128 = mem.sb("m_one128", [128, 128], BF16)
    s_scores = mem.sb("m_scores", [128, NT * E_EXP], F32)
    s_scoT = mem.sb("m_scoT", [16, T], F32)
    s_junk16 = mem.sb("m_junk16", [16, T], BF16)
    s_bis = mem.sb("m_bis", [16, 8], F32)
    s_maskT = mem.sb("m_maskT", [16, T], BF16)
    s_maskt = mem.sb("m_maskt", [128, NT * E_EXP], BF16)
    s_slot = mem.sb("m_slot", [128, NT * E_EXP], F32)
    s_rhs4 = mem.sb("m_rhs4", [128, NT * 64], BF16)
    s_P = [mem.sb(f"m_P{i}", [128, 256], BF16) for i in range(2)]
    s_ixw1 = mem.sb("m_ixw1", [4, 256], F32)
    s_col = [mem.sb(f"m_col{b}", [128, 128], F32)
             for b in range(B_PER_CORE)]
    s_idxu = [mem.sb(f"m_idxu{b}", [128, 32], U32)
              for b in range(B_PER_CORE)]
    s_wcol = [mem.sb(f"m_wcol{b}", [128, 32], F32)
              for b in range(B_PER_CORE)]
    s_shout = [mem.sb(f"m_shout{i}", [128, D], F32) for i in range(2)]
    s_xsel = mem.sb("m_xsel", [128, D], BF16)
    s_xselT = mem.sb("m_xselT", [128, ND * 256], BF16)
    s_hg = mem.sb("m_hg", [128, 512], BF16)
    s_eo = [mem.sb(f"m_eo{i}", [128, D], BF16) for i in range(2)]

    pb = [mem.ps([128, 512], F32) for i in range(6)]
    pt = mem.ps([128, 1024], BF16)
    block = en(nc.Block()) if not fused else None

    emit_identity(P, s_iden, s_scr128)
    P.op("vector", lambda e: e.tensor_copy(s_idenf[:], s_iden[:]),
         reads=["iden"], writes=["idenf"])
    P.op("gpsimd", lambda e: e.iota(s_iotaJ[:], [[1, 256]], base=1,
                                    channel_multiplier=0,
                                    allow_small_or_imprecise_dtypes=True),
         writes=["iotaJ"])
    P.op("gpsimd", lambda e: e.iota(s_scr128[:, 0:1], [[0, 1]], base=0,
                                    channel_multiplier=1,
                                    allow_small_or_imprecise_dtypes=True),
         writes=["iden_scr"])
    P.op("gpsimd", lambda e: e.iota(s_scr128[:, 1:129], [[1, 128]], base=0,
                                    channel_multiplier=0,
                                    allow_small_or_imprecise_dtypes=True),
         writes=["iden_scr"])
    P.op("vector", lambda e: e.tensor_scalar(
        s_tri[:], s_scr128[:, 1:129], s_scr128[:, 0:1], None, op0=ALU.is_ge),
        reads=["iden_scr"], writes=["tri"])
    P.op("vector", lambda e: e.memset(s_one128[:], 1.0), writes=["ones128"])
    for c in range(ND):
        P.dma("gpsimd", s_gate[:, c * E_EXP:(c + 1) * E_EXP],
              gateT[c * 128:(c + 1) * 128, :], writes=["gate"])

    def transpose_tile(src_tile, src_key, dst_view, dst_key):
        for c in range(ND):
            P.op("tensor", lambda e, c=c: e.transpose(
                pt[:, c * 128:(c + 1) * 128],
                src_tile[:, c * 128:(c + 1) * 128], s_iden[:]),
                reads=[src_key, "iden"], writes=[("pt", c)])
        if dst_view.ndim == 2:
            dst_view = dst_view.rearrange("p (c t) -> p c t", t=128)
        P.op("vector", lambda e: e.tensor_copy(
            dst_view, pt[:].rearrange("p (c t) -> p c t", t=128)),
            reads=[("pt", c) for c in range(ND)], writes=[dst_key])

    for b in range(B_PER_CORE):
        for tc in range(NT):
            i = tc % 2
            kxn = ("mxn", i)
            P.dma("sync", s_xn[i][:], xn[b, tc * 128:(tc + 1) * 128, :],
                  writes=[kxn])
            transpose_tile(s_xn[i], kxn, s_xnTt[:], "xnTt")
            for dc in range(ND):
                P.op("tensor", lambda e, dc=dc: e.matmul(
                    pb[0][:, 0:E_EXP], s_xnTt[:, dc * 128:(dc + 1) * 128],
                    s_gate[:, dc * E_EXP:(dc + 1) * E_EXP],
                    start=(dc == 0), stop=(dc == ND - 1)),
                    reads=["xnTt", "gate"], writes=[("pb", 0)],
                    relax=[("pb", 0)] if dc > 0 else [])
            P.op("vector", lambda e: e.tensor_reduce(
                s_scr128[:, 0:1], pb[0][:, 0:E_EXP], AX.X, ALU.max, negate=True),
                reads=[("pb", 0)], writes=["sm_scr"])
            P.op("scalar", lambda e, tc=tc: e.activation(
                s_scores[:, tc * E_EXP:(tc + 1) * E_EXP], pb[0][:, 0:E_EXP],
                AF.Exp, bias=s_scr128[:, 0:1], accum_out=s_scr128[:, 1:2]),
                reads=[("pb", 0), "sm_scr"], writes=["scores", "sm_scr"])
            P.op("vector", lambda e: e.reciprocal(s_scr128[:, 2:3], s_scr128[:, 1:2]),
                 reads=["sm_scr"], writes=["sm_scr"])
            P.op("vector", lambda e, tc=tc: e.tensor_scalar(
                s_scores[:, tc * E_EXP:(tc + 1) * E_EXP],
                s_scores[:, tc * E_EXP:(tc + 1) * E_EXP],
                s_scr128[:, 2:3], None, op0=ALU.mult),
                reads=["scores", "sm_scr"], writes=["scores"])
            P.op("tensor", lambda e, tc=tc: e.transpose(
                pb[1][0:E_EXP, 0:128],
                s_scores[:, tc * E_EXP:(tc + 1) * E_EXP], s_idenf[:]),
                reads=["scores", "idenf"], writes=[("pb", 1)])
            P.op("vector", lambda e, tc=tc: e.tensor_copy(
                s_scoT[:, tc * 128:(tc + 1) * 128], pb[1][0:E_EXP, 0:128]),
                reads=[("pb", 1)], writes=["scoT"])

        kb = "bis"
        P.op("vector", lambda e: e.memset(s_bis[:, 0:1], 0.0), writes=[kb])
        P.op("vector", lambda e: e.memset(s_bis[:, 1:2], 1.0), writes=[kb])
        P.op("vector", lambda e: e.memset(s_bis[:, 2:3], 0.5), writes=[kb])
        for it in range(30):
            P.op("vector", lambda e: e.tensor_scalar(
                s_junk16[:], s_scoT[:], s_bis[:, 2:3], 0.0,
                op0=ALU.is_gt, op1=ALU.add, accum_out=s_bis[:, 3:4]),
                reads=["scoT", kb], writes=["junk16", kb])
            P.op("vector", lambda e: e.tensor_scalar(
                s_bis[:, 4:5], s_bis[:, 3:4], float(CAP), None, op0=ALU.is_gt),
                reads=[kb], writes=[kb])
            P.op("vector", lambda e: e.tensor_tensor(
                s_bis[:, 5:6], s_bis[:, 2:3], s_bis[:, 0:1], op=ALU.subtract),
                reads=[kb], writes=[kb])
            P.op("vector", lambda e: e.tensor_tensor(
                s_bis[:, 5:6], s_bis[:, 5:6], s_bis[:, 4:5], op=ALU.mult),
                reads=[kb], writes=[kb])
            P.op("vector", lambda e: e.tensor_tensor(
                s_bis[:, 0:1], s_bis[:, 0:1], s_bis[:, 5:6], op=ALU.add),
                reads=[kb], writes=[kb])
            P.op("vector", lambda e: e.tensor_tensor(
                s_bis[:, 6:7], s_bis[:, 2:3], s_bis[:, 1:2], op=ALU.subtract),
                reads=[kb], writes=[kb])
            P.op("vector", lambda e: e.tensor_scalar(
                s_bis[:, 7:8], s_bis[:, 4:5], -1.0, 1.0, op0=ALU.mult, op1=ALU.add),
                reads=[kb], writes=[kb])
            P.op("vector", lambda e: e.tensor_tensor(
                s_bis[:, 6:7], s_bis[:, 6:7], s_bis[:, 7:8], op=ALU.mult),
                reads=[kb], writes=[kb])
            P.op("vector", lambda e: e.tensor_tensor(
                s_bis[:, 1:2], s_bis[:, 1:2], s_bis[:, 6:7], op=ALU.add),
                reads=[kb], writes=[kb])
            P.op("vector", lambda e: e.tensor_scalar(
                s_bis[:, 2:3], s_bis[:, 0:1], 0.5, None, op0=ALU.mult),
                reads=[kb], writes=[kb])
            P.op("vector", lambda e: e.scalar_tensor_tensor(
                s_bis[:, 2:3], s_bis[:, 1:2], 0.5, s_bis[:, 2:3],
                op0=ALU.mult, op1=ALU.add), reads=[kb], writes=[kb])
        P.op("vector", lambda e: e.tensor_scalar(
            s_maskT[:], s_scoT[:], s_bis[:, 2:3], None, op0=ALU.is_gt),
            reads=["scoT", kb], writes=["maskT"])
        for tc in range(NT):
            P.op("tensor", lambda e, tc=tc: e.transpose(
                pt[:, 0:E_EXP], s_maskT[:, tc * 128:(tc + 1) * 128],
                s_iden[0:16, 0:16]),
                reads=["maskT", "iden"], writes=[("pt", 0)])
            P.op("vector", lambda e, tc=tc: e.tensor_copy(
                s_maskt[:, tc * E_EXP:(tc + 1) * E_EXP], pt[:, 0:E_EXP]),
                reads=[("pt", 0)], writes=["maskt"])
        for tc in range(NT):
            kps = ("pslot", tc)
            P.op("tensor", lambda e, tc=tc: e.matmul(
                pb[2][:, tc * E_EXP:(tc + 1) * E_EXP], s_tri[:],
                s_maskt[:, tc * E_EXP:(tc + 1) * E_EXP],
                start=True, stop=(tc == 0)),
                reads=["tri", "maskt"], writes=[kps])
            for tp in range(tc):
                P.op("tensor", lambda e, tc=tc, tp=tp: e.matmul(
                    pb[2][:, tc * E_EXP:(tc + 1) * E_EXP], s_one128[:],
                    s_maskt[:, tp * E_EXP:(tp + 1) * E_EXP],
                    start=False, stop=(tp == tc - 1)),
                    reads=["ones128", "maskt"], writes=[kps], relax=[kps])
        P.op("vector", lambda e: e.tensor_tensor(
            s_slot[:], pb[2][:, 0:256], s_maskt[:], op=ALU.mult),
            reads=["maskt"] + [("pslot", tc) for tc in range(NT)],
            writes=["slot"])

        kr = "rhs4"
        r4 = s_rhs4[:].rearrange("p (x f) -> p x f", f=4)
        P.op("gpsimd", lambda e: e.iota(r4[:, :, 0:1], [[0, 256]], base=0,
                                        channel_multiplier=1,
                                        allow_small_or_imprecise_dtypes=True),
             writes=[kr])
        P.op("vector", lambda e: e.memset(r4[:, :, 1:2], 1.0), writes=[kr])
        P.op("gpsimd", lambda e: e.iota(r4[:, :, 3:4], [[1, NT], [0, E_EXP]],
                                        base=0, channel_multiplier=0,
                                        allow_small_or_imprecise_dtypes=True),
             writes=[kr])
        for tc in range(NT):
            P.op("vector", lambda e, tc=tc: e.tensor_copy(
                s_rhs4[:, tc * 64:(tc + 1) * 64].rearrange(
                    "p (x f) -> p x f", f=4)[:, :, 2:3],
                s_scores[:, tc * E_EXP:(tc + 1) * E_EXP].rearrange(
                    "p (x f) -> p x f", f=1)),
                reads=["scores"], writes=[kr])

        for ee in range(E_EXP):
            kx = "pixw"
            for tc in range(NT):
                ip = tc % 2
                kP = ("Ptile", ip)
                P.op("vector", lambda e, ee=ee, tc=tc, ip=ip: e.tensor_scalar(
                    s_P[ip][:], s_iotaJ[:],
                    s_slot[:, tc * E_EXP + ee: tc * E_EXP + ee + 1],
                    None, op0=ALU.is_equal),
                    reads=["iotaJ", "slot"], writes=[kP])
                P.op("tensor", lambda e, ee=ee, tc=tc, ip=ip: e.matmul(
                    pb[3][0:4, 0:256],
                    s_rhs4[:, tc * 64 + 4 * ee: tc * 64 + 4 * ee + 4],
                    s_P[ip][:], start=(tc == 0), stop=(tc == NT - 1)),
                    reads=[kr, kP], writes=[kx], relax=[kx] if tc > 0 else [])
            P.op("vector", lambda e: e.tensor_copy(s_ixw1[:], pb[3][0:4, 0:256]),
                 reads=[kx], writes=["ixw1"])
            for jc in range(2):
                ci = ee * 2 + jc
                P.op("tensor", lambda e, jc=jc: e.transpose(
                    pb[1][0:128, 0:4],
                    s_ixw1[0:4, jc * 128:(jc + 1) * 128], s_idenf[0:4, 0:4]),
                    reads=["ixw1", "idenf"], writes=[("pb", 1)])
                kc = ("col", b)
                P.op("vector", lambda e, b=b, ci=ci: e.tensor_copy(
                    s_col[b][:, (ci % 32) * 4:(ci % 32) * 4 + 4],
                    pb[1][0:128, 0:4]),
                    reads=[("pb", 1)], writes=[kc])
        kc = ("col", b)
        for ci in range(32):
            c4 = ci * 4
            P.op("vector", lambda e, b=b, c4=c4: e.scalar_tensor_tensor(
                s_col[b][:, c4:c4 + 1], s_col[b][:, c4 + 3:c4 + 4], 128.0,
                s_col[b][:, c4:c4 + 1], op0=ALU.mult, op1=ALU.add),
                reads=[kc], writes=[kc])
            P.op("vector", lambda e, b=b, c4=c4: e.tensor_scalar(
                s_col[b][:, c4:c4 + 1], s_col[b][:, c4:c4 + 1],
                float(b * T), None, op0=ALU.add),
                reads=[kc], writes=[kc])
            P.op("vector", lambda e, b=b, c4=c4: e.scalar_tensor_tensor(
                s_col[b][:, c4:c4 + 1], s_col[b][:, c4 + 1:c4 + 2], -BIG_IDX,
                s_col[b][:, c4:c4 + 1], op0=ALU.mult, op1=ALU.add),
                reads=[kc], writes=[kc])
            P.op("vector", lambda e, b=b, c4=c4: e.tensor_scalar(
                s_col[b][:, c4:c4 + 1], s_col[b][:, c4:c4 + 1], BIG_IDX, None,
                op0=ALU.add), reads=[kc], writes=[kc])
            P.op("vector", lambda e, b=b, ci=ci, c4=c4: e.tensor_copy(
                s_idxu[b][:, ci:ci + 1], s_col[b][:, c4:c4 + 1]),
                reads=[kc], writes=[("idxu", b)])
            P.op("vector", lambda e, b=b, ci=ci, c4=c4: e.tensor_copy(
                s_wcol[b][:, ci:ci + 1], s_col[b][:, c4 + 2:c4 + 3]),
                reads=[kc], writes=[("wcol", b)])

    P.dma("gpsimd", s_w3[:].rearrange("p (c d) -> p c d", d=D),
          WsdT.rearrange("(c p) d -> p c d", p=128), writes=["w3"])
    P.dma("gpsimd", s_w1[:].rearrange("p (c f) -> p c f", f=F_SH),
          WsgT.rearrange("(c p) f -> p c f", p=128), writes=["w1"])
    P.dma("gpsimd", s_w2[:].rearrange("p (c f) -> p c f", f=F_SH),
          WsuT.rearrange("(c p) f -> p c f", p=128), writes=["w2"])
    for b in range(B_PER_CORE):
        for tg in range(4):
            tbase = tg * 512
            for tq in range(4):
                i = tq % 2
                kxn = ("mxn", i)
                tci = tg * 4 + tq
                P.dma("sync", s_xn[i][:], xn[b, tci * 128:(tci + 1) * 128, :],
                      writes=[kxn])
                transpose_tile(
                    s_xn[i], kxn,
                    s_xnTg[:].rearrange("p (c t) -> p c t", t=512)[
                        :, :, tq * 128:(tq + 1) * 128],
                    "xnTg")
            for fc in range(16):
                for dc in range(ND):
                    P.op("tensor", lambda e, dc=dc, fc=fc: e.matmul(
                        pb[0][:], s_w1[:, dc * F_SH + fc * 128: dc * F_SH + (fc + 1) * 128],
                        s_xnTg[:, dc * 512:(dc + 1) * 512],
                        start=(dc == 0), stop=(dc == ND - 1)),
                        reads=["w1", "xnTg"], writes=[("pb", 0)],
                        relax=[("pb", 0)] if dc > 0 else [])
                    P.op("tensor", lambda e, dc=dc, fc=fc: e.matmul(
                        pb[1][:], s_w2[:, dc * F_SH + fc * 128: dc * F_SH + (fc + 1) * 128],
                        s_xnTg[:, dc * 512:(dc + 1) * 512],
                        start=(dc == 0), stop=(dc == ND - 1)),
                        reads=["w2", "xnTg"], writes=[("pb", 1)],
                        relax=[("pb", 1)] if dc > 0 else [])
                P.op("scalar", lambda e: e.activation(
                    s_hg[:], pb[0][:], AF.Gelu),
                    reads=[("pb", 0)], writes=["hg"])
                P.op("vector", lambda e, fc=fc: e.tensor_tensor(
                    s_hs[:, fc * 512:(fc + 1) * 512], s_hg[:], pb[1][:],
                    op=ALU.mult),
                    reads=["hg", ("pb", 1)], writes=["hs"])
            for tq in range(4):
                for half in range(2):
                    kpd = ("pb", 2 + half)
                    for fc in range(16):
                        P.op("tensor", lambda e, fc=fc, tq=tq, half=half: e.matmul(
                            pb[2 + half][:],
                            s_hs[:, fc * 512 + tq * 128: fc * 512 + (tq + 1) * 128],
                            s_w3[:, fc * D + half * 512: fc * D + half * 512 + 512],
                            start=(fc == 0), stop=(fc == 15)),
                            reads=["hs", "w3"], writes=[kpd],
                            relax=[kpd] if fc > 0 else [])
                i = tq % 2
                P.op("vector", lambda e, i=i: e.tensor_copy(
                    s_shout[i][:, 0:512], pb[2][:]),
                    reads=[("pb", 2)], writes=[("shout", i)])
                P.op("vector", lambda e, i=i: e.tensor_copy(
                    s_shout[i][:, 512:1024], pb[3][:]),
                    reads=[("pb", 3)], writes=[("shout", i)])
                P.dma("sync", moe[b, tbase + tq * 128: tbase + (tq + 1) * 128, :],
                      s_shout[i][:], reads=[("shout", i)],
                      writes=[("moe", b)], is_output=True)

    bc_reg = nc.gpsimd.to_reg(B_PER_CORE * T - 1)
    wg3 = WgT.rearrange("e (c p) f -> e p c f", p=128)
    wu3 = WuT.rearrange("e (c p) f -> e p c f", p=128)
    wd3 = WdT.rearrange("e (c p) d -> e p c d", p=128)
    for ee in range(E_EXP):
        P.dma("gpsimd", s_w1[:].rearrange("p (c f) -> p c f", f=F_EXP),
              wg3[ee], writes=["w1"], tag=f"wg{ee}")
        P.dma("gpsimd", s_w2[:].rearrange("p (c f) -> p c f", f=F_EXP),
              wu3[ee], writes=["w2"], tag=f"wu{ee}")
        P.dma("gpsimd", s_w3[:].rearrange("p (c d) -> p c d", d=D),
              wd3[ee], writes=["w3"], tag=f"wd{ee}")
        for b in range(B_PER_CORE):
            kxs = "xselT"
            for jc in range(2):
                ci = ee * 2 + jc
                P.op("gpsimd", lambda e, b=b, ci=ci: e.indirect_dma_start(
                    out=s_xsel[:], out_offset=None,
                    in_=xn.rearrange("b t d -> (b t) d"),
                    in_offset=IndirectOffsetOnAxis(
                        s_idxu[b][:, ci:ci + 1], axis=0),
                    bounds_check=bc_reg, oob_is_err=False),
                    reads=[("idxu", b)], writes=["xsel"], inc=16)
                transpose_tile(
                    s_xsel, "xsel",
                    s_xselT[:].rearrange("p (c j) -> p c j", j=256)[
                        :, :, jc * 128:(jc + 1) * 128],
                    kxs)
            for fc in range(16):
                for dc in range(ND):
                    P.op("tensor", lambda e, dc=dc, fc=fc: e.matmul(
                        pb[0][:, 0:256],
                        s_w1[:, dc * F_EXP + fc * 128: dc * F_EXP + (fc + 1) * 128],
                        s_xselT[:, dc * 256:(dc + 1) * 256],
                        start=(dc == 0), stop=(dc == ND - 1)),
                        reads=["w1", kxs], writes=[("pb", 0)],
                        relax=[("pb", 0)] if dc > 0 else [])
                    P.op("tensor", lambda e, dc=dc, fc=fc: e.matmul(
                        pb[1][:, 0:256],
                        s_w2[:, dc * F_EXP + fc * 128: dc * F_EXP + (fc + 1) * 128],
                        s_xselT[:, dc * 256:(dc + 1) * 256],
                        start=(dc == 0), stop=(dc == ND - 1)),
                        reads=["w2", kxs], writes=[("pb", 1)],
                        relax=[("pb", 1)] if dc > 0 else [])
                P.op("scalar", lambda e: e.activation(
                    s_hg[:, 0:256], pb[0][:, 0:256], AF.Gelu),
                    reads=[("pb", 0)], writes=["hg"])
                P.op("vector", lambda e, fc=fc: e.tensor_tensor(
                    s_hs[:, fc * 256:(fc + 1) * 256], s_hg[:, 0:256],
                    pb[1][:, 0:256], op=ALU.mult),
                    reads=["hg", ("pb", 1)], writes=["hs"])
            for jc in range(2):
                ci = ee * 2 + jc
                for half in range(2):
                    kpd = ("pb", 4 + half)
                    for fc in range(16):
                        P.op("tensor", lambda e, fc=fc, jc=jc, half=half: e.matmul(
                            pb[4 + half][:],
                            s_hs[:, fc * 256 + jc * 128: fc * 256 + (jc + 1) * 128],
                            s_w3[:, fc * D + half * 512: fc * D + half * 512 + 512],
                            start=(fc == 0), stop=(fc == 15)),
                            reads=["hs", "w3"], writes=[kpd],
                            relax=[kpd] if fc > 0 else [])
                i = jc % 2
                keo = ("eo", i)
                P.op("vector", lambda e, b=b, ci=ci, i=i: e.tensor_scalar(
                    s_eo[i][:, 0:512], pb[4][:], s_wcol[b][:, ci:ci + 1], None,
                    op0=ALU.mult),
                    reads=[("pb", 4), ("wcol", b)], writes=[keo])
                P.op("vector", lambda e, b=b, ci=ci, i=i: e.tensor_scalar(
                    s_eo[i][:, 512:1024], pb[5][:], s_wcol[b][:, ci:ci + 1], None,
                    op0=ALU.mult),
                    reads=[("pb", 5), ("wcol", b)], writes=[keo])
                sc = P.op("gpsimd", lambda e, b=b, ci=ci, i=i: e.indirect_dma_start(
                    out=moe.rearrange("b t d -> (b t) d"),
                    out_offset=IndirectOffsetOnAxis(
                        s_idxu[b][:, ci:ci + 1], axis=0),
                    in_=s_eo[i][:], in_offset=None,
                    bounds_check=bc_reg, oob_is_err=False,
                    compute_op=ALU.add),
                    reads=[keo, ("idxu", b)], writes=[("moe", b)], inc=16)
                P.out_dmas.append(sc)
    if not fused:
        P.finalize(block, sems, dma_sems)
    return P


def build_combine_ln1(nc, stack, P=None, mem=None, io=None):
    fused = P is not None
    if not fused:
        P = Prog(nc)
    x = io["x"] if fused else nc.dram_tensor("x", [B_PER_CORE, T, D], F32, kind="ExternalInput").ap()
    s0 = io["s0"] if fused else nc.dram_tensor("s0", [B_PER_CORE, T, D], BF16, kind="ExternalInput").ap()
    moe = io["moe"] if fused else nc.dram_tensor("moe", [B_PER_CORE, T, D], F32, kind="ExternalInput").ap()
    x1 = io["x1"] if fused else nc.dram_tensor("x1", [B_PER_CORE, T, D], F32, kind="ExternalOutput").ap()
    xn1 = io["xn1"] if fused else nc.dram_tensor("xn1", [B_PER_CORE, T, D], BF16, kind="ExternalOutput").ap()
    xq1 = io["xq1"] if fused else nc.dram_tensor("xq1", [B_PER_CORE, T, D], BF16, kind="ExternalOutput").ap()
    xs127 = io["xs127"] if fused else nc.dram_tensor("xs127", [B_PER_CORE, T], F32, kind="ExternalOutput").ap()

    if not fused:
        sems, dma_sems = make_sems(nc, stack)
        mem = Mem(nc, stack)
    en = stack.enter_context
    s_x = [mem.sb(f"c_x{i}", [128, D], F32) for i in range(2)]
    s_s0 = [mem.sb(f"c_s0{i}", [128, D], BF16) for i in range(2)]
    s_moe = [mem.sb(f"c_moe{i}", [128, D], F32) for i in range(2)]
    s_g0 = mem.sb("c_g0", [128, D], BF16)
    s_x1 = [mem.sb(f"c_x1{i}", [128, D], F32) for i in range(2)]
    s_tmp = mem.sb("c_tmp", [128, D], F32)
    s_tmp2 = mem.sb("c_tmp2", [128, D], F32)
    s_xn1 = [mem.sb(f"c_xn1{i}", [128, D], BF16) for i in range(2)]
    s_xq = [mem.sb(f"c_xq{i}", [128, D], BF16) for i in range(2)]
    s_junk = mem.sb("c_junk", [128, D], BF16)
    s_small = mem.sb("c_small", [128, 16], F32)
    block = en(nc.Block()) if not fused else None

    for b in range(B_PER_CORE):
        for tc in range(NT):
            i = tc % 2
            kx, ks, km = ("cx", i), ("cs", i), ("cm", i)
            sl = slice(tc * 128, (tc + 1) * 128)
            P.dma("sync", s_x[i][:], x[b, sl, :], writes=[kx])
            P.dma("sync", s_s0[i][:], s0[b, sl, :], writes=[ks])
            P.dma("sync", s_moe[i][:], moe[b, sl, :], writes=[km])
            P.op("scalar", lambda e, i=i: e.activation(
                s_g0[:], s_moe[i][:], AF.Silu), reads=[km], writes=["cg0"])
            P.op("vector", lambda e, i=i: e.tensor_tensor(
                s_g0[:], s_g0[:], s_s0[i][:], op=ALU.mult),
                reads=["cg0", ks], writes=["cg0"])
            kx1 = ("cx1", i)
            P.op("vector", lambda e, i=i: e.tensor_tensor(
                s_x1[i][:], s_x[i][:], s_g0[:], op=ALU.add),
                reads=[kx, "cg0"], writes=[kx1])
            P.dma("sync", x1[b, sl, :], s_x1[i][:], reads=[kx1], is_output=True)

            def consume(m_ap, r_ap, ksmall, i=i, kx1=kx1, b=b, tc=tc, sl=sl):
                kxn = ("cxn1", i)
                P.op("vector", lambda e, i=i: e.tensor_scalar(
                    s_xn1[i][:], s_x1[i][:], m_ap, r_ap,
                    op0=ALU.subtract, op1=ALU.mult),
                    reads=[kx1, ksmall], writes=[kxn])
                P.dma("sync", xn1[b, sl, :], s_xn1[i][:], reads=[kxn],
                      is_output=True)
                P.op("vector", lambda e, i=i: e.tensor_scalar(
                    s_tmp[:], s_x1[i][:], m_ap, None, op0=ALU.subtract),
                    reads=[kx1, ksmall], writes=["ctmp"])
                P.op("vector", lambda e: e.tensor_reduce(
                    s_small[:, 8:9], s_tmp[:], AX.X, ALU.max,
                    apply_absolute_value=True),
                    reads=["ctmp"], writes=["cq"])
                P.op("vector", lambda e: e.tensor_tensor(
                    s_small[:, 8:9], s_small[:, 8:9], r_ap, op=ALU.mult),
                    reads=["cq", ksmall], writes=["cq"])
                P.op("vector", lambda e: e.tensor_scalar(
                    s_small[:, 8:9], s_small[:, 8:9], 1e-5, None, op0=ALU.max),
                    reads=["cq"], writes=["cq"])
                P.op("vector", lambda e: e.reciprocal(
                    s_small[:, 9:10], s_small[:, 8:9]), reads=["cq"], writes=["cq"])
                P.op("vector", lambda e: e.tensor_scalar(
                    s_small[:, 10:11], s_small[:, 9:10], 127.0, None,
                    op0=ALU.mult), reads=["cq"], writes=["cq"])
                P.op("vector", lambda e: e.tensor_tensor(
                    s_small[:, 10:11], s_small[:, 10:11], r_ap, op=ALU.mult),
                    reads=["cq", ksmall], writes=["cq"])
                P.op("vector", lambda e: e.tensor_scalar(
                    s_small[:, 11:12], s_small[:, 8:9], 1.0 / 127.0, None,
                    op0=ALU.mult), reads=["cq"], writes=["cq"])
                P.dma("sync", xs127[b, sl], s_small[:, 11:12], reads=["cq"],
                      is_output=True)
                kxq = ("cxq", i)
                P.op("vector", lambda e: e.tensor_scalar(
                    s_tmp2[:], s_tmp[:], s_small[:, 10:11], MAGIC,
                    op0=ALU.mult, op1=ALU.add),
                    reads=["ctmp", "cq"], writes=["ctmp2"])
                P.op("vector", lambda e: e.tensor_scalar(
                    s_tmp2[:], s_tmp2[:], MAGIC, None, op0=ALU.subtract),
                    reads=["ctmp2"], writes=["ctmp2"])
                P.op("vector", lambda e, i=i: e.tensor_scalar(
                    s_xq[i][:], s_tmp2[:], 127.0, -128.0,
                    op0=ALU.min, op1=ALU.max),
                    reads=["ctmp2"], writes=[kxq])
                P.dma("sync", xq1[b, sl, :], s_xq[i][:], reads=[kxq],
                      is_output=True)

            emit_ln_tile(P, None, f"c{b}_{tc}", kx1, s_x1[i], s_junk, s_small,
                         0, consume)
    if not fused:
        P.finalize(block, sems, dma_sems)
    return P


def build_attn1(nc, stack, P=None, mem=None, io=None):
    fused = P is not None
    if not fused:
        P = Prog(nc)
    xn1 = io["xn1"] if fused else nc.dram_tensor("xn1", [B_PER_CORE, T, D], BF16, kind="ExternalInput").ap()
    wqkv1T = io["wqkv1T"] if fused else nc.dram_tensor("wqkv1T", [D, D], F32, kind="ExternalInput").ap()
    wo1T = io["wo1T"] if fused else nc.dram_tensor("wo1T", [D, D], F32, kind="ExternalInput").ap()
    s1 = io["s1"] if fused else nc.dram_tensor("s1", [B_PER_CORE, T, D], BF16, kind="ExternalOutput").ap()
    qtd = nc.dram_tensor("a1_qtd", [256, 128], BF16, kind="Internal").ap()

    if not fused:
        sems, dma_sems = make_sems(nc, stack)
        mem = Mem(nc, stack)
    en = stack.enter_context
    s_wq = mem.sb("a1_wq", [128, ND * D], BF16)
    s_wo = mem.sb("a1_wo", [128, ND * D], BF16)
    s_iden = mem.sb("a1_iden", [128, 128], BF16)
    s_idenf = mem.sb("a1_idenf", [128, 128], F32)
    s_scr128 = mem.sb("a1_scr128", [128, 129], F32)
    s_ones1f = mem.sb("a1_ones1f", [1, 128], F32)
    s_mcol = mem.sb("a1_mcol", [128, HEADS], F32)
    s_xn = [mem.sb(f"a1_xn{i}", [128, D], BF16) for i in range(2)]
    s_xnTt = mem.sb("a1_xnTt", [128, D], BF16)
    s_yT = mem.sb("a1_yT", [128, ND * T], BF16)
    s_ytok = mem.sb("a1_ytok", [128, NT * D], BF16)
    s_sq = mem.sb("a1_sq", [128, D], F32)
    s_q2 = mem.sb("a1_q2", [128, NT * HEADS], F32)
    s_cneg = mem.sb("a1_cneg", [128, NT * HEADS], F32)
    s_mrow = mem.sb("a1_mrow", [1, 2 * HEADS], F32)
    s_q2T = mem.sb("a1_q2T", [128, 256], BF16)
    s_ka = [mem.sb(f"a1_ka{i}", [65, T], BF16) for i in range(2)]
    s_qa = [mem.sb(f"a1_qa{i}", [65, T], BF16) for i in range(2)]
    s_vaug = [mem.sb(f"a1_vaug{i}", [128, 65], BF16) for i in range(2)]
    s_se = [mem.sb(f"a1_se{i}", [128, T], BF16) for i in range(2)]
    s_drow = mem.sb("a1_drow", [1, T], F32)
    s_au = mem.sb("a1_au", [128, 512], BF16)
    s_aT = mem.sb("a1_aT", [128, ND * T], BF16)
    s_s1 = [mem.sb(f"a1_s1{i}", [128, D], BF16) for i in range(2)]
    pS = [mem.ps([128, 1024], F32) for _ in range(2)]
    pv = [mem.ps([128, 512], F32) for g in range(3)]
    pBt = mem.pB[:]
    pv.append(pBt.bitcast(F32))
    kB = ("pav", 3)
    block = en(nc.Block()) if not fused else None

    emit_identity(P, s_iden, s_scr128)
    P.op("vector", lambda e: e.tensor_copy(s_idenf[:], s_iden[:]),
         reads=["iden"], writes=["idenf"])
    P.op("vector", lambda e: e.memset(s_ones1f[:], 1.0), writes=["ones1f"])
    for i in range(2):
        P.op("vector", lambda e, i=i: e.memset(s_ka[i][64:65, :], 1.0),
             writes=[("kaones", i)])
        P.op("vector", lambda e, i=i: e.memset(s_vaug[i][:, 64:65], 1.0),
             writes=[("vones", i)])
    for c in range(ND):
        P.dma("gpsimd", s_wq[:, c * D:(c + 1) * D], wqkv1T[c * 128:(c + 1) * 128, :],
              writes=[("wq", c)])
        P.dma("gpsimd", s_wo[:, c * D:(c + 1) * D], wo1T[c * 128:(c + 1) * 128, :],
              writes=[("wo", c)])

    for b in range(B_PER_CORE):
        for tc in range(NT):
            i = tc % 2
            kxn = ("axn", i)
            P.dma("sync", s_xn[i][:], xn1[b, tc * 128:(tc + 1) * 128, :],
                  writes=[kxn])
            for c in range(ND):
                P.op("tensor", lambda e, c=c, i=i: e.transpose(
                    pBt[:, c * 128:(c + 1) * 128],
                    s_xn[i][:, c * 128:(c + 1) * 128], s_iden[:]),
                    reads=[kxn, "iden"], writes=[kB])
            P.op("vector", lambda e: e.tensor_copy(s_xnTt[:], pBt[:]),
                 reads=[kB], writes=["xnTt"])
            for half in range(2):
                for dc in range(ND):
                    P.op("tensor", lambda e, half=half, dc=dc: e.matmul(
                        pS[0][:, half * 512:(half + 1) * 512],
                        s_xnTt[:, dc * 128:(dc + 1) * 128],
                        s_wq[:, dc * D + half * 512: dc * D + (half + 1) * 512],
                        start=(dc == 0), stop=(dc == ND - 1)),
                        reads=[("wq", dc), "xnTt"], writes=[("pS", 0)],
                        relax=[("pS", 0)] if dc > 0 else [])
            P.op("vector", lambda e, tc=tc: e.tensor_copy(
                s_ytok[:, tc * D:(tc + 1) * D], pS[0][:]),
                reads=[("pS", 0)], writes=["ytok"])
            for c in range(ND):
                P.op("tensor", lambda e, c=c, tc=tc: e.transpose(
                    pBt[:, c * 128:(c + 1) * 128],
                    s_ytok[:, tc * D + c * 128: tc * D + (c + 1) * 128],
                    s_iden[:]),
                    reads=["ytok", "iden"], writes=[kB])
            P.op("vector", lambda e, tc=tc: e.tensor_copy(
                s_yT[:].rearrange("p (c t) -> p c t", t=T)[:, :, tc * 128:(tc + 1) * 128],
                pBt[:].rearrange("p (c t) -> p c t", t=128)),
                reads=[kB], writes=["yT"])

        for tc in range(NT):
            P.op("scalar", lambda e, tc=tc: e.activation(
                s_sq[:], s_ytok[:, tc * D:(tc + 1) * D], AF.Square),
                reads=["ytok"], writes=["sq"])
            P.op("vector", lambda e, tc=tc: e.tensor_reduce(
                s_q2[:, tc * HEADS:(tc + 1) * HEADS],
                s_sq[:].rearrange("p (h f) -> p h f", f=HD), AX.X, ALU.add),
                reads=["sq"], writes=["q2"])
        P.op("scalar", lambda e: e.activation(s_q2[:], s_q2[:], AF.Sqrt),
             reads=["q2"], writes=["q2"])
        P.op("gpsimd", lambda e: e.tensor_reduce(
            s_cneg[0:1, 0:256], s_q2[:], AX.C, ALU.max),
            reads=["q2"], writes=["cneg"])
        P.op("vector", lambda e: e.tensor_reduce(
            s_mrow[0:1, 0:HEADS],
            s_cneg[0:1, 0:256].rearrange("o (tc h) -> o h tc", h=HEADS),
            AX.X, ALU.max), reads=["cneg"], writes=["mrow"])
        P.op("tensor", lambda e: e.matmul(
            pS[0][:, 0:HEADS], s_ones1f[:], s_mrow[0:1, 0:HEADS],
            start=True, stop=True), reads=["ones1f", "mrow"], writes=[("pS", 0)])
        P.op("vector", lambda e: e.tensor_copy(s_mcol[:], pS[0][:, 0:HEADS]),
             reads=[("pS", 0)], writes=["mcol"])
        for h in range(HEADS):
            P.op("vector", lambda e, h=h: e.tensor_scalar(
                s_cneg[:].rearrange("p (tc hh) -> p tc hh", hh=HEADS)[:, :, h:h + 1],
                s_q2[:].rearrange("p (tc hh) -> p tc hh", hh=HEADS)[:, :, h:h + 1],
                s_mcol[:, h:h + 1], -1.0, op0=ALU.mult, op1=ALU.mult),
                reads=["q2", "mcol"], writes=["cneg"])
        for half in range(2):
            P.op("tensor", lambda e, half=half: e.transpose(
                pS[0][:, 0:128], s_cneg[:, half * 128:(half + 1) * 128],
                s_idenf[:]), reads=["cneg", "idenf"], writes=[("pS", 0)])
            P.op("vector", lambda e, half=half: e.tensor_copy(
                s_q2T[:, half * 128:(half + 1) * 128], pS[0][:, 0:128]),
                reads=[("pS", 0)], writes=["q2T"])
            P.dma("sync", qtd[half * 128:(half + 1) * 128, :],
                  s_q2T[:, half * 128:(half + 1) * 128],
                  reads=["q2T"], writes=["qtd"])

        def fill_kqa(h):
            i = h % 2
            po = (h % 2) * 64
            co = h // 2
            P.op("gpsimd", lambda e, po=po, co=co, i=i: e.tensor_copy(
                s_ka[i][0:64, :], s_yT[po:po + 64, co * T:(co + 1) * T]),
                reads=["yT"], writes=[("ka", i)])
            P.op("gpsimd", lambda e, po=po, co=co, i=i: e.tensor_copy(
                s_qa[i][0:64, :], s_yT[po:po + 64, co * T:(co + 1) * T]),
                reads=["yT"], writes=[("qa", i)])
            P.dma("sync", s_qa[i][64:65, :].rearrange("o (tc p) -> o tc p", p=128),
                  qtd.rearrange("(tc h) p -> h tc p", h=HEADS)[h].unsqueeze(0),
                  reads=["qtd"], writes=[("qa", i)])

        fill_kqa(0)
        for h in range(HEADS):
            hi = h % 2
            po = (h % 2) * 64
            co = h // 2
            if h + 1 < HEADS:
                fill_kqa(h + 1)
            for ktc in range(NT):
                ip = ktc % 2
                kse = ("se", ip)
                kvau = ("vaug", ip)
                P.op("vector", lambda e, ip=ip, h=h, ktc=ktc: e.tensor_copy(
                    s_vaug[ip][:, 0:64],
                    s_ytok[:, ktc * D + h * 64: ktc * D + (h + 1) * 64]),
                    reads=["ytok", ("vones", ip)], writes=[kvau])
                for pshalf in range(2):
                    for qh in range(2):
                        qg = pshalf * 2 + qh
                        kps = ("pS", pshalf)
                        P.op("tensor", lambda e, ktc=ktc, qg=qg, qh=qh,
                             pshalf=pshalf, hi=hi: e.matmul(
                                 pS[pshalf][:, qh * 512:(qh + 1) * 512],
                                 s_ka[hi][:, ktc * 128:(ktc + 1) * 128],
                                 s_qa[hi][:, qg * 512:(qg + 1) * 512],
                                 start=True, stop=True),
                             reads=[("ka", hi), ("qa", hi), ("kaones", hi)],
                             writes=[kps], relax=[kps])
                    P.op("scalar", lambda e, ip=ip, pshalf=pshalf: e.activation(
                        s_se[ip][:, pshalf * 1024:(pshalf + 1) * 1024],
                        pS[pshalf][:], AF.Exp, scale=0.125),
                        reads=[("pS", pshalf)], writes=[kse])
                for qg in range(4):
                    kpa = ("pav", qg)
                    P.op("tensor", lambda e, ktc=ktc, qg=qg, ip=ip: e.matmul(
                        pv[qg][0:65, 0:512], s_vaug[ip][:],
                        s_se[ip][:, qg * 512:(qg + 1) * 512],
                        start=(ktc == 0), stop=(ktc == NT - 1)),
                        reads=[kvau, kse], writes=[kpa],
                        relax=[kpa] if ktc > 0 else [])
            for qg in range(4):
                P.op("vector", lambda e, qg=qg: e.tensor_copy(
                    s_drow[0:1, qg * 512:(qg + 1) * 512], pv[qg][64:65, 0:512]),
                    reads=[("pav", qg)], writes=["drow"])
            P.op("scalar", lambda e: e.activation(s_drow[:], s_drow[:], AF.Ln),
                 reads=["drow"], writes=["drow"])
            P.op("scalar", lambda e: e.activation(
                s_drow[:], s_drow[:], AF.Exp, scale=-1.0),
                reads=["drow"], writes=["drow"])
            for qg in range(4):
                P.op("tensor", lambda e, qg=qg: e.matmul(
                    pS[0][0:64, 0:512], s_ones1f[0:1, 0:64],
                    s_drow[0:1, qg * 512:(qg + 1) * 512], start=True, stop=True),
                    reads=["ones1f", "drow"], writes=[("pS", 0)])
                P.op("vector", lambda e, qg=qg: e.tensor_copy(
                    s_au[0:64, 0:512], pv[qg][0:64, 0:512]),
                    reads=[("pav", qg)], writes=["au"])
                P.op("vector", lambda e, po=po, co=co, qg=qg: e.tensor_tensor(
                    s_aT[po:po + 64, co * T + qg * 512: co * T + (qg + 1) * 512],
                    s_au[0:64, 0:512], pS[0][0:64, 0:512], op=ALU.mult),
                    reads=["au", ("pS", 0)], writes=["aT"])

        for tc in range(NT):
            i = tc % 2
            ks1 = ("s1t", i)
            for half in range(2):
                bank = pv[half]
                kpa = ("pav", half)
                for fc in range(ND):
                    P.op("tensor", lambda e, fc=fc, tc=tc, half=half, bank=bank: e.matmul(
                        bank[:, 0:512],
                        s_aT[:, fc * T + tc * 128: fc * T + (tc + 1) * 128],
                        s_wo[:, fc * D + half * 512: fc * D + (half + 1) * 512],
                        start=(fc == 0), stop=(fc == ND - 1)),
                        reads=["aT", ("wo", fc)], writes=[kpa],
                        relax=[kpa] if fc > 0 else [])
                P.op("vector", lambda e, i=i, half=half, bank=bank: e.tensor_copy(
                    s_s1[i][:, half * 512:(half + 1) * 512], bank[:, 0:512]),
                    reads=[kpa], writes=[ks1])
            P.dma("sync", s1[b, tc * 128:(tc + 1) * 128, :], s_s1[i][:],
                  reads=[ks1], is_output=True)
    if not fused:
        P.finalize(block, sems, dma_sems)
    return P


def build_ffn1(nc, stack, P=None, mem=None, io=None):
    fused = P is not None
    if not fused:
        P = Prog(nc)
    xq1 = io["xq1"] if fused else nc.dram_tensor("xq1", [B_PER_CORE, T, D], BF16, kind="ExternalInput").ap()
    xs127 = io["xs127"] if fused else nc.dram_tensor("xs127", [B_PER_CORE, T], F32, kind="ExternalInput").ap()
    x1 = io["x1"] if fused else nc.dram_tensor("x1", [B_PER_CORE, T, D], F32, kind="ExternalInput").ap()
    s1 = io["s1"] if fused else nc.dram_tensor("s1", [B_PER_CORE, T, D], BF16, kind="ExternalInput").ap()
    W1T = io["W1T"] if fused else nc.dram_tensor("W1T", [D, H_FFN], F32, kind="ExternalInput").ap()
    b1c = io["b1c"] if fused else nc.dram_tensor("b1c", [128, 32], F32, kind="ExternalInput").ap()
    W2T = io["W2T"] if fused else nc.dram_tensor("W2T", [H_FFN, D], F32, kind="ExternalInput").ap()
    b2r = io["b2r"] if fused else nc.dram_tensor("b2r", [1, D], F32, kind="ExternalInput").ap()
    out = io["out"] if fused else nc.dram_tensor("out", [B_PER_CORE, T, D], F32, kind="ExternalOutput").ap()

    if not fused:
        sems, dma_sems = make_sems(nc, stack)
        mem = Mem(nc, stack)
    en = stack.enter_context
    s_w1t = mem.sb("f_w1t", [128, ND * H_FFN], BF16)
    s_w2t = mem.sb("f_w2t", [128, 32 * D], BF16)
    s_wld = mem.sb("f_wld", [128, 1024], F32)
    s_iden = mem.sb("f_iden", [128, 128], BF16)
    s_idenf = mem.sb("f_idenf", [128, 128], F32)
    s_scr128 = mem.sb("f_scr128", [128, 129], F32)
    s_ones1f = mem.sb("f_ones1f", [1, 128], F32)
    s_ones1b = mem.sb("f_ones1b", [1, 128], BF16)
    s_acc = mem.sb("f_acc", [128, 33], F32)
    s_ws = mem.sb("f_ws", [1, 4], F32)
    s_wscol = mem.sb("f_wscol", [128, 4], F32)
    s_hstore = mem.sb("f_hstore", [128, 32 * 512], BF16)
    s_xq = [mem.sb(f"f_xq{i}", [128, D], BF16) for i in range(2)]
    s_xqT = mem.sb("f_xqT", [128, ND * 512], BF16)
    _g1view = None
    s_xsrow = mem.sb("f_xsrow", [1, T], F32)
    s_hsrow = s_xsrow[0:1, 0:512]
    s_rq2 = s_xsrow[0:1, 512:1024]
    s_sclrow = mem.sb("f_sclrow", [1, T], F32)
    s_sclb = mem.sb("f_sclb", [128, 512], F32)
    s_hdq = mem.sb("f_hdq", [128, 512], F32)
    s_hmax = mem.sb("f_hmax", [128, 512], BF16)
    s_hmin = mem.sb("f_hmin", [128, 512], BF16)
    s_rq2b = s_sclb
    s_hquant = s_hdq
    s_scl2 = mem.sb("f_scl2", [128, 4], F32)
    s_b2b = mem.sb("f_b2b", [128, D], BF16)
    s_x1 = [mem.sb("f_x1", [128, D], F32)] * 2
    s_s1 = s_xq
    s_g1 = None
    s_o = s_x1
    ph = mem.ps([128, 512], F32)
    pg = [mem.ps([128, 512], F32) for i in range(2)]
    pbc = mem.ps([128, 512], F32)
    pt = mem.ps([128, 1024], BF16)
    block = en(nc.Block()) if not fused else None

    emit_identity(P, s_iden, s_scr128)
    P.op("vector", lambda e: e.tensor_copy(s_idenf[:], s_iden[:]),
         reads=["iden"], writes=["idenf"])
    P.op("vector", lambda e: e.memset(s_ones1f[:], 1.0), writes=["ones1f"])
    P.op("vector", lambda e: e.memset(s_ones1b[:], 1.0), writes=["ones1f"])
    P.dma("gpsimd", s_b2b[0:1, :], b2r, writes=["b2r"])
    s_onescolf = mem.sb("f_onescolf", [128, 1], F32)
    s_b1 = mem.sb("f_b1", [128, 32], F32)
    P.dma("sync", s_b1[:], b1c, writes=["b1"])
    P.op("vector", lambda e: e.memset(s_onescolf[:], 1.0), writes=["onescolf"])

    for wsidx, (WT, wtern, nchunk, width, kw) in enumerate([
            (W1T, s_w1t, ND, H_FFN, "w1t"),
            (W2T, s_w2t, 32, D, "w2t")]):
        nw = max(1, width // 1024)
        wt = width // nw
        for c in range(nchunk):
            for wh in range(nw):
                P.dma("sync", s_wld[:, 0:wt],
                      WT[c * 128:(c + 1) * 128, wh * wt:(wh + 1) * wt],
                      writes=["wld"], tag=f"abs{wsidx}_{c}_{wh}")
                P.op("scalar", lambda e, c=c, wh=wh, nw=nw, wt=wt: e.activation(
                    s_hstore[:, 0:wt], s_wld[:, 0:wt], AF.Abs,
                    accum_out=s_acc[:, c * nw + wh:c * nw + wh + 1]),
                    reads=["wld"], writes=["hstore", "acc"])
        P.op("vector", lambda e, nchunk=nchunk, nw=nw: e.tensor_reduce(
            s_acc[:, 32:33], s_acc[:, 0:nchunk * nw], AX.X, ALU.add),
            reads=["acc"], writes=["acc"])
        P.op("tensor", lambda e: e.matmul(
            pbc[0:1, 0:1], s_acc[:, 32:33], s_onescolf[:],
            start=True, stop=True),
            reads=["acc", "onescolf"], writes=[("pbc",)])
        P.op("vector", lambda e, wsidx=wsidx, width=width, nchunk=nchunk: e.tensor_scalar(
            s_ws[0:1, wsidx:wsidx + 1], pbc[0:1, 0:1],
            1.0 / (nchunk * 128 * width), 1e-5, op0=ALU.mult, op1=ALU.max),
            reads=[("pbc",)], writes=["ws"])
        P.op("vector", lambda e, wsidx=wsidx: e.reciprocal(
            s_ws[0:1, wsidx + 2:wsidx + 3], s_ws[0:1, wsidx:wsidx + 1]),
            reads=["ws"], writes=["ws"])
        P.op("tensor", lambda e, wsidx=wsidx: e.matmul(
            pbc[:, 0:1], s_ones1f[:], s_ws[0:1, wsidx + 2:wsidx + 3],
            start=True, stop=True), reads=["ones1f", "ws"], writes=[("pbc",)])
        P.op("vector", lambda e, wsidx=wsidx: e.tensor_copy(
            s_wscol[:, wsidx:wsidx + 1], pbc[:, 0:1]),
            reads=[("pbc",)], writes=["wscol"])
        if wsidx == 1:
            P.op("tensor", lambda e: e.matmul(
                pbc[:, 0:1], s_ones1f[:], s_ws[0:1, 1:2],
                start=True, stop=True), reads=["ones1f", "ws"], writes=[("pbc",)])
            P.op("vector", lambda e: e.tensor_copy(
                s_wscol[:, 2:3], pbc[:, 0:1]), reads=[("pbc",)], writes=["ws"])
        for c in range(nchunk):
            for wh in range(nw):
                P.dma("sync", s_wld[:, 0:wt],
                      WT[c * 128:(c + 1) * 128, wh * wt:(wh + 1) * wt],
                      writes=["wld"], tag=f"tern{wsidx}_{c}_{wh}")
                P.op("vector", lambda e, wsidx=wsidx, wt=wt: e.tensor_scalar(
                    s_wld[:, 0:wt], s_wld[:, 0:wt],
                    s_wscol[:, wsidx:wsidx + 1], None, op0=ALU.mult),
                    reads=["wld", "wscol"], writes=["wld"])
                P.op("vector", lambda e, wt=wt: e.tensor_scalar(
                    s_wld[:, 0:wt], s_wld[:, 0:wt], MAGIC, MAGIC,
                    op0=ALU.add, op1=ALU.subtract),
                    reads=["wld"], writes=["wld"])
                P.op("vector", lambda e, c=c, wh=wh, wt=wt, width=width,
                     wtern=wtern, kw=kw: e.tensor_scalar(
                    wtern[:, c * width + wh * wt: c * width + (wh + 1) * wt],
                    s_wld[:, 0:wt], 1.0, -1.0, op0=ALU.min, op1=ALU.max),
                    reads=["wld"], writes=[kw])

    for half in range(2):
        P.op("tensor", lambda e, half=half: e.matmul(
            pbc[:], s_ones1b[:], s_b2b[0:1, half * 512:(half + 1) * 512],
            start=True, stop=True), reads=["ones1f", "b2r"], writes=[("pbc",)])
        P.op("vector", lambda e, half=half: e.tensor_copy(
            s_hdq[:, 0:512], pbc[:]),
            reads=[("pbc",)], writes=[("b2s", half)])
        P.op("vector", lambda e, half=half: e.tensor_copy(
            s_b2b[:, half * 512:(half + 1) * 512], s_hdq[:, 0:512]),
            reads=[("b2s", half)], writes=["b2b"])

    for b in range(B_PER_CORE):
        P.dma("sync", s_xsrow[:], xs127[b].unsqueeze(0), writes=["xsrow"])
        P.op("vector", lambda e: e.tensor_scalar(
            s_sclrow[:], s_xsrow[:], s_ws[0:1, 0:1], None, op0=ALU.mult),
            reads=["xsrow", "ws"], writes=["sclrow"])
        for tg in range(4):
            for tq in range(4):
                i = tq % 2
                tci = tg * 4 + tq
                kxq = ("fxq", i)
                P.dma("sync", s_xq[i][:], xq1[b, tci * 128:(tci + 1) * 128, :],
                      writes=[kxq])
                for c in range(ND):
                    P.op("tensor", lambda e, c=c, i=i: e.transpose(
                        pt[:, c * 128:(c + 1) * 128],
                        s_xq[i][:, c * 128:(c + 1) * 128], s_iden[:]),
                        reads=[kxq, "iden"], writes=[("pt", c)])
                P.op("vector", lambda e, tq=tq: e.tensor_copy(
                    s_xqT[:].rearrange("p (c t) -> p c t", t=512)[
                        :, :, tq * 128:(tq + 1) * 128],
                    pt[:].rearrange("p (c t) -> p c t", t=128)),
                    reads=[("pt", c) for c in range(ND)], writes=["xqT"])
            P.op("tensor", lambda e, tg=tg: e.matmul(
                pbc[:], s_ones1f[:], s_sclrow[0:1, tg * 512:(tg + 1) * 512],
                start=True, stop=True), reads=["ones1f", "sclrow"], writes=[("pbc",)])
            P.op("vector", lambda e: e.tensor_copy(s_sclb[:], pbc[:]),
                 reads=[("pbc",)], writes=["sclb"])
            P.op("vector", lambda e: e.memset(s_hmax[:], 0.0), writes=["hmax"])
            P.op("vector", lambda e: e.memset(s_hmin[:], 0.0), writes=["hmin"])
            for Hc in range(32):
                for dc in range(ND):
                    P.op("tensor", lambda e, Hc=Hc, dc=dc: e.matmul(
                        ph[:], s_w1t[:, dc * H_FFN + Hc * 128: dc * H_FFN + (Hc + 1) * 128],
                        s_xqT[:, dc * 512:(dc + 1) * 512],
                        start=(dc == 0), stop=(dc == ND - 1)),
                        reads=["w1t", "xqT"], writes=[("ph",)],
                        relax=[("ph",)] if dc > 0 else [])
                P.op("vector", lambda e: e.tensor_tensor(
                    s_hdq[:], ph[:], s_sclb[:], op=ALU.mult),
                    reads=[("ph",), "sclb"], writes=["hdq"])
                P.op("vector", lambda e, Hc=Hc: e.tensor_scalar(
                    s_hdq[:], s_hdq[:], s_b1[:, Hc:Hc + 1], None, op0=ALU.add),
                    reads=["hdq", "b1"], writes=["hdq"])
                P.op("scalar", lambda e, Hc=Hc: e.activation(
                    s_hstore[:, Hc * 512:(Hc + 1) * 512], s_hdq[:], AF.Gelu),
                    reads=["hdq"], writes=["hstore"])
                P.op("vector", lambda e, Hc=Hc: e.tensor_tensor(
                    s_hmax[:], s_hmax[:], s_hstore[:, Hc * 512:(Hc + 1) * 512],
                    op=ALU.max),
                    reads=["hmax", "hstore"], writes=["hmax"])
                P.op("vector", lambda e, Hc=Hc: e.tensor_tensor(
                    s_hmin[:], s_hmin[:], s_hstore[:, Hc * 512:(Hc + 1) * 512],
                    op=ALU.min),
                    reads=["hmin", "hstore"], writes=["hmin"])
            P.op("vector", lambda e: e.tensor_scalar(
                s_hmin[:], s_hmin[:], -1.0, None, op0=ALU.mult),
                reads=["hmin"], writes=["hmin"])
            P.op("vector", lambda e: e.tensor_tensor(
                s_hmax[:], s_hmax[:], s_hmin[:], op=ALU.max),
                reads=["hmax", "hmin"], writes=["hmax"])
            P.op("gpsimd", lambda e: e.tensor_reduce(
                s_hsrow, s_hmax[:], AX.C, ALU.max),
                reads=["hmax"], writes=["hsrow", "xsrow"])
            P.op("vector", lambda e: e.tensor_scalar(
                s_hsrow, s_hsrow, 1e-5, None, op0=ALU.max),
                reads=["hsrow"], writes=["hsrow"])
            P.op("vector", lambda e: e.reciprocal(s_rq2, s_hsrow),
                 reads=["hsrow"], writes=["rq2"])
            P.op("vector", lambda e: e.tensor_scalar(
                s_rq2, s_rq2, 127.0, None, op0=ALU.mult),
                reads=["rq2"], writes=["rq2"])
            P.op("tensor", lambda e: e.matmul(
                pbc[:], s_ones1f[:], s_rq2, start=True, stop=True),
                reads=["ones1f", "rq2"], writes=[("pbc",)])
            P.op("vector", lambda e: e.tensor_copy(s_rq2b[:], pbc[:]),
                 reads=[("pbc",)], writes=["sclb"])
            for tq in range(4):
                P.op("tensor", lambda e, tq=tq: e.transpose(
                    pbc[0:128, 0:1], s_hsrow[0:1, tq * 128:(tq + 1) * 128],
                    s_idenf[0:1, 0:1]),
                    reads=["hsrow", "idenf"], writes=[("pbc",)])
                P.op("vector", lambda e, tq=tq: e.tensor_copy(
                    s_scl2[:, tq:tq + 1], pbc[0:128, 0:1]),
                    reads=[("pbc",)], writes=["scl2"])
            P.op("vector", lambda e: e.tensor_scalar(
                s_scl2[:], s_scl2[:], s_wscol[:, 2:3], None, op0=ALU.mult),
                reads=["scl2", "ws"], writes=["scl2"])
            P.op("vector", lambda e: e.tensor_scalar(
                s_scl2[:], s_scl2[:], 1.0 / 127.0, None, op0=ALU.mult),
                reads=["scl2"], writes=["scl2"])
            for Hc in range(32):
                P.op("vector", lambda e, Hc=Hc: e.tensor_tensor(
                    s_hquant[:], s_hstore[:, Hc * 512:(Hc + 1) * 512],
                    s_rq2b[:], op=ALU.mult),
                    reads=["hstore", "sclb"], writes=["hdq"])
                P.op("vector", lambda e: e.tensor_scalar(
                    s_hquant[:], s_hquant[:], MAGIC, MAGIC,
                    op0=ALU.add, op1=ALU.subtract),
                    reads=["hdq"], writes=["hdq"])
                P.op("vector", lambda e, Hc=Hc: e.tensor_scalar(
                    s_hstore[:, Hc * 512:(Hc + 1) * 512], s_hquant[:],
                    127.0, -128.0, op0=ALU.min, op1=ALU.max),
                    reads=["hdq"], writes=["hstore"])
            for tq in range(4):
                i = tq % 2
                tci = tg * 4 + tq
                kx1, ks1 = ("fx1",), ("fs1", i)
                P.dma("sync", s_x1[i][:], x1[b, tci * 128:(tci + 1) * 128, :],
                      writes=[kx1])
                P.dma("sync", s_s1[i][:], s1[b, tci * 128:(tci + 1) * 128, :],
                      writes=[ks1, ("fxq", i)])
                for half in range(2):
                    kpg = ("pg", half)
                    for Hc in range(32):
                        P.op("tensor", lambda e, Hc=Hc, tq=tq, half=half: e.matmul(
                            pg[half][:],
                            s_hstore[:, Hc * 512 + tq * 128: Hc * 512 + (tq + 1) * 128],
                            s_w2t[:, Hc * D + half * 512: Hc * D + (half + 1) * 512],
                            start=(Hc == 0), stop=(Hc == 31)),
                            reads=["hstore", "w2t"], writes=[kpg],
                            relax=[kpg] if Hc > 0 else [])
                    P.op("vector", lambda e, half=half, tq=tq: e.tensor_scalar(
                        s_xqT[:, 1024 + half * 512: 1024 + (half + 1) * 512], pg[half][:],
                        s_scl2[:, tq:tq + 1], None, op0=ALU.mult),
                        reads=[("pg", half), "scl2"], writes=["g1", "xqT"])
                P.op("vector", lambda e: e.tensor_tensor(
                    s_xqT[:, 1024:2048], s_xqT[:, 1024:2048], s_b2b[:], op=ALU.add),
                    reads=["g1", "b2b"], writes=["g1"])
                P.op("scalar", lambda e: e.activation(
                    s_xqT[:, 1024:2048], s_xqT[:, 1024:2048], AF.Silu), reads=["g1"], writes=["g1"])
                ko = ("fo",)
                P.op("vector", lambda e, i=i: e.tensor_tensor(
                    s_xqT[:, 1024:2048], s_xqT[:, 1024:2048], s_s1[i][:], op=ALU.mult),
                    reads=["g1", ks1], writes=["g1"])
                P.op("vector", lambda e, i=i: e.tensor_tensor(
                    s_o[i][:], s_x1[i][:], s_xqT[:, 1024:2048], op=ALU.add),
                    reads=[kx1, "g1"], writes=[ko, kx1])
                P.dma("sync", out[b, tci * 128:(tci + 1) * 128, :], s_o[i][:],
                      reads=[ko], is_output=True)
    if not fused:
        P.finalize(block, sems, dma_sems)
    return P



def prep_weights(inputs):
    tr = lambda a: np.ascontiguousarray(np.asarray(a).T)
    w = {}
    w["wqkvT"] = tr(inputs["L0_attn_w_qkv"])
    w["woT"] = tr(inputs["L0_attn_w_out"])
    w["gateT"] = tr(inputs["L0_ffn_gate_w"])
    w["WgT"] = np.ascontiguousarray(np.asarray(inputs["L0_ffn_Wg"]).transpose(0, 2, 1))
    w["WuT"] = np.ascontiguousarray(np.asarray(inputs["L0_ffn_Wu"]).transpose(0, 2, 1))
    w["WdT"] = np.ascontiguousarray(np.asarray(inputs["L0_ffn_Wd"]).transpose(0, 2, 1))
    w["WsgT"] = tr(inputs["L0_ffn_Ws_g"])
    w["WsuT"] = tr(inputs["L0_ffn_Ws_u"])
    w["WsdT"] = tr(inputs["L0_ffn_Ws_d"])
    w["wqkv1T"] = tr(inputs["L1_attn_W_qkv"])
    w["wo1T"] = tr(inputs["L1_attn_W_o"])
    w["W1T"] = tr(inputs["L1_ffn_W1"])
    w["b1c"] = np.ascontiguousarray(np.asarray(inputs["L1_ffn_b1"]).reshape(H_FFN // 128, 128).T)
    w["W2T"] = tr(inputs["L1_ffn_W2"])
    w["b2c"] = np.ascontiguousarray(np.asarray(inputs["L1_ffn_b2"]).reshape(D // 128, 128).T)
    w["b2r"] = np.ascontiguousarray(np.asarray(inputs["L1_ffn_b2"]).reshape(1, D))
    return w


def build_fused(nc, stack):
    P = Prog(nc)
    sems, dma_sems = make_sems(nc, stack)
    mem = Mem(nc, stack)
    io = {}

    def ext(name, shape, dt):
        io[name] = nc.dram_tensor(name, shape, dt, kind="ExternalInput").ap()

    def mid(name, shape, dt):
        io[name] = nc.dram_tensor(name, shape, dt, kind="Internal").ap()

    ext("x", [B_PER_CORE, T, D], F32)
    ext("wqkvT", [D, 3 * D], F32)
    ext("woT", [D, D], F32)
    ext("gateT", [D, E_EXP], F32)
    ext("WgT", [E_EXP, D, F_EXP], F32)
    ext("WuT", [E_EXP, D, F_EXP], F32)
    ext("WdT", [E_EXP, F_EXP, D], F32)
    ext("WsgT", [D, F_SH], F32)
    ext("WsuT", [D, F_SH], F32)
    ext("WsdT", [F_SH, D], F32)
    ext("wqkv1T", [D, D], F32)
    ext("wo1T", [D, D], F32)
    ext("W1T", [D, H_FFN], F32)
    ext("b1c", [128, 32], F32)
    ext("W2T", [H_FFN, D], F32)
    ext("b2r", [1, D], F32)
    mid("xn", [B_PER_CORE, T, D], BF16)
    mid("s0", [B_PER_CORE, T, D], BF16)
    mid("moe", [B_PER_CORE, T, D], F32)
    mid("x1", [B_PER_CORE, T, D], F32)
    mid("xn1", [B_PER_CORE, T, D], BF16)
    mid("xq1", [B_PER_CORE, T, D], BF16)
    mid("xs127", [B_PER_CORE, T], F32)
    mid("s1", [B_PER_CORE, T, D], BF16)
    io["out"] = nc.dram_tensor("out", [B_PER_CORE, T, D], F32,
                               kind="ExternalOutput").ap()

    for bld in (build_ln0, build_attn0, build_moe0, build_combine_ln1,
                build_attn1, build_ffn1):
        mem.reset()
        bld(nc, stack, P=P, mem=mem, io=io)
        P.fence()

    block = stack.enter_context(nc.Block())
    P.finalize(block, sems, dma_sems)
    return P


FUSED = True


def _launch(build_fn, in_maps):
    from contextlib import ExitStack
    with ExitStack() as stack:
        nc = bass.Bass("TRN2", target_bir_lowering=False,
                       detect_race_conditions=False)
        build_fn(nc, stack)
        res = run_bass_kernel_spmd(nc, in_maps, core_ids=list(range(NCORES)))
    return [{k: np.asarray(v) for k, v in r.items()} for r in res.results]


def kernel(**inputs):
    w = prep_weights(inputs)
    x = np.ascontiguousarray(np.asarray(inputs["L0_attn_x"], dtype=np.float32))
    shards = [x[c * B_PER_CORE:(c + 1) * B_PER_CORE] for c in range(NCORES)]

    if FUSED:
        rf = _launch(build_fused, [
            {"x": shards[c], "wqkvT": w["wqkvT"], "woT": w["woT"],
             "gateT": w["gateT"], "WgT": w["WgT"], "WuT": w["WuT"],
             "WdT": w["WdT"], "WsgT": w["WsgT"], "WsuT": w["WsuT"],
             "WsdT": w["WsdT"], "wqkv1T": w["wqkv1T"], "wo1T": w["wo1T"],
             "W1T": w["W1T"], "b1c": w["b1c"], "W2T": w["W2T"],
             "b2r": w["b2r"]} for c in range(NCORES)])
        out = np.concatenate([r["out"] for r in rf], axis=0)
        return out.astype(np.float32)

    r1 = _launch(build_ln0, [{"x": s} for s in shards])
    xn = [r["xn"] for r in r1]

    r2 = _launch(build_attn0, [
        {"xn": xn[c], "wqkvT": w["wqkvT"], "woT": w["woT"]}
        for c in range(NCORES)])
    s0 = [r["s0"] for r in r2]

    r3 = _launch(build_moe0, [
        {"xn": xn[c], "gateT": w["gateT"], "WgT": w["WgT"], "WuT": w["WuT"],
         "WdT": w["WdT"], "WsgT": w["WsgT"], "WsuT": w["WsuT"],
         "WsdT": w["WsdT"]} for c in range(NCORES)])
    moe = [r["moe"] for r in r3]

    r4 = _launch(build_combine_ln1, [
        {"x": shards[c], "s0": s0[c], "moe": moe[c]} for c in range(NCORES)])

    r5 = _launch(build_attn1, [
        {"xn1": r4[c]["xn1"], "wqkv1T": w["wqkv1T"], "wo1T": w["wo1T"]}
        for c in range(NCORES)])

    r6 = _launch(build_ffn1, [
        {"xq1": r4[c]["xq1"], "xs127": r4[c]["xs127"], "x1": r4[c]["x1"],
         "s1": r5[c]["s1"], "W1T": w["W1T"], "b1c": w["b1c"],
         "W2T": w["W2T"], "b2r": w["b2r"]} for c in range(NCORES)])

    out = np.concatenate([r["out"] for r in r6], axis=0)
    return out.astype(np.float32)


# version 05
# speedup vs baseline: 1.1825x; 1.0626x over previous
import numpy as np
import concourse.bass as bass
import concourse.mybir as mybir
from concourse.bass import IndirectOffsetOnAxis
from concourse.bass_utils import run_bass_kernel_spmd

F32 = mybir.dt.float32
BF16 = mybir.dt.bfloat16
U32 = mybir.dt.uint32
AF = mybir.ActivationFunctionType
ALU = mybir.AluOpType
AX = mybir.AxisListType

MAGIC = 1.5 * 2 ** 23
NCORES = 8
B_PER_CORE = 2
T = 2048
D = 1024
NT = T // 128
ND = D // 128
HEADS = 16
HD = 64
E_EXP = 16
CAP = 256
F_EXP = 2048
F_SH = 2048
H_FFN = 4096
LN_EPS = 1e-5
TWO_PI_HI = np.float32(6.2831855)
TWO_PI_LO = np.float32(2 * np.pi - np.float64(np.float32(6.2831855)))
BIG_IDX = 1.0e7

ENGINES = ("sync", "vector", "scalar", "tensor", "gpsimd")


class Prog:

    def __init__(self, nc):
        self.nc = nc
        self.ops = []
        self.last_write = {}
        self.readers = {}
        self.out_dmas = []

    def op(self, engine, fn, reads=(), writes=(), inc=1, tag="", relax=()):
        idx = len(self.ops)
        deps = {}
        for d in getattr(self, "_fence_deps", ()):
            deps.setdefault(d, set())
        for k in reads:
            w = self.last_write.get(k)
            if w is not None:
                deps.setdefault(w, set()).add(k)
        for k in writes:
            w = self.last_write.get(k)
            if w is not None:
                deps.setdefault(w, set()).add(k)
            for rd in self.readers.get(k, ()):
                deps.setdefault(rd, set()).add(k)
        self.ops.append(dict(engine=engine, fn=fn, inc=inc, idx=idx,
                             deps=deps, need_inc=False, tag=tag,
                             relax=frozenset(relax)))
        for k in reads:
            self.readers.setdefault(k, []).append(idx)
        for k in writes:
            self.last_write[k] = idx
            self.readers[k] = []
        return idx

    def dma(self, queue, out, in_, reads=(), writes=(), tag="", is_output=False):
        idx = self.op(queue, lambda e: e.dma_start(out=out, in_=in_),
                      reads=reads, writes=writes, inc=16, tag=tag)
        if is_output:
            self.out_dmas.append(idx)
        return idx

    def dma_transpose(self, queue, out, in_, reads=(), writes=(), tag=""):
        return self.op(queue, lambda e: e.dma_start_transpose(out, in_),
                       reads=reads, writes=writes, inc=16, tag=tag)

    def fence(self):
        start = getattr(self, "_fence_start", 0)
        prior = set(range(start, len(self.ops)))
        markers = []
        for eng in ENGINES:
            i = self.op(eng, lambda e: e.nop(), tag="fence")
            self.ops[i]["deps"] = {d: set() for d in prior}
            markers.append(i)
        self._fence_deps = markers
        self._fence_start = len(self.ops)

    NSLOT = 6
    SERIALIZE = False

    DRIVER_MODE = False
    NSEM_DRIVER = 1
    DRIVER_Q = "sync"
    ALL_INC = False
    NO_DEDUP = False
    TENSOR_EDGES = False
    OWN_EDGES = True

    def finalize(self, block, sems, dma_sems):
        if Prog.DRIVER_MODE:
            nsem = Prog.NSEM_DRIVER
            gsems = [dma_sems[Prog.DRIVER_Q][i] for i in range(nsem)]
            cums = []
            tot = [0] * nsem
            for i, op in enumerate(self.ops):
                cums.append(tuple(tot))
                tot[i % nsem] += op["inc"]
                op["dsem"] = i % nsem

            def section(eng_name, eng):
                for op, before in zip(self.ops, cums):
                    if op["engine"] != eng_name:
                        continue
                    for s in range(nsem):
                        if before[s]:
                            eng.wait_ge(gsems[s], before[s])
                    inst = op["fn"](eng)
                    inst.then_inc(gsems[op["dsem"]], op["inc"])
                if eng_name == "sync":
                    for s in range(nsem):
                        if tot[s]:
                            eng.wait_ge(gsems[s], tot[s])
            for name in ENGINES:
                deco = getattr(block, name)

                def make(n):
                    def body(eng):
                        section(n, eng)
                    return body
                deco(make(name))
            return

        if Prog.SERIALIZE:
            for op in self.ops:
                op["deps"] = {i: set() for i in range(op["idx"])}
        if Prog.ALL_INC:
            for op in self.ops:
                op["need_inc"] = True
        """sems: per-engine compute sems. dma_sems: dict queue ->
        list of NSLOT slot semaphores (queues that issue DMAs)."""
        ops = self.ops
        slot_counts = {}
        for op in ops:
            if op["inc"] == 16:
                q = op["engine"]
                cnt = slot_counts.setdefault(q, [0] * Prog.NSLOT)
                seq = sum(cnt)
                s = seq % Prog.NSLOT
                cnt[s] += 1
                op["slot"] = s
                op["slot_ord"] = cnt[s]
                op["need_inc"] = True
        def edge_needed(op, d, via):
            dop = ops[d]
            if dop["engine"] != op["engine"]:
                return True
            if not Prog.OWN_EDGES:
                return False
            if op["engine"] == "tensor" and dop["engine"] == "tensor":
                if via and via <= op["relax"]:
                    return False
            return True

        for op in ops:
            for d, via in op["deps"].items():
                dop = ops[d]
                if dop["inc"] != 1:
                    continue
                if edge_needed(op, d, via):
                    dop["need_inc"] = True

        cum = {e: 0 for e in ENGINES}
        for op in ops:
            if op["inc"] == 1 and op["need_inc"]:
                cum[op["engine"]] += 1
            op["cum_after"] = cum[op["engine"]]

        clocks = {e: {} for e in ENGINES}

        def want(op, key, v):
            eng = op["engine"]
            if not Prog.NO_DEDUP and clocks[eng].get(key, 0) >= v:
                return
            if Prog.NO_DEDUP and op["waits"].get(key, 0) >= v:
                return
            op["waits"][key] = max(op["waits"].get(key, 0), v)
            clocks[eng][key] = v

        for op in ops:
            eng = op["engine"]
            op["waits"] = {}
            for d in sorted(op["deps"]):
                dop = ops[d]
                via = op["deps"][d]
                if dop["inc"] == 16:
                    key = ("dma", dop["engine"], dop["slot"])
                    want(op, key, 16 * dop["slot_ord"])
                elif edge_needed(op, d, via):
                    want(op, ("eng", dop["engine"]), dop["cum_after"])
            if op["inc"] == 16 and op["slot_ord"] > 1:
                want(op, ("dma", op["engine"], op["slot"]),
                     16 * (op["slot_ord"] - 1))

        final_waits = {}
        for idx in self.out_dmas:
            op = ops[idx]
            key = ("dma", op["engine"], op["slot"])
            final_waits[key] = max(final_waits.get(key, 0), 16 * op["slot_ord"])

        def sem_of(key):
            if key[0] == "dma":
                return dma_sems[key[1]][key[2]]
            return sems[key[1]]

        def section(eng_name, eng):
            for op in ops:
                if op["engine"] != eng_name:
                    continue
                for key in sorted(op["waits"], key=str):
                    eng.wait_ge(sem_of(key), op["waits"][key])
                inst = op["fn"](eng)
                if op["need_inc"]:
                    if op["inc"] == 16:
                        inst.then_inc(dma_sems[eng_name][op["slot"]], 16)
                    else:
                        inst.then_inc(sems[eng_name], 1)
            if eng_name == "sync":
                for key in sorted(final_waits, key=str):
                    eng.wait_ge(sem_of(key), final_waits[key])

        for name in ENGINES:
            deco = getattr(block, name)

            def make(n):
                def body(eng):
                    section(n, eng)
                return body
            deco(make(name))


class Mem:

    COLS = 106016

    def __init__(self, nc, stack):
        en = stack.enter_context
        self.arena = en(nc.sbuf_tensor("arena", [128, Mem.COLS], BF16))
        self.a16 = self.arena[:]
        self.a32 = self.arena[:].bitcast(F32)
        self.au32 = self.arena[:].bitcast(U32)
        self.p8 = en(nc.psum_tensor("pp8", [128, 2048], F32))
        self.p4 = [en(nc.psum_tensor(f"pp4_{i}", [128, 512], F32))
                   for i in range(3)]
        self.pB = en(nc.psum_tensor("ppB", [128, 1024], BF16))
        self.reset()

    def reset(self):
        self.off = 0
        self.ps_off = 0
        self.ps_i = 0

    def sb(self, name, shape, dtype):
        p, n = shape
        width = {BF16: 1, F32: 2, U32: 2}[dtype] * n
        width = (width + 31) & ~31
        assert self.off + width <= Mem.COLS, \
            f"SBUF arena overflow at {name}: {self.off}+{width}"
        base = self.off
        self.off += width
        if dtype == BF16:
            return self.a16[0:p, base:base + n]
        if dtype == F32:
            return self.a32[0:p, base // 2: base // 2 + n]
        return self.au32[0:p, base // 2: base // 2 + n]

    def ps(self, shape, dtype):
        p, n = shape
        if dtype == BF16:
            assert n <= 1024
            return self.pB[0:p, 0:n]
        nb = (n + 511) // 512
        if self.ps_off + nb * 512 <= 2048:
            base = self.ps_off
            self.ps_off += nb * 512
            return self.p8[0:p, base:base + n]
        assert nb == 1 and self.ps_i < 3, "PSUM pool overflow"
        t = self.p4[self.ps_i]
        self.ps_i += 1
        return t[0:p, 0:n]


DMA_QUEUES = ("sync", "gpsimd", "scalar", "vector")


def make_sems(nc, stack):
    sems = {e: stack.enter_context(nc.semaphore(f"sem_{e}")) for e in ENGINES}
    dma_sems = {q: [stack.enter_context(nc.semaphore(f"dsem_{q}_{i}"))
                    for i in range(Prog.NSLOT)] for q in DMA_QUEUES}
    nums = [h.num for h in sems.values()]
    nums += [h.num for hs in dma_sems.values() for h in hs]
    nc.gpsimd.sem_clear(range(min(nums), max(nums) + 1))
    nc.all_engine_barrier()
    return sems, dma_sems



def alloc_psum_banks(nc, stack, n_f32=8, n_bf16=0, prefix="pb"):
    banks = []
    for i in range(n_f32):
        banks.append(stack.enter_context(
            nc.psum_tensor(f"{prefix}{i}", [128, 512], F32)))
    for i in range(n_bf16):
        banks.append(stack.enter_context(
            nc.psum_tensor(f"{prefix}b{i}", [128, 1024], BF16)))
    return banks


def emit_ln_tile(P, sb, tag, x_tile_key, s_x, s_junk, s_small, col0,
                 out_fn):
    ksmall = tag + "_stats"
    P.op("vector", lambda e: e.tensor_reduce(
        s_small[:, col0 + 2:col0 + 3], s_x[:], AX.X, ALU.add),
        reads=[x_tile_key], writes=[ksmall])
    P.op("scalar", lambda e: e.activation(
        s_junk[:], s_x[:], AF.Square, accum_out=s_small[:, col0 + 3:col0 + 4]),
        reads=[x_tile_key], writes=[ksmall, tag + "_junk"])
    P.op("vector", lambda e: e.tensor_scalar(
        s_small[:, col0:col0 + 1], s_small[:, col0 + 2:col0 + 3],
        1.0 / D, None, op0=ALU.mult), reads=[ksmall], writes=[ksmall])
    P.op("vector", lambda e: e.tensor_scalar(
        s_small[:, col0 + 4:col0 + 5], s_small[:, col0 + 3:col0 + 4],
        1.0 / D, None, op0=ALU.mult), reads=[ksmall], writes=[ksmall])
    P.op("vector", lambda e: e.tensor_tensor(
        s_small[:, col0 + 5:col0 + 6], s_small[:, col0:col0 + 1],
        s_small[:, col0:col0 + 1], op=ALU.mult), reads=[ksmall], writes=[ksmall])
    P.op("vector", lambda e: e.tensor_tensor(
        s_small[:, col0 + 4:col0 + 5], s_small[:, col0 + 4:col0 + 5],
        s_small[:, col0 + 5:col0 + 6], op=ALU.subtract),
        reads=[ksmall], writes=[ksmall])
    P.op("vector", lambda e: e.tensor_scalar(
        s_small[:, col0 + 4:col0 + 5], s_small[:, col0 + 4:col0 + 5],
        LN_EPS, None, op0=ALU.add), reads=[ksmall], writes=[ksmall])
    P.op("scalar", lambda e: e.activation(
        s_small[:, col0 + 5:col0 + 6], s_small[:, col0 + 4:col0 + 5], AF.Ln),
        reads=[ksmall], writes=[ksmall])
    P.op("scalar", lambda e: e.activation(
        s_small[:, col0 + 1:col0 + 2], s_small[:, col0 + 5:col0 + 6],
        AF.Exp, scale=-0.5), reads=[ksmall], writes=[ksmall])
    out_fn(s_small[:, col0:col0 + 1], s_small[:, col0 + 1:col0 + 2], ksmall)


def build_ln0(nc, stack, P=None, mem=None, io=None):
    fused = P is not None
    if not fused:
        P = Prog(nc)
    x = io["x"] if fused else nc.dram_tensor("x", [B_PER_CORE, T, D], F32, kind="ExternalInput").ap()
    xn = io["xn"] if fused else nc.dram_tensor("xn", [B_PER_CORE, T, D], BF16, kind="ExternalOutput").ap()

    if not fused:
        sems, dma_sems = make_sems(nc, stack)
        mem = Mem(nc, stack)
    block = stack.enter_context(nc.Block()) if not fused else None

    s_x = [mem.sb(f"s_x{i}", [128, D], F32) for i in range(3)]
    s_xn = [mem.sb(f"s_xn{i}", [128, D], BF16) for i in range(3)]
    s_junk = mem.sb("s_junk", [128, D], BF16)
    s_small = mem.sb("s_small", [128, 8], F32)

    for b in range(B_PER_CORE):
        for tc in range(NT):
            i = (b * NT + tc) % 3
            kx = ("sx", i)
            kxn = ("sxn", i)
            P.dma("sync", s_x[i][:], x[b, tc * 128:(tc + 1) * 128, :],
                  writes=[kx], tag=f"ldx{b}_{tc}")

            def consume(m_ap, r_ap, ksmall, i=i, kx=kx, kxn=kxn, b=b, tc=tc):
                P.op("vector", lambda e: e.tensor_scalar(
                    s_xn[i][:], s_x[i][:], m_ap, r_ap,
                    op0=ALU.subtract, op1=ALU.mult),
                    reads=[kx, ksmall], writes=[kxn])
                P.dma("sync", xn[b, tc * 128:(tc + 1) * 128, :], s_xn[i][:],
                      reads=[kxn], tag=f"stxn{b}_{tc}", is_output=True)

            emit_ln_tile(P, None, f"ln{b}_{tc}", kx, s_x[i], s_junk, s_small,
                         0, consume)

    if not fused:
        P.finalize(block, sems, dma_sems)
    return P


def emit_identity(P, s_iden, s_scr128):
    P.op("gpsimd", lambda e: e.iota(s_scr128[:, 0:1], [[0, 1]], base=0,
                                    channel_multiplier=1,
                                    allow_small_or_imprecise_dtypes=True),
         writes=["iden_scr"])
    P.op("gpsimd", lambda e: e.iota(s_scr128[:, 1:129], [[1, 128]], base=0,
                                    channel_multiplier=0,
                                    allow_small_or_imprecise_dtypes=True),
         writes=["iden_scr"])
    P.op("vector", lambda e: e.tensor_scalar(
        s_iden[:], s_scr128[:, 1:129], s_scr128[:, 0:1], None, op0=ALU.is_equal),
        reads=["iden_scr"], writes=["iden"])


def emit_rope_tables(P, nc, mem):
    s_if = mem.sb("s_invfreq", [128, 32], F32)
    s_tc = mem.sb("s_tcol", [128, 16], F32)
    ctab = mem.sb("s_ctab", [128, 512], F32)
    stab = mem.sb("s_stab", [128, 512], F32)
    scr = [mem.sb(f"s_ropescr{i}", [128, 32], F32) for i in range(3)]
    P.op("gpsimd", lambda e: e.iota(s_if[:], [[1, 32]], base=0,
                                    channel_multiplier=0,
                                    allow_small_or_imprecise_dtypes=True),
         writes=["ropeI"])
    P.op("scalar", lambda e: e.activation(s_if[:], s_if[:], AF.Exp,
                                          scale=-float(np.log(10000.0) / 32.0)),
         reads=["ropeI"], writes=["ropeI"])
    emit_rope_tables.debug = (s_if, s_tc)
    P.op("gpsimd", lambda e: e.iota(
        s_tc[:], [[128, NT]], base=0, channel_multiplier=1,
        allow_small_or_imprecise_dtypes=True), writes=["ropeT"])
    for tc in range(NT):
        P.op("vector", lambda e, tc=tc: e.tensor_scalar(
            scr[0][:], s_if[:], s_tc[:, tc:tc + 1], None, op0=ALU.mult),
            reads=["ropeI", "ropeT"], writes=["rscr0"])
        for which, bias, tab in ((0, 0.0, stab), (1, np.pi / 2, ctab)):
            P.op("vector", lambda e, bias=bias: e.tensor_scalar(
                scr[1][:], scr[0][:], float(bias), float(1.0 / (2 * np.pi)),
                op0=ALU.add, op1=ALU.mult), reads=["rscr0"], writes=["rscr1"])
            P.op("vector", lambda e: e.tensor_scalar(
                scr[1][:], scr[1][:], MAGIC, MAGIC, op0=ALU.add, op1=ALU.subtract),
                reads=["rscr1"], writes=["rscr1"])
            P.op("vector", lambda e, bias=bias: e.tensor_scalar(
                scr[2][:], scr[0][:], float(bias), None, op0=ALU.add),
                reads=["rscr0"], writes=["rscr2"])
            P.op("vector", lambda e: e.scalar_tensor_tensor(
                scr[2][:], scr[1][:], -float(TWO_PI_HI), scr[2][:],
                op0=ALU.mult, op1=ALU.add), reads=["rscr1", "rscr2"], writes=["rscr2"])
            P.op("vector", lambda e: e.scalar_tensor_tensor(
                scr[2][:], scr[1][:], -float(TWO_PI_LO), scr[2][:],
                op0=ALU.mult, op1=ALU.add), reads=["rscr1", "rscr2"], writes=["rscr2"])
            P.op("vector", lambda e: e.tensor_scalar(
                scr[2][:], scr[2][:], 3.1415925, -3.1415925,
                op0=ALU.min, op1=ALU.max), reads=["rscr2"], writes=["rscr2"])
            P.op("scalar", lambda e, tab=tab, tc=tc: e.activation(
                tab[:, tc * 32:(tc + 1) * 32], scr[2][:], AF.Sin),
                reads=["rscr2"], writes=["ropetab"])
    return ctab, stab


def emit_rope_full_tables(P, ctab, stab, c_full, s_full, tc):
    cs = ctab[:, tc * 32:(tc + 1) * 32].rearrange("p (o f) -> p o f", o=1)
    ss = stab[:, tc * 32:(tc + 1) * 32].rearrange("p (o f) -> p o f", o=1)
    P.op("vector", lambda e: e.tensor_copy(
        c_full[:].rearrange("p (g f) -> p g f", f=32), cs.to_broadcast([128, 32, 32])),
        reads=["ropetab"], writes=["cfull"])
    sv = s_full[:].rearrange("p (h x f) -> p (h x) f", x=2, f=32)
    P.op("vector", lambda e: e.tensor_scalar(
        sv[:, 0::2, :], ss.to_broadcast([128, 16, 32]), -1.0, None, op0=ALU.mult),
        reads=["ropetab"], writes=["sfull"])
    P.op("vector", lambda e: e.tensor_scalar(
        sv[:, 1::2, :], ss.to_broadcast([128, 16, 32]), 1.0, None, op0=ALU.mult),
        reads=["ropetab"], writes=["sfull"])


def emit_rope_feat(P, c_full, s_full, s_w, kw, s_sw, ksw, out_tile, out_key):
    wv = s_w[:].rearrange("p (h x f) -> p h x f", x=2, f=32)
    swv = s_sw[:].rearrange("p (h x f) -> p h x f", x=2, f=32)
    P.op("vector", lambda e: e.tensor_copy(swv[:, :, 0, :], wv[:, :, 1, :]),
         reads=[kw], writes=[ksw])
    P.op("vector", lambda e: e.tensor_copy(swv[:, :, 1, :], wv[:, :, 0, :]),
         reads=[kw], writes=[ksw])
    P.op("gpsimd", lambda e: e.tensor_tensor(s_sw[:], s_sw[:], s_full[:], op=ALU.mult),
         reads=[ksw, "sfull"], writes=[ksw])
    P.op("vector", lambda e: e.tensor_tensor(s_w[:], s_w[:], c_full[:], op=ALU.mult),
         reads=[kw, "cfull"], writes=[kw])
    P.op("vector", lambda e: e.tensor_tensor(s_w[:], s_w[:], s_sw[:], op=ALU.add),
         reads=[kw, ksw], writes=[kw])
    P.op("vector", lambda e: e.tensor_scalar(s_sw[:], s_w[:], 0.0, None, op0=ALU.min),
         reads=[kw], writes=[ksw])
    P.op("scalar", lambda e: e.activation(s_sw[:], s_sw[:], AF.Exp),
         reads=[ksw], writes=[ksw])
    P.op("vector", lambda e: e.scalar_tensor_tensor(
        out_tile[:], s_w[:], 0.0, s_sw[:], op0=ALU.max, op1=ALU.add),
        reads=[kw, ksw], writes=[out_key])


ATTN0_CUT = "full"


def build_attn0(nc, stack, P=None, mem=None, io=None):
    fused = P is not None
    if not fused:
        P = Prog(nc)
    xn = io["xn"] if fused else nc.dram_tensor("xn", [B_PER_CORE, T, D], BF16, kind="ExternalInput").ap()
    wqkvT = io["wqkvT"] if fused else nc.dram_tensor("wqkvT", [D, 3 * D], F32, kind="ExternalInput").ap()
    woT = io["woT"] if fused else nc.dram_tensor("woT", [D, D], F32, kind="ExternalInput").ap()
    s0 = io["s0"] if fused else nc.dram_tensor("s0", [B_PER_CORE, T, D], BF16, kind="ExternalOutput").ap()

    if not fused:
        sems, dma_sems = make_sems(nc, stack)
        mem = Mem(nc, stack)
    en = stack.enter_context
    s_wqkv = mem.sb("s_wqkv", [128, ND * 3 * D], BF16)
    s_wo = mem.sb("s_wo", [128, ND * D], BF16)
    s_iden = mem.sb("s_iden", [128, 128], BF16)
    s_scr128 = mem.sb("s_scr128", [128, 129], F32)
    c_full = mem.sb("s_cfull", [128, D], F32)
    s_full = mem.sb("s_sfull", [128, D], F32)
    s_xn = [mem.sb(f"s_xn{i}", [128, D], BF16) for i in range(2)]
    s_xnT = [mem.sb(f"s_xnT{i}", [128, D], BF16) for i in range(2)]
    s_q = mem.sb("s_q", [128, D], F32)
    s_qsw = mem.sb("s_qsw", [128, D], F32)
    s_k = mem.sb("s_k", [128, D], F32)
    s_ksw = mem.sb("s_ksw", [128, D], F32)
    s_v = mem.sb("s_v", [128, D], BF16)
    s_kf = mem.sb("s_kf", [128, D], BF16)
    s_qf = [[mem.sb(f"s_qf{b}_{i}", [128, D], BF16) for i in range(NT)]
            for b in range(B_PER_CORE)]
    s_kvks = [mem.sb(f"s_kvks{b}", [128, 8 * 65], BF16)
              for b in range(B_PER_CORE)]
    s_kvacc = [mem.sb(f"s_kvacc{b}", [128, 8 * 65], F32)
               for b in range(B_PER_CORE)]
    s_ones = mem.sb("s_ones1", [128, 1], BF16)
    s_qfT = mem.sb("s_qfT", [128, D], BF16)
    s_rec = mem.sb("s_rec", [128, 16], F32)
    s_at = mem.sb("s_at", [128, D], BF16)
    s_atT = mem.sb("s_atT", [128, D], BF16)
    s_s0 = [mem.sb(f"s_s0{i}", [128, D], BF16) for i in range(2)]

    pb = [mem.ps([128, 512], F32) for i in range(6)]
    pt = mem.ps([128, 1024], BF16)
    block = en(nc.Block()) if not fused else None

    for c in range(ND):
        P.dma("gpsimd", s_wqkv[:, c * 3 * D:(c + 1) * 3 * D],
              wqkvT[c * 128:(c + 1) * 128, :], writes=[("wqkv", c)])
        P.dma("gpsimd", s_wo[:, c * D:(c + 1) * D],
              woT[c * 128:(c + 1) * 128, :], writes=[("wo", c)])
    emit_identity(P, s_iden, s_scr128)
    ctab, stab = emit_rope_tables(P, nc, mem)
    P.op("vector", lambda e: e.memset(s_ones[:], 1.0), writes=["ones1"])

    KVB = {0: (pb[2], pb[3]), 1: (pb[4], pb[5])}

    if ATTN0_CUT in ("wload", "tables"):
        P.dma("sync", s0[0, 0:128, :], s_wqkv[:, 0:1024], reads=[("wqkv", 0)],
              is_output=True)
        if ATTN0_CUT == "tables":
            P.dma("gpsimd", s0[1, 0:128, 0:512], ctab[:], reads=["ropetab"], is_output=True)
            P.dma("gpsimd", s0[1, 0:128, 512:1024], stab[:], reads=["ropetab"], is_output=True)
            dbg_if, dbg_tc = emit_rope_tables.debug
            P.dma("gpsimd", s0[1, 128:256, 0:32], dbg_if[:], reads=["ropeI"], is_output=True)
            P.dma("gpsimd", s0[1, 128:256, 32:48], dbg_tc[:], reads=["ropeT"], is_output=True)
        P.finalize(block, sems, dma_sems)
        return P

    for tc in range(NT if ATTN0_CUT != "passa1" else 1):
        emit_rope_full_tables(P, ctab, stab, c_full, s_full, tc)
        for b in range(B_PER_CORE):
            i = b % 2
            kxn = ("sxn", i)
            kxnT = ("sxnT", i)
            P.dma("sync", s_xn[i][:], xn[b, tc * 128:(tc + 1) * 128, :],
                  writes=[kxn])
            for c in range(ND):
                P.op("tensor", lambda e, c=c, i=i: e.transpose(
                    pt[:, c * 128:(c + 1) * 128],
                    s_xn[i][:, c * 128:(c + 1) * 128], s_iden[:]),
                    reads=[kxn, "iden"], writes=[("pt", c)])
            P.op("vector", lambda e, i=i: e.tensor_copy(s_xnT[i][:], pt[:]),
                 reads=[("pt", c) for c in range(ND)], writes=[kxnT])
            for ng in range(6):
                bank = pb[ng % 2]
                kbank = ("pb", ng % 2)
                for dc in range(ND):
                    P.op("tensor", lambda e, ng=ng, dc=dc, i=i, bank=bank: e.matmul(
                        bank[:], s_xnT[i][:, dc * 128:(dc + 1) * 128],
                        s_wqkv[:, dc * 3 * D + ng * 512: dc * 3 * D + (ng + 1) * 512],
                        start=(dc == 0), stop=(dc == ND - 1)),
                        reads=[kxnT, ("wqkv", dc)], writes=[kbank],
                        relax=[kbank] if dc > 0 else [])
                if ng < 2:
                    half = ng
                    P.op("scalar", lambda e, bank=bank, half=half: e.activation(
                        s_q[:, half * 512:(half + 1) * 512], bank[:], AF.Copy,
                        scale=0.125), reads=[kbank], writes=["s_q"])
                elif ng < 4:
                    half = ng - 2
                    P.op("scalar", lambda e, bank=bank, half=half: e.activation(
                        s_k[:, half * 512:(half + 1) * 512], bank[:], AF.Copy),
                        reads=[kbank], writes=["s_k"])
                else:
                    half = ng - 4
                    P.op("vector", lambda e, bank=bank, half=half: e.tensor_copy(
                        s_v[:, half * 512:(half + 1) * 512], bank[:]),
                        reads=[kbank], writes=["s_v"])
            emit_rope_feat(P, c_full, s_full, s_q, "s_q", s_qsw, "s_qsw",
                           s_qf[b][tc], ("qf", b, tc))
            emit_rope_feat(P, c_full, s_full, s_k, "s_k", s_ksw, "s_ksw",
                           s_kf, "s_kf")
            kvb = KVB[b]
            for h in range(HEADS):
                bank = kvb[h // 8]
                kkv = ("pb", 2 + 2 * b + h // 8)
                po = (h % 2) * 64
                fo = ((h // 2) % 4) * 65
                P.op("tensor", lambda e, bank=bank, po=po, fo=fo, h=h: e.matmul(
                    bank[po:po + 64, fo:fo + 64],
                    s_kf[:, h * 64:(h + 1) * 64], s_v[:, h * 64:(h + 1) * 64],
                    start=True, stop=False),
                    reads=["s_kf", "s_v"], writes=[kkv])
                P.op("tensor", lambda e, bank=bank, po=po, fo=fo, h=h: e.matmul(
                    bank[po:po + 64, fo + 64:fo + 65],
                    s_kf[:, h * 64:(h + 1) * 64], s_ones[:],
                    start=False, stop=True),
                    reads=["s_kf", "ones1"], writes=[kkv], relax=[kkv])
            for half in range(2):
                kkv = ("pb", 2 + 2 * b + half)
                if tc == 0:
                    P.op("vector", lambda e, b=b, half=half, kvb=kvb: e.tensor_copy(
                        s_kvacc[b][:, half * 260:(half + 1) * 260],
                        kvb[half][:, 0:260]),
                        reads=[kkv], writes=[("kvacc", b)])
                else:
                    P.op("vector", lambda e, b=b, half=half, kvb=kvb: e.tensor_tensor(
                        s_kvacc[b][:, half * 260:(half + 1) * 260],
                        s_kvacc[b][:, half * 260:(half + 1) * 260],
                        kvb[half][:, 0:260], op=ALU.add),
                        reads=[kkv, ("kvacc", b)], writes=[("kvacc", b)])

    for b in range(B_PER_CORE):
        P.op("vector", lambda e, b=b: e.tensor_copy(
            s_kvks[b][:], s_kvacc[b][:]),
            reads=[("kvacc", b)], writes=[("kvks", b)])

    if ATTN0_CUT in ("passa", "passa1"):
        P.dma("sync", s0[0, 0:128, 0:520], s_kvks[0][:, 0:520], reads=[("kvks", 0)],
              is_output=True)
        P.dma("sync", s0[1, 0:128, :], s_qf[0][0][:], reads=[("qf", 0, 0)],
              is_output=True)
        P.finalize(block, sems, dma_sems)
        return P

    for b in range(B_PER_CORE):
        for tc in range(NT):
            kqfT = "qfT"
            for c in range(ND):
                P.op("tensor", lambda e, c=c, b=b, tc=tc: e.transpose(
                    pt[:, c * 128:(c + 1) * 128],
                    s_qf[b][tc][:, c * 128:(c + 1) * 128], s_iden[:]),
                    reads=[("qf", b, tc), "iden"], writes=[("pt", c)])
            P.op("vector", lambda e: e.tensor_copy(s_qfT[:], pt[:]),
                 reads=[("pt", c) for c in range(ND)], writes=[kqfT])
            if ATTN0_CUT == "passb00":
                P.dma("sync", s0[b, tc * 128:(tc + 1) * 128, :], s_qfT[:],
                      reads=["qfT"], is_output=True)
                continue
            for h in range(HEADS):
                bank = pb[h // 4]
                kbank = ("pb", h // 4)
                fo = (h % 4) * 65
                po = (h % 2) * 64
                co = (h // 2) * 128
                kvfo = ((h // 2) % 4) * 65
                P.op("tensor", lambda e, bank=bank, fo=fo, po=po, co=co,
                     kvfo=kvfo, b=b, h=h: e.matmul(
                         bank[:, fo:fo + 65],
                         s_qfT[po:po + 64, co:co + 128],
                         s_kvks[b][po:po + 64, (h // 8) * 260 + kvfo:(h // 8) * 260 + kvfo + 65],
                         start=(h % 4 == 0), stop=(h % 4 == 3)),
                     reads=[kqfT, ("kvks", b)], writes=[kbank])
            if ATTN0_CUT == "passb0":
                for g in range(4):
                    P.op("vector", lambda e, g=g: e.tensor_copy(
                        s_at[:, g * 256:g * 256 + 256], pb[g][:, 0:256]),
                        reads=[("pb", g)], writes=["sat"])
                P.dma("sync", s0[b, tc * 128:(tc + 1) * 128, :], s_at[:],
                      reads=["sat"], is_output=True)
                continue
            for g in range(4):
                bank = pb[g]
                kbank = ("pb", g)
                bv = bank[:, 0:260].rearrange("p (h x) -> p h x", x=65)
                P.op("vector", lambda e, bv=bv, g=g: e.tensor_scalar(
                    s_rec[:, g * 4:(g + 1) * 4], bv[:, 0:4, 64:65], 1e-6, None,
                    op0=ALU.max), reads=[kbank], writes=["rec"])
            P.op("vector", lambda e: e.reciprocal(s_rec[:], s_rec[:]),
                 reads=["rec"], writes=["rec"])
            for g in range(4):
                bank = pb[g]
                kbank = ("pb", g)
                bv = bank[:, 0:260].rearrange("p (h x) -> p h x", x=65)
                rv = s_rec[:, g * 4:(g + 1) * 4].rearrange(
                    "p (h o) -> p h o", o=1).to_broadcast([128, 4, 64])
                ov = s_at[:, g * 256:(g + 1) * 256].rearrange(
                    "p (h x) -> p h x", x=64)
                P.op("vector", lambda e, bv=bv, rv=rv, ov=ov: e.tensor_tensor(
                    ov, bv[:, 0:4, 0:64], rv, op=ALU.mult),
                    reads=[kbank, "rec"], writes=["sat"])
            if ATTN0_CUT == "passb1":
                P.dma("sync", s0[b, tc * 128:(tc + 1) * 128, :], s_at[:],
                      reads=["sat"], is_output=True)
                continue
            for c in range(ND):
                P.op("tensor", lambda e, c=c: e.transpose(
                    pt[:, c * 128:(c + 1) * 128],
                    s_at[:, c * 128:(c + 1) * 128], s_iden[:]),
                    reads=["sat", "iden"], writes=[("pt", c)])
            P.op("vector", lambda e: e.tensor_copy(s_atT[:], pt[:]),
                 reads=[("pt", c) for c in range(ND)], writes=["atT"])
            for ng in range(2):
                bank = pb[4 + ng]
                kbank = ("pb", 4 + ng)
                for dc in range(ND):
                    P.op("tensor", lambda e, ng=ng, dc=dc, bank=bank: e.matmul(
                        bank[:], s_atT[:, dc * 128:(dc + 1) * 128],
                        s_wo[:, dc * D + ng * 512: dc * D + (ng + 1) * 512],
                        start=(dc == 0), stop=(dc == ND - 1)),
                        reads=["atT", ("wo", dc)], writes=[kbank],
                        relax=[kbank] if dc > 0 else [])
                P.op("vector", lambda e, ng=ng, bank=bank, b=b, tc=tc: e.tensor_copy(
                    s_s0[tc % 2][:, ng * 512:(ng + 1) * 512], bank[:]),
                    reads=[kbank], writes=[("ss0", tc % 2)])
            P.dma("sync", s0[b, tc * 128:(tc + 1) * 128, :], s_s0[tc % 2][:],
                  reads=[("ss0", tc % 2)], is_output=True, tag=f"st_s0_{b}_{tc}")

    if not fused:
        P.finalize(block, sems, dma_sems)
    return P


def build_moe0(nc, stack, P=None, mem=None, io=None):
    fused = P is not None
    if not fused:
        P = Prog(nc)
    xn = io["xn"] if fused else nc.dram_tensor("xn", [B_PER_CORE, T, D], BF16, kind="ExternalInput").ap()
    gateT = io["gateT"] if fused else nc.dram_tensor("gateT", [D, E_EXP], F32, kind="ExternalInput").ap()
    WgT = io["WgT"] if fused else nc.dram_tensor("WgT", [E_EXP, D, F_EXP], F32, kind="ExternalInput").ap()
    WuT = io["WuT"] if fused else nc.dram_tensor("WuT", [E_EXP, D, F_EXP], F32, kind="ExternalInput").ap()
    WdT = io["WdT"] if fused else nc.dram_tensor("WdT", [E_EXP, F_EXP, D], F32, kind="ExternalInput").ap()
    WsgT = io["WsgT"] if fused else nc.dram_tensor("WsgT", [D, F_SH], F32, kind="ExternalInput").ap()
    WsuT = io["WsuT"] if fused else nc.dram_tensor("WsuT", [D, F_SH], F32, kind="ExternalInput").ap()
    WsdT = io["WsdT"] if fused else nc.dram_tensor("WsdT", [F_SH, D], F32, kind="ExternalInput").ap()
    moe = io["moe"] if fused else nc.dram_tensor("moe", [B_PER_CORE, T, D], F32, kind="ExternalOutput").ap()

    if not fused:
        sems, dma_sems = make_sems(nc, stack)
        mem = Mem(nc, stack)
    en = stack.enter_context
    s_idxu = [mem.sb(f"m_idxu{b}", [128, 32], U32)
              for b in range(B_PER_CORE)]
    s_wcol = [mem.sb(f"m_wcol{b}", [128, 32], F32)
              for b in range(B_PER_CORE)]
    s_iden = mem.sb("m_iden", [128, 128], BF16)
    s_idenf = mem.sb("m_idenf", [128, 128], F32)
    s_scr128 = mem.sb("m_scr128", [128, 129], F32)
    s_iotaJ = mem.sb("m_iotaJ", [128, 256], BF16)
    s_slot = [mem.sb(f"m_slot{b}", [128, NT * E_EXP], F32)
              for b in range(B_PER_CORE)]
    s_rhs4 = [mem.sb(f"m_rhs4{b}", [128, NT * 64], BF16)
              for b in range(B_PER_CORE)]
    s_col = [mem.sb(f"m_col{b}", [128, 8], F32)
             for b in range(B_PER_CORE)]
    s_ixw1 = mem.sb("m_ixw1", [4, 256], F32)
    mark = mem.off
    s_w1 = mem.sb("m_w1", [128, ND * F_EXP], BF16)
    s_w2 = mem.sb("m_w2", [128, ND * F_EXP], BF16)
    s_w3 = mem.sb("m_w3", [128, 16 * D], BF16)
    s_hs = mem.sb("m_hs", [128, 16 * 512], BF16)
    s_xn = [mem.sb(f"m_xn{i}", [128, D], BF16) for i in range(2)]
    s_xnTt = mem.sb("m_xnTt", [128, D], BF16)
    s_xnTg = mem.sb("m_xnTg", [128, ND * 512], BF16)
    s_gate = mem.sb("m_gate", [128, ND * E_EXP], BF16)
    s_tri = mem.sb("m_tri", [128, 128], BF16)
    s_one128 = mem.sb("m_one128", [128, 128], BF16)
    s_scores = mem.sb("m_scores", [128, NT * E_EXP], F32)
    s_scoT = [mem.sb(f"m_scoT{b}", [16, T], F32) for b in range(B_PER_CORE)]
    s_junk16 = mem.sb("m_junk16", [16, T], BF16)
    s_bis = [mem.sb(f"m_bis{b}", [16, 8], F32) for b in range(B_PER_CORE)]
    s_maskT = mem.sb("m_maskT", [16, T], BF16)
    s_maskt = mem.sb("m_maskt", [128, NT * E_EXP], BF16)
    s_shout = [mem.sb(f"m_shout{i}", [128, D], F32) for i in range(2)]
    s_hg = mem.sb("m_hg", [128, 512], BF16)

    pb = [mem.ps([128, 512], F32) for i in range(6)]
    pt = mem.ps([128, 1024], BF16)
    block = en(nc.Block()) if not fused else None

    emit_identity(P, s_iden, s_scr128)
    P.op("vector", lambda e: e.tensor_copy(s_idenf[:], s_iden[:]),
         reads=["iden"], writes=["idenf"])
    P.op("gpsimd", lambda e: e.iota(s_iotaJ[:], [[1, 256]], base=1,
                                    channel_multiplier=0,
                                    allow_small_or_imprecise_dtypes=True),
         writes=["iotaJ"])
    P.op("gpsimd", lambda e: e.iota(s_scr128[:, 0:1], [[0, 1]], base=0,
                                    channel_multiplier=1,
                                    allow_small_or_imprecise_dtypes=True),
         writes=["iden_scr"])
    P.op("gpsimd", lambda e: e.iota(s_scr128[:, 1:129], [[1, 128]], base=0,
                                    channel_multiplier=0,
                                    allow_small_or_imprecise_dtypes=True),
         writes=["iden_scr"])
    P.op("vector", lambda e: e.tensor_scalar(
        s_tri[:], s_scr128[:, 1:129], s_scr128[:, 0:1], None, op0=ALU.is_ge),
        reads=["iden_scr"], writes=["tri"])
    P.op("vector", lambda e: e.memset(s_one128[:], 1.0), writes=["ones128"])
    for c in range(ND):
        P.dma("gpsimd", s_gate[:, c * E_EXP:(c + 1) * E_EXP],
              gateT[c * 128:(c + 1) * 128, :], writes=["gate"])
    P.dma("gpsimd", s_w3[:].rearrange("p (c d) -> p c d", d=D),
          WsdT.rearrange("(c p) d -> p c d", p=128), writes=["w3"])
    P.dma("gpsimd", s_w1[:].rearrange("p (c f) -> p c f", f=F_SH),
          WsgT.rearrange("(c p) f -> p c f", p=128), writes=["w1"])
    P.dma("gpsimd", s_w2[:].rearrange("p (c f) -> p c f", f=F_SH),
          WsuT.rearrange("(c p) f -> p c f", p=128), writes=["w2"])

    def transpose_tile(src_tile, src_key, dst_view, dst_key):
        for c in range(ND):
            P.op("tensor", lambda e, c=c: e.transpose(
                pt[:, c * 128:(c + 1) * 128],
                src_tile[:, c * 128:(c + 1) * 128], s_iden[:]),
                reads=[src_key, "iden"], writes=[("pt", c)])
        if dst_view.ndim == 2:
            dst_view = dst_view.rearrange("p (c t) -> p c t", t=128)
        P.op("vector", lambda e: e.tensor_copy(
            dst_view, pt[:].rearrange("p (c t) -> p c t", t=128)),
            reads=[("pt", c) for c in range(ND)], writes=[dst_key])

    for b in range(B_PER_CORE):
        for tc in range(NT):
            i = tc % 2
            kxn = ("mxn", i)
            P.dma("sync", s_xn[i][:], xn[b, tc * 128:(tc + 1) * 128, :],
                  writes=[kxn])
            transpose_tile(s_xn[i], kxn, s_xnTt[:], "xnTt")
            for dc in range(ND):
                P.op("tensor", lambda e, dc=dc: e.matmul(
                    pb[0][:, 0:E_EXP], s_xnTt[:, dc * 128:(dc + 1) * 128],
                    s_gate[:, dc * E_EXP:(dc + 1) * E_EXP],
                    start=(dc == 0), stop=(dc == ND - 1)),
                    reads=["xnTt", "gate"], writes=[("pb", 0)],
                    relax=[("pb", 0)] if dc > 0 else [])
            P.op("vector", lambda e: e.tensor_reduce(
                s_scr128[:, 0:1], pb[0][:, 0:E_EXP], AX.X, ALU.max, negate=True),
                reads=[("pb", 0)], writes=["sm_scr"])
            P.op("scalar", lambda e, tc=tc: e.activation(
                s_scores[:, tc * E_EXP:(tc + 1) * E_EXP], pb[0][:, 0:E_EXP],
                AF.Exp, bias=s_scr128[:, 0:1], accum_out=s_scr128[:, 1:2]),
                reads=[("pb", 0), "sm_scr"], writes=["scores", "sm_scr"])
            P.op("vector", lambda e: e.reciprocal(s_scr128[:, 2:3], s_scr128[:, 1:2]),
                 reads=["sm_scr"], writes=["sm_scr"])
            P.op("vector", lambda e, tc=tc: e.tensor_scalar(
                s_scores[:, tc * E_EXP:(tc + 1) * E_EXP],
                s_scores[:, tc * E_EXP:(tc + 1) * E_EXP],
                s_scr128[:, 2:3], None, op0=ALU.mult),
                reads=["scores", "sm_scr"], writes=["scores"])
            P.op("tensor", lambda e, tc=tc: e.transpose(
                pb[1][0:E_EXP, 0:128],
                s_scores[:, tc * E_EXP:(tc + 1) * E_EXP], s_idenf[:]),
                reads=["scores", "idenf"], writes=[("pb", 1)])
            P.op("vector", lambda e, b=b, tc=tc: e.tensor_copy(
                s_scoT[b][:, tc * 128:(tc + 1) * 128], pb[1][0:E_EXP, 0:128]),
                reads=[("pb", 1)], writes=[("scoT", b)])

        kr = ("rhs4", b)
        r4 = s_rhs4[b][:].rearrange("p (x f) -> p x f", f=4)
        P.op("gpsimd", lambda e, r4=r4: e.iota(
            r4[:, :, 0:1], [[0, 256]], base=0, channel_multiplier=1,
            allow_small_or_imprecise_dtypes=True), writes=[kr])
        P.op("vector", lambda e, r4=r4: e.memset(r4[:, :, 1:2], 1.0), writes=[kr])
        P.op("gpsimd", lambda e, r4=r4: e.iota(
            r4[:, :, 3:4], [[1, NT], [0, E_EXP]], base=0, channel_multiplier=0,
            allow_small_or_imprecise_dtypes=True), writes=[kr])
        for tc in range(NT):
            P.op("vector", lambda e, b=b, tc=tc: e.tensor_copy(
                s_rhs4[b][:, tc * 64:(tc + 1) * 64].rearrange(
                    "p (x f) -> p x f", f=4)[:, :, 2:3],
                s_scores[:, tc * E_EXP:(tc + 1) * E_EXP].rearrange(
                    "p (x f) -> p x f", f=1)),
                reads=["scores"], writes=[kr])

        kb = ("bis", b)
        sb = s_bis[b]
        P.op("vector", lambda e, sb=sb: e.memset(sb[:, 0:1], 0.0), writes=[kb])
        P.op("vector", lambda e, sb=sb: e.memset(sb[:, 1:2], 1.0), writes=[kb])
        P.op("vector", lambda e, sb=sb: e.memset(sb[:, 2:3], 0.5), writes=[kb])
        for it in range(30):
            P.op("vector", lambda e, b=b, sb=sb: e.tensor_scalar(
                s_junk16[:], s_scoT[b][:], sb[:, 2:3], 0.0,
                op0=ALU.is_gt, op1=ALU.add, accum_out=sb[:, 3:4]),
                reads=[("scoT", b), kb], writes=["junk16", kb])
            P.op("vector", lambda e, sb=sb: e.tensor_scalar(
                sb[:, 4:5], sb[:, 3:4], float(CAP), None, op0=ALU.is_gt),
                reads=[kb], writes=[kb])
            P.op("vector", lambda e, sb=sb: e.tensor_tensor(
                sb[:, 5:6], sb[:, 2:3], sb[:, 0:1], op=ALU.subtract),
                reads=[kb], writes=[kb])
            P.op("vector", lambda e, sb=sb: e.tensor_tensor(
                sb[:, 5:6], sb[:, 5:6], sb[:, 4:5], op=ALU.mult),
                reads=[kb], writes=[kb])
            P.op("vector", lambda e, sb=sb: e.tensor_tensor(
                sb[:, 0:1], sb[:, 0:1], sb[:, 5:6], op=ALU.add),
                reads=[kb], writes=[kb])
            P.op("vector", lambda e, sb=sb: e.tensor_tensor(
                sb[:, 6:7], sb[:, 2:3], sb[:, 1:2], op=ALU.subtract),
                reads=[kb], writes=[kb])
            P.op("vector", lambda e, sb=sb: e.tensor_scalar(
                sb[:, 7:8], sb[:, 4:5], -1.0, 1.0, op0=ALU.mult, op1=ALU.add),
                reads=[kb], writes=[kb])
            P.op("vector", lambda e, sb=sb: e.tensor_tensor(
                sb[:, 6:7], sb[:, 6:7], sb[:, 7:8], op=ALU.mult),
                reads=[kb], writes=[kb])
            P.op("vector", lambda e, sb=sb: e.tensor_tensor(
                sb[:, 1:2], sb[:, 1:2], sb[:, 6:7], op=ALU.add),
                reads=[kb], writes=[kb])
            P.op("vector", lambda e, sb=sb: e.tensor_scalar(
                sb[:, 2:3], sb[:, 0:1], 0.5, None, op0=ALU.mult),
                reads=[kb], writes=[kb])
            P.op("vector", lambda e, sb=sb: e.scalar_tensor_tensor(
                sb[:, 2:3], sb[:, 1:2], 0.5, sb[:, 2:3],
                op0=ALU.mult, op1=ALU.add), reads=[kb], writes=[kb])

    for b in range(B_PER_CORE):
        for tg in range(4):
            tbase = tg * 512
            for tq in range(4):
                i = tq % 2
                kxn = ("mxn", i)
                tci = tg * 4 + tq
                P.dma("sync", s_xn[i][:], xn[b, tci * 128:(tci + 1) * 128, :],
                      writes=[kxn])
                transpose_tile(
                    s_xn[i], kxn,
                    s_xnTg[:].rearrange("p (c t) -> p c t", t=512)[
                        :, :, tq * 128:(tq + 1) * 128],
                    "xnTg")
            for fc in range(16):
                for dc in range(ND):
                    P.op("tensor", lambda e, dc=dc, fc=fc: e.matmul(
                        pb[0][:], s_w1[:, dc * F_SH + fc * 128: dc * F_SH + (fc + 1) * 128],
                        s_xnTg[:, dc * 512:(dc + 1) * 512],
                        start=(dc == 0), stop=(dc == ND - 1)),
                        reads=["w1", "xnTg"], writes=[("pb", 0)],
                        relax=[("pb", 0)] if dc > 0 else [])
                    P.op("tensor", lambda e, dc=dc, fc=fc: e.matmul(
                        pb[1][:], s_w2[:, dc * F_SH + fc * 128: dc * F_SH + (fc + 1) * 128],
                        s_xnTg[:, dc * 512:(dc + 1) * 512],
                        start=(dc == 0), stop=(dc == ND - 1)),
                        reads=["w2", "xnTg"], writes=[("pb", 1)],
                        relax=[("pb", 1)] if dc > 0 else [])
                P.op("scalar", lambda e: e.activation(
                    s_hg[:], pb[0][:], AF.Gelu),
                    reads=[("pb", 0)], writes=["hg"])
                P.op("vector", lambda e, fc=fc: e.tensor_tensor(
                    s_hs[:, fc * 512:(fc + 1) * 512], s_hg[:], pb[1][:],
                    op=ALU.mult),
                    reads=["hg", ("pb", 1)], writes=["hs"])
            for tq in range(4):
                for half in range(2):
                    kpd = ("pb", 2 + half)
                    for fc in range(16):
                        P.op("tensor", lambda e, fc=fc, tq=tq, half=half: e.matmul(
                            pb[2 + half][:],
                            s_hs[:, fc * 512 + tq * 128: fc * 512 + (tq + 1) * 128],
                            s_w3[:, fc * D + half * 512: fc * D + half * 512 + 512],
                            start=(fc == 0), stop=(fc == 15)),
                            reads=["hs", "w3"], writes=[kpd],
                            relax=[kpd] if fc > 0 else [])
                i = tq % 2
                P.op("vector", lambda e, i=i: e.tensor_copy(
                    s_shout[i][:, 0:512], pb[2][:]),
                    reads=[("pb", 2)], writes=[("shout", i)])
                P.op("vector", lambda e, i=i: e.tensor_copy(
                    s_shout[i][:, 512:1024], pb[3][:]),
                    reads=[("pb", 3)], writes=[("shout", i)])
                P.dma("sync", moe[b, tbase + tq * 128: tbase + (tq + 1) * 128, :],
                      s_shout[i][:], reads=[("shout", i)],
                      writes=[("moe", b)], is_output=True)

    for b in range(B_PER_CORE):
        P.op("vector", lambda e, b=b: e.tensor_scalar(
            s_maskT[:], s_scoT[b][:], s_bis[b][:, 2:3], None, op0=ALU.is_gt),
            reads=[("scoT", b), ("bis", b)], writes=["maskT"])
        for tc in range(NT):
            P.op("tensor", lambda e, tc=tc: e.transpose(
                pt[:, 0:E_EXP], s_maskT[:, tc * 128:(tc + 1) * 128],
                s_iden[0:16, 0:16]),
                reads=["maskT", "iden"], writes=[("pt", 0)])
            P.op("vector", lambda e, tc=tc: e.tensor_copy(
                s_maskt[:, tc * E_EXP:(tc + 1) * E_EXP], pt[:, 0:E_EXP]),
                reads=[("pt", 0)], writes=["maskt"])
        for tc in range(NT):
            kps = ("pslot", tc)
            P.op("tensor", lambda e, tc=tc: e.matmul(
                pb[2][:, tc * E_EXP:(tc + 1) * E_EXP], s_tri[:],
                s_maskt[:, tc * E_EXP:(tc + 1) * E_EXP],
                start=True, stop=(tc == 0)),
                reads=["tri", "maskt"], writes=[kps, ("pb", 2)])
            for tp in range(tc):
                P.op("tensor", lambda e, tc=tc, tp=tp: e.matmul(
                    pb[2][:, tc * E_EXP:(tc + 1) * E_EXP], s_one128[:],
                    s_maskt[:, tp * E_EXP:(tp + 1) * E_EXP],
                    start=False, stop=(tp == tc - 1)),
                    reads=["ones128", "maskt"], writes=[kps, ("pb", 2)],
                    relax=[kps, ("pb", 2)])
        P.op("vector", lambda e, b=b: e.tensor_tensor(
            s_slot[b][:], pb[2][:, 0:256], s_maskt[:], op=ALU.mult),
            reads=["maskt"] + [("pslot", tc) for tc in range(NT)],
            writes=[("slot", b)])

    P.fence()
    mem.off = mark
    s2_w1 = [mem.sb(f"m2_w1{i}", [128, ND * F_EXP], BF16) for i in range(2)]
    s2_w2 = [mem.sb(f"m2_w2{i}", [128, ND * F_EXP], BF16) for i in range(2)]
    s2_w3 = mem.sb("m2_w3", [128, 16 * D], BF16)
    s2_hs = [mem.sb(f"m2_hs{i}", [128, 16 * 256], BF16) for i in range(2)]
    s2_xsel = [mem.sb(f"m2_xsel{i}", [128, D], BF16) for i in range(2)]
    s2_xselT = [mem.sb(f"m2_xselT{i}", [128, ND * 256], BF16) for i in range(2)]
    s2_hg = mem.sb("m2_hg", [128, 256], BF16)
    s2_P = [mem.sb(f"m2_P{i}", [128, 256], BF16) for i in range(2)]
    s2_eo = [mem.sb(f"m2_eo{i}", [128, D], BF16) for i in range(4)]

    bc_reg = nc.gpsimd.to_reg(B_PER_CORE * T - 1)
    wg3 = WgT.rearrange("e (c p) f -> e p c f", p=128)
    wu3 = WuT.rearrange("e (c p) f -> e p c f", p=128)
    wd3 = WdT.rearrange("e (c p) d -> e p c d", p=128)

    def load_w12(ee):
        sel = ee % 2
        P.dma("gpsimd", s2_w1[sel][:].rearrange("p (c f) -> p c f", f=F_EXP),
              wg3[ee], writes=[("w1", sel)], tag=f"wg{ee}")
        P.dma("gpsimd", s2_w2[sel][:].rearrange("p (c f) -> p c f", f=F_EXP),
              wu3[ee], writes=[("w2", sel)], tag=f"wu{ee}")

    def load_w3(ee):
        P.dma("gpsimd", s2_w3[:].rearrange("p (c d) -> p c d", d=D),
              wd3[ee], writes=["w3"], tag=f"wd{ee}")

    def machinery(ee, b):
        kx = "pixw"
        for tc in range(NT):
            ip = tc % 2
            kP = ("Ptile", ip)
            P.op("vector", lambda e, b=b, ee=ee, tc=tc, ip=ip: e.tensor_scalar(
                s2_P[ip][:], s_iotaJ[:],
                s_slot[b][:, tc * E_EXP + ee: tc * E_EXP + ee + 1],
                None, op0=ALU.is_equal),
                reads=["iotaJ", ("slot", b)], writes=[kP])
            P.op("tensor", lambda e, b=b, ee=ee, tc=tc, ip=ip: e.matmul(
                pb[2][0:4, 0:256],
                s_rhs4[b][:, tc * 64 + 4 * ee: tc * 64 + 4 * ee + 4],
                s2_P[ip][:], start=(tc == 0), stop=(tc == NT - 1)),
                reads=[("rhs4", b), kP], writes=[kx, ("pb", 2)],
                relax=[kx, ("pb", 2)] if tc > 0 else [])
        P.op("vector", lambda e: e.tensor_copy(s_ixw1[:], pb[2][0:4, 0:256]),
             reads=[kx], writes=["ixw1"])
        for jc in range(2):
            ci = ee * 2 + jc
            c4 = jc * 4
            kc = ("col", b)
            P.op("tensor", lambda e, jc=jc: e.transpose(
                pb[3][0:128, 0:4],
                s_ixw1[0:4, jc * 128:(jc + 1) * 128], s_idenf[0:4, 0:4]),
                reads=["ixw1", "idenf"], writes=[("pb", 3)])
            P.op("vector", lambda e, b=b, c4=c4: e.tensor_copy(
                s_col[b][:, c4:c4 + 4], pb[3][0:128, 0:4]),
                reads=[("pb", 3)], writes=[kc])
            P.op("vector", lambda e, b=b, c4=c4: e.scalar_tensor_tensor(
                s_col[b][:, c4:c4 + 1], s_col[b][:, c4 + 3:c4 + 4], 128.0,
                s_col[b][:, c4:c4 + 1], op0=ALU.mult, op1=ALU.add),
                reads=[kc], writes=[kc])
            P.op("vector", lambda e, b=b, c4=c4: e.tensor_scalar(
                s_col[b][:, c4:c4 + 1], s_col[b][:, c4:c4 + 1],
                float(b * T), None, op0=ALU.add),
                reads=[kc], writes=[kc])
            P.op("vector", lambda e, b=b, c4=c4: e.scalar_tensor_tensor(
                s_col[b][:, c4:c4 + 1], s_col[b][:, c4 + 1:c4 + 2], -BIG_IDX,
                s_col[b][:, c4:c4 + 1], op0=ALU.mult, op1=ALU.add),
                reads=[kc], writes=[kc])
            P.op("vector", lambda e, b=b, c4=c4: e.tensor_scalar(
                s_col[b][:, c4:c4 + 1], s_col[b][:, c4:c4 + 1], BIG_IDX, None,
                op0=ALU.add), reads=[kc], writes=[kc])
            P.op("vector", lambda e, b=b, ci=ci, c4=c4: e.tensor_copy(
                s_idxu[b][:, ci:ci + 1], s_col[b][:, c4:c4 + 1]),
                reads=[kc], writes=[("idxu", b)])
            P.op("vector", lambda e, b=b, ci=ci, c4=c4: e.tensor_copy(
                s_wcol[b][:, ci:ci + 1], s_col[b][:, c4 + 2:c4 + 3]),
                reads=[kc], writes=[("wcol", b)])

    def gathers(ee, b):
        xsi = b % 2
        kxs = ("xselT", xsi)
        for jc in range(2):
            ci = ee * 2 + jc
            P.op("gpsimd", lambda e, b=b, ci=ci, jc=jc: e.indirect_dma_start(
                out=s2_xsel[jc][:], out_offset=None,
                in_=xn.rearrange("b t d -> (b t) d"),
                in_offset=IndirectOffsetOnAxis(
                    s_idxu[b][:, ci:ci + 1], axis=0),
                bounds_check=bc_reg, oob_is_err=False),
                reads=[("idxu", b)], writes=[("xsel", jc)], inc=16)
            transpose_tile(
                s2_xsel[jc], ("xsel", jc),
                s2_xselT[xsi][:].rearrange("p (c j) -> p c j", j=256)[
                    :, :, jc * 128:(jc + 1) * 128],
                kxs)

    def compute(ee, b):
        sel = ee % 2
        xsi = b % 2
        kxs = ("xselT", xsi)
        khs = ("hs", xsi)
        for fc in range(16):
            for dc in range(ND):
                P.op("tensor", lambda e, dc=dc, fc=fc, sel=sel, xsi=xsi: e.matmul(
                    pb[0][:, 0:256],
                    s2_w1[sel][:, dc * F_EXP + fc * 128: dc * F_EXP + (fc + 1) * 128],
                    s2_xselT[xsi][:, dc * 256:(dc + 1) * 256],
                    start=(dc == 0), stop=(dc == ND - 1)),
                    reads=[("w1", sel), kxs], writes=[("pb", 0)],
                    relax=[("pb", 0)] if dc > 0 else [])
                P.op("tensor", lambda e, dc=dc, fc=fc, sel=sel, xsi=xsi: e.matmul(
                    pb[1][:, 0:256],
                    s2_w2[sel][:, dc * F_EXP + fc * 128: dc * F_EXP + (fc + 1) * 128],
                    s2_xselT[xsi][:, dc * 256:(dc + 1) * 256],
                    start=(dc == 0), stop=(dc == ND - 1)),
                    reads=[("w2", sel), kxs], writes=[("pb", 1)],
                    relax=[("pb", 1)] if dc > 0 else [])
            P.op("scalar", lambda e: e.activation(
                s2_hg[:], pb[0][:, 0:256], AF.Gelu),
                reads=[("pb", 0)], writes=["hg2"])
            P.op("vector", lambda e, fc=fc, xsi=xsi: e.tensor_tensor(
                s2_hs[xsi][:, fc * 256:(fc + 1) * 256], s2_hg[:],
                pb[1][:, 0:256], op=ALU.mult),
                reads=["hg2", ("pb", 1)], writes=[khs])
        for jc in range(2):
            ci = ee * 2 + jc
            for half in range(2):
                kpd = ("pb", 4 + half)
                for fc in range(16):
                    P.op("tensor", lambda e, fc=fc, jc=jc, half=half,
                         xsi=xsi: e.matmul(
                        pb[4 + half][:],
                        s2_hs[xsi][:, fc * 256 + jc * 128: fc * 256 + (jc + 1) * 128],
                        s2_w3[:, fc * D + half * 512: fc * D + half * 512 + 512],
                        start=(fc == 0), stop=(fc == 15)),
                        reads=[khs, "w3"], writes=[kpd],
                        relax=[kpd] if fc > 0 else [])
            i = b * 2 + jc
            keo = ("eo", i)
            P.op("vector", lambda e, b=b, ci=ci, i=i: e.tensor_scalar(
                s2_eo[i][:, 0:512], pb[4][:], s_wcol[b][:, ci:ci + 1], None,
                op0=ALU.mult),
                reads=[("pb", 4), ("wcol", b)], writes=[keo])
            P.op("vector", lambda e, b=b, ci=ci, i=i: e.tensor_scalar(
                s2_eo[i][:, 512:1024], pb[5][:], s_wcol[b][:, ci:ci + 1], None,
                op0=ALU.mult),
                reads=[("pb", 5), ("wcol", b)], writes=[keo])

    def scatters(ee, b):
        for jc in range(2):
            ci = ee * 2 + jc
            i = b * 2 + jc
            sc = P.op("gpsimd", lambda e, b=b, ci=ci, i=i: e.indirect_dma_start(
                out=moe.rearrange("b t d -> (b t) d"),
                out_offset=IndirectOffsetOnAxis(
                    s_idxu[b][:, ci:ci + 1], axis=0),
                in_=s2_eo[i][:], in_offset=None,
                bounds_check=bc_reg, oob_is_err=False,
                compute_op=ALU.add),
                reads=[("eo", i), ("idxu", b)], writes=[("moe", b)], inc=16)
            P.out_dmas.append(sc)

    load_w12(0)
    load_w3(0)
    for ee in range(E_EXP):
        for b in range(B_PER_CORE):
            machinery(ee, b)
            gathers(ee, b)
        if ee + 1 < E_EXP:
            load_w12(ee + 1)
        for b in range(B_PER_CORE):
            compute(ee, b)
        if ee + 1 < E_EXP:
            load_w3(ee + 1)
        for b in range(B_PER_CORE):
            scatters(ee, b)
    if not fused:
        P.finalize(block, sems, dma_sems)
    return P


def build_combine_ln1(nc, stack, P=None, mem=None, io=None):
    fused = P is not None
    if not fused:
        P = Prog(nc)
    x = io["x"] if fused else nc.dram_tensor("x", [B_PER_CORE, T, D], F32, kind="ExternalInput").ap()
    s0 = io["s0"] if fused else nc.dram_tensor("s0", [B_PER_CORE, T, D], BF16, kind="ExternalInput").ap()
    moe = io["moe"] if fused else nc.dram_tensor("moe", [B_PER_CORE, T, D], F32, kind="ExternalInput").ap()
    x1 = io["x1"] if fused else nc.dram_tensor("x1", [B_PER_CORE, T, D], F32, kind="ExternalOutput").ap()
    xn1 = io["xn1"] if fused else nc.dram_tensor("xn1", [B_PER_CORE, T, D], BF16, kind="ExternalOutput").ap()
    xq1 = io["xq1"] if fused else nc.dram_tensor("xq1", [B_PER_CORE, T, D], BF16, kind="ExternalOutput").ap()
    xs127 = io["xs127"] if fused else nc.dram_tensor("xs127", [B_PER_CORE, T], F32, kind="ExternalOutput").ap()

    if not fused:
        sems, dma_sems = make_sems(nc, stack)
        mem = Mem(nc, stack)
    en = stack.enter_context
    s_x = [mem.sb(f"c_x{i}", [128, D], F32) for i in range(2)]
    s_s0 = [mem.sb(f"c_s0{i}", [128, D], BF16) for i in range(2)]
    s_moe = [mem.sb(f"c_moe{i}", [128, D], F32) for i in range(2)]
    s_g0 = mem.sb("c_g0", [128, D], BF16)
    s_x1 = [mem.sb(f"c_x1{i}", [128, D], F32) for i in range(2)]
    s_tmp = mem.sb("c_tmp", [128, D], F32)
    s_tmp2 = mem.sb("c_tmp2", [128, D], F32)
    s_xn1 = [mem.sb(f"c_xn1{i}", [128, D], BF16) for i in range(2)]
    s_xq = [mem.sb(f"c_xq{i}", [128, D], BF16) for i in range(2)]
    s_junk = mem.sb("c_junk", [128, D], BF16)
    s_small = mem.sb("c_small", [128, 16], F32)
    block = en(nc.Block()) if not fused else None

    for b in range(B_PER_CORE):
        for tc in range(NT):
            i = tc % 2
            kx, ks, km = ("cx", i), ("cs", i), ("cm", i)
            sl = slice(tc * 128, (tc + 1) * 128)
            P.dma("sync", s_x[i][:], x[b, sl, :], writes=[kx])
            P.dma("sync", s_s0[i][:], s0[b, sl, :], writes=[ks])
            P.dma("sync", s_moe[i][:], moe[b, sl, :], writes=[km])
            P.op("scalar", lambda e, i=i: e.activation(
                s_g0[:], s_moe[i][:], AF.Silu), reads=[km], writes=["cg0"])
            P.op("vector", lambda e, i=i: e.tensor_tensor(
                s_g0[:], s_g0[:], s_s0[i][:], op=ALU.mult),
                reads=["cg0", ks], writes=["cg0"])
            kx1 = ("cx1", i)
            P.op("vector", lambda e, i=i: e.tensor_tensor(
                s_x1[i][:], s_x[i][:], s_g0[:], op=ALU.add),
                reads=[kx, "cg0"], writes=[kx1])
            P.dma("sync", x1[b, sl, :], s_x1[i][:], reads=[kx1], is_output=True)

            def consume(m_ap, r_ap, ksmall, i=i, kx1=kx1, b=b, tc=tc, sl=sl):
                kxn = ("cxn1", i)
                P.op("vector", lambda e, i=i: e.tensor_scalar(
                    s_xn1[i][:], s_x1[i][:], m_ap, r_ap,
                    op0=ALU.subtract, op1=ALU.mult),
                    reads=[kx1, ksmall], writes=[kxn])
                P.dma("sync", xn1[b, sl, :], s_xn1[i][:], reads=[kxn],
                      is_output=True)
                P.op("vector", lambda e, i=i: e.tensor_scalar(
                    s_tmp[:], s_x1[i][:], m_ap, None, op0=ALU.subtract),
                    reads=[kx1, ksmall], writes=["ctmp"])
                P.op("vector", lambda e: e.tensor_reduce(
                    s_small[:, 8:9], s_tmp[:], AX.X, ALU.max,
                    apply_absolute_value=True),
                    reads=["ctmp"], writes=["cq"])
                P.op("vector", lambda e: e.tensor_tensor(
                    s_small[:, 8:9], s_small[:, 8:9], r_ap, op=ALU.mult),
                    reads=["cq", ksmall], writes=["cq"])
                P.op("vector", lambda e: e.tensor_scalar(
                    s_small[:, 8:9], s_small[:, 8:9], 1e-5, None, op0=ALU.max),
                    reads=["cq"], writes=["cq"])
                P.op("vector", lambda e: e.reciprocal(
                    s_small[:, 9:10], s_small[:, 8:9]), reads=["cq"], writes=["cq"])
                P.op("vector", lambda e: e.tensor_scalar(
                    s_small[:, 10:11], s_small[:, 9:10], 127.0, None,
                    op0=ALU.mult), reads=["cq"], writes=["cq"])
                P.op("vector", lambda e: e.tensor_tensor(
                    s_small[:, 10:11], s_small[:, 10:11], r_ap, op=ALU.mult),
                    reads=["cq", ksmall], writes=["cq"])
                P.op("vector", lambda e: e.tensor_scalar(
                    s_small[:, 11:12], s_small[:, 8:9], 1.0 / 127.0, None,
                    op0=ALU.mult), reads=["cq"], writes=["cq"])
                P.dma("sync", xs127[b, sl], s_small[:, 11:12], reads=["cq"],
                      is_output=True)
                kxq = ("cxq", i)
                P.op("vector", lambda e: e.tensor_scalar(
                    s_tmp2[:], s_tmp[:], s_small[:, 10:11], MAGIC,
                    op0=ALU.mult, op1=ALU.add),
                    reads=["ctmp", "cq"], writes=["ctmp2"])
                P.op("vector", lambda e: e.tensor_scalar(
                    s_tmp2[:], s_tmp2[:], MAGIC, None, op0=ALU.subtract),
                    reads=["ctmp2"], writes=["ctmp2"])
                P.op("vector", lambda e, i=i: e.tensor_scalar(
                    s_xq[i][:], s_tmp2[:], 127.0, -128.0,
                    op0=ALU.min, op1=ALU.max),
                    reads=["ctmp2"], writes=[kxq])
                P.dma("sync", xq1[b, sl, :], s_xq[i][:], reads=[kxq],
                      is_output=True)

            emit_ln_tile(P, None, f"c{b}_{tc}", kx1, s_x1[i], s_junk, s_small,
                         0, consume)
    if not fused:
        P.finalize(block, sems, dma_sems)
    return P


def build_attn1(nc, stack, P=None, mem=None, io=None):
    fused = P is not None
    if not fused:
        P = Prog(nc)
    xn1 = io["xn1"] if fused else nc.dram_tensor("xn1", [B_PER_CORE, T, D], BF16, kind="ExternalInput").ap()
    wqkv1T = io["wqkv1T"] if fused else nc.dram_tensor("wqkv1T", [D, D], F32, kind="ExternalInput").ap()
    wo1T = io["wo1T"] if fused else nc.dram_tensor("wo1T", [D, D], F32, kind="ExternalInput").ap()
    s1 = io["s1"] if fused else nc.dram_tensor("s1", [B_PER_CORE, T, D], BF16, kind="ExternalOutput").ap()
    qtd = nc.dram_tensor("a1_qtd", [256, 128], BF16, kind="Internal").ap()

    if not fused:
        sems, dma_sems = make_sems(nc, stack)
        mem = Mem(nc, stack)
    en = stack.enter_context
    s_wq = mem.sb("a1_wq", [128, ND * D], BF16)
    s_wo = mem.sb("a1_wo", [128, ND * D], BF16)
    s_iden = mem.sb("a1_iden", [128, 128], BF16)
    s_idenf = mem.sb("a1_idenf", [128, 128], F32)
    s_scr128 = mem.sb("a1_scr128", [128, 129], F32)
    s_ones1f = mem.sb("a1_ones1f", [1, 128], F32)
    s_mcol = mem.sb("a1_mcol", [128, HEADS], F32)
    s_xn = [mem.sb(f"a1_xn{i}", [128, D], BF16) for i in range(2)]
    s_xnTt = mem.sb("a1_xnTt", [128, D], BF16)
    s_yT = mem.sb("a1_yT", [128, ND * T], BF16)
    s_ytok = mem.sb("a1_ytok", [128, NT * D], BF16)
    s_sq = mem.sb("a1_sq", [128, D], F32)
    s_q2 = mem.sb("a1_q2", [128, NT * HEADS], F32)
    s_cneg = mem.sb("a1_cneg", [128, NT * HEADS], F32)
    s_mrow = mem.sb("a1_mrow", [1, 2 * HEADS], F32)
    s_q2T = mem.sb("a1_q2T", [128, 256], BF16)
    s_ka = [mem.sb(f"a1_ka{i}", [65, T], BF16) for i in range(2)]
    s_qa = [mem.sb(f"a1_qa{i}", [65, T], BF16) for i in range(2)]
    s_vaug = [mem.sb(f"a1_vaug{i}", [128, 65], BF16) for i in range(2)]
    s_se = [mem.sb(f"a1_se{i}", [128, T], BF16) for i in range(2)]
    s_drow = mem.sb("a1_drow", [1, T], F32)
    s_au = mem.sb("a1_au", [128, 512], BF16)
    s_aT = mem.sb("a1_aT", [128, ND * T], BF16)
    s_s1 = [mem.sb(f"a1_s1{i}", [128, D], BF16) for i in range(2)]
    pS = [mem.ps([128, 1024], F32) for _ in range(2)]
    pv = [mem.ps([128, 512], F32) for g in range(3)]
    pBt = mem.pB[:]
    pv.append(pBt.bitcast(F32))
    kB = ("pav", 3)
    block = en(nc.Block()) if not fused else None

    emit_identity(P, s_iden, s_scr128)
    P.op("vector", lambda e: e.tensor_copy(s_idenf[:], s_iden[:]),
         reads=["iden"], writes=["idenf"])
    P.op("vector", lambda e: e.memset(s_ones1f[:], 1.0), writes=["ones1f"])
    for i in range(2):
        P.op("vector", lambda e, i=i: e.memset(s_ka[i][64:65, :], 1.0),
             writes=[("kaones", i)])
        P.op("vector", lambda e, i=i: e.memset(s_vaug[i][:, 64:65], 1.0),
             writes=[("vones", i)])
    for c in range(ND):
        P.dma("gpsimd", s_wq[:, c * D:(c + 1) * D], wqkv1T[c * 128:(c + 1) * 128, :],
              writes=[("wq", c)])
        P.dma("gpsimd", s_wo[:, c * D:(c + 1) * D], wo1T[c * 128:(c + 1) * 128, :],
              writes=[("wo", c)])

    for b in range(B_PER_CORE):
        for tc in range(NT):
            i = tc % 2
            kxn = ("axn", i)
            P.dma("sync", s_xn[i][:], xn1[b, tc * 128:(tc + 1) * 128, :],
                  writes=[kxn])
            for c in range(ND):
                P.op("tensor", lambda e, c=c, i=i: e.transpose(
                    pBt[:, c * 128:(c + 1) * 128],
                    s_xn[i][:, c * 128:(c + 1) * 128], s_iden[:]),
                    reads=[kxn, "iden"], writes=[kB])
            P.op("vector", lambda e: e.tensor_copy(s_xnTt[:], pBt[:]),
                 reads=[kB], writes=["xnTt"])
            for half in range(2):
                for dc in range(ND):
                    P.op("tensor", lambda e, half=half, dc=dc: e.matmul(
                        pS[0][:, half * 512:(half + 1) * 512],
                        s_xnTt[:, dc * 128:(dc + 1) * 128],
                        s_wq[:, dc * D + half * 512: dc * D + (half + 1) * 512],
                        start=(dc == 0), stop=(dc == ND - 1)),
                        reads=[("wq", dc), "xnTt"], writes=[("pS", 0)],
                        relax=[("pS", 0)] if dc > 0 else [])
            P.op("vector", lambda e, tc=tc: e.tensor_copy(
                s_ytok[:, tc * D:(tc + 1) * D], pS[0][:]),
                reads=[("pS", 0)], writes=["ytok"])
            for c in range(ND):
                P.op("tensor", lambda e, c=c, tc=tc: e.transpose(
                    pBt[:, c * 128:(c + 1) * 128],
                    s_ytok[:, tc * D + c * 128: tc * D + (c + 1) * 128],
                    s_iden[:]),
                    reads=["ytok", "iden"], writes=[kB])
            P.op("vector", lambda e, tc=tc: e.tensor_copy(
                s_yT[:].rearrange("p (c t) -> p c t", t=T)[:, :, tc * 128:(tc + 1) * 128],
                pBt[:].rearrange("p (c t) -> p c t", t=128)),
                reads=[kB], writes=["yT"])

        for tc in range(NT):
            P.op("scalar", lambda e, tc=tc: e.activation(
                s_sq[:], s_ytok[:, tc * D:(tc + 1) * D], AF.Square),
                reads=["ytok"], writes=["sq"])
            P.op("vector", lambda e, tc=tc: e.tensor_reduce(
                s_q2[:, tc * HEADS:(tc + 1) * HEADS],
                s_sq[:].rearrange("p (h f) -> p h f", f=HD), AX.X, ALU.add),
                reads=["sq"], writes=["q2"])
        P.op("scalar", lambda e: e.activation(s_q2[:], s_q2[:], AF.Sqrt),
             reads=["q2"], writes=["q2"])
        P.op("gpsimd", lambda e: e.tensor_reduce(
            s_cneg[0:1, 0:256], s_q2[:], AX.C, ALU.max),
            reads=["q2"], writes=["cneg"])
        P.op("vector", lambda e: e.tensor_reduce(
            s_mrow[0:1, 0:HEADS],
            s_cneg[0:1, 0:256].rearrange("o (tc h) -> o h tc", h=HEADS),
            AX.X, ALU.max), reads=["cneg"], writes=["mrow"])
        P.op("tensor", lambda e: e.matmul(
            pS[0][:, 0:HEADS], s_ones1f[:], s_mrow[0:1, 0:HEADS],
            start=True, stop=True), reads=["ones1f", "mrow"], writes=[("pS", 0)])
        P.op("vector", lambda e: e.tensor_copy(s_mcol[:], pS[0][:, 0:HEADS]),
             reads=[("pS", 0)], writes=["mcol"])
        for h in range(HEADS):
            P.op("vector", lambda e, h=h: e.tensor_scalar(
                s_cneg[:].rearrange("p (tc hh) -> p tc hh", hh=HEADS)[:, :, h:h + 1],
                s_q2[:].rearrange("p (tc hh) -> p tc hh", hh=HEADS)[:, :, h:h + 1],
                s_mcol[:, h:h + 1], -1.0, op0=ALU.mult, op1=ALU.mult),
                reads=["q2", "mcol"], writes=["cneg"])
        for half in range(2):
            P.op("tensor", lambda e, half=half: e.transpose(
                pS[0][:, 0:128], s_cneg[:, half * 128:(half + 1) * 128],
                s_idenf[:]), reads=["cneg", "idenf"], writes=[("pS", 0)])
            P.op("vector", lambda e, half=half: e.tensor_copy(
                s_q2T[:, half * 128:(half + 1) * 128], pS[0][:, 0:128]),
                reads=[("pS", 0)], writes=["q2T"])
            P.dma("sync", qtd[half * 128:(half + 1) * 128, :],
                  s_q2T[:, half * 128:(half + 1) * 128],
                  reads=["q2T"], writes=["qtd"])

        def fill_kqa(h):
            i = h % 2
            po = (h % 2) * 64
            co = h // 2
            P.op("gpsimd", lambda e, po=po, co=co, i=i: e.tensor_copy(
                s_ka[i][0:64, :], s_yT[po:po + 64, co * T:(co + 1) * T]),
                reads=["yT"], writes=[("ka", i)])
            P.op("gpsimd", lambda e, po=po, co=co, i=i: e.tensor_copy(
                s_qa[i][0:64, :], s_yT[po:po + 64, co * T:(co + 1) * T]),
                reads=["yT"], writes=[("qa", i)])
            P.dma("sync", s_qa[i][64:65, :].rearrange("o (tc p) -> o tc p", p=128),
                  qtd.rearrange("(tc h) p -> h tc p", h=HEADS)[h].unsqueeze(0),
                  reads=["qtd"], writes=[("qa", i)])

        fill_kqa(0)
        for h in range(HEADS):
            hi = h % 2
            po = (h % 2) * 64
            co = h // 2
            if h + 1 < HEADS:
                fill_kqa(h + 1)
            for ktc in range(NT):
                ip = ktc % 2
                kse = ("se", ip)
                kvau = ("vaug", ip)
                P.op("vector", lambda e, ip=ip, h=h, ktc=ktc: e.tensor_copy(
                    s_vaug[ip][:, 0:64],
                    s_ytok[:, ktc * D + h * 64: ktc * D + (h + 1) * 64]),
                    reads=["ytok", ("vones", ip)], writes=[kvau])
                for pshalf in range(2):
                    for qh in range(2):
                        qg = pshalf * 2 + qh
                        kps = ("pS", pshalf)
                        P.op("tensor", lambda e, ktc=ktc, qg=qg, qh=qh,
                             pshalf=pshalf, hi=hi: e.matmul(
                                 pS[pshalf][:, qh * 512:(qh + 1) * 512],
                                 s_ka[hi][:, ktc * 128:(ktc + 1) * 128],
                                 s_qa[hi][:, qg * 512:(qg + 1) * 512],
                                 start=True, stop=True),
                             reads=[("ka", hi), ("qa", hi), ("kaones", hi)],
                             writes=[kps], relax=[kps])
                    P.op("scalar", lambda e, ip=ip, pshalf=pshalf: e.activation(
                        s_se[ip][:, pshalf * 1024:(pshalf + 1) * 1024],
                        pS[pshalf][:], AF.Exp, scale=0.125),
                        reads=[("pS", pshalf)], writes=[kse])
                for qg in range(4):
                    kpa = ("pav", qg)
                    P.op("tensor", lambda e, ktc=ktc, qg=qg, ip=ip: e.matmul(
                        pv[qg][0:65, 0:512], s_vaug[ip][:],
                        s_se[ip][:, qg * 512:(qg + 1) * 512],
                        start=(ktc == 0), stop=(ktc == NT - 1)),
                        reads=[kvau, kse], writes=[kpa],
                        relax=[kpa] if ktc > 0 else [])
            for qg in range(4):
                P.op("vector", lambda e, qg=qg: e.tensor_copy(
                    s_drow[0:1, qg * 512:(qg + 1) * 512], pv[qg][64:65, 0:512]),
                    reads=[("pav", qg)], writes=["drow"])
            P.op("scalar", lambda e: e.activation(s_drow[:], s_drow[:], AF.Ln),
                 reads=["drow"], writes=["drow"])
            P.op("scalar", lambda e: e.activation(
                s_drow[:], s_drow[:], AF.Exp, scale=-1.0),
                reads=["drow"], writes=["drow"])
            for qg in range(4):
                P.op("tensor", lambda e, qg=qg: e.matmul(
                    pS[0][0:64, 0:512], s_ones1f[0:1, 0:64],
                    s_drow[0:1, qg * 512:(qg + 1) * 512], start=True, stop=True),
                    reads=["ones1f", "drow"], writes=[("pS", 0)])
                P.op("vector", lambda e, qg=qg: e.tensor_copy(
                    s_au[0:64, 0:512], pv[qg][0:64, 0:512]),
                    reads=[("pav", qg)], writes=["au"])
                P.op("vector", lambda e, po=po, co=co, qg=qg: e.tensor_tensor(
                    s_aT[po:po + 64, co * T + qg * 512: co * T + (qg + 1) * 512],
                    s_au[0:64, 0:512], pS[0][0:64, 0:512], op=ALU.mult),
                    reads=["au", ("pS", 0)], writes=["aT"])

        for tc in range(NT):
            i = tc % 2
            ks1 = ("s1t", i)
            for half in range(2):
                bank = pv[half]
                kpa = ("pav", half)
                for fc in range(ND):
                    P.op("tensor", lambda e, fc=fc, tc=tc, half=half, bank=bank: e.matmul(
                        bank[:, 0:512],
                        s_aT[:, fc * T + tc * 128: fc * T + (tc + 1) * 128],
                        s_wo[:, fc * D + half * 512: fc * D + (half + 1) * 512],
                        start=(fc == 0), stop=(fc == ND - 1)),
                        reads=["aT", ("wo", fc)], writes=[kpa],
                        relax=[kpa] if fc > 0 else [])
                P.op("vector", lambda e, i=i, half=half, bank=bank: e.tensor_copy(
                    s_s1[i][:, half * 512:(half + 1) * 512], bank[:, 0:512]),
                    reads=[kpa], writes=[ks1])
            P.dma("sync", s1[b, tc * 128:(tc + 1) * 128, :], s_s1[i][:],
                  reads=[ks1], is_output=True)
    if not fused:
        P.finalize(block, sems, dma_sems)
    return P


def build_ffn1(nc, stack, P=None, mem=None, io=None):
    fused = P is not None
    if not fused:
        P = Prog(nc)
    xq1 = io["xq1"] if fused else nc.dram_tensor("xq1", [B_PER_CORE, T, D], BF16, kind="ExternalInput").ap()
    xs127 = io["xs127"] if fused else nc.dram_tensor("xs127", [B_PER_CORE, T], F32, kind="ExternalInput").ap()
    x1 = io["x1"] if fused else nc.dram_tensor("x1", [B_PER_CORE, T, D], F32, kind="ExternalInput").ap()
    s1 = io["s1"] if fused else nc.dram_tensor("s1", [B_PER_CORE, T, D], BF16, kind="ExternalInput").ap()
    W1T = io["W1T"] if fused else nc.dram_tensor("W1T", [D, H_FFN], F32, kind="ExternalInput").ap()
    b1c = io["b1c"] if fused else nc.dram_tensor("b1c", [128, 32], F32, kind="ExternalInput").ap()
    W2T = io["W2T"] if fused else nc.dram_tensor("W2T", [H_FFN, D], F32, kind="ExternalInput").ap()
    b2r = io["b2r"] if fused else nc.dram_tensor("b2r", [1, D], F32, kind="ExternalInput").ap()
    out = io["out"] if fused else nc.dram_tensor("out", [B_PER_CORE, T, D], F32, kind="ExternalOutput").ap()

    if not fused:
        sems, dma_sems = make_sems(nc, stack)
        mem = Mem(nc, stack)
    en = stack.enter_context
    s_w1t = mem.sb("f_w1t", [128, ND * H_FFN], BF16)
    s_w2t = mem.sb("f_w2t", [128, 32 * D], BF16)
    s_wld = mem.sb("f_wld", [128, 1024], F32)
    s_iden = mem.sb("f_iden", [128, 128], BF16)
    s_idenf = mem.sb("f_idenf", [128, 128], F32)
    s_scr128 = mem.sb("f_scr128", [128, 129], F32)
    s_ones1f = mem.sb("f_ones1f", [1, 128], F32)
    s_ones1b = mem.sb("f_ones1b", [1, 128], BF16)
    s_acc = mem.sb("f_acc", [128, 33], F32)
    s_ws = mem.sb("f_ws", [1, 4], F32)
    s_wscol = mem.sb("f_wscol", [128, 4], F32)
    s_hstore = mem.sb("f_hstore", [128, 32 * 512], BF16)
    s_xq = [mem.sb(f"f_xq{i}", [128, D], BF16) for i in range(2)]
    s_xqT = mem.sb("f_xqT", [128, ND * 512], BF16)
    _g1view = None
    s_xsrow = mem.sb("f_xsrow", [1, T], F32)
    s_hsrow = s_xsrow[0:1, 0:512]
    s_rq2 = s_xsrow[0:1, 512:1024]
    s_sclrow = mem.sb("f_sclrow", [1, T], F32)
    s_sclb = mem.sb("f_sclb", [128, 512], F32)
    s_hdq = mem.sb("f_hdq", [128, 512], F32)
    s_hmax = mem.sb("f_hmax", [128, 512], BF16)
    s_hmin = mem.sb("f_hmin", [128, 512], BF16)
    s_rq2b = s_sclb
    s_hquant = s_hdq
    s_scl2 = mem.sb("f_scl2", [128, 4], F32)
    s_b2b = mem.sb("f_b2b", [128, D], BF16)
    s_x1 = [mem.sb("f_x1", [128, D], F32)] * 2
    s_s1 = s_xq
    s_g1 = None
    s_o = s_x1
    ph = mem.ps([128, 512], F32)
    pg = [mem.ps([128, 512], F32) for i in range(2)]
    pbc = mem.ps([128, 512], F32)
    pt = mem.ps([128, 1024], BF16)
    block = en(nc.Block()) if not fused else None

    emit_identity(P, s_iden, s_scr128)
    P.op("vector", lambda e: e.tensor_copy(s_idenf[:], s_iden[:]),
         reads=["iden"], writes=["idenf"])
    P.op("vector", lambda e: e.memset(s_ones1f[:], 1.0), writes=["ones1f"])
    P.op("vector", lambda e: e.memset(s_ones1b[:], 1.0), writes=["ones1f"])
    P.dma("gpsimd", s_b2b[0:1, :], b2r, writes=["b2r"])
    s_onescolf = mem.sb("f_onescolf", [128, 1], F32)
    s_b1 = mem.sb("f_b1", [128, 32], F32)
    P.dma("sync", s_b1[:], b1c, writes=["b1"])
    P.op("vector", lambda e: e.memset(s_onescolf[:], 1.0), writes=["onescolf"])

    for wsidx, (WT, wtern, nchunk, width, kw) in enumerate([
            (W1T, s_w1t, ND, H_FFN, "w1t"),
            (W2T, s_w2t, 32, D, "w2t")]):
        nw = max(1, width // 1024)
        wt = width // nw
        for c in range(nchunk):
            for wh in range(nw):
                P.dma("sync", s_wld[:, 0:wt],
                      WT[c * 128:(c + 1) * 128, wh * wt:(wh + 1) * wt],
                      writes=["wld"], tag=f"abs{wsidx}_{c}_{wh}")
                P.op("scalar", lambda e, c=c, wh=wh, nw=nw, wt=wt: e.activation(
                    s_hstore[:, 0:wt], s_wld[:, 0:wt], AF.Abs,
                    accum_out=s_acc[:, c * nw + wh:c * nw + wh + 1]),
                    reads=["wld"], writes=["hstore", "acc"])
        P.op("vector", lambda e, nchunk=nchunk, nw=nw: e.tensor_reduce(
            s_acc[:, 32:33], s_acc[:, 0:nchunk * nw], AX.X, ALU.add),
            reads=["acc"], writes=["acc"])
        P.op("tensor", lambda e: e.matmul(
            pbc[0:1, 0:1], s_acc[:, 32:33], s_onescolf[:],
            start=True, stop=True),
            reads=["acc", "onescolf"], writes=[("pbc",)])
        P.op("vector", lambda e, wsidx=wsidx, width=width, nchunk=nchunk: e.tensor_scalar(
            s_ws[0:1, wsidx:wsidx + 1], pbc[0:1, 0:1],
            1.0 / (nchunk * 128 * width), 1e-5, op0=ALU.mult, op1=ALU.max),
            reads=[("pbc",)], writes=["ws"])
        P.op("vector", lambda e, wsidx=wsidx: e.reciprocal(
            s_ws[0:1, wsidx + 2:wsidx + 3], s_ws[0:1, wsidx:wsidx + 1]),
            reads=["ws"], writes=["ws"])
        P.op("tensor", lambda e, wsidx=wsidx: e.matmul(
            pbc[:, 0:1], s_ones1f[:], s_ws[0:1, wsidx + 2:wsidx + 3],
            start=True, stop=True), reads=["ones1f", "ws"], writes=[("pbc",)])
        P.op("vector", lambda e, wsidx=wsidx: e.tensor_copy(
            s_wscol[:, wsidx:wsidx + 1], pbc[:, 0:1]),
            reads=[("pbc",)], writes=["wscol"])
        if wsidx == 1:
            P.op("tensor", lambda e: e.matmul(
                pbc[:, 0:1], s_ones1f[:], s_ws[0:1, 1:2],
                start=True, stop=True), reads=["ones1f", "ws"], writes=[("pbc",)])
            P.op("vector", lambda e: e.tensor_copy(
                s_wscol[:, 2:3], pbc[:, 0:1]), reads=[("pbc",)], writes=["ws"])
        for c in range(nchunk):
            for wh in range(nw):
                P.dma("sync", s_wld[:, 0:wt],
                      WT[c * 128:(c + 1) * 128, wh * wt:(wh + 1) * wt],
                      writes=["wld"], tag=f"tern{wsidx}_{c}_{wh}")
                P.op("vector", lambda e, wsidx=wsidx, wt=wt: e.tensor_scalar(
                    s_wld[:, 0:wt], s_wld[:, 0:wt],
                    s_wscol[:, wsidx:wsidx + 1], None, op0=ALU.mult),
                    reads=["wld", "wscol"], writes=["wld"])
                P.op("vector", lambda e, wt=wt: e.tensor_scalar(
                    s_wld[:, 0:wt], s_wld[:, 0:wt], MAGIC, MAGIC,
                    op0=ALU.add, op1=ALU.subtract),
                    reads=["wld"], writes=["wld"])
                P.op("vector", lambda e, c=c, wh=wh, wt=wt, width=width,
                     wtern=wtern, kw=kw: e.tensor_scalar(
                    wtern[:, c * width + wh * wt: c * width + (wh + 1) * wt],
                    s_wld[:, 0:wt], 1.0, -1.0, op0=ALU.min, op1=ALU.max),
                    reads=["wld"], writes=[kw])

    for half in range(2):
        P.op("tensor", lambda e, half=half: e.matmul(
            pbc[:], s_ones1b[:], s_b2b[0:1, half * 512:(half + 1) * 512],
            start=True, stop=True), reads=["ones1f", "b2r"], writes=[("pbc",)])
        P.op("vector", lambda e, half=half: e.tensor_copy(
            s_hdq[:, 0:512], pbc[:]),
            reads=[("pbc",)], writes=[("b2s", half)])
        P.op("vector", lambda e, half=half: e.tensor_copy(
            s_b2b[:, half * 512:(half + 1) * 512], s_hdq[:, 0:512]),
            reads=[("b2s", half)], writes=["b2b"])

    for b in range(B_PER_CORE):
        P.dma("sync", s_xsrow[:], xs127[b].unsqueeze(0), writes=["xsrow"])
        P.op("vector", lambda e: e.tensor_scalar(
            s_sclrow[:], s_xsrow[:], s_ws[0:1, 0:1], None, op0=ALU.mult),
            reads=["xsrow", "ws"], writes=["sclrow"])
        for tg in range(4):
            for tq in range(4):
                i = tq % 2
                tci = tg * 4 + tq
                kxq = ("fxq", i)
                P.dma("sync", s_xq[i][:], xq1[b, tci * 128:(tci + 1) * 128, :],
                      writes=[kxq])
                for c in range(ND):
                    P.op("tensor", lambda e, c=c, i=i: e.transpose(
                        pt[:, c * 128:(c + 1) * 128],
                        s_xq[i][:, c * 128:(c + 1) * 128], s_iden[:]),
                        reads=[kxq, "iden"], writes=[("pt", c)])
                P.op("vector", lambda e, tq=tq: e.tensor_copy(
                    s_xqT[:].rearrange("p (c t) -> p c t", t=512)[
                        :, :, tq * 128:(tq + 1) * 128],
                    pt[:].rearrange("p (c t) -> p c t", t=128)),
                    reads=[("pt", c) for c in range(ND)], writes=["xqT"])
            P.op("tensor", lambda e, tg=tg: e.matmul(
                pbc[:], s_ones1f[:], s_sclrow[0:1, tg * 512:(tg + 1) * 512],
                start=True, stop=True), reads=["ones1f", "sclrow"], writes=[("pbc",)])
            P.op("vector", lambda e: e.tensor_copy(s_sclb[:], pbc[:]),
                 reads=[("pbc",)], writes=["sclb"])
            P.op("vector", lambda e: e.memset(s_hmax[:], 0.0), writes=["hmax"])
            P.op("vector", lambda e: e.memset(s_hmin[:], 0.0), writes=["hmin"])
            for Hc in range(32):
                for dc in range(ND):
                    P.op("tensor", lambda e, Hc=Hc, dc=dc: e.matmul(
                        ph[:], s_w1t[:, dc * H_FFN + Hc * 128: dc * H_FFN + (Hc + 1) * 128],
                        s_xqT[:, dc * 512:(dc + 1) * 512],
                        start=(dc == 0), stop=(dc == ND - 1)),
                        reads=["w1t", "xqT"], writes=[("ph",)],
                        relax=[("ph",)] if dc > 0 else [])
                P.op("vector", lambda e: e.tensor_tensor(
                    s_hdq[:], ph[:], s_sclb[:], op=ALU.mult),
                    reads=[("ph",), "sclb"], writes=["hdq"])
                P.op("vector", lambda e, Hc=Hc: e.tensor_scalar(
                    s_hdq[:], s_hdq[:], s_b1[:, Hc:Hc + 1], None, op0=ALU.add),
                    reads=["hdq", "b1"], writes=["hdq"])
                P.op("scalar", lambda e, Hc=Hc: e.activation(
                    s_hstore[:, Hc * 512:(Hc + 1) * 512], s_hdq[:], AF.Gelu),
                    reads=["hdq"], writes=["hstore"])
                P.op("vector", lambda e, Hc=Hc: e.tensor_tensor(
                    s_hmax[:], s_hmax[:], s_hstore[:, Hc * 512:(Hc + 1) * 512],
                    op=ALU.max),
                    reads=["hmax", "hstore"], writes=["hmax"])
                P.op("vector", lambda e, Hc=Hc: e.tensor_tensor(
                    s_hmin[:], s_hmin[:], s_hstore[:, Hc * 512:(Hc + 1) * 512],
                    op=ALU.min),
                    reads=["hmin", "hstore"], writes=["hmin"])
            P.op("vector", lambda e: e.tensor_scalar(
                s_hmin[:], s_hmin[:], -1.0, None, op0=ALU.mult),
                reads=["hmin"], writes=["hmin"])
            P.op("vector", lambda e: e.tensor_tensor(
                s_hmax[:], s_hmax[:], s_hmin[:], op=ALU.max),
                reads=["hmax", "hmin"], writes=["hmax"])
            P.op("gpsimd", lambda e: e.tensor_reduce(
                s_hsrow, s_hmax[:], AX.C, ALU.max),
                reads=["hmax"], writes=["hsrow", "xsrow"])
            P.op("vector", lambda e: e.tensor_scalar(
                s_hsrow, s_hsrow, 1e-5, None, op0=ALU.max),
                reads=["hsrow"], writes=["hsrow"])
            P.op("vector", lambda e: e.reciprocal(s_rq2, s_hsrow),
                 reads=["hsrow"], writes=["rq2"])
            P.op("vector", lambda e: e.tensor_scalar(
                s_rq2, s_rq2, 127.0, None, op0=ALU.mult),
                reads=["rq2"], writes=["rq2"])
            P.op("tensor", lambda e: e.matmul(
                pbc[:], s_ones1f[:], s_rq2, start=True, stop=True),
                reads=["ones1f", "rq2"], writes=[("pbc",)])
            P.op("vector", lambda e: e.tensor_copy(s_rq2b[:], pbc[:]),
                 reads=[("pbc",)], writes=["sclb"])
            for tq in range(4):
                P.op("tensor", lambda e, tq=tq: e.transpose(
                    pbc[0:128, 0:1], s_hsrow[0:1, tq * 128:(tq + 1) * 128],
                    s_idenf[0:1, 0:1]),
                    reads=["hsrow", "idenf"], writes=[("pbc",)])
                P.op("vector", lambda e, tq=tq: e.tensor_copy(
                    s_scl2[:, tq:tq + 1], pbc[0:128, 0:1]),
                    reads=[("pbc",)], writes=["scl2"])
            P.op("vector", lambda e: e.tensor_scalar(
                s_scl2[:], s_scl2[:], s_wscol[:, 2:3], None, op0=ALU.mult),
                reads=["scl2", "ws"], writes=["scl2"])
            P.op("vector", lambda e: e.tensor_scalar(
                s_scl2[:], s_scl2[:], 1.0 / 127.0, None, op0=ALU.mult),
                reads=["scl2"], writes=["scl2"])
            for Hc in range(32):
                P.op("vector", lambda e, Hc=Hc: e.tensor_tensor(
                    s_hquant[:], s_hstore[:, Hc * 512:(Hc + 1) * 512],
                    s_rq2b[:], op=ALU.mult),
                    reads=["hstore", "sclb"], writes=["hdq"])
                P.op("vector", lambda e: e.tensor_scalar(
                    s_hquant[:], s_hquant[:], MAGIC, MAGIC,
                    op0=ALU.add, op1=ALU.subtract),
                    reads=["hdq"], writes=["hdq"])
                P.op("vector", lambda e, Hc=Hc: e.tensor_scalar(
                    s_hstore[:, Hc * 512:(Hc + 1) * 512], s_hquant[:],
                    127.0, -128.0, op0=ALU.min, op1=ALU.max),
                    reads=["hdq"], writes=["hstore"])
            for tq in range(4):
                i = tq % 2
                tci = tg * 4 + tq
                kx1, ks1 = ("fx1",), ("fs1", i)
                P.dma("sync", s_x1[i][:], x1[b, tci * 128:(tci + 1) * 128, :],
                      writes=[kx1])
                P.dma("sync", s_s1[i][:], s1[b, tci * 128:(tci + 1) * 128, :],
                      writes=[ks1, ("fxq", i)])
                for half in range(2):
                    kpg = ("pg", half)
                    for Hc in range(32):
                        P.op("tensor", lambda e, Hc=Hc, tq=tq, half=half: e.matmul(
                            pg[half][:],
                            s_hstore[:, Hc * 512 + tq * 128: Hc * 512 + (tq + 1) * 128],
                            s_w2t[:, Hc * D + half * 512: Hc * D + (half + 1) * 512],
                            start=(Hc == 0), stop=(Hc == 31)),
                            reads=["hstore", "w2t"], writes=[kpg],
                            relax=[kpg] if Hc > 0 else [])
                    P.op("vector", lambda e, half=half, tq=tq: e.tensor_scalar(
                        s_xqT[:, 1024 + half * 512: 1024 + (half + 1) * 512], pg[half][:],
                        s_scl2[:, tq:tq + 1], None, op0=ALU.mult),
                        reads=[("pg", half), "scl2"], writes=["g1", "xqT"])
                P.op("vector", lambda e: e.tensor_tensor(
                    s_xqT[:, 1024:2048], s_xqT[:, 1024:2048], s_b2b[:], op=ALU.add),
                    reads=["g1", "b2b"], writes=["g1"])
                P.op("scalar", lambda e: e.activation(
                    s_xqT[:, 1024:2048], s_xqT[:, 1024:2048], AF.Silu), reads=["g1"], writes=["g1"])
                ko = ("fo",)
                P.op("vector", lambda e, i=i: e.tensor_tensor(
                    s_xqT[:, 1024:2048], s_xqT[:, 1024:2048], s_s1[i][:], op=ALU.mult),
                    reads=["g1", ks1], writes=["g1"])
                P.op("vector", lambda e, i=i: e.tensor_tensor(
                    s_o[i][:], s_x1[i][:], s_xqT[:, 1024:2048], op=ALU.add),
                    reads=[kx1, "g1"], writes=[ko, kx1])
                P.dma("sync", out[b, tci * 128:(tci + 1) * 128, :], s_o[i][:],
                      reads=[ko], is_output=True)
    if not fused:
        P.finalize(block, sems, dma_sems)
    return P



def prep_weights(inputs):
    tr = lambda a: np.ascontiguousarray(np.asarray(a).T)
    w = {}
    w["wqkvT"] = tr(inputs["L0_attn_w_qkv"])
    w["woT"] = tr(inputs["L0_attn_w_out"])
    w["gateT"] = tr(inputs["L0_ffn_gate_w"])
    w["WgT"] = np.ascontiguousarray(np.asarray(inputs["L0_ffn_Wg"]).transpose(0, 2, 1))
    w["WuT"] = np.ascontiguousarray(np.asarray(inputs["L0_ffn_Wu"]).transpose(0, 2, 1))
    w["WdT"] = np.ascontiguousarray(np.asarray(inputs["L0_ffn_Wd"]).transpose(0, 2, 1))
    w["WsgT"] = tr(inputs["L0_ffn_Ws_g"])
    w["WsuT"] = tr(inputs["L0_ffn_Ws_u"])
    w["WsdT"] = tr(inputs["L0_ffn_Ws_d"])
    w["wqkv1T"] = tr(inputs["L1_attn_W_qkv"])
    w["wo1T"] = tr(inputs["L1_attn_W_o"])
    w["W1T"] = tr(inputs["L1_ffn_W1"])
    w["b1c"] = np.ascontiguousarray(np.asarray(inputs["L1_ffn_b1"]).reshape(H_FFN // 128, 128).T)
    w["W2T"] = tr(inputs["L1_ffn_W2"])
    w["b2c"] = np.ascontiguousarray(np.asarray(inputs["L1_ffn_b2"]).reshape(D // 128, 128).T)
    w["b2r"] = np.ascontiguousarray(np.asarray(inputs["L1_ffn_b2"]).reshape(1, D))
    return w


def build_fused(nc, stack):
    P = Prog(nc)
    sems, dma_sems = make_sems(nc, stack)
    mem = Mem(nc, stack)
    io = {}

    def ext(name, shape, dt):
        io[name] = nc.dram_tensor(name, shape, dt, kind="ExternalInput").ap()

    def mid(name, shape, dt):
        io[name] = nc.dram_tensor(name, shape, dt, kind="Internal").ap()

    ext("x", [B_PER_CORE, T, D], F32)
    ext("wqkvT", [D, 3 * D], F32)
    ext("woT", [D, D], F32)
    ext("gateT", [D, E_EXP], F32)
    ext("WgT", [E_EXP, D, F_EXP], F32)
    ext("WuT", [E_EXP, D, F_EXP], F32)
    ext("WdT", [E_EXP, F_EXP, D], F32)
    ext("WsgT", [D, F_SH], F32)
    ext("WsuT", [D, F_SH], F32)
    ext("WsdT", [F_SH, D], F32)
    ext("wqkv1T", [D, D], F32)
    ext("wo1T", [D, D], F32)
    ext("W1T", [D, H_FFN], F32)
    ext("b1c", [128, 32], F32)
    ext("W2T", [H_FFN, D], F32)
    ext("b2r", [1, D], F32)
    mid("xn", [B_PER_CORE, T, D], BF16)
    mid("s0", [B_PER_CORE, T, D], BF16)
    mid("moe", [B_PER_CORE, T, D], F32)
    mid("x1", [B_PER_CORE, T, D], F32)
    mid("xn1", [B_PER_CORE, T, D], BF16)
    mid("xq1", [B_PER_CORE, T, D], BF16)
    mid("xs127", [B_PER_CORE, T], F32)
    mid("s1", [B_PER_CORE, T, D], BF16)
    io["out"] = nc.dram_tensor("out", [B_PER_CORE, T, D], F32,
                               kind="ExternalOutput").ap()

    for bld in (build_ln0, build_attn0, build_moe0, build_combine_ln1,
                build_attn1, build_ffn1):
        mem.reset()
        bld(nc, stack, P=P, mem=mem, io=io)
        P.fence()

    block = stack.enter_context(nc.Block())
    P.finalize(block, sems, dma_sems)
    return P


FUSED = True


def _launch(build_fn, in_maps):
    from contextlib import ExitStack
    with ExitStack() as stack:
        nc = bass.Bass("TRN2", target_bir_lowering=False,
                       detect_race_conditions=False)
        build_fn(nc, stack)
        res = run_bass_kernel_spmd(nc, in_maps, core_ids=list(range(NCORES)))
    return [{k: np.asarray(v) for k, v in r.items()} for r in res.results]


def kernel(**inputs):
    w = prep_weights(inputs)
    x = np.ascontiguousarray(np.asarray(inputs["L0_attn_x"], dtype=np.float32))
    shards = [x[c * B_PER_CORE:(c + 1) * B_PER_CORE] for c in range(NCORES)]

    if FUSED:
        rf = _launch(build_fused, [
            {"x": shards[c], "wqkvT": w["wqkvT"], "woT": w["woT"],
             "gateT": w["gateT"], "WgT": w["WgT"], "WuT": w["WuT"],
             "WdT": w["WdT"], "WsgT": w["WsgT"], "WsuT": w["WsuT"],
             "WsdT": w["WsdT"], "wqkv1T": w["wqkv1T"], "wo1T": w["wo1T"],
             "W1T": w["W1T"], "b1c": w["b1c"], "W2T": w["W2T"],
             "b2r": w["b2r"]} for c in range(NCORES)])
        out = np.concatenate([r["out"] for r in rf], axis=0)
        return out.astype(np.float32)

    r1 = _launch(build_ln0, [{"x": s} for s in shards])
    xn = [r["xn"] for r in r1]

    r2 = _launch(build_attn0, [
        {"xn": xn[c], "wqkvT": w["wqkvT"], "woT": w["woT"]}
        for c in range(NCORES)])
    s0 = [r["s0"] for r in r2]

    r3 = _launch(build_moe0, [
        {"xn": xn[c], "gateT": w["gateT"], "WgT": w["WgT"], "WuT": w["WuT"],
         "WdT": w["WdT"], "WsgT": w["WsgT"], "WsuT": w["WsuT"],
         "WsdT": w["WsdT"]} for c in range(NCORES)])
    moe = [r["moe"] for r in r3]

    r4 = _launch(build_combine_ln1, [
        {"x": shards[c], "s0": s0[c], "moe": moe[c]} for c in range(NCORES)])

    r5 = _launch(build_attn1, [
        {"xn1": r4[c]["xn1"], "wqkv1T": w["wqkv1T"], "wo1T": w["wo1T"]}
        for c in range(NCORES)])

    r6 = _launch(build_ffn1, [
        {"xq1": r4[c]["xq1"], "xs127": r4[c]["xs127"], "x1": r4[c]["x1"],
         "s1": r5[c]["s1"], "W1T": w["W1T"], "b1c": w["b1c"],
         "W2T": w["W2T"], "b2r": w["b2r"]} for c in range(NCORES)])

    out = np.concatenate([r["out"] for r in r6], axis=0)
    return out.astype(np.float32)


# version 06
# speedup vs baseline: 1.2197x; 1.0314x over previous
import numpy as np
import concourse.bass as bass
import concourse.mybir as mybir
from concourse.bass import IndirectOffsetOnAxis
from concourse.bass_isa import ReduceOp
from concourse.bass_utils import run_bass_kernel_spmd

F32 = mybir.dt.float32
BF16 = mybir.dt.bfloat16
U32 = mybir.dt.uint32
AF = mybir.ActivationFunctionType
ALU = mybir.AluOpType
AX = mybir.AxisListType

MAGIC = 1.5 * 2 ** 23
NCORES = 8
B_PER_CORE = 2
T = 2048
D = 1024
NT = T // 128
ND = D // 128
HEADS = 16
HD = 64
E_EXP = 16
CAP = 256
F_EXP = 2048
F_SH = 2048
H_FFN = 4096
LN_EPS = 1e-5
TWO_PI_HI = np.float32(6.2831855)
TWO_PI_LO = np.float32(2 * np.pi - np.float64(np.float32(6.2831855)))
BIG_IDX = 1.0e7

ENGINES = ("sync", "vector", "scalar", "tensor", "gpsimd")


class Prog:

    def __init__(self, nc):
        self.nc = nc
        self.ops = []
        self.last_write = {}
        self.readers = {}
        self.out_dmas = []

    def op(self, engine, fn, reads=(), writes=(), inc=1, tag="", relax=()):
        idx = len(self.ops)
        deps = {}
        for d in getattr(self, "_fence_deps", ()):
            deps.setdefault(d, set())
        for k in reads:
            w = self.last_write.get(k)
            if w is not None:
                deps.setdefault(w, set()).add(k)
        for k in writes:
            w = self.last_write.get(k)
            if w is not None:
                deps.setdefault(w, set()).add(k)
            for rd in self.readers.get(k, ()):
                deps.setdefault(rd, set()).add(k)
        self.ops.append(dict(engine=engine, fn=fn, inc=inc, idx=idx,
                             deps=deps, need_inc=False, tag=tag,
                             relax=frozenset(relax)))
        for k in reads:
            self.readers.setdefault(k, []).append(idx)
        for k in writes:
            self.last_write[k] = idx
            self.readers[k] = []
        return idx

    def dma(self, queue, out, in_, reads=(), writes=(), tag="", is_output=False):
        idx = self.op(queue, lambda e: e.dma_start(out=out, in_=in_),
                      reads=reads, writes=writes, inc=16, tag=tag)
        if is_output:
            self.out_dmas.append(idx)
        return idx

    def dma_transpose(self, queue, out, in_, reads=(), writes=(), tag=""):
        return self.op(queue, lambda e: e.dma_start_transpose(out, in_),
                       reads=reads, writes=writes, inc=16, tag=tag)

    def fence(self):
        start = getattr(self, "_fence_start", 0)
        prior = set(range(start, len(self.ops)))
        markers = []
        for eng in ENGINES:
            i = self.op(eng, lambda e: e.nop(), tag="fence")
            self.ops[i]["deps"] = {d: set() for d in prior}
            markers.append(i)
        self._fence_deps = markers
        self._fence_start = len(self.ops)

    NSLOT = 6
    SERIALIZE = False

    DRIVER_MODE = False
    NSEM_DRIVER = 1
    DRIVER_Q = "sync"
    ALL_INC = False
    NO_DEDUP = False
    TENSOR_EDGES = False
    OWN_EDGES = True

    def finalize(self, block, sems, dma_sems):
        if Prog.DRIVER_MODE:
            nsem = Prog.NSEM_DRIVER
            gsems = [dma_sems[Prog.DRIVER_Q][i] for i in range(nsem)]
            cums = []
            tot = [0] * nsem
            for i, op in enumerate(self.ops):
                cums.append(tuple(tot))
                tot[i % nsem] += op["inc"]
                op["dsem"] = i % nsem

            def section(eng_name, eng):
                for op, before in zip(self.ops, cums):
                    if op["engine"] != eng_name:
                        continue
                    for s in range(nsem):
                        if before[s]:
                            eng.wait_ge(gsems[s], before[s])
                    inst = op["fn"](eng)
                    inst.then_inc(gsems[op["dsem"]], op["inc"])
                if eng_name == "sync":
                    for s in range(nsem):
                        if tot[s]:
                            eng.wait_ge(gsems[s], tot[s])
            for name in ENGINES:
                deco = getattr(block, name)

                def make(n):
                    def body(eng):
                        section(n, eng)
                    return body
                deco(make(name))
            return

        if Prog.SERIALIZE:
            for op in self.ops:
                op["deps"] = {i: set() for i in range(op["idx"])}
        if Prog.ALL_INC:
            for op in self.ops:
                op["need_inc"] = True
        """sems: per-engine compute sems. dma_sems: dict queue ->
        list of NSLOT slot semaphores (queues that issue DMAs)."""
        ops = self.ops
        slot_counts = {}
        for op in ops:
            if op["inc"] == 16:
                q = op["engine"]
                cnt = slot_counts.setdefault(q, [0] * Prog.NSLOT)
                seq = sum(cnt)
                s = seq % Prog.NSLOT
                cnt[s] += 1
                op["slot"] = s
                op["slot_ord"] = cnt[s]
                op["need_inc"] = True
        def edge_needed(op, d, via):
            dop = ops[d]
            if dop["engine"] != op["engine"]:
                return True
            if not Prog.OWN_EDGES:
                return False
            if op["engine"] == "tensor" and dop["engine"] == "tensor":
                if via and via <= op["relax"]:
                    return False
            return True

        for op in ops:
            for d, via in op["deps"].items():
                dop = ops[d]
                if dop["inc"] != 1:
                    continue
                if edge_needed(op, d, via):
                    dop["need_inc"] = True

        cum = {e: 0 for e in ENGINES}
        for op in ops:
            if op["inc"] == 1 and op["need_inc"]:
                cum[op["engine"]] += 1
            op["cum_after"] = cum[op["engine"]]

        clocks = {e: {} for e in ENGINES}

        def want(op, key, v):
            eng = op["engine"]
            if not Prog.NO_DEDUP and clocks[eng].get(key, 0) >= v:
                return
            if Prog.NO_DEDUP and op["waits"].get(key, 0) >= v:
                return
            op["waits"][key] = max(op["waits"].get(key, 0), v)
            clocks[eng][key] = v

        for op in ops:
            eng = op["engine"]
            op["waits"] = {}
            for d in sorted(op["deps"]):
                dop = ops[d]
                via = op["deps"][d]
                if dop["inc"] == 16:
                    key = ("dma", dop["engine"], dop["slot"])
                    want(op, key, 16 * dop["slot_ord"])
                elif edge_needed(op, d, via):
                    want(op, ("eng", dop["engine"]), dop["cum_after"])
            if op["inc"] == 16 and op["slot_ord"] > 1:
                want(op, ("dma", op["engine"], op["slot"]),
                     16 * (op["slot_ord"] - 1))

        final_waits = {}
        for idx in self.out_dmas:
            op = ops[idx]
            key = ("dma", op["engine"], op["slot"])
            final_waits[key] = max(final_waits.get(key, 0), 16 * op["slot_ord"])

        def sem_of(key):
            if key[0] == "dma":
                return dma_sems[key[1]][key[2]]
            return sems[key[1]]

        def section(eng_name, eng):
            for op in ops:
                if op["engine"] != eng_name:
                    continue
                for key in sorted(op["waits"], key=str):
                    eng.wait_ge(sem_of(key), op["waits"][key])
                inst = op["fn"](eng)
                if op["need_inc"]:
                    if op["inc"] == 16:
                        inst.then_inc(dma_sems[eng_name][op["slot"]], 16)
                    else:
                        inst.then_inc(sems[eng_name], 1)
            if eng_name == "sync":
                for key in sorted(final_waits, key=str):
                    eng.wait_ge(sem_of(key), final_waits[key])

        for name in ENGINES:
            deco = getattr(block, name)

            def make(n):
                def body(eng):
                    section(n, eng)
                return body
            deco(make(name))


class Mem:

    COLS = 106016

    def __init__(self, nc, stack):
        en = stack.enter_context
        self.arena = en(nc.sbuf_tensor("arena", [128, Mem.COLS], BF16))
        self.a16 = self.arena[:]
        self.a32 = self.arena[:].bitcast(F32)
        self.au32 = self.arena[:].bitcast(U32)
        self.p8 = en(nc.psum_tensor("pp8", [128, 2048], F32))
        self.p4 = [en(nc.psum_tensor(f"pp4_{i}", [128, 512], F32))
                   for i in range(3)]
        self.pB = en(nc.psum_tensor("ppB", [128, 1024], BF16))
        self.reset()

    def reset(self):
        self.off = 0
        self.ps_off = 0
        self.ps_i = 0

    def sb(self, name, shape, dtype):
        p, n = shape
        width = {BF16: 1, F32: 2, U32: 2}[dtype] * n
        width = (width + 31) & ~31
        assert self.off + width <= Mem.COLS, \
            f"SBUF arena overflow at {name}: {self.off}+{width}"
        base = self.off
        self.off += width
        if dtype == BF16:
            return self.a16[0:p, base:base + n]
        if dtype == F32:
            return self.a32[0:p, base // 2: base // 2 + n]
        return self.au32[0:p, base // 2: base // 2 + n]

    def ps(self, shape, dtype):
        p, n = shape
        if dtype == BF16:
            assert n <= 1024
            return self.pB[0:p, 0:n]
        nb = (n + 511) // 512
        if self.ps_off + nb * 512 <= 2048:
            base = self.ps_off
            self.ps_off += nb * 512
            return self.p8[0:p, base:base + n]
        assert nb == 1 and self.ps_i < 3, "PSUM pool overflow"
        t = self.p4[self.ps_i]
        self.ps_i += 1
        return t[0:p, 0:n]


DMA_QUEUES = ("sync", "gpsimd", "scalar", "vector")


def make_sems(nc, stack):
    sems = {e: stack.enter_context(nc.semaphore(f"sem_{e}")) for e in ENGINES}
    dma_sems = {q: [stack.enter_context(nc.semaphore(f"dsem_{q}_{i}"))
                    for i in range(Prog.NSLOT)] for q in DMA_QUEUES}
    nums = [h.num for h in sems.values()]
    nums += [h.num for hs in dma_sems.values() for h in hs]
    nc.gpsimd.sem_clear(range(min(nums), max(nums) + 1))
    nc.all_engine_barrier()
    return sems, dma_sems



def alloc_psum_banks(nc, stack, n_f32=8, n_bf16=0, prefix="pb"):
    banks = []
    for i in range(n_f32):
        banks.append(stack.enter_context(
            nc.psum_tensor(f"{prefix}{i}", [128, 512], F32)))
    for i in range(n_bf16):
        banks.append(stack.enter_context(
            nc.psum_tensor(f"{prefix}b{i}", [128, 1024], BF16)))
    return banks


def emit_ln_tile(P, sb, tag, x_tile_key, s_x, s_junk, s_small, col0,
                 out_fn):
    ksmall = tag + "_stats"
    P.op("vector", lambda e: e.tensor_reduce(
        s_small[:, col0 + 2:col0 + 3], s_x[:], AX.X, ALU.add),
        reads=[x_tile_key], writes=[ksmall])
    P.op("scalar", lambda e: e.activation(
        s_junk[:], s_x[:], AF.Square, accum_out=s_small[:, col0 + 3:col0 + 4]),
        reads=[x_tile_key], writes=[ksmall, tag + "_junk"])
    P.op("vector", lambda e: e.tensor_scalar(
        s_small[:, col0:col0 + 1], s_small[:, col0 + 2:col0 + 3],
        1.0 / D, None, op0=ALU.mult), reads=[ksmall], writes=[ksmall])
    P.op("vector", lambda e: e.tensor_scalar(
        s_small[:, col0 + 4:col0 + 5], s_small[:, col0 + 3:col0 + 4],
        1.0 / D, None, op0=ALU.mult), reads=[ksmall], writes=[ksmall])
    P.op("vector", lambda e: e.tensor_tensor(
        s_small[:, col0 + 5:col0 + 6], s_small[:, col0:col0 + 1],
        s_small[:, col0:col0 + 1], op=ALU.mult), reads=[ksmall], writes=[ksmall])
    P.op("vector", lambda e: e.tensor_tensor(
        s_small[:, col0 + 4:col0 + 5], s_small[:, col0 + 4:col0 + 5],
        s_small[:, col0 + 5:col0 + 6], op=ALU.subtract),
        reads=[ksmall], writes=[ksmall])
    P.op("vector", lambda e: e.tensor_scalar(
        s_small[:, col0 + 4:col0 + 5], s_small[:, col0 + 4:col0 + 5],
        LN_EPS, None, op0=ALU.add), reads=[ksmall], writes=[ksmall])
    P.op("scalar", lambda e: e.activation(
        s_small[:, col0 + 5:col0 + 6], s_small[:, col0 + 4:col0 + 5], AF.Ln),
        reads=[ksmall], writes=[ksmall])
    P.op("scalar", lambda e: e.activation(
        s_small[:, col0 + 1:col0 + 2], s_small[:, col0 + 5:col0 + 6],
        AF.Exp, scale=-0.5), reads=[ksmall], writes=[ksmall])
    out_fn(s_small[:, col0:col0 + 1], s_small[:, col0 + 1:col0 + 2], ksmall)


def build_ln0(nc, stack, P=None, mem=None, io=None):
    fused = P is not None
    if not fused:
        P = Prog(nc)
    x = io["x"] if fused else nc.dram_tensor("x", [B_PER_CORE, T, D], F32, kind="ExternalInput").ap()
    xn = io["xn"] if fused else nc.dram_tensor("xn", [B_PER_CORE, T, D], BF16, kind="ExternalOutput").ap()

    if not fused:
        sems, dma_sems = make_sems(nc, stack)
        mem = Mem(nc, stack)
    block = stack.enter_context(nc.Block()) if not fused else None

    s_x = [mem.sb(f"s_x{i}", [128, D], F32) for i in range(3)]
    s_xn = [mem.sb(f"s_xn{i}", [128, D], BF16) for i in range(3)]
    s_junk = mem.sb("s_junk", [128, D], BF16)
    s_small = mem.sb("s_small", [128, 8], F32)

    for b in range(B_PER_CORE):
        for tc in range(NT):
            i = (b * NT + tc) % 3
            kx = ("sx", i)
            kxn = ("sxn", i)
            P.dma("sync", s_x[i][:], x[b, tc * 128:(tc + 1) * 128, :],
                  writes=[kx], tag=f"ldx{b}_{tc}")

            def consume(m_ap, r_ap, ksmall, i=i, kx=kx, kxn=kxn, b=b, tc=tc):
                P.op("vector", lambda e: e.tensor_scalar(
                    s_xn[i][:], s_x[i][:], m_ap, r_ap,
                    op0=ALU.subtract, op1=ALU.mult),
                    reads=[kx, ksmall], writes=[kxn])
                P.dma("sync", xn[b, tc * 128:(tc + 1) * 128, :], s_xn[i][:],
                      reads=[kxn], tag=f"stxn{b}_{tc}", is_output=True)

            emit_ln_tile(P, None, f"ln{b}_{tc}", kx, s_x[i], s_junk, s_small,
                         0, consume)

    if not fused:
        P.finalize(block, sems, dma_sems)
    return P


def emit_identity(P, s_iden, s_scr128):
    P.op("gpsimd", lambda e: e.iota(s_scr128[:, 0:1], [[0, 1]], base=0,
                                    channel_multiplier=1,
                                    allow_small_or_imprecise_dtypes=True),
         writes=["iden_scr"])
    P.op("gpsimd", lambda e: e.iota(s_scr128[:, 1:129], [[1, 128]], base=0,
                                    channel_multiplier=0,
                                    allow_small_or_imprecise_dtypes=True),
         writes=["iden_scr"])
    P.op("vector", lambda e: e.tensor_scalar(
        s_iden[:], s_scr128[:, 1:129], s_scr128[:, 0:1], None, op0=ALU.is_equal),
        reads=["iden_scr"], writes=["iden"])


def emit_rope_tables(P, nc, mem):
    s_if = mem.sb("s_invfreq", [128, 32], F32)
    s_tc = mem.sb("s_tcol", [128, 16], F32)
    ctab = mem.sb("s_ctab", [128, 512], F32)
    stab = mem.sb("s_stab", [128, 512], F32)
    scr = [mem.sb(f"s_ropescr{i}", [128, 32], F32) for i in range(3)]
    P.op("gpsimd", lambda e: e.iota(s_if[:], [[1, 32]], base=0,
                                    channel_multiplier=0,
                                    allow_small_or_imprecise_dtypes=True),
         writes=["ropeI"])
    P.op("scalar", lambda e: e.activation(s_if[:], s_if[:], AF.Exp,
                                          scale=-float(np.log(10000.0) / 32.0)),
         reads=["ropeI"], writes=["ropeI"])
    emit_rope_tables.debug = (s_if, s_tc)
    P.op("gpsimd", lambda e: e.iota(
        s_tc[:], [[128, NT]], base=0, channel_multiplier=1,
        allow_small_or_imprecise_dtypes=True), writes=["ropeT"])
    for tc in range(NT):
        P.op("vector", lambda e, tc=tc: e.tensor_scalar(
            scr[0][:], s_if[:], s_tc[:, tc:tc + 1], None, op0=ALU.mult),
            reads=["ropeI", "ropeT"], writes=["rscr0"])
        for which, bias, tab in ((0, 0.0, stab), (1, np.pi / 2, ctab)):
            P.op("vector", lambda e, bias=bias: e.tensor_scalar(
                scr[1][:], scr[0][:], float(bias), float(1.0 / (2 * np.pi)),
                op0=ALU.add, op1=ALU.mult), reads=["rscr0"], writes=["rscr1"])
            P.op("vector", lambda e: e.tensor_scalar(
                scr[1][:], scr[1][:], MAGIC, MAGIC, op0=ALU.add, op1=ALU.subtract),
                reads=["rscr1"], writes=["rscr1"])
            P.op("vector", lambda e, bias=bias: e.tensor_scalar(
                scr[2][:], scr[0][:], float(bias), None, op0=ALU.add),
                reads=["rscr0"], writes=["rscr2"])
            P.op("vector", lambda e: e.scalar_tensor_tensor(
                scr[2][:], scr[1][:], -float(TWO_PI_HI), scr[2][:],
                op0=ALU.mult, op1=ALU.add), reads=["rscr1", "rscr2"], writes=["rscr2"])
            P.op("vector", lambda e: e.scalar_tensor_tensor(
                scr[2][:], scr[1][:], -float(TWO_PI_LO), scr[2][:],
                op0=ALU.mult, op1=ALU.add), reads=["rscr1", "rscr2"], writes=["rscr2"])
            P.op("vector", lambda e: e.tensor_scalar(
                scr[2][:], scr[2][:], 3.1415925, -3.1415925,
                op0=ALU.min, op1=ALU.max), reads=["rscr2"], writes=["rscr2"])
            P.op("scalar", lambda e, tab=tab, tc=tc: e.activation(
                tab[:, tc * 32:(tc + 1) * 32], scr[2][:], AF.Sin),
                reads=["rscr2"], writes=["ropetab"])
    return ctab, stab


def emit_rope_full_tables(P, ctab, stab, c_full, s_full, tc):
    cs = ctab[:, tc * 32:(tc + 1) * 32].rearrange("p (o f) -> p o f", o=1)
    ss = stab[:, tc * 32:(tc + 1) * 32].rearrange("p (o f) -> p o f", o=1)
    P.op("vector", lambda e: e.tensor_copy(
        c_full[:].rearrange("p (g f) -> p g f", f=32), cs.to_broadcast([128, 32, 32])),
        reads=["ropetab"], writes=["cfull"])
    sv = s_full[:].rearrange("p (h x f) -> p (h x) f", x=2, f=32)
    P.op("vector", lambda e: e.tensor_scalar(
        sv[:, 0::2, :], ss.to_broadcast([128, 16, 32]), -1.0, None, op0=ALU.mult),
        reads=["ropetab"], writes=["sfull"])
    P.op("vector", lambda e: e.tensor_scalar(
        sv[:, 1::2, :], ss.to_broadcast([128, 16, 32]), 1.0, None, op0=ALU.mult),
        reads=["ropetab"], writes=["sfull"])


def emit_rope_feat(P, c_full, s_full, s_w, kw, s_sw, ksw, out_tile, out_key):
    wv = s_w[:].rearrange("p (h x f) -> p h x f", x=2, f=32)
    swv = s_sw[:].rearrange("p (h x f) -> p h x f", x=2, f=32)
    P.op("vector", lambda e: e.tensor_copy(swv[:, :, 0, :], wv[:, :, 1, :]),
         reads=[kw], writes=[ksw])
    P.op("vector", lambda e: e.tensor_copy(swv[:, :, 1, :], wv[:, :, 0, :]),
         reads=[kw], writes=[ksw])
    P.op("gpsimd", lambda e: e.tensor_tensor(s_sw[:], s_sw[:], s_full[:], op=ALU.mult),
         reads=[ksw, "sfull"], writes=[ksw])
    P.op("vector", lambda e: e.tensor_tensor(s_w[:], s_w[:], c_full[:], op=ALU.mult),
         reads=[kw, "cfull"], writes=[kw])
    P.op("vector", lambda e: e.tensor_tensor(s_w[:], s_w[:], s_sw[:], op=ALU.add),
         reads=[kw, ksw], writes=[kw])
    P.op("vector", lambda e: e.tensor_scalar(s_sw[:], s_w[:], 0.0, None, op0=ALU.min),
         reads=[kw], writes=[ksw])
    P.op("scalar", lambda e: e.activation(s_sw[:], s_sw[:], AF.Exp),
         reads=[ksw], writes=[ksw])
    P.op("vector", lambda e: e.scalar_tensor_tensor(
        out_tile[:], s_w[:], 0.0, s_sw[:], op0=ALU.max, op1=ALU.add),
        reads=[kw, ksw], writes=[out_key])


ATTN0_CUT = "full"


def build_attn0(nc, stack, P=None, mem=None, io=None):
    fused = P is not None
    if not fused:
        P = Prog(nc)
    xn = io["xn"] if fused else nc.dram_tensor("xn", [B_PER_CORE, T, D], BF16, kind="ExternalInput").ap()
    wqkvT = io["wqkvT"] if fused else nc.dram_tensor("wqkvT", [D, 3 * D], F32, kind="ExternalInput").ap()
    woT = io["woT"] if fused else nc.dram_tensor("woT", [D, D], F32, kind="ExternalInput").ap()
    s0 = io["s0"] if fused else nc.dram_tensor("s0", [B_PER_CORE, T, D], BF16, kind="ExternalOutput").ap()

    if not fused:
        sems, dma_sems = make_sems(nc, stack)
        mem = Mem(nc, stack)
    en = stack.enter_context
    s_wqkv = mem.sb("s_wqkv", [128, ND * 3 * D], BF16)
    s_wo = mem.sb("s_wo", [128, ND * D], BF16)
    s_iden = mem.sb("s_iden", [128, 128], BF16)
    s_scr128 = mem.sb("s_scr128", [128, 129], F32)
    c_full = mem.sb("s_cfull", [128, D], F32)
    s_full = mem.sb("s_sfull", [128, D], F32)
    s_xn = [mem.sb(f"s_xn{i}", [128, D], BF16) for i in range(2)]
    s_xnT = [mem.sb(f"s_xnT{i}", [128, D], BF16) for i in range(2)]
    s_q = mem.sb("s_q", [128, D], F32)
    s_qsw = mem.sb("s_qsw", [128, D], F32)
    s_k = mem.sb("s_k", [128, D], F32)
    s_ksw = mem.sb("s_ksw", [128, D], F32)
    s_v = mem.sb("s_v", [128, D], BF16)
    s_kf = mem.sb("s_kf", [128, D], BF16)
    s_qf = [[mem.sb(f"s_qf{b}_{i}", [128, D], BF16) for i in range(NT)]
            for b in range(B_PER_CORE)]
    s_kvks = [mem.sb(f"s_kvks{b}", [128, 8 * 65], BF16)
              for b in range(B_PER_CORE)]
    s_kvacc = [mem.sb(f"s_kvacc{b}", [128, 8 * 65], F32)
               for b in range(B_PER_CORE)]
    s_ones = mem.sb("s_ones1", [128, 1], BF16)
    s_qfT = mem.sb("s_qfT", [128, D], BF16)
    s_rec = mem.sb("s_rec", [128, 16], F32)
    s_at = mem.sb("s_at", [128, D], BF16)
    s_atT = mem.sb("s_atT", [128, D], BF16)
    s_s0 = [mem.sb(f"s_s0{i}", [128, D], BF16) for i in range(2)]

    pb = [mem.ps([128, 512], F32) for i in range(6)]
    pt = mem.ps([128, 1024], BF16)
    block = en(nc.Block()) if not fused else None

    for c in range(ND):
        P.dma("gpsimd", s_wqkv[:, c * 3 * D:(c + 1) * 3 * D],
              wqkvT[c * 128:(c + 1) * 128, :], writes=[("wqkv", c)])
        P.dma("gpsimd", s_wo[:, c * D:(c + 1) * D],
              woT[c * 128:(c + 1) * 128, :], writes=[("wo", c)])
    emit_identity(P, s_iden, s_scr128)
    ctab, stab = emit_rope_tables(P, nc, mem)
    P.op("vector", lambda e: e.memset(s_ones[:], 1.0), writes=["ones1"])

    KVB = {0: (pb[2], pb[3]), 1: (pb[4], pb[5])}

    if ATTN0_CUT in ("wload", "tables"):
        P.dma("sync", s0[0, 0:128, :], s_wqkv[:, 0:1024], reads=[("wqkv", 0)],
              is_output=True)
        if ATTN0_CUT == "tables":
            P.dma("gpsimd", s0[1, 0:128, 0:512], ctab[:], reads=["ropetab"], is_output=True)
            P.dma("gpsimd", s0[1, 0:128, 512:1024], stab[:], reads=["ropetab"], is_output=True)
            dbg_if, dbg_tc = emit_rope_tables.debug
            P.dma("gpsimd", s0[1, 128:256, 0:32], dbg_if[:], reads=["ropeI"], is_output=True)
            P.dma("gpsimd", s0[1, 128:256, 32:48], dbg_tc[:], reads=["ropeT"], is_output=True)
        P.finalize(block, sems, dma_sems)
        return P

    for tc in range(NT if ATTN0_CUT != "passa1" else 1):
        emit_rope_full_tables(P, ctab, stab, c_full, s_full, tc)
        for b in range(B_PER_CORE):
            i = b % 2
            kxn = ("sxn", i)
            kxnT = ("sxnT", i)
            P.dma("sync", s_xn[i][:], xn[b, tc * 128:(tc + 1) * 128, :],
                  writes=[kxn])
            for c in range(ND):
                P.op("tensor", lambda e, c=c, i=i: e.transpose(
                    pt[:, c * 128:(c + 1) * 128],
                    s_xn[i][:, c * 128:(c + 1) * 128], s_iden[:]),
                    reads=[kxn, "iden"], writes=[("pt", c)])
            P.op("vector", lambda e, i=i: e.tensor_copy(s_xnT[i][:], pt[:]),
                 reads=[("pt", c) for c in range(ND)], writes=[kxnT])
            for ng in range(6):
                bank = pb[ng % 2]
                kbank = ("pb", ng % 2)
                for dc in range(ND):
                    P.op("tensor", lambda e, ng=ng, dc=dc, i=i, bank=bank: e.matmul(
                        bank[:], s_xnT[i][:, dc * 128:(dc + 1) * 128],
                        s_wqkv[:, dc * 3 * D + ng * 512: dc * 3 * D + (ng + 1) * 512],
                        start=(dc == 0), stop=(dc == ND - 1)),
                        reads=[kxnT, ("wqkv", dc)], writes=[kbank],
                        relax=[kbank] if dc > 0 else [])
                if ng < 2:
                    half = ng
                    P.op("scalar", lambda e, bank=bank, half=half: e.activation(
                        s_q[:, half * 512:(half + 1) * 512], bank[:], AF.Copy,
                        scale=0.125), reads=[kbank], writes=["s_q"])
                elif ng < 4:
                    half = ng - 2
                    P.op("scalar", lambda e, bank=bank, half=half: e.activation(
                        s_k[:, half * 512:(half + 1) * 512], bank[:], AF.Copy),
                        reads=[kbank], writes=["s_k"])
                else:
                    half = ng - 4
                    P.op("vector", lambda e, bank=bank, half=half: e.tensor_copy(
                        s_v[:, half * 512:(half + 1) * 512], bank[:]),
                        reads=[kbank], writes=["s_v"])
            emit_rope_feat(P, c_full, s_full, s_q, "s_q", s_qsw, "s_qsw",
                           s_qf[b][tc], ("qf", b, tc))
            emit_rope_feat(P, c_full, s_full, s_k, "s_k", s_ksw, "s_ksw",
                           s_kf, "s_kf")
            kvb = KVB[b]
            for h in range(HEADS):
                bank = kvb[h // 8]
                kkv = ("pb", 2 + 2 * b + h // 8)
                po = (h % 2) * 64
                fo = ((h // 2) % 4) * 65
                P.op("tensor", lambda e, bank=bank, po=po, fo=fo, h=h: e.matmul(
                    bank[po:po + 64, fo:fo + 64],
                    s_kf[:, h * 64:(h + 1) * 64], s_v[:, h * 64:(h + 1) * 64],
                    start=True, stop=False),
                    reads=["s_kf", "s_v"], writes=[kkv])
                P.op("tensor", lambda e, bank=bank, po=po, fo=fo, h=h: e.matmul(
                    bank[po:po + 64, fo + 64:fo + 65],
                    s_kf[:, h * 64:(h + 1) * 64], s_ones[:],
                    start=False, stop=True),
                    reads=["s_kf", "ones1"], writes=[kkv], relax=[kkv])
            for half in range(2):
                kkv = ("pb", 2 + 2 * b + half)
                if tc == 0:
                    P.op("vector", lambda e, b=b, half=half, kvb=kvb: e.tensor_copy(
                        s_kvacc[b][:, half * 260:(half + 1) * 260],
                        kvb[half][:, 0:260]),
                        reads=[kkv], writes=[("kvacc", b)])
                else:
                    P.op("vector", lambda e, b=b, half=half, kvb=kvb: e.tensor_tensor(
                        s_kvacc[b][:, half * 260:(half + 1) * 260],
                        s_kvacc[b][:, half * 260:(half + 1) * 260],
                        kvb[half][:, 0:260], op=ALU.add),
                        reads=[kkv, ("kvacc", b)], writes=[("kvacc", b)])

    for b in range(B_PER_CORE):
        P.op("vector", lambda e, b=b: e.tensor_copy(
            s_kvks[b][:], s_kvacc[b][:]),
            reads=[("kvacc", b)], writes=[("kvks", b)])

    if ATTN0_CUT in ("passa", "passa1"):
        P.dma("sync", s0[0, 0:128, 0:520], s_kvks[0][:, 0:520], reads=[("kvks", 0)],
              is_output=True)
        P.dma("sync", s0[1, 0:128, :], s_qf[0][0][:], reads=[("qf", 0, 0)],
              is_output=True)
        P.finalize(block, sems, dma_sems)
        return P

    for b in range(B_PER_CORE):
        for tc in range(NT):
            kqfT = "qfT"
            for c in range(ND):
                P.op("tensor", lambda e, c=c, b=b, tc=tc: e.transpose(
                    pt[:, c * 128:(c + 1) * 128],
                    s_qf[b][tc][:, c * 128:(c + 1) * 128], s_iden[:]),
                    reads=[("qf", b, tc), "iden"], writes=[("pt", c)])
            P.op("vector", lambda e: e.tensor_copy(s_qfT[:], pt[:]),
                 reads=[("pt", c) for c in range(ND)], writes=[kqfT])
            if ATTN0_CUT == "passb00":
                P.dma("sync", s0[b, tc * 128:(tc + 1) * 128, :], s_qfT[:],
                      reads=["qfT"], is_output=True)
                continue
            for h in range(HEADS):
                bank = pb[h // 4]
                kbank = ("pb", h // 4)
                fo = (h % 4) * 65
                po = (h % 2) * 64
                co = (h // 2) * 128
                kvfo = ((h // 2) % 4) * 65
                P.op("tensor", lambda e, bank=bank, fo=fo, po=po, co=co,
                     kvfo=kvfo, b=b, h=h: e.matmul(
                         bank[:, fo:fo + 65],
                         s_qfT[po:po + 64, co:co + 128],
                         s_kvks[b][po:po + 64, (h // 8) * 260 + kvfo:(h // 8) * 260 + kvfo + 65],
                         start=(h % 4 == 0), stop=(h % 4 == 3)),
                     reads=[kqfT, ("kvks", b)], writes=[kbank])
            if ATTN0_CUT == "passb0":
                for g in range(4):
                    P.op("vector", lambda e, g=g: e.tensor_copy(
                        s_at[:, g * 256:g * 256 + 256], pb[g][:, 0:256]),
                        reads=[("pb", g)], writes=["sat"])
                P.dma("sync", s0[b, tc * 128:(tc + 1) * 128, :], s_at[:],
                      reads=["sat"], is_output=True)
                continue
            for g in range(4):
                bank = pb[g]
                kbank = ("pb", g)
                bv = bank[:, 0:260].rearrange("p (h x) -> p h x", x=65)
                P.op("vector", lambda e, bv=bv, g=g: e.tensor_scalar(
                    s_rec[:, g * 4:(g + 1) * 4], bv[:, 0:4, 64:65], 1e-6, None,
                    op0=ALU.max), reads=[kbank], writes=["rec"])
            P.op("vector", lambda e: e.reciprocal(s_rec[:], s_rec[:]),
                 reads=["rec"], writes=["rec"])
            for g in range(4):
                bank = pb[g]
                kbank = ("pb", g)
                bv = bank[:, 0:260].rearrange("p (h x) -> p h x", x=65)
                rv = s_rec[:, g * 4:(g + 1) * 4].rearrange(
                    "p (h o) -> p h o", o=1).to_broadcast([128, 4, 64])
                ov = s_at[:, g * 256:(g + 1) * 256].rearrange(
                    "p (h x) -> p h x", x=64)
                P.op("vector", lambda e, bv=bv, rv=rv, ov=ov: e.tensor_tensor(
                    ov, bv[:, 0:4, 0:64], rv, op=ALU.mult),
                    reads=[kbank, "rec"], writes=["sat"])
            if ATTN0_CUT == "passb1":
                P.dma("sync", s0[b, tc * 128:(tc + 1) * 128, :], s_at[:],
                      reads=["sat"], is_output=True)
                continue
            for c in range(ND):
                P.op("tensor", lambda e, c=c: e.transpose(
                    pt[:, c * 128:(c + 1) * 128],
                    s_at[:, c * 128:(c + 1) * 128], s_iden[:]),
                    reads=["sat", "iden"], writes=[("pt", c)])
            P.op("vector", lambda e: e.tensor_copy(s_atT[:], pt[:]),
                 reads=[("pt", c) for c in range(ND)], writes=["atT"])
            for ng in range(2):
                bank = pb[4 + ng]
                kbank = ("pb", 4 + ng)
                for dc in range(ND):
                    P.op("tensor", lambda e, ng=ng, dc=dc, bank=bank: e.matmul(
                        bank[:], s_atT[:, dc * 128:(dc + 1) * 128],
                        s_wo[:, dc * D + ng * 512: dc * D + (ng + 1) * 512],
                        start=(dc == 0), stop=(dc == ND - 1)),
                        reads=["atT", ("wo", dc)], writes=[kbank],
                        relax=[kbank] if dc > 0 else [])
                P.op("vector", lambda e, ng=ng, bank=bank, b=b, tc=tc: e.tensor_copy(
                    s_s0[tc % 2][:, ng * 512:(ng + 1) * 512], bank[:]),
                    reads=[kbank], writes=[("ss0", tc % 2)])
            P.dma("sync", s0[b, tc * 128:(tc + 1) * 128, :], s_s0[tc % 2][:],
                  reads=[("ss0", tc % 2)], is_output=True, tag=f"st_s0_{b}_{tc}")

    if not fused:
        P.finalize(block, sems, dma_sems)
    return P


def build_moe0(nc, stack, P=None, mem=None, io=None):
    fused = P is not None
    if not fused:
        P = Prog(nc)
    xn = io["xn"] if fused else nc.dram_tensor("xn", [B_PER_CORE, T, D], BF16, kind="ExternalInput").ap()
    gateT = io["gateT"] if fused else nc.dram_tensor("gateT", [D, E_EXP], F32, kind="ExternalInput").ap()
    WgT = io["WgT"] if fused else nc.dram_tensor("WgT", [E_EXP, D, F_EXP], F32, kind="ExternalInput").ap()
    WuT = io["WuT"] if fused else nc.dram_tensor("WuT", [E_EXP, D, F_EXP], F32, kind="ExternalInput").ap()
    WdT = io["WdT"] if fused else nc.dram_tensor("WdT", [E_EXP, F_EXP, D], F32, kind="ExternalInput").ap()
    WsgT = io["WsgT"] if fused else nc.dram_tensor("WsgT", [D, F_SH], F32, kind="ExternalInput").ap()
    WsuT = io["WsuT"] if fused else nc.dram_tensor("WsuT", [D, F_SH], F32, kind="ExternalInput").ap()
    WsdT = io["WsdT"] if fused else nc.dram_tensor("WsdT", [F_SH, D], F32, kind="ExternalInput").ap()
    moe = io["moe"] if fused else nc.dram_tensor("moe", [B_PER_CORE, T, D], F32, kind="ExternalOutput").ap()

    if not fused:
        sems, dma_sems = make_sems(nc, stack)
        mem = Mem(nc, stack)
    en = stack.enter_context
    s_idxu = [mem.sb(f"m_idxu{b}", [128, 32], U32)
              for b in range(B_PER_CORE)]
    s_wcol = [mem.sb(f"m_wcol{b}", [128, 32], F32)
              for b in range(B_PER_CORE)]
    s_iden = mem.sb("m_iden", [128, 128], BF16)
    s_idenf = mem.sb("m_idenf", [128, 128], F32)
    s_scr128 = mem.sb("m_scr128", [128, 129], F32)
    s_iotaJ = mem.sb("m_iotaJ", [128, 256], BF16)
    s_slot = [mem.sb(f"m_slot{b}", [128, NT * E_EXP], F32)
              for b in range(B_PER_CORE)]
    s_rhs4 = [mem.sb(f"m_rhs4{b}", [128, NT * 64], BF16)
              for b in range(B_PER_CORE)]
    s_col = [mem.sb(f"m_col{b}", [128, 8], F32)
             for b in range(B_PER_CORE)]
    s_ixw1 = mem.sb("m_ixw1", [4, 256], F32)
    mark = mem.off
    s_w1 = mem.sb("m_w1", [128, ND * F_EXP], BF16)
    s_w2 = mem.sb("m_w2", [128, ND * F_EXP], BF16)
    s_w3 = mem.sb("m_w3", [128, 16 * D], BF16)
    s_hs = mem.sb("m_hs", [128, 16 * 512], BF16)
    s_xn = [mem.sb(f"m_xn{i}", [128, D], BF16) for i in range(2)]
    s_xnTt = mem.sb("m_xnTt", [128, D], BF16)
    s_xnTg = mem.sb("m_xnTg", [128, ND * 512], BF16)
    s_gate = mem.sb("m_gate", [128, ND * E_EXP], BF16)
    s_tri = mem.sb("m_tri", [128, 128], BF16)
    s_one128 = mem.sb("m_one128", [128, 128], BF16)
    s_scores = mem.sb("m_scores", [128, NT * E_EXP], F32)
    s_scoT = [mem.sb(f"m_scoT{b}", [16, T], F32) for b in range(B_PER_CORE)]
    s_junk16 = mem.sb("m_junk16", [16, T], BF16)
    s_bis = [mem.sb(f"m_bis{b}", [16, 8], F32) for b in range(B_PER_CORE)]
    s_maskT = mem.sb("m_maskT", [16, T], BF16)
    s_maskt = mem.sb("m_maskt", [128, NT * E_EXP], BF16)
    s_shout = [mem.sb(f"m_shout{i}", [128, D], F32) for i in range(2)]
    s_hg = mem.sb("m_hg", [128, 512], BF16)

    pb = [mem.ps([128, 512], F32) for i in range(6)]
    pt = mem.ps([128, 1024], BF16)
    block = en(nc.Block()) if not fused else None

    emit_identity(P, s_iden, s_scr128)
    P.op("vector", lambda e: e.tensor_copy(s_idenf[:], s_iden[:]),
         reads=["iden"], writes=["idenf"])
    P.op("gpsimd", lambda e: e.iota(s_iotaJ[:], [[1, 256]], base=1,
                                    channel_multiplier=0,
                                    allow_small_or_imprecise_dtypes=True),
         writes=["iotaJ"])
    P.op("gpsimd", lambda e: e.iota(s_scr128[:, 0:1], [[0, 1]], base=0,
                                    channel_multiplier=1,
                                    allow_small_or_imprecise_dtypes=True),
         writes=["iden_scr"])
    P.op("gpsimd", lambda e: e.iota(s_scr128[:, 1:129], [[1, 128]], base=0,
                                    channel_multiplier=0,
                                    allow_small_or_imprecise_dtypes=True),
         writes=["iden_scr"])
    P.op("vector", lambda e: e.tensor_scalar(
        s_tri[:], s_scr128[:, 1:129], s_scr128[:, 0:1], None, op0=ALU.is_ge),
        reads=["iden_scr"], writes=["tri"])
    P.op("vector", lambda e: e.memset(s_one128[:], 1.0), writes=["ones128"])
    for c in range(ND):
        P.dma("gpsimd", s_gate[:, c * E_EXP:(c + 1) * E_EXP],
              gateT[c * 128:(c + 1) * 128, :], writes=["gate"])
    P.dma("gpsimd", s_w3[:].rearrange("p (c d) -> p c d", d=D),
          WsdT.rearrange("(c p) d -> p c d", p=128), writes=["w3"])
    P.dma("gpsimd", s_w1[:].rearrange("p (c f) -> p c f", f=F_SH),
          WsgT.rearrange("(c p) f -> p c f", p=128), writes=["w1"])
    P.dma("gpsimd", s_w2[:].rearrange("p (c f) -> p c f", f=F_SH),
          WsuT.rearrange("(c p) f -> p c f", p=128), writes=["w2"])

    def transpose_tile(src_tile, src_key, dst_view, dst_key):
        for c in range(ND):
            P.op("tensor", lambda e, c=c: e.transpose(
                pt[:, c * 128:(c + 1) * 128],
                src_tile[:, c * 128:(c + 1) * 128], s_iden[:]),
                reads=[src_key, "iden"], writes=[("pt", c)])
        if dst_view.ndim == 2:
            dst_view = dst_view.rearrange("p (c t) -> p c t", t=128)
        P.op("vector", lambda e: e.tensor_copy(
            dst_view, pt[:].rearrange("p (c t) -> p c t", t=128)),
            reads=[("pt", c) for c in range(ND)], writes=[dst_key])

    for b in range(B_PER_CORE):
        for tc in range(NT):
            i = tc % 2
            kxn = ("mxn", i)
            P.dma("sync", s_xn[i][:], xn[b, tc * 128:(tc + 1) * 128, :],
                  writes=[kxn])
            transpose_tile(s_xn[i], kxn, s_xnTt[:], "xnTt")
            for dc in range(ND):
                P.op("tensor", lambda e, dc=dc: e.matmul(
                    pb[0][:, 0:E_EXP], s_xnTt[:, dc * 128:(dc + 1) * 128],
                    s_gate[:, dc * E_EXP:(dc + 1) * E_EXP],
                    start=(dc == 0), stop=(dc == ND - 1)),
                    reads=["xnTt", "gate"], writes=[("pb", 0)],
                    relax=[("pb", 0)] if dc > 0 else [])
            P.op("vector", lambda e: e.tensor_reduce(
                s_scr128[:, 0:1], pb[0][:, 0:E_EXP], AX.X, ALU.max, negate=True),
                reads=[("pb", 0)], writes=["sm_scr"])
            P.op("scalar", lambda e, tc=tc: e.activation(
                s_scores[:, tc * E_EXP:(tc + 1) * E_EXP], pb[0][:, 0:E_EXP],
                AF.Exp, bias=s_scr128[:, 0:1], accum_out=s_scr128[:, 1:2]),
                reads=[("pb", 0), "sm_scr"], writes=["scores", "sm_scr"])
            P.op("vector", lambda e: e.reciprocal(s_scr128[:, 2:3], s_scr128[:, 1:2]),
                 reads=["sm_scr"], writes=["sm_scr"])
            P.op("vector", lambda e, tc=tc: e.tensor_scalar(
                s_scores[:, tc * E_EXP:(tc + 1) * E_EXP],
                s_scores[:, tc * E_EXP:(tc + 1) * E_EXP],
                s_scr128[:, 2:3], None, op0=ALU.mult),
                reads=["scores", "sm_scr"], writes=["scores"])
            P.op("tensor", lambda e, tc=tc: e.transpose(
                pb[1][0:E_EXP, 0:128],
                s_scores[:, tc * E_EXP:(tc + 1) * E_EXP], s_idenf[:]),
                reads=["scores", "idenf"], writes=[("pb", 1)])
            P.op("vector", lambda e, b=b, tc=tc: e.tensor_copy(
                s_scoT[b][:, tc * 128:(tc + 1) * 128], pb[1][0:E_EXP, 0:128]),
                reads=[("pb", 1)], writes=[("scoT", b)])

        kr = ("rhs4", b)
        r4 = s_rhs4[b][:].rearrange("p (x f) -> p x f", f=4)
        P.op("gpsimd", lambda e, r4=r4: e.iota(
            r4[:, :, 0:1], [[0, 256]], base=0, channel_multiplier=1,
            allow_small_or_imprecise_dtypes=True), writes=[kr])
        P.op("vector", lambda e, r4=r4: e.memset(r4[:, :, 1:2], 1.0), writes=[kr])
        P.op("gpsimd", lambda e, r4=r4: e.iota(
            r4[:, :, 3:4], [[1, NT], [0, E_EXP]], base=0, channel_multiplier=0,
            allow_small_or_imprecise_dtypes=True), writes=[kr])
        for tc in range(NT):
            P.op("vector", lambda e, b=b, tc=tc: e.tensor_copy(
                s_rhs4[b][:, tc * 64:(tc + 1) * 64].rearrange(
                    "p (x f) -> p x f", f=4)[:, :, 2:3],
                s_scores[:, tc * E_EXP:(tc + 1) * E_EXP].rearrange(
                    "p (x f) -> p x f", f=1)),
                reads=["scores"], writes=[kr])

        kb = ("bis", b)
        sb = s_bis[b]
        P.op("vector", lambda e, sb=sb: e.memset(sb[:, 0:1], 0.0), writes=[kb])
        P.op("vector", lambda e, sb=sb: e.memset(sb[:, 1:2], 1.0), writes=[kb])
        P.op("vector", lambda e, sb=sb: e.memset(sb[:, 2:3], 0.5), writes=[kb])
        for it in range(30):
            P.op("vector", lambda e, b=b, sb=sb: e.tensor_scalar(
                s_junk16[:], s_scoT[b][:], sb[:, 2:3], 0.0,
                op0=ALU.is_gt, op1=ALU.add, accum_out=sb[:, 3:4]),
                reads=[("scoT", b), kb], writes=["junk16", kb])
            P.op("vector", lambda e, sb=sb: e.tensor_scalar(
                sb[:, 4:5], sb[:, 3:4], float(CAP), None, op0=ALU.is_gt),
                reads=[kb], writes=[kb])
            P.op("vector", lambda e, sb=sb: e.tensor_tensor(
                sb[:, 5:6], sb[:, 2:3], sb[:, 0:1], op=ALU.subtract),
                reads=[kb], writes=[kb])
            P.op("vector", lambda e, sb=sb: e.tensor_tensor(
                sb[:, 5:6], sb[:, 5:6], sb[:, 4:5], op=ALU.mult),
                reads=[kb], writes=[kb])
            P.op("vector", lambda e, sb=sb: e.tensor_tensor(
                sb[:, 0:1], sb[:, 0:1], sb[:, 5:6], op=ALU.add),
                reads=[kb], writes=[kb])
            P.op("vector", lambda e, sb=sb: e.tensor_tensor(
                sb[:, 6:7], sb[:, 2:3], sb[:, 1:2], op=ALU.subtract),
                reads=[kb], writes=[kb])
            P.op("vector", lambda e, sb=sb: e.tensor_scalar(
                sb[:, 7:8], sb[:, 4:5], -1.0, 1.0, op0=ALU.mult, op1=ALU.add),
                reads=[kb], writes=[kb])
            P.op("vector", lambda e, sb=sb: e.tensor_tensor(
                sb[:, 6:7], sb[:, 6:7], sb[:, 7:8], op=ALU.mult),
                reads=[kb], writes=[kb])
            P.op("vector", lambda e, sb=sb: e.tensor_tensor(
                sb[:, 1:2], sb[:, 1:2], sb[:, 6:7], op=ALU.add),
                reads=[kb], writes=[kb])
            P.op("vector", lambda e, sb=sb: e.tensor_scalar(
                sb[:, 2:3], sb[:, 0:1], 0.5, None, op0=ALU.mult),
                reads=[kb], writes=[kb])
            P.op("vector", lambda e, sb=sb: e.scalar_tensor_tensor(
                sb[:, 2:3], sb[:, 1:2], 0.5, sb[:, 2:3],
                op0=ALU.mult, op1=ALU.add), reads=[kb], writes=[kb])

    for b in range(B_PER_CORE):
        for tg in range(4):
            tbase = tg * 512
            for tq in range(4):
                i = tq % 2
                kxn = ("mxn", i)
                tci = tg * 4 + tq
                P.dma("sync", s_xn[i][:], xn[b, tci * 128:(tci + 1) * 128, :],
                      writes=[kxn])
                transpose_tile(
                    s_xn[i], kxn,
                    s_xnTg[:].rearrange("p (c t) -> p c t", t=512)[
                        :, :, tq * 128:(tq + 1) * 128],
                    "xnTg")
            for fc in range(16):
                for dc in range(ND):
                    P.op("tensor", lambda e, dc=dc, fc=fc: e.matmul(
                        pb[0][:], s_w1[:, dc * F_SH + fc * 128: dc * F_SH + (fc + 1) * 128],
                        s_xnTg[:, dc * 512:(dc + 1) * 512],
                        start=(dc == 0), stop=(dc == ND - 1)),
                        reads=["w1", "xnTg"], writes=[("pb", 0)],
                        relax=[("pb", 0)] if dc > 0 else [])
                    P.op("tensor", lambda e, dc=dc, fc=fc: e.matmul(
                        pb[1][:], s_w2[:, dc * F_SH + fc * 128: dc * F_SH + (fc + 1) * 128],
                        s_xnTg[:, dc * 512:(dc + 1) * 512],
                        start=(dc == 0), stop=(dc == ND - 1)),
                        reads=["w2", "xnTg"], writes=[("pb", 1)],
                        relax=[("pb", 1)] if dc > 0 else [])
                P.op("scalar", lambda e: e.activation(
                    s_hg[:], pb[0][:], AF.Gelu),
                    reads=[("pb", 0)], writes=["hg"])
                P.op("vector", lambda e, fc=fc: e.tensor_tensor(
                    s_hs[:, fc * 512:(fc + 1) * 512], s_hg[:], pb[1][:],
                    op=ALU.mult),
                    reads=["hg", ("pb", 1)], writes=["hs"])
            for tq in range(4):
                for half in range(2):
                    kpd = ("pb", 2 + half)
                    for fc in range(16):
                        P.op("tensor", lambda e, fc=fc, tq=tq, half=half: e.matmul(
                            pb[2 + half][:],
                            s_hs[:, fc * 512 + tq * 128: fc * 512 + (tq + 1) * 128],
                            s_w3[:, fc * D + half * 512: fc * D + half * 512 + 512],
                            start=(fc == 0), stop=(fc == 15)),
                            reads=["hs", "w3"], writes=[kpd],
                            relax=[kpd] if fc > 0 else [])
                i = tq % 2
                P.op("vector", lambda e, i=i: e.tensor_copy(
                    s_shout[i][:, 0:512], pb[2][:]),
                    reads=[("pb", 2)], writes=[("shout", i)])
                P.op("vector", lambda e, i=i: e.tensor_copy(
                    s_shout[i][:, 512:1024], pb[3][:]),
                    reads=[("pb", 3)], writes=[("shout", i)])
                P.dma("sync", moe[b, tbase + tq * 128: tbase + (tq + 1) * 128, :],
                      s_shout[i][:], reads=[("shout", i)],
                      writes=[("moe", b)], is_output=True)

    for b in range(B_PER_CORE):
        P.op("vector", lambda e, b=b: e.tensor_scalar(
            s_maskT[:], s_scoT[b][:], s_bis[b][:, 2:3], None, op0=ALU.is_gt),
            reads=[("scoT", b), ("bis", b)], writes=["maskT"])
        for tc in range(NT):
            P.op("tensor", lambda e, tc=tc: e.transpose(
                pt[:, 0:E_EXP], s_maskT[:, tc * 128:(tc + 1) * 128],
                s_iden[0:16, 0:16]),
                reads=["maskT", "iden"], writes=[("pt", 0)])
            P.op("vector", lambda e, tc=tc: e.tensor_copy(
                s_maskt[:, tc * E_EXP:(tc + 1) * E_EXP], pt[:, 0:E_EXP]),
                reads=[("pt", 0)], writes=["maskt"])
        for tc in range(NT):
            kps = ("pslot", tc)
            P.op("tensor", lambda e, tc=tc: e.matmul(
                pb[2][:, tc * E_EXP:(tc + 1) * E_EXP], s_tri[:],
                s_maskt[:, tc * E_EXP:(tc + 1) * E_EXP],
                start=True, stop=(tc == 0)),
                reads=["tri", "maskt"], writes=[kps, ("pb", 2)])
            for tp in range(tc):
                P.op("tensor", lambda e, tc=tc, tp=tp: e.matmul(
                    pb[2][:, tc * E_EXP:(tc + 1) * E_EXP], s_one128[:],
                    s_maskt[:, tp * E_EXP:(tp + 1) * E_EXP],
                    start=False, stop=(tp == tc - 1)),
                    reads=["ones128", "maskt"], writes=[kps, ("pb", 2)],
                    relax=[kps, ("pb", 2)])
        P.op("vector", lambda e, b=b: e.tensor_tensor(
            s_slot[b][:], pb[2][:, 0:256], s_maskt[:], op=ALU.mult),
            reads=["maskt"] + [("pslot", tc) for tc in range(NT)],
            writes=[("slot", b)])

    P.fence()
    mem.off = mark
    s2_w1 = [mem.sb(f"m2_w1{i}", [128, ND * F_EXP], BF16) for i in range(2)]
    s2_w2 = [mem.sb(f"m2_w2{i}", [128, ND * F_EXP], BF16) for i in range(2)]
    s2_w3 = mem.sb("m2_w3", [128, 16 * D], BF16)
    s2_hs = [mem.sb(f"m2_hs{i}", [128, 16 * 256], BF16) for i in range(2)]
    s2_xsel = [mem.sb(f"m2_xsel{i}", [128, D], BF16) for i in range(2)]
    s2_xselT = [mem.sb(f"m2_xselT{i}", [128, ND * 256], BF16) for i in range(2)]
    s2_hg = mem.sb("m2_hg", [128, 256], BF16)
    s2_P = [mem.sb(f"m2_P{i}", [128, 256], BF16) for i in range(2)]
    s2_eo = [mem.sb(f"m2_eo{i}", [128, D], BF16) for i in range(4)]

    bc_reg = nc.gpsimd.to_reg(B_PER_CORE * T - 1)
    wg3 = WgT.rearrange("e (c p) f -> e p c f", p=128)
    wu3 = WuT.rearrange("e (c p) f -> e p c f", p=128)
    wd3 = WdT.rearrange("e (c p) d -> e p c d", p=128)

    def load_w12(ee):
        sel = ee % 2
        P.dma("gpsimd", s2_w1[sel][:].rearrange("p (c f) -> p c f", f=F_EXP),
              wg3[ee], writes=[("w1", sel)], tag=f"wg{ee}")
        P.dma("gpsimd", s2_w2[sel][:].rearrange("p (c f) -> p c f", f=F_EXP),
              wu3[ee], writes=[("w2", sel)], tag=f"wu{ee}")

    def load_w3(ee):
        P.dma("gpsimd", s2_w3[:].rearrange("p (c d) -> p c d", d=D),
              wd3[ee], writes=["w3"], tag=f"wd{ee}")

    def machinery(ee, b):
        kx = "pixw"
        for tc in range(NT):
            ip = tc % 2
            kP = ("Ptile", ip)
            P.op("vector", lambda e, b=b, ee=ee, tc=tc, ip=ip: e.tensor_scalar(
                s2_P[ip][:], s_iotaJ[:],
                s_slot[b][:, tc * E_EXP + ee: tc * E_EXP + ee + 1],
                None, op0=ALU.is_equal),
                reads=["iotaJ", ("slot", b)], writes=[kP])
            P.op("tensor", lambda e, b=b, ee=ee, tc=tc, ip=ip: e.matmul(
                pb[2][0:4, 0:256],
                s_rhs4[b][:, tc * 64 + 4 * ee: tc * 64 + 4 * ee + 4],
                s2_P[ip][:], start=(tc == 0), stop=(tc == NT - 1)),
                reads=[("rhs4", b), kP], writes=[kx, ("pb", 2)],
                relax=[kx, ("pb", 2)] if tc > 0 else [])
        P.op("vector", lambda e: e.tensor_copy(s_ixw1[:], pb[2][0:4, 0:256]),
             reads=[kx], writes=["ixw1"])
        for jc in range(2):
            ci = ee * 2 + jc
            c4 = jc * 4
            kc = ("col", b)
            P.op("tensor", lambda e, jc=jc: e.transpose(
                pb[3][0:128, 0:4],
                s_ixw1[0:4, jc * 128:(jc + 1) * 128], s_idenf[0:4, 0:4]),
                reads=["ixw1", "idenf"], writes=[("pb", 3)])
            P.op("vector", lambda e, b=b, c4=c4: e.tensor_copy(
                s_col[b][:, c4:c4 + 4], pb[3][0:128, 0:4]),
                reads=[("pb", 3)], writes=[kc])
            P.op("vector", lambda e, b=b, c4=c4: e.scalar_tensor_tensor(
                s_col[b][:, c4:c4 + 1], s_col[b][:, c4 + 3:c4 + 4], 128.0,
                s_col[b][:, c4:c4 + 1], op0=ALU.mult, op1=ALU.add),
                reads=[kc], writes=[kc])
            P.op("vector", lambda e, b=b, c4=c4: e.tensor_scalar(
                s_col[b][:, c4:c4 + 1], s_col[b][:, c4:c4 + 1],
                float(b * T), None, op0=ALU.add),
                reads=[kc], writes=[kc])
            P.op("vector", lambda e, b=b, c4=c4: e.scalar_tensor_tensor(
                s_col[b][:, c4:c4 + 1], s_col[b][:, c4 + 1:c4 + 2], -BIG_IDX,
                s_col[b][:, c4:c4 + 1], op0=ALU.mult, op1=ALU.add),
                reads=[kc], writes=[kc])
            P.op("vector", lambda e, b=b, c4=c4: e.tensor_scalar(
                s_col[b][:, c4:c4 + 1], s_col[b][:, c4:c4 + 1], BIG_IDX, None,
                op0=ALU.add), reads=[kc], writes=[kc])
            P.op("vector", lambda e, b=b, ci=ci, c4=c4: e.tensor_copy(
                s_idxu[b][:, ci:ci + 1], s_col[b][:, c4:c4 + 1]),
                reads=[kc], writes=[("idxu", b)])
            P.op("vector", lambda e, b=b, ci=ci, c4=c4: e.tensor_copy(
                s_wcol[b][:, ci:ci + 1], s_col[b][:, c4 + 2:c4 + 3]),
                reads=[kc], writes=[("wcol", b)])

    def gathers(ee, b):
        xsi = b % 2
        kxs = ("xselT", xsi)
        for jc in range(2):
            ci = ee * 2 + jc
            P.op("gpsimd", lambda e, b=b, ci=ci, jc=jc: e.indirect_dma_start(
                out=s2_xsel[jc][:], out_offset=None,
                in_=xn.rearrange("b t d -> (b t) d"),
                in_offset=IndirectOffsetOnAxis(
                    s_idxu[b][:, ci:ci + 1], axis=0),
                bounds_check=bc_reg, oob_is_err=False),
                reads=[("idxu", b)], writes=[("xsel", jc)], inc=16)
            transpose_tile(
                s2_xsel[jc], ("xsel", jc),
                s2_xselT[xsi][:].rearrange("p (c j) -> p c j", j=256)[
                    :, :, jc * 128:(jc + 1) * 128],
                kxs)

    def compute(ee, b):
        sel = ee % 2
        xsi = b % 2
        kxs = ("xselT", xsi)
        khs = ("hs", xsi)
        for fc in range(16):
            for dc in range(ND):
                P.op("tensor", lambda e, dc=dc, fc=fc, sel=sel, xsi=xsi: e.matmul(
                    pb[0][:, 0:256],
                    s2_w1[sel][:, dc * F_EXP + fc * 128: dc * F_EXP + (fc + 1) * 128],
                    s2_xselT[xsi][:, dc * 256:(dc + 1) * 256],
                    start=(dc == 0), stop=(dc == ND - 1)),
                    reads=[("w1", sel), kxs], writes=[("pb", 0)],
                    relax=[("pb", 0)] if dc > 0 else [])
                P.op("tensor", lambda e, dc=dc, fc=fc, sel=sel, xsi=xsi: e.matmul(
                    pb[1][:, 0:256],
                    s2_w2[sel][:, dc * F_EXP + fc * 128: dc * F_EXP + (fc + 1) * 128],
                    s2_xselT[xsi][:, dc * 256:(dc + 1) * 256],
                    start=(dc == 0), stop=(dc == ND - 1)),
                    reads=[("w2", sel), kxs], writes=[("pb", 1)],
                    relax=[("pb", 1)] if dc > 0 else [])
            P.op("scalar", lambda e: e.activation(
                s2_hg[:], pb[0][:, 0:256], AF.Gelu),
                reads=[("pb", 0)], writes=["hg2"])
            P.op("vector", lambda e, fc=fc, xsi=xsi: e.tensor_tensor(
                s2_hs[xsi][:, fc * 256:(fc + 1) * 256], s2_hg[:],
                pb[1][:, 0:256], op=ALU.mult),
                reads=["hg2", ("pb", 1)], writes=[khs])
        for jc in range(2):
            ci = ee * 2 + jc
            for half in range(2):
                kpd = ("pb", 4 + half)
                for fc in range(16):
                    P.op("tensor", lambda e, fc=fc, jc=jc, half=half,
                         xsi=xsi: e.matmul(
                        pb[4 + half][:],
                        s2_hs[xsi][:, fc * 256 + jc * 128: fc * 256 + (jc + 1) * 128],
                        s2_w3[:, fc * D + half * 512: fc * D + half * 512 + 512],
                        start=(fc == 0), stop=(fc == 15)),
                        reads=[khs, "w3"], writes=[kpd],
                        relax=[kpd] if fc > 0 else [])
            i = b * 2 + jc
            keo = ("eo", i)
            P.op("vector", lambda e, b=b, ci=ci, i=i: e.tensor_scalar(
                s2_eo[i][:, 0:512], pb[4][:], s_wcol[b][:, ci:ci + 1], None,
                op0=ALU.mult),
                reads=[("pb", 4), ("wcol", b)], writes=[keo])
            P.op("vector", lambda e, b=b, ci=ci, i=i: e.tensor_scalar(
                s2_eo[i][:, 512:1024], pb[5][:], s_wcol[b][:, ci:ci + 1], None,
                op0=ALU.mult),
                reads=[("pb", 5), ("wcol", b)], writes=[keo])

    def scatters(ee, b):
        for jc in range(2):
            ci = ee * 2 + jc
            i = b * 2 + jc
            sc = P.op("gpsimd", lambda e, b=b, ci=ci, i=i: e.indirect_dma_start(
                out=moe.rearrange("b t d -> (b t) d"),
                out_offset=IndirectOffsetOnAxis(
                    s_idxu[b][:, ci:ci + 1], axis=0),
                in_=s2_eo[i][:], in_offset=None,
                bounds_check=bc_reg, oob_is_err=False,
                compute_op=ALU.add),
                reads=[("eo", i), ("idxu", b)], writes=[("moe", b)], inc=16)
            P.out_dmas.append(sc)

    load_w12(0)
    load_w3(0)
    for ee in range(E_EXP):
        for b in range(B_PER_CORE):
            machinery(ee, b)
            gathers(ee, b)
        if ee + 1 < E_EXP:
            load_w12(ee + 1)
        for b in range(B_PER_CORE):
            compute(ee, b)
        if ee + 1 < E_EXP:
            load_w3(ee + 1)
        for b in range(B_PER_CORE):
            scatters(ee, b)
    if not fused:
        P.finalize(block, sems, dma_sems)
    return P


def build_combine_ln1(nc, stack, P=None, mem=None, io=None):
    fused = P is not None
    if not fused:
        P = Prog(nc)
    x = io["x"] if fused else nc.dram_tensor("x", [B_PER_CORE, T, D], F32, kind="ExternalInput").ap()
    s0 = io["s0"] if fused else nc.dram_tensor("s0", [B_PER_CORE, T, D], BF16, kind="ExternalInput").ap()
    moe = io["moe"] if fused else nc.dram_tensor("moe", [B_PER_CORE, T, D], F32, kind="ExternalInput").ap()
    x1 = io["x1"] if fused else nc.dram_tensor("x1", [B_PER_CORE, T, D], F32, kind="ExternalOutput").ap()
    xn1 = io["xn1"] if fused else nc.dram_tensor("xn1", [B_PER_CORE, T, D], BF16, kind="ExternalOutput").ap()
    xq1 = io["xq1"] if fused else nc.dram_tensor("xq1", [B_PER_CORE, T, D], BF16, kind="ExternalOutput").ap()
    xs127 = io["xs127"] if fused else nc.dram_tensor("xs127", [B_PER_CORE, T], F32, kind="ExternalOutput").ap()

    if not fused:
        sems, dma_sems = make_sems(nc, stack)
        mem = Mem(nc, stack)
    en = stack.enter_context
    s_x = [mem.sb(f"c_x{i}", [128, D], F32) for i in range(2)]
    s_s0 = [mem.sb(f"c_s0{i}", [128, D], BF16) for i in range(2)]
    s_moe = [mem.sb(f"c_moe{i}", [128, D], F32) for i in range(2)]
    s_g0 = mem.sb("c_g0", [128, D], BF16)
    s_x1 = [mem.sb(f"c_x1{i}", [128, D], F32) for i in range(2)]
    s_tmp = mem.sb("c_tmp", [128, D], F32)
    s_tmp2 = mem.sb("c_tmp2", [128, D], F32)
    s_xn1 = [mem.sb(f"c_xn1{i}", [128, D], BF16) for i in range(2)]
    s_xq = [mem.sb(f"c_xq{i}", [128, D], BF16) for i in range(2)]
    s_junk = mem.sb("c_junk", [128, D], BF16)
    s_small = mem.sb("c_small", [128, 16], F32)
    block = en(nc.Block()) if not fused else None

    for b in range(B_PER_CORE):
        for tc in range(NT):
            i = tc % 2
            kx, ks, km = ("cx", i), ("cs", i), ("cm", i)
            sl = slice(tc * 128, (tc + 1) * 128)
            P.dma("sync", s_x[i][:], x[b, sl, :], writes=[kx])
            P.dma("sync", s_s0[i][:], s0[b, sl, :], writes=[ks])
            P.dma("sync", s_moe[i][:], moe[b, sl, :], writes=[km])
            P.op("scalar", lambda e, i=i: e.activation(
                s_g0[:], s_moe[i][:], AF.Silu), reads=[km], writes=["cg0"])
            P.op("vector", lambda e, i=i: e.tensor_tensor(
                s_g0[:], s_g0[:], s_s0[i][:], op=ALU.mult),
                reads=["cg0", ks], writes=["cg0"])
            kx1 = ("cx1", i)
            P.op("vector", lambda e, i=i: e.tensor_tensor(
                s_x1[i][:], s_x[i][:], s_g0[:], op=ALU.add),
                reads=[kx, "cg0"], writes=[kx1])
            P.dma("sync", x1[b, sl, :], s_x1[i][:], reads=[kx1], is_output=True)

            def consume(m_ap, r_ap, ksmall, i=i, kx1=kx1, b=b, tc=tc, sl=sl):
                kxn = ("cxn1", i)
                P.op("vector", lambda e, i=i: e.tensor_scalar(
                    s_xn1[i][:], s_x1[i][:], m_ap, r_ap,
                    op0=ALU.subtract, op1=ALU.mult),
                    reads=[kx1, ksmall], writes=[kxn])
                P.dma("sync", xn1[b, sl, :], s_xn1[i][:], reads=[kxn],
                      is_output=True)
                P.op("vector", lambda e, i=i: e.tensor_scalar(
                    s_tmp[:], s_x1[i][:], m_ap, None, op0=ALU.subtract),
                    reads=[kx1, ksmall], writes=["ctmp"])
                P.op("vector", lambda e: e.tensor_reduce(
                    s_small[:, 8:9], s_tmp[:], AX.X, ALU.max,
                    apply_absolute_value=True),
                    reads=["ctmp"], writes=["cq"])
                P.op("vector", lambda e: e.tensor_tensor(
                    s_small[:, 8:9], s_small[:, 8:9], r_ap, op=ALU.mult),
                    reads=["cq", ksmall], writes=["cq"])
                P.op("vector", lambda e: e.tensor_scalar(
                    s_small[:, 8:9], s_small[:, 8:9], 1e-5, None, op0=ALU.max),
                    reads=["cq"], writes=["cq"])
                P.op("vector", lambda e: e.reciprocal(
                    s_small[:, 9:10], s_small[:, 8:9]), reads=["cq"], writes=["cq"])
                P.op("vector", lambda e: e.tensor_scalar(
                    s_small[:, 10:11], s_small[:, 9:10], 127.0, None,
                    op0=ALU.mult), reads=["cq"], writes=["cq"])
                P.op("vector", lambda e: e.tensor_tensor(
                    s_small[:, 10:11], s_small[:, 10:11], r_ap, op=ALU.mult),
                    reads=["cq", ksmall], writes=["cq"])
                P.op("vector", lambda e: e.tensor_scalar(
                    s_small[:, 11:12], s_small[:, 8:9], 1.0 / 127.0, None,
                    op0=ALU.mult), reads=["cq"], writes=["cq"])
                P.dma("sync", xs127[b, sl], s_small[:, 11:12], reads=["cq"],
                      is_output=True)
                kxq = ("cxq", i)
                P.op("vector", lambda e: e.tensor_scalar(
                    s_tmp2[:], s_tmp[:], s_small[:, 10:11], MAGIC,
                    op0=ALU.mult, op1=ALU.add),
                    reads=["ctmp", "cq"], writes=["ctmp2"])
                P.op("vector", lambda e: e.tensor_scalar(
                    s_tmp2[:], s_tmp2[:], MAGIC, None, op0=ALU.subtract),
                    reads=["ctmp2"], writes=["ctmp2"])
                P.op("vector", lambda e, i=i: e.tensor_scalar(
                    s_xq[i][:], s_tmp2[:], 127.0, -128.0,
                    op0=ALU.min, op1=ALU.max),
                    reads=["ctmp2"], writes=[kxq])
                P.dma("sync", xq1[b, sl, :], s_xq[i][:], reads=[kxq],
                      is_output=True)

            emit_ln_tile(P, None, f"c{b}_{tc}", kx1, s_x1[i], s_junk, s_small,
                         0, consume)
    if not fused:
        P.finalize(block, sems, dma_sems)
    return P


def build_attn1(nc, stack, P=None, mem=None, io=None):
    fused = P is not None
    if not fused:
        P = Prog(nc)
    xn1 = io["xn1"] if fused else nc.dram_tensor("xn1", [B_PER_CORE, T, D], BF16, kind="ExternalInput").ap()
    wqkv1T = io["wqkv1T"] if fused else nc.dram_tensor("wqkv1T", [D, D], F32, kind="ExternalInput").ap()
    wo1T = io["wo1T"] if fused else nc.dram_tensor("wo1T", [D, D], F32, kind="ExternalInput").ap()
    s1 = io["s1"] if fused else nc.dram_tensor("s1", [B_PER_CORE, T, D], BF16, kind="ExternalOutput").ap()
    qtd = nc.dram_tensor("a1_qtd", [256, 128], BF16, kind="Internal").ap()

    if not fused:
        sems, dma_sems = make_sems(nc, stack)
        mem = Mem(nc, stack)
    en = stack.enter_context
    s_wq = mem.sb("a1_wq", [128, ND * D], BF16)
    s_wo = mem.sb("a1_wo", [128, ND * D], BF16)
    s_iden = mem.sb("a1_iden", [128, 128], BF16)
    s_idenf = mem.sb("a1_idenf", [128, 128], F32)
    s_scr128 = mem.sb("a1_scr128", [128, 129], F32)
    s_ones1f = mem.sb("a1_ones1f", [1, 128], F32)
    s_mcol = mem.sb("a1_mcol", [128, HEADS], F32)
    s_xn = [mem.sb(f"a1_xn{i}", [128, D], BF16) for i in range(2)]
    s_xnTt = mem.sb("a1_xnTt", [128, D], BF16)
    s_yT = mem.sb("a1_yT", [128, ND * T], BF16)
    s_ytok = mem.sb("a1_ytok", [128, NT * D], BF16)
    s_sq = mem.sb("a1_sq", [128, D], F32)
    s_q2 = mem.sb("a1_q2", [128, NT * HEADS], F32)
    s_cneg = mem.sb("a1_cneg", [128, NT * HEADS], F32)
    s_mrow = mem.sb("a1_mrow", [1, 2 * HEADS], F32)
    s_q2T = mem.sb("a1_q2T", [128, 256], BF16)
    s_ka = [mem.sb(f"a1_ka{i}", [65, T], BF16) for i in range(2)]
    s_qa = [mem.sb(f"a1_qa{i}", [65, T], BF16) for i in range(2)]
    s_vaug = [mem.sb(f"a1_vaug{i}", [128, 65], BF16) for i in range(2)]
    s_se = [mem.sb(f"a1_se{i}", [128, T], BF16) for i in range(2)]
    s_drow = mem.sb("a1_drow", [1, T], F32)
    s_au = mem.sb("a1_au", [128, 512], BF16)
    s_aT = mem.sb("a1_aT", [128, ND * T], BF16)
    s_s1 = [mem.sb(f"a1_s1{i}", [128, D], BF16) for i in range(2)]
    pS = [mem.ps([128, 1024], F32) for _ in range(2)]
    pv = [mem.ps([128, 512], F32) for g in range(3)]
    pBt = mem.pB[:]
    pv.append(pBt.bitcast(F32))
    kB = ("pav", 3)
    block = en(nc.Block()) if not fused else None

    emit_identity(P, s_iden, s_scr128)
    P.op("vector", lambda e: e.tensor_copy(s_idenf[:], s_iden[:]),
         reads=["iden"], writes=["idenf"])
    P.op("vector", lambda e: e.memset(s_ones1f[:], 1.0), writes=["ones1f"])
    for i in range(2):
        P.op("vector", lambda e, i=i: e.memset(s_ka[i][64:65, :], 1.0),
             writes=[("kaones", i)])
        P.op("vector", lambda e, i=i: e.memset(s_vaug[i][:, 64:65], 1.0),
             writes=[("vones", i)])
    for c in range(ND):
        P.dma("gpsimd", s_wq[:, c * D:(c + 1) * D], wqkv1T[c * 128:(c + 1) * 128, :],
              writes=[("wq", c)])
        P.dma("gpsimd", s_wo[:, c * D:(c + 1) * D], wo1T[c * 128:(c + 1) * 128, :],
              writes=[("wo", c)])

    for b in range(B_PER_CORE):
        for tc in range(NT):
            i = tc % 2
            kxn = ("axn", i)
            P.dma("sync", s_xn[i][:], xn1[b, tc * 128:(tc + 1) * 128, :],
                  writes=[kxn])
            for c in range(ND):
                P.op("tensor", lambda e, c=c, i=i: e.transpose(
                    pBt[:, c * 128:(c + 1) * 128],
                    s_xn[i][:, c * 128:(c + 1) * 128], s_iden[:]),
                    reads=[kxn, "iden"], writes=[kB])
            P.op("vector", lambda e: e.tensor_copy(s_xnTt[:], pBt[:]),
                 reads=[kB], writes=["xnTt"])
            for half in range(2):
                for dc in range(ND):
                    P.op("tensor", lambda e, half=half, dc=dc: e.matmul(
                        pS[0][:, half * 512:(half + 1) * 512],
                        s_xnTt[:, dc * 128:(dc + 1) * 128],
                        s_wq[:, dc * D + half * 512: dc * D + (half + 1) * 512],
                        start=(dc == 0), stop=(dc == ND - 1)),
                        reads=[("wq", dc), "xnTt"], writes=[("pS", 0)],
                        relax=[("pS", 0)] if dc > 0 else [])
            P.op("vector", lambda e, tc=tc: e.tensor_copy(
                s_ytok[:, tc * D:(tc + 1) * D], pS[0][:]),
                reads=[("pS", 0)], writes=["ytok"])
            for c in range(ND):
                P.op("tensor", lambda e, c=c, tc=tc: e.transpose(
                    pBt[:, c * 128:(c + 1) * 128],
                    s_ytok[:, tc * D + c * 128: tc * D + (c + 1) * 128],
                    s_iden[:]),
                    reads=["ytok", "iden"], writes=[kB])
            P.op("vector", lambda e, tc=tc: e.tensor_copy(
                s_yT[:].rearrange("p (c t) -> p c t", t=T)[:, :, tc * 128:(tc + 1) * 128],
                pBt[:].rearrange("p (c t) -> p c t", t=128)),
                reads=[kB], writes=["yT"])

        for tc in range(NT):
            P.op("scalar", lambda e, tc=tc: e.activation(
                s_sq[:], s_ytok[:, tc * D:(tc + 1) * D], AF.Square),
                reads=["ytok"], writes=["sq"])
            P.op("vector", lambda e, tc=tc: e.tensor_reduce(
                s_q2[:, tc * HEADS:(tc + 1) * HEADS],
                s_sq[:].rearrange("p (h f) -> p h f", f=HD), AX.X, ALU.add),
                reads=["sq"], writes=["q2"])
        P.op("scalar", lambda e: e.activation(s_q2[:], s_q2[:], AF.Sqrt),
             reads=["q2"], writes=["q2"])
        P.op("gpsimd", lambda e: e.tensor_reduce(
            s_cneg[0:1, 0:256], s_q2[:], AX.C, ALU.max),
            reads=["q2"], writes=["cneg"])
        P.op("vector", lambda e: e.tensor_reduce(
            s_mrow[0:1, 0:HEADS],
            s_cneg[0:1, 0:256].rearrange("o (tc h) -> o h tc", h=HEADS),
            AX.X, ALU.max), reads=["cneg"], writes=["mrow"])
        P.op("tensor", lambda e: e.matmul(
            pS[0][:, 0:HEADS], s_ones1f[:], s_mrow[0:1, 0:HEADS],
            start=True, stop=True), reads=["ones1f", "mrow"], writes=[("pS", 0)])
        P.op("vector", lambda e: e.tensor_copy(s_mcol[:], pS[0][:, 0:HEADS]),
             reads=[("pS", 0)], writes=["mcol"])
        for h in range(HEADS):
            P.op("vector", lambda e, h=h: e.tensor_scalar(
                s_cneg[:].rearrange("p (tc hh) -> p tc hh", hh=HEADS)[:, :, h:h + 1],
                s_q2[:].rearrange("p (tc hh) -> p tc hh", hh=HEADS)[:, :, h:h + 1],
                s_mcol[:, h:h + 1], -1.0, op0=ALU.mult, op1=ALU.mult),
                reads=["q2", "mcol"], writes=["cneg"])
        for half in range(2):
            P.op("tensor", lambda e, half=half: e.transpose(
                pS[0][:, 0:128], s_cneg[:, half * 128:(half + 1) * 128],
                s_idenf[:]), reads=["cneg", "idenf"], writes=[("pS", 0)])
            P.op("vector", lambda e, half=half: e.tensor_copy(
                s_q2T[:, half * 128:(half + 1) * 128], pS[0][:, 0:128]),
                reads=[("pS", 0)], writes=["q2T"])
            P.dma("sync", qtd[half * 128:(half + 1) * 128, :],
                  s_q2T[:, half * 128:(half + 1) * 128],
                  reads=["q2T"], writes=["qtd"])

        def fill_kqa(h):
            i = h % 2
            po = (h % 2) * 64
            co = h // 2
            P.op("gpsimd", lambda e, po=po, co=co, i=i: e.tensor_copy(
                s_ka[i][0:64, :], s_yT[po:po + 64, co * T:(co + 1) * T]),
                reads=["yT"], writes=[("ka", i)])
            P.op("gpsimd", lambda e, po=po, co=co, i=i: e.tensor_copy(
                s_qa[i][0:64, :], s_yT[po:po + 64, co * T:(co + 1) * T]),
                reads=["yT"], writes=[("qa", i)])
            P.dma("sync", s_qa[i][64:65, :].rearrange("o (tc p) -> o tc p", p=128),
                  qtd.rearrange("(tc h) p -> h tc p", h=HEADS)[h].unsqueeze(0),
                  reads=["qtd"], writes=[("qa", i)])

        fill_kqa(0)
        for h in range(HEADS):
            hi = h % 2
            po = (h % 2) * 64
            co = h // 2
            if h + 1 < HEADS:
                fill_kqa(h + 1)
            for ktc in range(NT):
                ip = ktc % 2
                kse = ("se", ip)
                kvau = ("vaug", ip)
                P.op("vector", lambda e, ip=ip, h=h, ktc=ktc: e.tensor_copy(
                    s_vaug[ip][:, 0:64],
                    s_ytok[:, ktc * D + h * 64: ktc * D + (h + 1) * 64]),
                    reads=["ytok", ("vones", ip)], writes=[kvau])
                for pshalf in range(2):
                    for qh in range(2):
                        qg = pshalf * 2 + qh
                        kps = ("pS", pshalf)
                        P.op("tensor", lambda e, ktc=ktc, qg=qg, qh=qh,
                             pshalf=pshalf, hi=hi: e.matmul(
                                 pS[pshalf][:, qh * 512:(qh + 1) * 512],
                                 s_ka[hi][:, ktc * 128:(ktc + 1) * 128],
                                 s_qa[hi][:, qg * 512:(qg + 1) * 512],
                                 start=True, stop=True),
                             reads=[("ka", hi), ("qa", hi), ("kaones", hi)],
                             writes=[kps], relax=[kps])
                    P.op("scalar", lambda e, ip=ip, pshalf=pshalf: e.activation(
                        s_se[ip][:, pshalf * 1024:(pshalf + 1) * 1024],
                        pS[pshalf][:], AF.Exp, scale=0.125),
                        reads=[("pS", pshalf)], writes=[kse])
                for qg in range(4):
                    kpa = ("pav", qg)
                    P.op("tensor", lambda e, ktc=ktc, qg=qg, ip=ip: e.matmul(
                        pv[qg][0:65, 0:512], s_vaug[ip][:],
                        s_se[ip][:, qg * 512:(qg + 1) * 512],
                        start=(ktc == 0), stop=(ktc == NT - 1)),
                        reads=[kvau, kse], writes=[kpa],
                        relax=[kpa] if ktc > 0 else [])
            for qg in range(4):
                P.op("vector", lambda e, qg=qg: e.tensor_copy(
                    s_drow[0:1, qg * 512:(qg + 1) * 512], pv[qg][64:65, 0:512]),
                    reads=[("pav", qg)], writes=["drow"])
            P.op("scalar", lambda e: e.activation(s_drow[:], s_drow[:], AF.Ln),
                 reads=["drow"], writes=["drow"])
            P.op("scalar", lambda e: e.activation(
                s_drow[:], s_drow[:], AF.Exp, scale=-1.0),
                reads=["drow"], writes=["drow"])
            for qg in range(4):
                P.op("tensor", lambda e, qg=qg: e.matmul(
                    pS[0][0:64, 0:512], s_ones1f[0:1, 0:64],
                    s_drow[0:1, qg * 512:(qg + 1) * 512], start=True, stop=True),
                    reads=["ones1f", "drow"], writes=[("pS", 0)])
                P.op("vector", lambda e, qg=qg: e.tensor_copy(
                    s_au[0:64, 0:512], pv[qg][0:64, 0:512]),
                    reads=[("pav", qg)], writes=["au"])
                P.op("vector", lambda e, po=po, co=co, qg=qg: e.tensor_tensor(
                    s_aT[po:po + 64, co * T + qg * 512: co * T + (qg + 1) * 512],
                    s_au[0:64, 0:512], pS[0][0:64, 0:512], op=ALU.mult),
                    reads=["au", ("pS", 0)], writes=["aT"])

        for tc in range(NT):
            i = tc % 2
            ks1 = ("s1t", i)
            for half in range(2):
                bank = pv[half]
                kpa = ("pav", half)
                for fc in range(ND):
                    P.op("tensor", lambda e, fc=fc, tc=tc, half=half, bank=bank: e.matmul(
                        bank[:, 0:512],
                        s_aT[:, fc * T + tc * 128: fc * T + (tc + 1) * 128],
                        s_wo[:, fc * D + half * 512: fc * D + (half + 1) * 512],
                        start=(fc == 0), stop=(fc == ND - 1)),
                        reads=["aT", ("wo", fc)], writes=[kpa],
                        relax=[kpa] if fc > 0 else [])
                P.op("vector", lambda e, i=i, half=half, bank=bank: e.tensor_copy(
                    s_s1[i][:, half * 512:(half + 1) * 512], bank[:, 0:512]),
                    reads=[kpa], writes=[ks1])
            P.dma("sync", s1[b, tc * 128:(tc + 1) * 128, :], s_s1[i][:],
                  reads=[ks1], is_output=True)
    if not fused:
        P.finalize(block, sems, dma_sems)
    return P


def build_ffn1(nc, stack, P=None, mem=None, io=None):
    fused = P is not None
    if not fused:
        P = Prog(nc)
    xq1 = io["xq1"] if fused else nc.dram_tensor("xq1", [B_PER_CORE, T, D], BF16, kind="ExternalInput").ap()
    xs127 = io["xs127"] if fused else nc.dram_tensor("xs127", [B_PER_CORE, T], F32, kind="ExternalInput").ap()
    x1 = io["x1"] if fused else nc.dram_tensor("x1", [B_PER_CORE, T, D], F32, kind="ExternalInput").ap()
    s1 = io["s1"] if fused else nc.dram_tensor("s1", [B_PER_CORE, T, D], BF16, kind="ExternalInput").ap()
    W1T = io["W1T"] if fused else nc.dram_tensor("W1T", [D, H_FFN], F32, kind="ExternalInput").ap()
    b1c = io["b1c"] if fused else nc.dram_tensor("b1c", [128, 32], F32, kind="ExternalInput").ap()
    W2T = io["W2T"] if fused else nc.dram_tensor("W2T", [H_FFN, D], F32, kind="ExternalInput").ap()
    b2r = io["b2r"] if fused else nc.dram_tensor("b2r", [1, D], F32, kind="ExternalInput").ap()
    out = io["out"] if fused else nc.dram_tensor("out", [B_PER_CORE, T, D], F32, kind="ExternalOutput").ap()

    if not fused:
        sems, dma_sems = make_sems(nc, stack)
        mem = Mem(nc, stack)
    en = stack.enter_context
    s_w1t = mem.sb("f_w1t", [128, ND * H_FFN], BF16)
    s_w2t = mem.sb("f_w2t", [128, 32 * D], BF16)
    s_wld = mem.sb("f_wld", [128, 1024], F32)
    s_iden = mem.sb("f_iden", [128, 128], BF16)
    s_idenf = mem.sb("f_idenf", [128, 128], F32)
    s_scr128 = mem.sb("f_scr128", [128, 129], F32)
    s_ones1f = mem.sb("f_ones1f", [1, 128], F32)
    s_ones1b = mem.sb("f_ones1b", [1, 128], BF16)
    s_acc = mem.sb("f_acc", [128, 33], F32)
    s_ws = mem.sb("f_ws", [1, 4], F32)
    s_wscol = mem.sb("f_wscol", [128, 4], F32)
    s_hstore = mem.sb("f_hstore", [128, 32 * 512], BF16)
    s_xq = [mem.sb(f"f_xq{i}", [128, D], BF16) for i in range(2)]
    s_xqT = mem.sb("f_xqT", [128, ND * 512], BF16)
    _g1view = None
    s_xsrow = mem.sb("f_xsrow", [1, T], F32)
    s_hsrow = s_xsrow[0:1, 0:512]
    s_rq2 = s_xsrow[0:1, 512:1024]
    s_sclrow = mem.sb("f_sclrow", [1, T], F32)
    s_sclb = mem.sb("f_sclb", [128, 512], F32)
    s_hdq = mem.sb("f_hdq", [128, 512], F32)
    s_hmax = mem.sb("f_hmax", [128, 512], BF16)
    s_hmin = mem.sb("f_hmin", [128, 512], BF16)
    s_rq2b = s_sclb
    s_hquant = s_hdq
    s_scl2 = mem.sb("f_scl2", [128, 4], F32)
    s_b2b = mem.sb("f_b2b", [128, D], BF16)
    s_x1 = [mem.sb("f_x1", [128, D], F32)] * 2
    s_s1 = s_xq
    s_g1 = None
    s_o = s_x1
    ph = [mem.ps([128, 512], F32) for i in range(2)]
    pg = [mem.ps([128, 512], F32) for i in range(2)]
    pbc = mem.ps([128, 512], F32)
    pt = mem.ps([128, 1024], BF16)
    block = en(nc.Block()) if not fused else None

    emit_identity(P, s_iden, s_scr128)
    P.op("vector", lambda e: e.tensor_copy(s_idenf[:], s_iden[:]),
         reads=["iden"], writes=["idenf"])
    P.op("vector", lambda e: e.memset(s_ones1f[:], 1.0), writes=["ones1f"])
    P.op("vector", lambda e: e.memset(s_ones1b[:], 1.0), writes=["ones1f"])
    P.dma("gpsimd", s_b2b[0:1, :], b2r, writes=["b2r"])
    s_onescolf = mem.sb("f_onescolf", [128, 1], F32)
    s_b1 = mem.sb("f_b1", [128, 32], F32)
    P.dma("sync", s_b1[:], b1c, writes=["b1"])
    P.op("vector", lambda e: e.memset(s_onescolf[:], 1.0), writes=["onescolf"])

    for wsidx, (WT, wtern, nchunk, width, kw) in enumerate([
            (W1T, s_w1t, ND, H_FFN, "w1t"),
            (W2T, s_w2t, 32, D, "w2t")]):
        nw = max(1, width // 1024)
        wt = width // nw
        for c in range(nchunk):
            for wh in range(nw):
                P.dma("sync", s_wld[:, 0:wt],
                      WT[c * 128:(c + 1) * 128, wh * wt:(wh + 1) * wt],
                      writes=["wld"], tag=f"abs{wsidx}_{c}_{wh}")
                P.op("scalar", lambda e, c=c, wh=wh, nw=nw, wt=wt: e.activation(
                    s_hstore[:, 0:wt], s_wld[:, 0:wt], AF.Abs,
                    accum_out=s_acc[:, c * nw + wh:c * nw + wh + 1]),
                    reads=["wld"], writes=["hstore", "acc"])
        P.op("vector", lambda e, nchunk=nchunk, nw=nw: e.tensor_reduce(
            s_acc[:, 32:33], s_acc[:, 0:nchunk * nw], AX.X, ALU.add),
            reads=["acc"], writes=["acc"])
        P.op("tensor", lambda e: e.matmul(
            pbc[0:1, 0:1], s_acc[:, 32:33], s_onescolf[:],
            start=True, stop=True),
            reads=["acc", "onescolf"], writes=[("pbc",)])
        P.op("vector", lambda e, wsidx=wsidx, width=width, nchunk=nchunk: e.tensor_scalar(
            s_ws[0:1, wsidx:wsidx + 1], pbc[0:1, 0:1],
            1.0 / (nchunk * 128 * width), 1e-5, op0=ALU.mult, op1=ALU.max),
            reads=[("pbc",)], writes=["ws"])
        P.op("vector", lambda e, wsidx=wsidx: e.reciprocal(
            s_ws[0:1, wsidx + 2:wsidx + 3], s_ws[0:1, wsidx:wsidx + 1]),
            reads=["ws"], writes=["ws"])
        P.op("tensor", lambda e, wsidx=wsidx: e.matmul(
            pbc[:, 0:1], s_ones1f[:], s_ws[0:1, wsidx + 2:wsidx + 3],
            start=True, stop=True), reads=["ones1f", "ws"], writes=[("pbc",)])
        P.op("vector", lambda e, wsidx=wsidx: e.tensor_copy(
            s_wscol[:, wsidx:wsidx + 1], pbc[:, 0:1]),
            reads=[("pbc",)], writes=["wscol"])
        if wsidx == 1:
            P.op("tensor", lambda e: e.matmul(
                pbc[:, 0:1], s_ones1f[:], s_ws[0:1, 1:2],
                start=True, stop=True), reads=["ones1f", "ws"], writes=[("pbc",)])
            P.op("vector", lambda e: e.tensor_copy(
                s_wscol[:, 2:3], pbc[:, 0:1]), reads=[("pbc",)], writes=["ws"])
        for c in range(nchunk):
            for wh in range(nw):
                P.dma("sync", s_wld[:, 0:wt],
                      WT[c * 128:(c + 1) * 128, wh * wt:(wh + 1) * wt],
                      writes=["wld"], tag=f"tern{wsidx}_{c}_{wh}")
                P.op("vector", lambda e, wsidx=wsidx, wt=wt: e.tensor_scalar(
                    s_wld[:, 0:wt], s_wld[:, 0:wt],
                    s_wscol[:, wsidx:wsidx + 1], None, op0=ALU.mult),
                    reads=["wld", "wscol"], writes=["wld"])
                P.op("vector", lambda e, wt=wt: e.tensor_scalar(
                    s_wld[:, 0:wt], s_wld[:, 0:wt], MAGIC, MAGIC,
                    op0=ALU.add, op1=ALU.subtract),
                    reads=["wld"], writes=["wld"])
                P.op("vector", lambda e, c=c, wh=wh, wt=wt, width=width,
                     wtern=wtern, kw=kw: e.tensor_scalar(
                    wtern[:, c * width + wh * wt: c * width + (wh + 1) * wt],
                    s_wld[:, 0:wt], 1.0, -1.0, op0=ALU.min, op1=ALU.max),
                    reads=["wld"], writes=[kw])

    for half in range(2):
        P.op("tensor", lambda e, half=half: e.matmul(
            pbc[:], s_ones1b[:], s_b2b[0:1, half * 512:(half + 1) * 512],
            start=True, stop=True), reads=["ones1f", "b2r"], writes=[("pbc",)])
        P.op("vector", lambda e, half=half: e.tensor_copy(
            s_hdq[:, 0:512], pbc[:]),
            reads=[("pbc",)], writes=[("b2s", half)])
        P.op("vector", lambda e, half=half: e.tensor_copy(
            s_b2b[:, half * 512:(half + 1) * 512], s_hdq[:, 0:512]),
            reads=[("b2s", half)], writes=["b2b"])

    for b in range(B_PER_CORE):
        P.dma("sync", s_xsrow[:], xs127[b].unsqueeze(0), writes=["xsrow"])
        P.op("vector", lambda e: e.tensor_scalar(
            s_sclrow[:], s_xsrow[:], s_ws[0:1, 0:1], None, op0=ALU.mult),
            reads=["xsrow", "ws"], writes=["sclrow"])
        for tg in range(4):
            for tq in range(4):
                i = tq % 2
                tci = tg * 4 + tq
                kxq = ("fxq", i)
                P.dma("sync", s_xq[i][:], xq1[b, tci * 128:(tci + 1) * 128, :],
                      writes=[kxq])
                for c in range(ND):
                    P.op("tensor", lambda e, c=c, i=i: e.transpose(
                        pt[:, c * 128:(c + 1) * 128],
                        s_xq[i][:, c * 128:(c + 1) * 128], s_iden[:]),
                        reads=[kxq, "iden"], writes=[("pt", c)])
                P.op("vector", lambda e, tq=tq: e.tensor_copy(
                    s_xqT[:].rearrange("p (c t) -> p c t", t=512)[
                        :, :, tq * 128:(tq + 1) * 128],
                    pt[:].rearrange("p (c t) -> p c t", t=128)),
                    reads=[("pt", c) for c in range(ND)], writes=["xqT"])
            P.op("tensor", lambda e, tg=tg: e.matmul(
                pbc[:], s_ones1f[:], s_sclrow[0:1, tg * 512:(tg + 1) * 512],
                start=True, stop=True), reads=["ones1f", "sclrow"], writes=[("pbc",)])
            P.op("vector", lambda e: e.tensor_copy(s_sclb[:], pbc[:]),
                 reads=[("pbc",)], writes=["sclb"])
            for Hc in range(32):
                php = ph[Hc % 2]
                kph = ("ph", Hc % 2)
                for dc in range(ND):
                    P.op("tensor", lambda e, Hc=Hc, dc=dc, php=php: e.matmul(
                        php[:], s_w1t[:, dc * H_FFN + Hc * 128: dc * H_FFN + (Hc + 1) * 128],
                        s_xqT[:, dc * 512:(dc + 1) * 512],
                        start=(dc == 0), stop=(dc == ND - 1)),
                        reads=["w1t", "xqT"], writes=[kph],
                        relax=[kph] if dc > 0 else [])
                P.op("vector", lambda e, php=php: e.tensor_tensor(
                    s_hdq[:], php[:], s_sclb[:], op=ALU.mult),
                    reads=[kph, "sclb"], writes=["hdq"])
                P.op("scalar", lambda e, Hc=Hc: e.activation(
                    s_hstore[:, Hc * 512:(Hc + 1) * 512], s_hdq[:], AF.Gelu,
                    bias=s_b1[:, Hc:Hc + 1]),
                    reads=["hdq", "b1"], writes=[("hst", Hc)])
                P.op("vector", lambda e, Hc=Hc: e.tensor_tensor(
                    s_hmax[:], s_hmax[:], s_hstore[:, Hc * 512:(Hc + 1) * 512],
                    op=ALU.max) if Hc else e.tensor_copy(
                    s_hmax[:], s_hstore[:, 0:512]),
                    reads=["hmax", ("hst", Hc)] if Hc else [("hst", Hc)],
                    writes=["hmax"])
                P.op("vector", lambda e, Hc=Hc: e.tensor_tensor(
                    s_hmin[:], s_hmin[:], s_hstore[:, Hc * 512:(Hc + 1) * 512],
                    op=ALU.min) if Hc else e.tensor_copy(
                    s_hmin[:], s_hstore[:, 0:512]),
                    reads=["hmin", ("hst", Hc)] if Hc else [("hst", Hc)],
                    writes=["hmin"])
            P.op("vector", lambda e: e.tensor_scalar(
                s_hdq[:], s_hmin[:], -1.0, None, op0=ALU.mult),
                reads=["hmin"], writes=["hdq"])
            P.op("vector", lambda e: e.tensor_tensor(
                s_hdq[:], s_hdq[:], s_hmax[:], op=ALU.max),
                reads=["hdq", "hmax"], writes=["hdq"])
            for sh in (64, 32):
                P.op("vector", lambda e, sh=sh: e.tensor_copy(
                    s_sclb[0:sh, :], s_hdq[sh:2 * sh, :]),
                    reads=["hdq"], writes=["sclb"])
                P.op("vector", lambda e, sh=sh: e.tensor_tensor(
                    s_hdq[0:sh, :], s_hdq[0:sh, :], s_sclb[0:sh, :],
                    op=ALU.max), reads=["hdq", "sclb"], writes=["hdq"])
            for tq in range(4):
                P.op("tensor", lambda e, tq=tq: e.transpose(
                    pbc[0:128, 0:32], s_hdq[0:32, tq * 128:(tq + 1) * 128],
                    s_idenf[0:32, 0:32]),
                    reads=["hdq", "idenf"], writes=[("pbc",)])
                P.op("vector", lambda e, tq=tq: e.tensor_reduce(
                    s_scl2[:, tq:tq + 1], pbc[0:128, 0:32], AX.X, ALU.max),
                    reads=[("pbc",)], writes=["scl2"])
                P.op("vector", lambda e, tq=tq: e.tensor_scalar(
                    s_scl2[:, tq:tq + 1], s_scl2[:, tq:tq + 1], 1e-5, None,
                    op0=ALU.max), reads=["scl2"], writes=["scl2"])
                P.op("vector", lambda e, tq=tq: e.reciprocal(
                    s_hdq[:, tq:tq + 1], s_scl2[:, tq:tq + 1]),
                    reads=["scl2"], writes=["hdq"])
                P.op("vector", lambda e, tq=tq: e.tensor_scalar(
                    s_hdq[:, tq:tq + 1], s_hdq[:, tq:tq + 1], 127.0, None,
                    op0=ALU.mult), reads=["hdq"], writes=["hdq"])
                P.op("tensor", lambda e, tq=tq: e.transpose(
                    pbc[0:1, 128:256], s_hdq[:, tq:tq + 1],
                    s_idenf[:]), reads=["hdq", "idenf"], writes=[("pbc",)])
                P.op("vector", lambda e, tq=tq: e.tensor_copy(
                    s_rq2[0:1, tq * 128:(tq + 1) * 128], pbc[0:1, 128:256]),
                    reads=[("pbc",)], writes=["rq2"])
            P.op("vector", lambda e: e.tensor_scalar(
                s_scl2[:], s_scl2[:], s_wscol[:, 2:3], None, op0=ALU.mult),
                reads=["scl2", "ws"], writes=["scl2"])
            P.op("vector", lambda e: e.tensor_scalar(
                s_scl2[:], s_scl2[:], 1.0 / 127.0, None, op0=ALU.mult),
                reads=["scl2"], writes=["scl2"])
            P.op("tensor", lambda e: e.matmul(
                pbc[:], s_ones1f[:], s_rq2, start=True, stop=True),
                reads=["ones1f", "rq2"], writes=[("pbc",)])
            P.op("vector", lambda e: e.tensor_copy(s_rq2b[:], pbc[:]),
                 reads=[("pbc",)], writes=["sclb"])
            for Hc in range(32):
                P.op("vector", lambda e, Hc=Hc: e.tensor_tensor(
                    s_hquant[:], s_hstore[:, Hc * 512:(Hc + 1) * 512],
                    s_rq2b[:], op=ALU.mult),
                    reads=[("hst", Hc), "sclb"], writes=["hdq"])
                P.op("vector", lambda e: e.tensor_scalar(
                    s_hquant[:], s_hquant[:], MAGIC, MAGIC,
                    op0=ALU.add, op1=ALU.subtract),
                    reads=["hdq"], writes=["hdq"])
                P.op("vector", lambda e, Hc=Hc: e.tensor_scalar(
                    s_hstore[:, Hc * 512:(Hc + 1) * 512], s_hquant[:],
                    127.0, -128.0, op0=ALU.min, op1=ALU.max),
                    reads=["hdq"], writes=[("hst", Hc)])
            for tq in range(4):
                i = tq % 2
                tci = tg * 4 + tq
                kx1, ks1 = ("fx1",), ("fs1", i)
                P.dma("sync", s_x1[i][:], x1[b, tci * 128:(tci + 1) * 128, :],
                      writes=[kx1])
                P.dma("sync", s_s1[i][:], s1[b, tci * 128:(tci + 1) * 128, :],
                      writes=[ks1, ("fxq", i)])
                for half in range(2):
                    kpg = ("pg", half)
                    for Hc in range(32):
                        P.op("tensor", lambda e, Hc=Hc, tq=tq, half=half: e.matmul(
                            pg[half][:],
                            s_hstore[:, Hc * 512 + tq * 128: Hc * 512 + (tq + 1) * 128],
                            s_w2t[:, Hc * D + half * 512: Hc * D + (half + 1) * 512],
                            start=(Hc == 0), stop=(Hc == 31)),
                            reads=[("hst", Hc), "w2t"], writes=[kpg],
                            relax=[kpg] if Hc > 0 else [])
                    P.op("vector", lambda e, half=half, tq=tq: e.tensor_scalar(
                        s_xqT[:, 1024 + half * 512: 1024 + (half + 1) * 512], pg[half][:],
                        s_scl2[:, tq:tq + 1], None, op0=ALU.mult),
                        reads=[("pg", half), "scl2"], writes=["g1", "xqT"])
                P.op("vector", lambda e: e.tensor_tensor(
                    s_xqT[:, 1024:2048], s_xqT[:, 1024:2048], s_b2b[:], op=ALU.add),
                    reads=["g1", "b2b"], writes=["g1"])
                P.op("scalar", lambda e: e.activation(
                    s_xqT[:, 1024:2048], s_xqT[:, 1024:2048], AF.Silu), reads=["g1"], writes=["g1"])
                ko = ("fo",)
                P.op("vector", lambda e, i=i: e.tensor_tensor(
                    s_xqT[:, 1024:2048], s_xqT[:, 1024:2048], s_s1[i][:], op=ALU.mult),
                    reads=["g1", ks1], writes=["g1"])
                P.op("vector", lambda e, i=i: e.tensor_tensor(
                    s_o[i][:], s_x1[i][:], s_xqT[:, 1024:2048], op=ALU.add),
                    reads=[kx1, "g1"], writes=[ko, kx1])
                P.dma("sync", out[b, tci * 128:(tci + 1) * 128, :], s_o[i][:],
                      reads=[ko], is_output=True)
    if not fused:
        P.finalize(block, sems, dma_sems)
    return P



def prep_weights(inputs):
    tr = lambda a: np.ascontiguousarray(np.asarray(a).T)
    w = {}
    w["wqkvT"] = tr(inputs["L0_attn_w_qkv"])
    w["woT"] = tr(inputs["L0_attn_w_out"])
    w["gateT"] = tr(inputs["L0_ffn_gate_w"])
    w["WgT"] = np.ascontiguousarray(np.asarray(inputs["L0_ffn_Wg"]).transpose(0, 2, 1))
    w["WuT"] = np.ascontiguousarray(np.asarray(inputs["L0_ffn_Wu"]).transpose(0, 2, 1))
    w["WdT"] = np.ascontiguousarray(np.asarray(inputs["L0_ffn_Wd"]).transpose(0, 2, 1))
    w["WsgT"] = tr(inputs["L0_ffn_Ws_g"])
    w["WsuT"] = tr(inputs["L0_ffn_Ws_u"])
    w["WsdT"] = tr(inputs["L0_ffn_Ws_d"])
    w["wqkv1T"] = tr(inputs["L1_attn_W_qkv"])
    w["wo1T"] = tr(inputs["L1_attn_W_o"])
    w["W1T"] = tr(inputs["L1_ffn_W1"])
    w["b1c"] = np.ascontiguousarray(np.asarray(inputs["L1_ffn_b1"]).reshape(H_FFN // 128, 128).T)
    w["W2T"] = tr(inputs["L1_ffn_W2"])
    w["b2c"] = np.ascontiguousarray(np.asarray(inputs["L1_ffn_b2"]).reshape(D // 128, 128).T)
    w["b2r"] = np.ascontiguousarray(np.asarray(inputs["L1_ffn_b2"]).reshape(1, D))
    return w


def build_fused(nc, stack):
    P = Prog(nc)
    sems, dma_sems = make_sems(nc, stack)
    mem = Mem(nc, stack)
    io = {}

    def ext(name, shape, dt):
        io[name] = nc.dram_tensor(name, shape, dt, kind="ExternalInput").ap()

    def mid(name, shape, dt):
        io[name] = nc.dram_tensor(name, shape, dt, kind="Internal").ap()

    ext("x", [B_PER_CORE, T, D], F32)
    ext("wqkvT", [D, 3 * D], F32)
    ext("woT", [D, D], F32)
    ext("gateT", [D, E_EXP], F32)
    ext("WgT", [E_EXP, D, F_EXP], F32)
    ext("WuT", [E_EXP, D, F_EXP], F32)
    ext("WdT", [E_EXP, F_EXP, D], F32)
    ext("WsgT", [D, F_SH], F32)
    ext("WsuT", [D, F_SH], F32)
    ext("WsdT", [F_SH, D], F32)
    ext("wqkv1T", [D, D], F32)
    ext("wo1T", [D, D], F32)
    ext("W1T", [D, H_FFN], F32)
    ext("b1c", [128, 32], F32)
    ext("W2T", [H_FFN, D], F32)
    ext("b2r", [1, D], F32)
    mid("xn", [B_PER_CORE, T, D], BF16)
    mid("s0", [B_PER_CORE, T, D], BF16)
    mid("moe", [B_PER_CORE, T, D], F32)
    mid("x1", [B_PER_CORE, T, D], F32)
    mid("xn1", [B_PER_CORE, T, D], BF16)
    mid("xq1", [B_PER_CORE, T, D], BF16)
    mid("xs127", [B_PER_CORE, T], F32)
    mid("s1", [B_PER_CORE, T, D], BF16)
    io["out"] = nc.dram_tensor("out", [B_PER_CORE, T, D], F32,
                               kind="ExternalOutput").ap()

    for bld in (build_ln0, build_attn0, build_moe0, build_combine_ln1,
                build_attn1, build_ffn1):
        mem.reset()
        bld(nc, stack, P=P, mem=mem, io=io)
        P.fence()

    block = stack.enter_context(nc.Block())
    P.finalize(block, sems, dma_sems)
    return P


FUSED = True


def _launch(build_fn, in_maps):
    from contextlib import ExitStack
    with ExitStack() as stack:
        nc = bass.Bass("TRN2", target_bir_lowering=False,
                       detect_race_conditions=False)
        build_fn(nc, stack)
        res = run_bass_kernel_spmd(nc, in_maps, core_ids=list(range(NCORES)))
    return [{k: np.asarray(v) for k, v in r.items()} for r in res.results]


def kernel(**inputs):
    w = prep_weights(inputs)
    x = np.ascontiguousarray(np.asarray(inputs["L0_attn_x"], dtype=np.float32))
    shards = [x[c * B_PER_CORE:(c + 1) * B_PER_CORE] for c in range(NCORES)]

    if FUSED:
        rf = _launch(build_fused, [
            {"x": shards[c], "wqkvT": w["wqkvT"], "woT": w["woT"],
             "gateT": w["gateT"], "WgT": w["WgT"], "WuT": w["WuT"],
             "WdT": w["WdT"], "WsgT": w["WsgT"], "WsuT": w["WsuT"],
             "WsdT": w["WsdT"], "wqkv1T": w["wqkv1T"], "wo1T": w["wo1T"],
             "W1T": w["W1T"], "b1c": w["b1c"], "W2T": w["W2T"],
             "b2r": w["b2r"]} for c in range(NCORES)])
        out = np.concatenate([r["out"] for r in rf], axis=0)
        return out.astype(np.float32)

    r1 = _launch(build_ln0, [{"x": s} for s in shards])
    xn = [r["xn"] for r in r1]

    r2 = _launch(build_attn0, [
        {"xn": xn[c], "wqkvT": w["wqkvT"], "woT": w["woT"]}
        for c in range(NCORES)])
    s0 = [r["s0"] for r in r2]

    r3 = _launch(build_moe0, [
        {"xn": xn[c], "gateT": w["gateT"], "WgT": w["WgT"], "WuT": w["WuT"],
         "WdT": w["WdT"], "WsgT": w["WsgT"], "WsuT": w["WsuT"],
         "WsdT": w["WsdT"]} for c in range(NCORES)])
    moe = [r["moe"] for r in r3]

    r4 = _launch(build_combine_ln1, [
        {"x": shards[c], "s0": s0[c], "moe": moe[c]} for c in range(NCORES)])

    r5 = _launch(build_attn1, [
        {"xn1": r4[c]["xn1"], "wqkv1T": w["wqkv1T"], "wo1T": w["wo1T"]}
        for c in range(NCORES)])

    r6 = _launch(build_ffn1, [
        {"xq1": r4[c]["xq1"], "xs127": r4[c]["xs127"], "x1": r4[c]["x1"],
         "s1": r5[c]["s1"], "W1T": w["W1T"], "b1c": w["b1c"],
         "W2T": w["W2T"], "b2r": w["b2r"]} for c in range(NCORES)])

    out = np.concatenate([r["out"] for r in r6], axis=0)
    return out.astype(np.float32)
